# version 10
# baseline (speedup 1.0000x reference)
import math
import numpy as np
from contextlib import ExitStack

import concourse.bass as bass
import concourse.mybir as mybir
from concourse.bass_utils import run_bass_kernel_spmd

F32 = mybir.dt.float32
BF16 = mybir.dt.bfloat16
AF = mybir.ActivationFunctionType
ALU = mybir.AluOpType
AX = mybir.AxisListType

N = 2048
D = 1024
NCHUNK = 8
CPAD = 2052
NEGBIG = -float(2.0 ** 127)
PI = math.pi
HID = 2730
NBLK = 22
BAU = 13

PHASES = ("l0ffn_a", "l0att", "l0ffn_c", "l1ffn_d", "l1att", "l1ffn_f")
L0FFN_DEPTH = 2
L0FFN_MLP = True


def _f32c(a):
    return np.ascontiguousarray(np.asarray(a, dtype=np.float32))



def host_prep(inp):
    sh = {}
    sh["l0a_wq"] = _f32c(inp["L0_attn_Wq"])
    sh["l0a_wkv"] = _f32c(inp["L0_attn_Wkv"])
    sh["l0a_wo"] = _f32c(inp["L0_attn_Wo"])
    sh["relemb"] = _f32c(inp["L0_attn_rel_emb"])

    def bucket(rel):
        n = rel
        nb = 16
        ret = 0
        if n < 0:
            ret = nb
        n = abs(n)
        max_exact = nb // 2
        if n < max_exact:
            return ret + n
        v = max_exact + int(
            math.log(max(n, 1) / max_exact) / math.log(128 / max_exact) * (nb - max_exact))
        return ret + min(v, nb - 1)

    m436 = np.zeros((436, 32), dtype=np.float32)
    for u in range(436):
        d = 217 - u
        m436[u, bucket(d)] += 1.0
        m436[u, 31] -= 1.0
    sh["m436t"] = _f32c(m436.T)
    ediff = np.zeros((32, 1), dtype=np.float32)
    ediff[15, 0] = 1.0
    ediff[31, 0] = -1.0
    sh["ediff"] = ediff

    sh["l0f_wqkv"] = _f32c(inp["L0_ffn_wqkv"])
    sh["l0f_wo"] = _f32c(inp["L0_ffn_wo"])
    w1 = _f32c(inp["L0_ffn_w1"]); w2 = _f32c(inp["L0_ffn_w2"]); b1 = _f32c(inp["L0_ffn_b1"])
    HIDP = 128 * NBLK
    w1a = np.zeros((2, 1024, HIDP), np.float32); w1a[:, :, :HID] = w1[:, :, :HID]
    w1g = np.zeros((2, 1024, HIDP), np.float32); w1g[:, :, :HID] = w1[:, :, HID:]
    w1pb = np.zeros((2, NBLK, 1024, 256), np.float32)
    for b in range(NBLK):
        w1pb[:, b, :, :128] = w1a[:, :, 128 * b:128 * b + 128]
        w1pb[:, b, :, 128:] = w1g[:, :, 128 * b:128 * b + 128]
    sh["w1pb"] = _f32c(w1pb)
    w2p = np.zeros((2, HIDP, 1024), np.float32); w2p[:, :HID, :] = w2
    sh["w2pb"] = _f32c(w2p.reshape(2, NBLK, 128, 1024))
    b1a = np.zeros((2, HIDP), np.float32); b1a[:, :HID] = b1[:, :HID]
    b1g = np.zeros((2, HIDP), np.float32); b1g[:, :HID] = b1[:, HID:]
    b1p = np.zeros((2, 44, 128), np.float32)
    for b in range(NBLK):
        b1p[:, 2 * b] = b1a[:, 128 * b:128 * b + 128]
        b1p[:, 2 * b + 1] = b1g[:, 128 * b:128 * b + 128]
    sh["b1p"] = _f32c(b1p.reshape(88, 128))

    sh["l1a_wqT"] = _f32c(np.asarray(inp["L1_attn_Wq"]).T)
    sh["l1a_wkT"] = _f32c(np.asarray(inp["L1_attn_Wk"]).T)
    sh["l1a_wvT"] = _f32c(np.asarray(inp["L1_attn_Wv"]).T)
    sh["l1a_woT"] = _f32c(np.asarray(inp["L1_attn_Wo"]).T)
    sh["l1a_pT"] = _f32c(np.transpose(np.asarray(inp["L1_attn_P"]), (0, 2, 1)))
    sh["l1f_wT"] = _f32c(np.asarray(inp["L1_ffn_weight"]).T)

    vecs = np.zeros((12, 1024), np.float32)
    vecs[0] = _f32c(inp["L0_attn_bo"])
    vecs[1] = _f32c(inp["L1_attn_bq"])
    vecs[2] = _f32c(inp["L1_attn_bk"])
    vecs[3] = _f32c(inp["L1_attn_bv"])
    vecs[4] = _f32c(inp["L1_attn_bo"])
    lng = _f32c(inp["L0_ffn_ln_g"]); lnb = _f32c(inp["L0_ffn_ln_b"]); b2 = _f32c(inp["L0_ffn_b2"])
    vecs[5], vecs[6], vecs[7] = lng[0], lnb[0], b2[0]
    vecs[8], vecs[9], vecs[10] = lng[1], lnb[1], b2[1]
    sh["vecs"] = vecs
    qs = _f32c(inp["L0_ffn_q_scale"]); ks = _f32c(inp["L0_ffn_k_scale"])
    sh["v2"] = _f32c(np.stack([np.tile(qs[0], 2), np.tile(ks[0], 2),
                               np.tile(qs[1], 2), np.tile(ks[1], 2)]))

    import jax
    import jax.numpy as jnp
    with jax.default_device(jax.devices("cpu")[0]):
        half_idx = jnp.arange(0, 64, 2, dtype=jnp.float32)
        inv_freq = 1.0 / 10000.0 ** (half_idx / 64)
        tpos = jnp.arange(2048, dtype=jnp.float32)
        freqs = tpos[:, None] * inv_freq[None, :]
        base = (half_idx + 0.4 * 64) / (1.4 * 64)
        power = (tpos - 1024) / float(256)
        rsc = base[None, :] ** power[:, None]
        rcos = jnp.cos(freqs)
        rsin = jnp.sin(freqs)
    rows = np.arange(128) % 32
    sh["rotc"] = _f32c(np.asarray(rcos).T[rows, :])
    sh["rots"] = _f32c(np.asarray(rsin).T[rows, :])
    sh["rotsc"] = _f32c(np.asarray(rsc).T[rows, :])

    ident = np.eye(128, dtype=np.float32)
    ones = np.ones((128, 128), dtype=np.float32)
    ones64 = np.zeros((128, 128), np.float32); ones64[:64, :64] = 1.0; ones64[64:, 64:] = 1.0
    pi_m = np.zeros((128, 128), np.float32)
    for blk in range(2):
        o = 64 * blk
        pi_m[o:o + 32, o + 32:o + 64] = -np.eye(32)
        pi_m[o + 32:o + 64, o:o + 32] = np.eye(32)
    r = np.arange(128)[:, None]; u = np.arange(128)[None, :]
    tri1 = np.where(u < r, np.float32(NEGBIG), np.float32(0.0)).astype(np.float32)
    tri2 = np.where(u > r, np.float32(NEGBIG), np.float32(0.0)).astype(np.float32)
    jrev = np.zeros((128, 128), np.float32)
    jrev[np.arange(128), 127 - np.arange(128)] = 1.0
    sh["aux7"] = np.stack([ident, ones, ones64, pi_m.T, tri1, tri2, jrev]).astype(np.float32)
    sh["aux_f32"] = np.stack([ident, ones]).astype(np.float32)
    sh["jcol"] = _f32c((np.arange(128) % 32).astype(np.float32)[:, None])

    x = np.asarray(inp["L0_attn_x"], dtype=np.float32)
    percore = [{"xt": _f32c(x[c].T)} for c in range(8)]
    return sh, percore



class P:
    def __init__(self, nc, stack):
        self.nc = nc
        self.T, self.V, self.S, self.G, self.Y = nc.tensor, nc.vector, nc.scalar, nc.gpsimd, nc.sync
        self.sems, self.cnt = {}, {}
        for name in ["dw", "dx", "t", "v", "s", "g", "b"]:
            self.sems[name] = stack.enter_context(nc.semaphore(name))
            self.cnt[name] = 0
        self.hwm = {"dw": 0, "dx": 0}
        self.issuer_waited = {"dw": 0, "dx": 0}

    def inc(self, inst, name, n=1):
        inst.then_inc(self.sems[name], n)
        self.cnt[name] += n

    def seq(self, eng, inst, name):
        self.inc(inst, name)
        eng.wait_ge(self.sems[name], self.cnt[name])

    def wait(self, eng, name, lag=0, upto=None):
        v = (self.cnt[name] - lag) if upto is None else upto
        if v > 0:
            eng.wait_ge(self.sems[name], v)
            if name in self.hwm and v > self.hwm[name]:
                self.hwm[name] = v

    def dma(self, eng, name, fn):
        if self.hwm[name] > self.issuer_waited[name]:
            eng.wait_ge(self.sems[name], self.hwm[name])
            self.issuer_waited[name] = self.hwm[name]
        self.inc(fn(), name, 16)

    def bar(self):
        for eng in (self.T, self.V, self.S, self.G):
            self.inc(eng.drain(), "b", 1)
        self.inc(self.Y.nop(), "b", 1)
        tgt = self.cnt["b"]
        for eng in (self.T, self.V, self.S, self.G, self.Y):
            eng.wait_ge(self.sems["b"], tgt)



def hbs(sb, c, c0, c1):
    return sb["hb"][:, c * CPAD + c0: c * CPAD + c1]


def fts(sb, c, c0, c1):
    return sb["ftile"][:, c * CPAD + c0: c * CPAD + c1]


def bau(sb, unit, c0, c1):
    return sb["ba"][:, unit * CPAD + c0: unit * CPAD + c1]



def build_nc():
    nc = bass.Bass("TRN2", target_bir_lowering=False)
    dr = {}

    def din(name, shape):
        dr[name] = nc.dram_tensor(name, list(shape), F32, kind="ExternalInput")

    din("xt", [D, N])
    din("l0a_wq", [D, D]); din("l0a_wkv", [D, 2 * D]); din("l0a_wo", [D, D])
    din("relemb", [32, 16]); din("m436t", [32, 436]); din("ediff", [32, 1])
    din("l0f_wqkv", [2, D, 1536]); din("l0f_wo", [2, 512, D])
    din("w1pb", [2, NBLK, D, 256]); din("w2pb", [2, NBLK, 128, D]); din("b1p", [88, 128])
    din("l1a_wqT", [D, D]); din("l1a_wkT", [D, D]); din("l1a_wvT", [D, D]); din("l1a_woT", [D, D])
    din("l1a_pT", [16, 64, 64]); din("l1f_wT", [D, D])
    din("vecs", [12, D]); din("v2", [4, 128]); din("aux7", [7, 128, 128])
    din("rotc", [128, N]); din("rots", [128, N]); din("rotsc", [128, N])
    din("aux_f32", [2, 128, 128]); din("jcol", [128, 1])
    dr["yt"] = nc.dram_tensor("yt", [D, N], F32, kind="ExternalOutput")
    dr["xm"] = nc.dram_tensor("xm", [D, N], F32)
    dr["gbuf"] = nc.dram_tensor("gbuf", [16, 576], F32)

    stack = ExitStack()
    sb = {}

    def sbuf(name, shape, dt):
        sb[name] = stack.enter_context(nc.sbuf_tensor(name, list(shape), dt))

    sbuf("hb", [128, NCHUNK * CPAD], BF16)
    sbuf("tbl", [128, 4 * N], BF16)
    sbuf("ftile", [128, NCHUNK * CPAD], F32)
    sbuf("xcb", [128, 2 * N], F32)
    sbuf("stat", [128, 2 * N], F32)
    sbuf("sc2", [128, 1024], F32)
    sbuf("ba", [128, BAU * CPAD], BF16)
    sbuf("vt", [128, 96], F32); sbuf("v2t", [128, 4], F32); sbuf("b1t", [128, 88], F32)
    sbuf("dct", [128, 16], F32); sbuf("sc1", [128, 64], F32)
    for nme in ["id_bf", "ones_bf", "o64_bf", "pit_bf", "tri1_bf", "tri2_bf", "j_bf"]:
        sbuf(nme, [128, 128], BF16)
    sbuf("id_f", [128, 128], F32); sbuf("ones_f", [128, 128], F32)
    sbuf("rel_sb", [32, 16], F32); sbuf("ediff_sb", [32, 1], F32)
    sbuf("jcol_sb", [128, 1], F32)
    sbuf("tmp16", [128, 16], F32)
    sbuf("pk_bf", [128, 1024], BF16)
    sbuf("stg", [128, 128], F32)

    ps = {}
    ps["A"] = stack.enter_context(nc.psum_tensor("psA", [128, 2048], F32))
    ps["B"] = stack.enter_context(nc.psum_tensor("psB", [128, 1024], F32))
    ps["T"] = stack.enter_context(nc.psum_tensor("psT", [128, 2048], BF16))

    p = P(nc, stack)
    emit_program(p, dr, sb, ps)
    stack.close()
    return nc


def emit_program(p, dr, sb, ps):
    emit_prologue(p, dr, sb, ps)
    on = lambda ph: ph in PHASES
    emit_boundary(p, dr, sb, ps, src="xt", dst="xm", delta=None, scale=0.0,
                  want_f=on("l0ffn_a"))
    if on("l0ffn_a"):
        emit_l0ffn(p, dr, sb, ps)
    emit_boundary(p, dr, sb, ps, src="xm", dst="xm",
                  delta=on("l0ffn_a"), scale=0.5, want_f=False)
    if on("l0att"):
        emit_l0att(p, dr, sb, ps)
    emit_boundary(p, dr, sb, ps, src="xm", dst="xm",
                  delta=on("l0att"), scale=1.0, want_f=on("l0ffn_c"))
    if on("l0ffn_c"):
        emit_l0ffn(p, dr, sb, ps)
    emit_boundary(p, dr, sb, ps, src="xm", dst="xm",
                  delta=on("l0ffn_c"), scale=0.5, want_f=False)
    if on("l1ffn_d") or on("l1ffn_f"):
        emit_l1quant(p, dr, sb, ps)
    if on("l1ffn_d"):
        emit_l1ffn(p, dr, sb, ps)
    emit_boundary(p, dr, sb, ps, src="xm", dst="xm",
                  delta=on("l1ffn_d"), scale=0.5, want_f=False)
    if on("l1att"):
        emit_l1att(p, dr, sb, ps)
    emit_boundary(p, dr, sb, ps, src="xm", dst="xm",
                  delta=on("l1att"), scale=1.0, want_f=False)
    if on("l1ffn_f"):
        emit_l1ffn(p, dr, sb, ps)
    emit_boundary(p, dr, sb, ps, src="xm", dst="yt",
                  delta=on("l1ffn_f"), scale=0.5, want_f=False, last=True)



def emit_prologue(p, dr, sb, ps):
    T, V, S, G, Y = p.T, p.V, p.S, p.G, p.Y

    for name, src in [("rel_sb", dr["relemb"]),
                      ("ediff_sb", dr["ediff"]), ("jcol_sb", dr["jcol"])]:
        p.dma(Y, "dx", lambda: Y.dma_start(out=sb[name][:, :], in_=src[:, :]))
    m436_sb = sb["stat"][0:32, 1536:1972]
    p.dma(Y, "dx", lambda: Y.dma_start(out=m436_sb, in_=dr["m436t"][:, :]))
    vload = sb["stat"][0:12, 0:1024]
    v2load = sb["stat"][0:4, 1024:1152]
    b1load = [sb["stat"][0:44, 1152 + 128 * l:1280 + 128 * l] for l in range(2)]
    p.dma(Y, "dx", lambda: Y.dma_start(out=vload, in_=dr["vecs"][:, :]))
    p.dma(Y, "dx", lambda: Y.dma_start(out=v2load, in_=dr["v2"][:, :]))
    for l in range(2):
        p.dma(Y, "dx", lambda: Y.dma_start(out=b1load[l], in_=dr["b1p"][44 * l:44 * l + 44, :]))
    p.dma(Y, "dx", lambda: Y.dma_start(out=sb["id_f"][:, :], in_=dr["aux_f32"][0, :, :]))
    p.dma(Y, "dx", lambda: Y.dma_start(out=sb["ones_f"][:, :], in_=dr["aux_f32"][1, :, :]))
    for i, name in enumerate(["id_bf", "ones_bf", "o64_bf", "pit_bf", "tri1_bf", "tri2_bf", "j_bf"]):
        p.dma(G, "dw", lambda: G.dma_start(out=sb[name][:, :], in_=dr["aux7"][i, :, :]))

    p.wait(T, "dx"); p.wait(T, "dw")

    def tr_small(dst_ap, src_ap):
        np_, nf = src_ap.shape[0], src_ap.shape[1]
        p.inc(T.transpose(ps["A"][0:nf, 0:np_], src_ap, sb["id_f"][0:np_, 0:np_]), "t")
        p.wait(V, "t")
        p.inc(V.tensor_copy(dst_ap, ps["A"][0:nf, 0:np_]), "v")
        p.wait(T, "v")

    for c in range(NCHUNK):
        tr_small(sb["vt"][:, 12 * c:12 * c + 12], vload[:, 128 * c:128 * c + 128])
    tr_small(sb["v2t"][:, :], v2load[:, :])
    for l in range(2):
        tr_small(sb["b1t"][:, 44 * l:44 * l + 44], b1load[l])

    for ch, ulen in enumerate([128, 128, 128, 52]):
        u0 = 128 * ch
        p.inc(T.matmul(ps["B"][0:ulen, 0:16], m436_sb[:, u0:u0 + ulen], sb["rel_sb"][:, :],
                       start=True, stop=True), "t")
        p.wait(V, "t")
        p.inc(V.tensor_copy(sb["tmp16"][0:ulen, 0:16], ps["B"][0:ulen, 0:16]), "v")
        p.wait(T, "v")
        p.inc(T.transpose(ps["B"][0:16, 128:128 + ulen], sb["tmp16"][0:ulen, 0:16],
                          sb["id_f"][0:ulen, 0:ulen]), "t")
        p.wait(V, "t"); p.wait(V, "dx")
        p.inc(V.tensor_copy(sb["stg"][0:16, 0:ulen], ps["B"][0:16, 128:128 + ulen]), "v")
        p.wait(Y, "v")
        p.dma(Y, "dx", lambda: Y.dma_start(out=dr["gbuf"][:, u0:u0 + ulen], in_=sb["stg"][0:16, 0:ulen]))
        p.wait(T, "v")

    p.inc(T.matmul(ps["B"][0:1, 0:16], sb["ediff_sb"][:, :], sb["rel_sb"][:, :],
                   start=True, stop=True), "t")
    p.wait(V, "t")
    p.inc(V.tensor_copy(sb["tmp16"][0:1, 0:16], ps["B"][0:1, 0:16]), "v")
    p.wait(T, "v")
    p.inc(T.matmul(ps["B"][:, 16:32], sb["ones_f"][0:1, :], sb["tmp16"][0:1, 0:16],
                   start=True, stop=True), "t")
    p.wait(V, "t")
    p.inc(V.tensor_copy(sb["dct"][:, :], ps["B"][:, 16:32]), "v")

    emit_rotary_tables(p, dr, sb, ps)
    p.wait(V, "v")
    for c in range(NCHUNK):
        V.memset(sb["hb"][:, c * CPAD:c * CPAD + 1], 0.0)
        last = V.memset(sb["ftile"][:, c * CPAD:c * CPAD + 1], 0.0)
    p.inc(last, "v")
    p.bar()


def emit_rotary_tables(p, dr, sb, ps):
    T, V, S, G, Y = p.T, p.V, p.S, p.G, p.Y
    cosb = sb["xcb"][:, 0:N]
    sinb = sb["xcb"][:, N:2 * N]
    scb = sb["stat"][:, 0:N]
    p.wait(Y, "v")
    p.dma(Y, "dx", lambda: Y.dma_start(out=cosb, in_=dr["rotc"][:, :]))
    p.dma(Y, "dx", lambda: Y.dma_start(out=sinb, in_=dr["rots"][:, :]))
    p.dma(Y, "dx", lambda: Y.dma_start(out=scb, in_=dr["rotsc"][:, :]))
    p.wait(V, "dx")
    tbl = sb["tbl"]
    p.seq(V, V.tensor_tensor(sb["sc2"][:, 0:1024], cosb[:, 0:1024], scb[:, 0:1024], ALU.mult), "v")
    p.seq(V, V.tensor_scalar(tbl[:, 0:1024], sb["sc2"][:, 0:1024], 8.0, None, ALU.mult, ALU.bypass), "v")
    p.seq(V, V.tensor_tensor(sb["sc2"][:, 0:1024], cosb[:, 1024:2048], scb[:, 1024:2048], ALU.mult), "v")
    p.seq(V, V.tensor_scalar(tbl[:, 1024:2048], sb["sc2"][:, 0:1024], 8.0, None, ALU.mult, ALU.bypass), "v")
    p.seq(V, V.tensor_tensor(sb["sc2"][:, 0:1024], sinb[:, 0:1024], scb[:, 0:1024], ALU.mult), "v")
    p.seq(V, V.tensor_scalar(tbl[:, N:N + 1024], sb["sc2"][:, 0:1024], 8.0, None, ALU.mult, ALU.bypass), "v")
    p.seq(V, V.tensor_tensor(sb["sc2"][:, 0:1024], sinb[:, 1024:2048], scb[:, 1024:2048], ALU.mult), "v")
    p.seq(V, V.tensor_scalar(tbl[:, N + 1024:2 * N], sb["sc2"][:, 0:1024], 8.0, None, ALU.mult, ALU.bypass), "v")
    p.seq(V, V.reciprocal(sb["sc2"][:, 0:1024], scb[:, 0:1024]), "v")
    p.seq(V, V.tensor_tensor(tbl[:, 2 * N:2 * N + 1024], cosb[:, 0:1024],
                             sb["sc2"][:, 0:1024], ALU.mult), "v")
    p.seq(V, V.tensor_tensor(tbl[:, 3 * N:3 * N + 1024], sinb[:, 0:1024],
                             sb["sc2"][:, 0:1024], ALU.mult), "v")
    p.seq(V, V.reciprocal(sb["sc2"][:, 0:1024], scb[:, 1024:2048]), "v")
    p.seq(V, V.tensor_tensor(tbl[:, 2 * N + 1024:3 * N], cosb[:, 1024:2048],
                             sb["sc2"][:, 0:1024], ALU.mult), "v")
    p.inc(V.tensor_tensor(tbl[:, 3 * N + 1024:4 * N], sinb[:, 1024:2048],
                          sb["sc2"][:, 0:1024], ALU.mult), "v")



def emit_boundary(p, dr, sb, ps, src, dst, delta, scale, want_f, last=False):
    T, V, S, G, Y = p.T, p.V, p.S, p.G, p.Y
    xsrc, xdst = dr[src], dr[dst]

    def xbc(c, c0, c1):
        return bau(sb, c, c0, c1)

    for c in range(NCHUNK):
        buf = sb["xcb"][:, (c % 2) * N:(c % 2) * N + N]
        p.wait(Y, "v"); p.wait(Y, "dx")
        p.dma(Y, "dx", lambda: Y.dma_start(out=buf, in_=xsrc[128 * c:128 * c + 128, :]))
        p.wait(V, "dx")
        lastv = None
        if delta:
            lastv = V.scalar_tensor_tensor(buf, fts(sb, c, 1, 1 + N), scale, buf, ALU.mult, ALU.add)
            if not last:
                p.seq(V, lastv, "v")
        if not last:
            lastv = V.tensor_copy(xbc(c, 0, N), buf)
        if lastv is not None:
            p.inc(lastv, "v")
        p.wait(Y, "v")
        if lastv is None:
            p.wait(Y, "dx")
        p.dma(Y, "dx", lambda: Y.dma_start(out=xdst[128 * c:128 * c + 128, :], in_=buf))
    if last:
        p.wait(Y, "dx")
        p.bar()
        return

    p.wait(S, "v")
    for tc in range(4):
        c0 = 512 * tc
        if tc > 0:
            p.wait(T, "v")
        for c in range(NCHUNK):
            par = c % 2
            sq = bau(sb, 8, 512 * par, 512 * par + 512)
            if tc > 0 or c > 1:
                p.wait(S, "t", lag=1)
            p.inc(S.activation(out=sq, in_=xbc(c, c0, c0 + 512), func=AF.Square), "s")
            p.wait(T, "s")
            T.matmul(ps["B"][:, 0:512], sb["ones_bf"][:, :], xbc(c, c0, c0 + 512),
                     start=(c == 0), stop=(c == NCHUNK - 1))
            p.inc(T.matmul(ps["B"][:, 512:1024], sb["ones_bf"][:, :], sq,
                           start=(c == 0), stop=(c == NCHUNK - 1)), "t")
        p.wait(V, "t")
        mean = sb["stat"][:, c0:c0 + 512]
        rstd = sb["stat"][:, N + c0:N + c0 + 512]
        p.seq(V, V.tensor_scalar(mean, ps["B"][:, 0:512], 1.0 / 1024.0, None,
                                 ALU.mult, ALU.bypass), "v")
        p.seq(V, V.tensor_scalar(rstd, ps["B"][:, 512:1024], 1.0 / 1024.0, None,
                                 ALU.mult, ALU.bypass), "v")
        p.seq(V, V.tensor_tensor(sb["sc2"][:, 0:512], mean, mean, ALU.mult), "v")
        p.seq(V, V.tensor_tensor(rstd, rstd, sb["sc2"][:, 0:512], ALU.subtract), "v")
        p.inc(V.tensor_scalar(rstd, rstd, 1e-5, None, ALU.add, ALU.bypass), "v")
        p.wait(S, "v")
        p.seq(S, S.activation(out=rstd, in_=rstd, func=AF.Ln), "s")
        p.inc(S.activation(out=rstd, in_=rstd, func=AF.Exp, scale=-0.5), "s")
    p.wait(V, "s")
    for c in range(NCHUNK):
        for tc in range(4):
            c0 = 512 * tc
            mean = sb["stat"][:, c0:c0 + 512]
            rstd = sb["stat"][:, N + c0:N + c0 + 512]
            out = fts(sb, c, 1 + c0, 1 + c0 + 512) if want_f else hbs(sb, c, 1 + c0, 1 + c0 + 512)
            p.seq(V, V.tensor_tensor(sb["sc2"][:, 512:1024], xbc(c, c0, c0 + 512),
                                     mean, ALU.subtract), "v")
            p.seq(V, V.tensor_tensor(out, sb["sc2"][:, 512:1024], rstd, ALU.mult), "v")
    p.inc(V.nop(), "b", 0) if False else None
    if want_f:
        emit_shiftcast(p, sb)
    p.bar()


def emit_shiftcast(p, sb):
    V = p.V
    p.wait(V, "v")
    for c in range(NCHUNK):
        if c < 4:
            lastv = V.tensor_copy(hbs(sb, c, 1, 1 + N), fts(sb, c, 1, 1 + N))
        else:
            lastv = V.tensor_copy(hbs(sb, c, 1, 1 + N), fts(sb, c, 0, N))
    p.inc(lastv, "v")



WQB_UNIT = 9


def wqbs(sb, ic, oc):
    base = WQB_UNIT * CPAD + 1024 * ic + 128 * oc
    return sb["ba"][:, base:base + 128]


def emit_l1quant(p, dr, sb, ps):
    T, V, S, G, Y = p.T, p.V, p.S, p.G, p.Y
    for c in range(NCHUNK):
        buf = sb["xcb"][:, (c % 2) * 1024:(c % 2) * 1024 + 1024]
        p.wait(Y, "v")
        p.dma(Y, "dx", lambda: Y.dma_start(out=buf, in_=dr["l1f_wT"][128 * c:128 * c + 128, :]))
        p.wait(V, "dx")
        p.inc(V.tensor_reduce(sb["sc1"][:, c:c + 1], buf, axis=AX.X, op=ALU.add,
                              apply_absolute_value=True), "v")
    p.wait(V, "v")
    p.inc(V.tensor_reduce(sb["sc1"][:, 8:9], sb["sc1"][:, 0:8], axis=AX.X, op=ALU.add), "v")
    p.wait(T, "v")
    p.inc(T.matmul(ps["B"][:, 0:1], sb["ones_f"][:, :], sb["sc1"][:, 8:9],
                   start=True, stop=True), "t")
    p.wait(V, "t")
    p.seq(V, V.tensor_scalar(sb["sc1"][:, 9:10], ps["B"][:, 0:1], 1.0 / (2.0 ** 20), 1e-5,
                             ALU.mult, ALU.max), "v")
    p.seq(V, V.tensor_scalar(sb["sc1"][:, 10:11], sb["sc1"][:, 9:10], 0.5, None,
                             ALU.mult, ALU.bypass), "v")
    p.inc(V.tensor_scalar(sb["sc1"][:, 11:12], sb["sc1"][:, 10:11], -1.0, None,
                          ALU.mult, ALU.bypass), "v")
    for c in range(NCHUNK):
        buf = sb["xcb"][:, (c % 2) * 1024:(c % 2) * 1024 + 1024]
        p.wait(Y, "v")
        p.dma(Y, "dx", lambda: Y.dma_start(out=buf, in_=dr["l1f_wT"][128 * c:128 * c + 128, :]))
        p.wait(V, "dx")
        ge = sb["sc2"][:, 0:1024]
        p.seq(V, V.tensor_scalar(ge, buf, sb["sc1"][:, 10:11], None, ALU.is_ge, ALU.bypass), "v")
        p.seq(V, V.tensor_scalar(buf, buf, sb["sc1"][:, 11:12], None, ALU.is_le, ALU.bypass), "v")
        p.seq(V, V.tensor_tensor(ge, ge, buf, ALU.subtract), "v")
        dst = sb["ba"][:, WQB_UNIT * CPAD + 1024 * c: WQB_UNIT * CPAD + 1024 * c + 1024]
        p.inc(V.tensor_scalar(dst, ge, sb["sc1"][:, 9:10], None, ALU.mult, ALU.bypass), "v")
    p.bar()


def emit_l1ffn(p, dr, sb, ps):
    T, V, S, G, Y = p.T, p.V, p.S, p.G, p.Y
    for oc in range(8):
        for t in range(4):
            for ic in range(NCHUNK):
                mm = T.matmul(ps["A"][:, 512 * t:512 * t + 512], wqbs(sb, ic, oc),
                              hbs(sb, ic, 1 + 512 * t, 1 + 512 * t + 512),
                              start=(ic == 0), stop=(ic == NCHUNK - 1))
            p.inc(mm, "t")
        p.wait(V, "t")
        for t in range(4):
            lastv = V.tensor_copy(fts(sb, oc, 1 + 512 * t, 1 + 512 * t + 512),
                                  ps["A"][:, 512 * t:512 * t + 512])
        p.inc(lastv, "v")
        p.wait(T, "v")
    p.bar()



def emit_l1att(p, dr, sb, ps):
    T, V, S, G, Y = p.T, p.V, p.S, p.G, p.Y

    for h in range(16):
        p.dma(G, "dw", lambda: G.dma_start(out=sb["pk_bf"][64 * (h % 2):64 * (h % 2) + 64,
                                          64 * (h // 2):64 * (h // 2) + 64],
                          in_=dr["l1a_pT"][h, :, :]))
    brow = bau(sb, 3, 0, 1024)
    p.dma(G, "dw", lambda: G.dma_start(out=brow[0:1, :], in_=dr["vecs"][3:4, :]))

    def wslice(t, col0, ncol):
        return bass.AP(t, col0, [[1024, 128], [131072, 8], [1, ncol]])
    woT = dr["l1a_woT"].ap().rearrange("(a b) c -> a b c", b=128)

    ptb = lambda h: sb["pk_bf"][64 * (h % 2):64 * (h % 2) + 64,
                                64 * (h // 2):64 * (h // 2) + 64]
    kvb = lambda h: sb["pk_bf"][64 * (h % 2):64 * (h % 2) + 64,
                                512 + 64 * (h // 2):512 + 64 * (h // 2) + 64]

    for hg in range(4):
        wvu = 4 + hg % 2
        p.wait(G, "t")
        p.dma(G, "dw", lambda: G.dma_start(out=bau(sb, wvu, 0, 2048),
                          in_=wslice(dr["l1a_wvT"], 256 * hg, 256)))
        p.wait(T, "dw")
        for tt in range(16):
            for ic in range(NCHUNK):
                T.matmul(ps["A"][:, 256 * (tt % 4):256 * (tt % 4) + 256],
                         hbs(sb, ic, 1 + 128 * tt, 1 + 128 * tt + 128),
                         bau(sb, wvu, 256 * ic, 256 * ic + 256),
                         start=(ic == 0), stop=False)
            p.inc(T.matmul(ps["A"][:, 256 * (tt % 4):256 * (tt % 4) + 256],
                           sb["ones_bf"][0:1, :], brow[0:1, 256 * hg:256 * hg + 256],
                           start=False, stop=True), "t")
            if tt % 4 == 3:
                p.wait(V, "t")
                for q in range(4):
                    tt2 = tt - 3 + q
                    lastv = V.tensor_copy(
                        bau(sb, 1 + tt2 // 8, 256 * (tt2 % 8), 256 * (tt2 % 8) + 256),
                        ps["A"][:, 256 * q:256 * q + 256])
                p.inc(lastv, "v")
                p.wait(T, "v")
        for hh in range(4):
            h = 4 * hg + hh
            b0 = 64 * (h % 2)
            wku = 1024 * (h % 2)
            p.wait(G, "t")
            p.dma(G, "dw", lambda: G.dma_start(out=bau(sb, 6, wku, wku + 512),
                              in_=wslice(dr["l1a_wkT"], 64 * h, 64)))
            p.wait(T, "dw")
            for t4 in range(4):
                mm = None
                for ic in range(NCHUNK):
                    mm = T.matmul(ps["A"][b0:b0 + 64, 512 * t4:512 * t4 + 512],
                                  bau(sb, 6, wku + 64 * ic, wku + 64 * ic + 64),
                                  hbs(sb, ic, 1 + 512 * t4, 1 + 512 * t4 + 512),
                                  start=(ic == 0), stop=(ic == NCHUNK - 1))
                p.inc(mm, "t")
            p.wait(S, "t")
            p.inc(S.activation(out=bau(sb, 8, 0, N)[b0:b0 + 64, :], in_=ps["A"][b0:b0 + 64, 0:N],
                               func=AF.Identity,
                               bias=sb["vt"][b0:b0 + 64, 12 * (h // 2) + 2:12 * (h // 2) + 3]), "s")
            p.wait(T, "s")
            for tt in range(16):
                p.inc(T.matmul(ps["B"][:, 64 * (tt % 8):64 * (tt % 8) + 64],
                               bau(sb, 8, 128 * tt, 128 * tt + 128)[b0:b0 + 64, :], ptb(h),
                               start=True, stop=True), "t")
                if tt % 8 == 7:
                    p.wait(S, "t")
                    for q in range(8):
                        tt2 = tt - 7 + q
                        lasts = S.activation(
                            out=bau(sb, 0, 256 * (tt2 % 8) + 64 * (tt2 // 8),
                                    256 * (tt2 % 8) + 64 * (tt2 // 8) + 64),
                            in_=ps["B"][:, 64 * q:64 * q + 64], func=AF.Relu, scale=0.125)
                    p.inc(lasts, "s")
                    p.wait(T, "s")
            for tt in range(16):
                mm = T.matmul(ps["B"][b0:b0 + 64, 512:576],
                              bau(sb, 0, 256 * (tt % 8) + 64 * (tt // 8),
                                  256 * (tt % 8) + 64 * (tt // 8) + 64),
                              bau(sb, 1 + tt // 8, 256 * (tt % 8) + 64 * hh,
                                  256 * (tt % 8) + 64 * hh + 64),
                              start=(tt == 0), stop=(tt == 15))
            p.inc(mm, "t")
            p.wait(V, "t")
            p.inc(V.tensor_copy(kvb(h), ps["B"][b0:b0 + 64, 512:576]), "v")
            p.wait(T, "v")

    p.wait(G, "t")
    woff = 4 * CPAD
    for ic in range(NCHUNK):
        p.dma(G, "dw", lambda: G.dma_start(out=sb["ba"][:, woff + 1024 * ic:woff + 1024 * ic + 1024],
                          in_=woT[ic, :, :]))
    p.wait(T, "dw")

    def wo(cc, cols):
        return sb["ba"][:, woff + 1024 * cc + cols[0]:woff + 1024 * cc + cols[1]]

    for tc in range(4):
        c0 = 512 * tc
        for h in range(16):
            pq = ps["B"] if h % 2 == 0 else ps["A"]
            b0 = 64 * (h % 2)
            wku = 1024 * (h % 2)
            if tc == 0 or True:
                p.wait(G, "t")
                p.dma(G, "dw", lambda: G.dma_start(out=bau(sb, 1, wku, wku + 512),
                                  in_=wslice(dr["l1a_wqT"], 64 * h, 64)))
                p.wait(T, "dw")
            mm = None
            for ic in range(NCHUNK):
                mm = T.matmul(pq[b0:b0 + 64, 0:512],
                              bau(sb, 1, wku + 64 * ic, wku + 64 * ic + 64),
                              hbs(sb, ic, 1 + c0, 1 + c0 + 512),
                              start=(ic == 0), stop=(ic == NCHUNK - 1))
            p.inc(mm, "t")
            p.wait(S, "t")
            p.inc(S.activation(out=bau(sb, 8, 0, 512)[b0:b0 + 64, :], in_=pq[b0:b0 + 64, 0:512],
                               func=AF.Identity,
                               bias=sb["vt"][b0:b0 + 64, 12 * (h // 2) + 1:12 * (h // 2) + 2]), "s")
            p.wait(T, "s")
            p.inc(T.matmul(pq[b0:b0 + 64, 512:1024], ptb(h), bau(sb, 8, 0, 512)[b0:b0 + 64, :],
                           start=True, stop=True), "t")
            p.wait(S, "t")
            p.inc(S.activation(out=bau(sb, 8, 512, 1024)[b0:b0 + 64, :],
                               in_=pq[b0:b0 + 64, 512:1024], func=AF.Relu, scale=0.125), "s")
            p.wait(T, "s")
            p.inc(T.matmul(pq[b0:b0 + 64, 0:512], kvb(h), bau(sb, 8, 512, 1024)[b0:b0 + 64, :],
                           start=True, stop=True), "t")
            p.wait(V, "t")
            cc = h // 2
            p.inc(V.tensor_copy(bau(sb, 2 + cc // 4, 512 * (cc % 4), 512 * (cc % 4) + 512)
                                [b0:b0 + 64, :], pq[b0:b0 + 64, 0:512]), "v")
            p.wait(T, "v")
        for dm in range(8):
            mm = None
            for cc in range(8):
                mm = T.matmul(ps["A"][:, 1024 + 512 * (dm % 2):1024 + 512 * (dm % 2) + 512],
                              wo(cc, (128 * dm, 128 * dm + 128)),
                              bau(sb, 2 + cc // 4, 512 * (cc % 4), 512 * (cc % 4) + 512),
                              start=(cc == 0), stop=(cc == 7))
            p.inc(mm, "t")
            p.wait(V, "t")
            p.inc(V.tensor_scalar(fts(sb, dm, 1 + c0, 1 + c0 + 512),
                                  ps["A"][:, 1024 + 512 * (dm % 2):1024 + 512 * (dm % 2) + 512],
                                  sb["vt"][:, 12 * dm + 4:12 * dm + 5], None, ALU.add, ALU.bypass), "v")
            p.wait(T, "v")
    p.bar()



def emit_l0ffn(p, dr, sb, ps):
    T, V, S, G, Y = p.T, p.V, p.S, p.G, p.Y
    tbl = sb["tbl"]

    def wqkv_slice(l, col0, ncol):
        return bass.AP(dr["l0f_wqkv"], 1572864 * l + col0, [[1536, 128], [196608, 8], [1, ncol]])

    def wo_slice(l, col0, ncol):
        return bass.AP(dr["l0f_wo"], 524288 * l + col0, [[1024, 128], [131072, 4], [1, ncol]])

    def vts(kidx, h):
        return bau(sb, kidx // 4, 512 * (kidx % 4) + 64 * h, 512 * (kidx % 4) + 64 * h + 64)

    for l in range(L0FFN_DEPTH):
        p.wait(G, "t"); p.wait(G, "s")
        p.dma(G, "dw", lambda: G.dma_start(out=bau(sb, 11, 0, 2048),
                                           in_=wqkv_slice(l, 1024, 256)))
        p.dma(G, "dw", lambda: G.dma_start(out=bau(sb, 12, 0, 2048),
                                           in_=wqkv_slice(l, 1280, 256)))
        p.wait(T, "dw"); p.wait(T, "v")
        vtm_ck = [p.cnt["v"]] * 2
        for tt in range(16):
            mm = None
            p.wait(T, "v", upto=vtm_ck[tt % 2])
            for half in range(2):
                u = 11 + half
                for ic in range(NCHUNK):
                    mm = T.matmul(ps["A"][:, 1024 * (tt % 2) + 256 * half:
                                          1024 * (tt % 2) + 256 * half + 256],
                                  hbs(sb, ic, 1 + 128 * tt, 1 + 128 * tt + 128),
                                  bau(sb, u, 256 * ic, 256 * ic + 256),
                                  start=(ic == 0), stop=(ic == NCHUNK - 1))
            p.inc(mm, "t")
            p.wait(V, "t")
            lastv = V.tensor_copy(bau(sb, tt // 4, 512 * (tt % 4), 512 * (tt % 4) + 512),
                                  ps["A"][:, 1024 * (tt % 2):1024 * (tt % 2) + 512])
            p.inc(lastv, "v")
            vtm_ck[tt % 2] = p.cnt["v"]
        for hp in range(4):
            p.wait(G, "t")
            p.dma(G, "dw", lambda: G.dma_start(out=bau(sb, 11, 0, 1024),
                                               in_=wqkv_slice(l, 128 * hp, 128)))
            p.dma(G, "dw", lambda: G.dma_start(out=bau(sb, 11, 1024, 2048),
                                               in_=wqkv_slice(l, 512 + 128 * hp, 128)))
            p.wait(T, "dw"); p.wait(T, "s"); p.wait(T, "v")
            qn_v_ck = [p.cnt["v"]] * 8
            ones_v_ck = [p.cnt["v"]] * 2
            pi_v_ck = [p.cnt["v"]] * 2
            sq_t_ck = [p.cnt["t"]] * 2
            qn_t_ck = [p.cnt["t"]] * 2
            sc2_s_ck = [p.cnt["s"]] * 2
            it = 0
            for which, unit, ct0, st0, scol in ((0, 4, 0, N, 2 * l), (1, 5, 2 * N, 3 * N, 2 * l + 1)):
                for tc in range(4):
                    c0 = 512 * tc
                    par = it % 2
                    it += 1
                    p.wait(T, "v", upto=qn_v_ck[4 * (1 - which) + tc])
                    mm = None
                    for ic in range(NCHUNK):
                        mm = T.matmul(ps["A"][:, c0:c0 + 512],
                                      bau(sb, 11, 1024 * which + 128 * ic,
                                          1024 * which + 128 * ic + 128),
                                      hbs(sb, ic, 1 + c0, 1 + c0 + 512),
                                      start=(ic == 0), stop=(ic == NCHUNK - 1))
                    p.inc(mm, "t")
                    p.wait(S, "t")
                    p.wait(S, "t", upto=sq_t_ck[par])
                    p.inc(S.activation(out=bau(sb, 12, 512 * par, 512 * par + 512),
                                       in_=ps["A"][:, c0:c0 + 512], func=AF.Square), "s")
                    p.wait(T, "s")
                    p.wait(T, "v", upto=ones_v_ck[par])
                    p.inc(T.matmul(ps["B"][:, 512 * par:512 * par + 512], sb["o64_bf"][:, :],
                                   bau(sb, 12, 512 * par, 512 * par + 512),
                                   start=True, stop=True), "t")
                    sq_t_ck[par] = p.cnt["t"]
                    p.wait(V, "t")
                    p.wait(V, "s", upto=sc2_s_ck[par])
                    p.seq(V, V.tensor_scalar(sb["sc2"][:, 512 * par:512 * par + 512],
                                             ps["B"][:, 512 * par:512 * par + 512],
                                             1e-12 ** 2, None, ALU.max, ALU.bypass), "v")
                    ones_v_ck[par] = p.cnt["v"]
                    p.wait(S, "v")
                    p.seq(S, S.activation(out=sb["sc2"][:, 512 * par:512 * par + 512],
                                          in_=sb["sc2"][:, 512 * par:512 * par + 512],
                                          func=AF.Ln), "s")
                    p.inc(S.activation(out=sb["sc2"][:, 512 * par:512 * par + 512],
                                       in_=sb["sc2"][:, 512 * par:512 * par + 512],
                                       func=AF.Exp, scale=-0.5), "s")
                    sc2_s_ck[par] = p.cnt["s"]
                    p.wait(V, "s")
                    p.seq(V, V.tensor_scalar(sb["sc2"][:, 512 * par:512 * par + 512],
                                             sb["sc2"][:, 512 * par:512 * par + 512],
                                             sb["v2t"][:, scol:scol + 1], None,
                                             ALU.mult, ALU.bypass), "v")
                    p.seq(V, V.tensor_tensor(bau(sb, 12, 1024 + 512 * par, 1536 + 512 * par),
                                             ps["A"][:, c0:c0 + 512],
                                             sb["sc2"][:, 512 * par:512 * par + 512],
                                             ALU.mult), "v")
                    v_qn = p.cnt["v"]
                    p.wait(T, "v", upto=v_qn)
                    p.inc(T.matmul(ps["A"][:, c0:c0 + 512], sb["pit_bf"][:, :],
                                   bau(sb, 12, 1024 + 512 * par, 1536 + 512 * par),
                                   start=True, stop=True), "t")
                    qn_t_ck[par] = p.cnt["t"]
                    p.wait(V, "t", upto=qn_t_ck[par])
                    p.seq(V, V.tensor_tensor(bau(sb, 12, 1024 + 512 * par, 1536 + 512 * par),
                                             bau(sb, 12, 1024 + 512 * par, 1536 + 512 * par),
                                             tbl[:, ct0 + c0:ct0 + c0 + 512], ALU.mult), "v")
                    p.seq(V, V.tensor_tensor(sb["sc2"][:, 512 * par:512 * par + 512],
                                             ps["A"][:, c0:c0 + 512],
                                             tbl[:, st0 + c0:st0 + c0 + 512], ALU.mult), "v")
                    p.inc(V.tensor_tensor(bau(sb, unit, c0, c0 + 512),
                                          bau(sb, 12, 1024 + 512 * par, 1536 + 512 * par),
                                          sb["sc2"][:, 512 * par:512 * par + 512], ALU.add), "v")
                    qn_v_ck[4 * which + tc] = p.cnt["v"]
            ck = p.__dict__.setdefault("_l0ffn_ck", {
                "exp_s": [0, 0], "tr_t": [0, 0], "pv_t": [0, 0], "norm_g": [0, 0],
                "oe_v": [0, 0]})
            for hh2 in range(2):
                h = 2 * hp + hh2
                b0 = 64 * (h % 2)

                def tile_geom(qt):
                    w, t = qt // 4, qt % 4
                    q0 = 512 * w + 128 * t
                    if w == 0:
                        return q0, 0, 128 * (t + 1), w, t
                    return q0, 512 * (w - 1) + 128 * t, 640, w, t

                def stage_a(qt):
                    q0, k0, nk, w, t = tile_geom(qt)
                    par = qt % 2
                    pbase = 1024 * par
                    p.wait(T, "s", upto=ck["exp_s"][par])
                    mm = None
                    if w == 0:
                        mm = T.matmul(ps["A"][:, pbase:pbase + nk],
                                      bau(sb, 4, q0, q0 + 128)[b0:b0 + 64, :],
                                      bau(sb, 5, k0, k0 + nk)[b0:b0 + 64, :],
                                      start=True, stop=False)
                        mm = T.matmul(ps["A"][:, pbase + 128 * t:pbase + 128 * t + 128],
                                      sb["id_bf"][:, :], sb["tri2_bf"][:, :],
                                      start=False, stop=True)
                    else:
                        T.matmul(ps["A"][:, pbase:pbase + 512],
                                 bau(sb, 4, q0, q0 + 128)[b0:b0 + 64, :],
                                 bau(sb, 5, k0, k0 + 512)[b0:b0 + 64, :],
                                 start=True, stop=False)
                        T.matmul(ps["A"][:, pbase + 512:pbase + 640],
                                 bau(sb, 4, q0, q0 + 128)[b0:b0 + 64, :],
                                 bau(sb, 5, k0 + 512, k0 + 640)[b0:b0 + 64, :],
                                 start=True, stop=False)
                        T.matmul(ps["A"][:, pbase:pbase + 128],
                                 sb["id_bf"][:, :], sb["tri1_bf"][:, :],
                                 start=False, stop=True)
                        mm = T.matmul(ps["A"][:, pbase + 512:pbase + 640],
                                      sb["id_bf"][:, :], sb["tri2_bf"][:, :],
                                      start=False, stop=True)
                    p.inc(mm, "t")
                    return p.cnt["t"]

                t_a = {}
                v_norm = {}
                t_a[0] = stage_a(0)
                for qt in range(16):
                    par = qt % 2
                    q0, k0, nk, w, t = tile_geom(qt)
                    if qt + 1 < 16:
                        t_a[qt + 1] = stage_a(qt + 1)
                    nm = sb["sc1"][:, 8 + 4 * par:9 + 4 * par]
                    rs = sb["sc1"][:, 9 + 4 * par:10 + 4 * par]
                    rc = sb["sc1"][:, 10 + 4 * par:11 + 4 * par]
                    pu = 6
                    pcol = 1024 * par
                    p.wait(V, "t", upto=t_a[qt])
                    p.wait(V, "s", upto=ck["exp_s"][par])
                    p.inc(V.reduce_max(nm, ps["A"][:, pcol:pcol + nk], axis=AX.X,
                                       negate=True), "v")
                    p.wait(S, "v")
                    p.wait(S, "t", upto=ck["tr_t"][par])
                    p.wait(S, "v", upto=ck["norm_g"][par])
                    p.inc(S.activation(out=bau(sb, pu, 640 * par, 640 * par + nk),
                                       in_=ps["A"][:, pcol:pcol + nk],
                                       func=AF.Exp, bias=nm, accum_out=rs), "s")
                    ck["exp_s"][par] = p.cnt["s"]
                    p.wait(V, "s")
                    p.seq(V, V.reciprocal(rc, rs), "v")
                    p.inc(V.tensor_scalar(bau(sb, pu, 640 * par, 640 * par + nk),
                                          bau(sb, pu, 640 * par, 640 * par + nk),
                                          rc, None, ALU.mult, ALU.bypass), "v")
                    v_norm_c = p.cnt["v"]
                    ck["norm_g"][par] = v_norm_c
                    nkb = nk // 128
                    p.wait(T, "v", upto=v_norm_c)
                    mm = None
                    for kb in range(nkb):
                        mm = T.transpose(ps["T"][:, 1024 * par + 128 * kb:
                                                 1024 * par + 128 * kb + 128],
                                         bau(sb, pu, 640 * par + 128 * kb,
                                             640 * par + 128 * kb + 128),
                                         sb["id_bf"][:, :])
                    p.inc(mm, "t")
                    ck["tr_t"][par] = p.cnt["t"]
                    t_tr = p.cnt["t"]
                    if par == 0:
                        p.wait(V, "t", upto=t_tr)
                        p.inc(V.tensor_copy(bau(sb, 12, 640 * par, 640 * par + nk),
                                            ps["T"][:, 1024 * par:1024 * par + nk]), "v")
                        pte = p.cnt["v"]
                        p.wait(T, "v", upto=pte)
                    else:
                        p.wait(S, "t", upto=t_tr)
                        p.inc(S.activation(out=bau(sb, 12, 640 * par, 640 * par + nk),
                                           in_=ps["T"][:, 1024 * par:1024 * par + nk],
                                           func=AF.Copy), "s")
                        pte = p.cnt["s"]
                        p.wait(T, "s", upto=pte)
                    p.wait(T, "v", upto=ck["oe_v"][par])
                    mm = None
                    for kb in range(nkb):
                        kidx = kb if w == 0 else 4 * (w - 1) + t + kb
                        mm = T.matmul(ps["B"][b0:b0 + 64, 512 * par:512 * par + 128],
                                      vts(kidx, h),
                                      bau(sb, 12, 640 * par + 128 * kb,
                                          640 * par + 128 * kb + 128),
                                      start=(kb == 0), stop=(kb == nkb - 1))
                    p.inc(mm, "t")
                    ck["pv_t"][par] = p.cnt["t"]
                    p.wait(V, "t", upto=p.cnt["t"])
                    p.inc(V.tensor_copy(bau(sb, 7 + h // 2, q0, q0 + 128)[b0:b0 + 64, :],
                                        ps["B"][b0:b0 + 64, 512 * par:512 * par + 128]), "v")
                    ck["oe_v"][par] = p.cnt["v"]
        for half in range(2):
            p.wait(G, "t")
            p.dma(G, "dw", lambda: G.dma_start(out=bau(sb, 11, 0, 2048),
                                               in_=wo_slice(l, 512 * half, 512)))
            p.wait(T, "dw"); p.wait(T, "v")
            for dm in range(4):
                dmg = 4 * half + dm
                for tc in range(4):
                    mm = None
                    for cc in range(4):
                        mm = T.matmul(ps["A"][:, 512 * tc:512 * tc + 512],
                                      bau(sb, 11, 512 * cc + 128 * dm, 512 * cc + 128 * dm + 128),
                                      bau(sb, 7 + cc, 512 * tc, 512 * tc + 512),
                                      start=(cc == 0), stop=(cc == 3))
                    p.inc(mm, "t")
                p.wait(V, "t")
                for tc in range(4):
                    c0 = 512 * tc
                    lastv = V.tensor_tensor(fts(sb, dmg, 1 + c0, 1 + c0 + 512),
                                            fts(sb, dmg, 1 + c0, 1 + c0 + 512),
                                            ps["A"][:, c0:c0 + 512], ALU.add)
                p.inc(lastv, "v")
                p.wait(T, "v")
        if L0FFN_MLP:
            emit_shiftcast(p, sb)
            emit_inner_ln(p, dr, sb, ps, l)
            emit_l0mlp(p, dr, sb, ps, l)
        if l == 0 and L0FFN_DEPTH == 2:
            emit_shiftcast(p, sb)
    p.bar()


def emit_inner_ln(p, dr, sb, ps, l):
    T, V, S, G, Y = p.T, p.V, p.S, p.G, p.Y
    p.wait(S, "v")
    for tc in range(4):
        c0 = 512 * tc
        if tc > 0:
            p.wait(T, "v")
        for c in range(NCHUNK):
            par = c % 2
            sq = bau(sb, 12, 512 * par, 512 * par + 512)
            if tc > 0 or c > 1:
                p.wait(S, "t", lag=1)
            p.inc(S.activation(out=sq, in_=hbs(sb, c, 1 + c0, 1 + c0 + 512), func=AF.Square), "s")
            p.wait(T, "s")
            T.matmul(ps["B"][:, 0:512], sb["ones_bf"][:, :], hbs(sb, c, 1 + c0, 1 + c0 + 512),
                     start=(c == 0), stop=(c == NCHUNK - 1))
            p.inc(T.matmul(ps["B"][:, 512:1024], sb["ones_bf"][:, :], sq,
                           start=(c == 0), stop=(c == NCHUNK - 1)), "t")
        p.wait(V, "t")
        mean = sb["stat"][:, c0:c0 + 512]
        rstd = sb["stat"][:, N + c0:N + c0 + 512]
        p.seq(V, V.tensor_scalar(mean, ps["B"][:, 0:512], 1.0 / 1024.0, None,
                                 ALU.mult, ALU.bypass), "v")
        p.seq(V, V.tensor_scalar(rstd, ps["B"][:, 512:1024], 1.0 / 1024.0, None,
                                 ALU.mult, ALU.bypass), "v")
        p.seq(V, V.tensor_tensor(sb["sc2"][:, 0:512], mean, mean, ALU.mult), "v")
        p.seq(V, V.tensor_tensor(rstd, rstd, sb["sc2"][:, 0:512], ALU.subtract), "v")
        p.inc(V.tensor_scalar(rstd, rstd, 1e-5, None, ALU.add, ALU.bypass), "v")
        p.wait(S, "v")
        p.seq(S, S.activation(out=rstd, in_=rstd, func=AF.Ln), "s")
        p.inc(S.activation(out=rstd, in_=rstd, func=AF.Exp, scale=-0.5), "s")
    p.wait(V, "s")
    for c in range(NCHUNK):
        for tc in range(4):
            c0 = 512 * tc
            mean = sb["stat"][:, c0:c0 + 512]
            rstd = sb["stat"][:, N + c0:N + c0 + 512]
            p.seq(V, V.tensor_tensor(sb["sc2"][:, 512:1024], hbs(sb, c, 1 + c0, 1 + c0 + 512),
                                     mean, ALU.subtract), "v")
            p.seq(V, V.tensor_tensor(sb["sc2"][:, 512:1024], sb["sc2"][:, 512:1024],
                                     rstd, ALU.mult), "v")
            p.seq(V, V.tensor_scalar(hbs(sb, c, 1 + c0, 1 + c0 + 512), sb["sc2"][:, 512:1024],
                                     sb["vt"][:, 12 * c + (5 if l == 0 else 8):
                                              12 * c + (5 if l == 0 else 8) + 1],
                                     sb["vt"][:, 12 * c + (6 if l == 0 else 9):
                                              12 * c + (6 if l == 0 else 9) + 1],
                                     ALU.mult, ALU.add), "v")


def emit_l0mlp(p, dr, sb, ps, l):
    T, V, S, G, Y = p.T, p.V, p.S, p.G, p.Y
    groups = [(0, 6), (6, 12), (12, 18), (18, 22)]

    def w2s(bb, c0, c1):
        return sb["ba"][:, 8 * CPAD + 1024 * bb + c0: 8 * CPAD + 1024 * bb + c1]

    w1_ck = [p.cnt["t"]] * 2
    ag_s_ck = [p.cnt["s"]] * 2
    ag_v_ck = [p.cnt["v"]] * 2
    sc2_v_ck = [p.cnt["v"]] * 2
    w2_t_ck = p.cnt["t"]
    for gi, (g0, g1) in enumerate(groups):
        for b in range(g0, g1):
            bu = 6 + b % 2
            if gi == 0 and b == g0:
                p.wait(G, "t")
            p.wait(G, "t", upto=w1_ck[b % 2])
            if b == g0:
                p.wait(G, "t", upto=w2_t_ck)
            p.dma(G, "dw", lambda: G.dma_start(
                out=bau(sb, bu, 0, 2048),
                in_=bass.AP(dr["w1pb"], 262144 * (NBLK * l + b),
                            [[256, 128], [32768, 8], [1, 256]])))
            p.dma(G, "dw", lambda: G.dma_start(
                out=w2s(b - g0, 0, 1024),
                in_=bass.AP(dr["w2pb"], 131072 * (NBLK * l + b), [[1024, 128], [1, 1024]])))
            p.wait(T, "dw")
            for tc in range(4):
                c0 = 512 * tc
                k = 4 * (b - g0) + tc
                par = k % 2
                pa = ps["B"] if par == 0 else ps["A"]
                p.wait(T, "s", upto=ag_s_ck[par])
                p.wait(T, "v", upto=ag_v_ck[par])
                mm = None
                for half in range(2):
                    for ic in range(NCHUNK):
                        mm = T.matmul(pa[:, 512 * half:512 * half + 512],
                                      bau(sb, bu, 256 * ic + 128 * half,
                                          256 * ic + 128 * half + 128),
                                      hbs(sb, ic, 1 + c0, 1 + c0 + 512),
                                      start=(ic == 0), stop=(ic == NCHUNK - 1))
                p.inc(mm, "t")
                t_after_mm = p.cnt["t"]
                p.wait(S, "t")
                p.wait(S, "v", upto=sc2_v_ck[par])
                p.inc(S.activation(out=sb["sc2"][:, 512 * par:512 * par + 512],
                                   in_=pa[:, 512:1024], func=AF.Gelu,
                                   bias=sb["b1t"][:, 44 * l + 2 * b + 1:44 * l + 2 * b + 2]), "s")
                ag_s_ck[par] = p.cnt["s"]
                p.wait(V, "s")
                if tc == 0:
                    p.wait(V, "t", upto=w2_t_ck)
                p.inc(V.scalar_tensor_tensor(bau(sb, b - g0, c0, c0 + 512), pa[:, 0:512],
                                             sb["b1t"][:, 44 * l + 2 * b:44 * l + 2 * b + 1],
                                             sb["sc2"][:, 512 * par:512 * par + 512],
                                             ALU.add, ALU.mult), "v")
                ag_v_ck[par] = p.cnt["v"]
                sc2_v_ck[par] = p.cnt["v"]
            w1_ck[b % 2] = t_after_mm
        p.wait(T, "v")
        rot_v_ck = [p.cnt["v"]] * 3
        banks = [(ps["A"], 0), (ps["A"], 1024), (ps["B"], 0)]
        for tc in range(4):
            c0 = 512 * tc
            for dq in range(4):
                rot = (4 * tc + dq) % 3
                pa, po = banks[rot]
                p.wait(T, "v", upto=rot_v_ck[rot])
                mm = None
                for b in range(g0, g1):
                    for dm in range(2):
                        dmg = 2 * dq + dm
                        mm = T.matmul(pa[:, po + 512 * dm:po + 512 * dm + 512],
                                      w2s(b - g0, 128 * dmg, 128 * dmg + 128),
                                      bau(sb, b - g0, c0, c0 + 512),
                                      start=(b == g0), stop=(b == g1 - 1))
                p.inc(mm, "t")
                p.wait(V, "t")
                for dm in range(2):
                    dmg = 2 * dq + dm
                    if g0 == 0:
                        lastv = V.scalar_tensor_tensor(
                            fts(sb, dmg, 1 + c0, 1 + c0 + 512), pa[:, po + 512 * dm:po + 512 * dm + 512],
                            sb["vt"][:, 12 * dmg + (7 if l == 0 else 10):
                                     12 * dmg + (7 if l == 0 else 10) + 1],
                            fts(sb, dmg, 1 + c0, 1 + c0 + 512), ALU.add, ALU.add)
                    else:
                        lastv = V.tensor_tensor(fts(sb, dmg, 1 + c0, 1 + c0 + 512),
                                                fts(sb, dmg, 1 + c0, 1 + c0 + 512),
                                                pa[:, po + 512 * dm:po + 512 * dm + 512], ALU.add)
                p.inc(lastv, "v")
                rot_v_ck[rot] = p.cnt["v"]
        w2_t_ck = p.cnt["t"]
        ag_v_ck = [p.cnt["v"], p.cnt["v"]]


def emit_l0att(p, dr, sb, ps):
    T, V, S, G, Y = p.T, p.V, p.S, p.G, p.Y
    p.wait(V, "dx")

    def wkv_slice(col0, ncol):
        return bass.AP(dr["l0a_wkv"], col0, [[2048, 128], [262144, 8], [1, ncol]])

    def wq_slice(col0, ncol):
        return bass.AP(dr["l0a_wq"], col0, [[1024, 128], [131072, 8], [1, ncol]])

    def wo_slice(row0):
        return bass.AP(dr["l0a_wo"], row0 * 1024, [[1024, 128], [131072, 2], [1, 1024]])

    def v4s(kc, hh):
        return bau(sb, 0 + kc // 8, 256 * (kc % 8) + 64 * hh, 256 * (kc % 8) + 64 * hh + 64)

    def pts_pv(kc):
        return bass.AP(sb["ba"], 8 * CPAD + 128 * kc, [[BAU * CPAD, 128], [CPAD, 2], [1, 128]])

    for g in range(4):
        p.wait(G, "t")
        p.dma(G, "dw", lambda: G.dma_start(out=bau(sb, 12, 0, 2048),
                                           in_=wkv_slice(1024 + 256 * g, 256)))
        p.wait(T, "dw")
        for tt in range(16):
            mm = None
            for ic in range(NCHUNK):
                mm = T.matmul(ps["A"][:, 256 * (tt % 4):256 * (tt % 4) + 256],
                              hbs(sb, ic, 1 + 128 * tt, 1 + 128 * tt + 128),
                              bau(sb, 12, 256 * ic, 256 * ic + 256),
                              start=(ic == 0), stop=(ic == NCHUNK - 1))
            p.inc(mm, "t")
            if tt % 4 == 3:
                p.wait(V, "t")
                for q in range(4):
                    tt2 = tt - 3 + q
                    lastv = V.tensor_copy(
                        bau(sb, tt2 // 8, 256 * (tt2 % 8), 256 * (tt2 % 8) + 256),
                        ps["A"][:, 256 * q:256 * q + 256])
                p.inc(lastv, "v")
                p.wait(T, "v")
        for hp in range(2):
            h0 = 4 * g + 2 * hp
            p.wait(G, "t")
            p.dma(G, "dw", lambda: G.dma_start(out=bau(sb, 10, 0, 1024),
                                               in_=wq_slice(64 * h0, 128)))
            p.dma(G, "dw", lambda: G.dma_start(out=bau(sb, 10, 1024, 2048),
                                               in_=wkv_slice(64 * h0, 128)))
            p.wait(T, "dw"); p.wait(T, "s"); p.wait(T, "v")
            for which, unit, scl in ((0, 4, 0.125), (1, 5, 1.0)):
                for tc in range(4):
                    c0 = 512 * tc
                    mm = None
                    for ic in range(NCHUNK):
                        mm = T.matmul(ps["B"][:, 512 * (tc % 2):512 * (tc % 2) + 512],
                                      bau(sb, 10, 1024 * which + 128 * ic,
                                          1024 * which + 128 * ic + 128),
                                      hbs(sb, ic, 1 + c0, 1 + c0 + 512),
                                      start=(ic == 0), stop=(ic == NCHUNK - 1))
                    p.inc(mm, "t")
                    p.wait(S, "t")
                    p.inc(S.activation(out=bau(sb, unit, c0, c0 + 512),
                                       in_=ps["B"][:, 512 * (tc % 2):512 * (tc % 2) + 512],
                                       func=AF.Identity, scale=scl), "s")
                    p.wait(T, "s", lag=1)
            p.wait(T, "s")
            for hh2 in range(2):
                h = h0 + hh2
                hh = 2 * hp + hh2
                b0 = 64 * (h % 2)
                hkoff = 448 * (h % 2)
                p.wait(G, "t")
                p.dma(G, "dw", lambda: G.dma_start(
                    out=bau(sb, 12, hkoff, hkoff + 308),
                    in_=bass.AP(dr["gbuf"], 576 * h, [[1, 128], [1, 308]])))
                p.wait(T, "dw")
                ck = p.__dict__.setdefault("_l0att_ck", {
                    "ttr_v": [0, 0], "exp_s": [0, 0, 0, 0], "tr_t": [0, 0],
                    "oe_v": 0, "nm_s": [0, 0]})

                def scbuf(k):
                    j = k % 4
                    t_ = sb["stat"] if j < 2 else sb["xcb"]
                    return t_[:, 2048 * (j % 2):2048 * (j % 2) + 2048]

                def mxcol(k, which):
                    return sb["sc1"][:, 32 + 8 * (k % 2) + which:33 + 8 * (k % 2) + which]

                def stage_scores(qt, half):
                    i0 = 128 * qt
                    sb0 = i0 - 90
                    slo = max(0, sb0)
                    shi = min(N, sb0 + 308)
                    hlo = 1024 * half
                    p.wait(T, "v", upto=ck["ttr_v"][half])
                    mm = None
                    for nb in range(2):
                        b_lo = hlo + 512 * nb
                        bank_strip = max(slo, b_lo) < min(shi, b_lo + 512)
                        mm = T.matmul(ps["A"][:, b_lo:b_lo + 512],
                                      bau(sb, 4, i0, i0 + 128)[b0:b0 + 64, :],
                                      bau(sb, 5, b_lo, b_lo + 512)[b0:b0 + 64, :],
                                      start=True, stop=not bank_strip)
                        if bank_strip:
                            a, b = max(slo, b_lo), min(shi, b_lo + 512)
                            mm = T.matmul(ps["A"][:, a:b], sb["j_bf"][:, :],
                                          bau(sb, 12, hkoff + a - sb0, hkoff + b - sb0),
                                          start=False, stop=True)
                    p.inc(mm, "t")
                    t_sc = p.cnt["t"]
                    p.wait(V, "t", upto=t_sc)
                    p.wait(V, "s", upto=ck["exp_s"][qt % 4])
                    p.wait(V, "s", upto=ck["nm_s"][qt % 2])
                    p.inc(V.tensor_scalar(scbuf(qt)[:, hlo:hlo + 1024],
                                          ps["A"][:, hlo:hlo + 1024], 0.0, None,
                                          ALU.add, ALU.max, accum_out=mxcol(qt, half)), "v")
                    ck["ttr_v"][half] = p.cnt["v"]

                stage_scores(0, 0)
                stage_scores(0, 1)
                for qt in range(16):
                    par = qt % 2
                    qp = qt // 2
                    i0 = 128 * qt
                    lw = max(0, i0 - 90)
                    nm = sb["sc1"][:, 16 + 8 * par:17 + 8 * par]
                    bl = sb["sc1"][:, 18 + 8 * par:19 + 8 * par]
                    s0 = sb["sc1"][:, 19 + 8 * par:20 + 8 * par]
                    s1 = sb["sc1"][:, 20 + 8 * par:21 + 8 * par]
                    s2 = sb["sc1"][:, 21 + 8 * par:22 + 8 * par]
                    rc = sb["sc1"][:, 22 + 8 * par:23 + 8 * par]
                    if qt + 1 < 16:
                        stage_scores(qt + 1, 0)
                        stage_scores(qt + 1, 1)
                    p.seq(V, V.tensor_tensor(nm, mxcol(qt, 0), mxcol(qt, 1), ALU.max), "v")
                    p.seq(V, V.tensor_scalar(nm, nm, -1.0, None, ALU.mult, ALU.bypass), "v")
                    p.inc(V.tensor_scalar(bl, nm, sb["dct"][:, h:h + 1], None,
                                          ALU.add, ALU.bypass), "v")
                    v_bias = p.cnt["v"]
                    pu = 6 + par
                    nsum = 0
                    sums = [s0, s1, s2]
                    cuts = []
                    if lw > 0:
                        cuts.append((0, lw, bl))
                    cuts.append((lw, N, nm))
                    p.wait(S, "v", upto=v_bias)
                    p.wait(S, "t", upto=ck["tr_t"][par])
                    lasts = None
                    for (a, b, bias) in cuts:
                        lasts = S.activation(out=bau(sb, pu, a, b), in_=scbuf(qt)[:, a:b],
                                             func=AF.Exp, bias=bias, accum_out=sums[nsum])
                        nsum += 1
                    p.inc(lasts, "s")
                    ck["exp_s"][qt % 4] = p.cnt["s"]
                    ck["nm_s"][qt % 2] = p.cnt["s"]
                    p.wait(V, "s")
                    for j in range(1, nsum):
                        p.seq(V, V.tensor_tensor(s0, s0, sums[j], ALU.add), "v")
                    p.seq(V, V.reciprocal(rc, s0), "v")
                    p.inc(V.tensor_scalar(bau(sb, pu, 0, N), bau(sb, pu, 0, N), rc,
                                          None, ALU.mult, ALU.bypass), "v")
                    v_norm = p.cnt["v"]
                    p.wait(T, "v", upto=v_norm)
                    for bt in range(2):
                        mm = None
                        for k8 in range(8):
                            kc = 8 * bt + k8
                            mm = T.transpose(ps["T"][:, 1024 * bt + 128 * k8:
                                                     1024 * bt + 128 * k8 + 128],
                                             bau(sb, pu, 128 * kc, 128 * kc + 128),
                                             sb["id_bf"][:, :])
                        p.inc(mm, "t")
                        t_tr = p.cnt["t"]
                        if bt == 0:
                            p.wait(V, "t", upto=t_tr)
                            p.inc(V.tensor_copy(bau(sb, 8 + par, 0, 1024),
                                                ps["T"][:, 0:1024]), "v")
                            p.wait(T, "v")
                        else:
                            p.wait(S, "t", upto=t_tr)
                            p.inc(S.activation(out=bau(sb, 8 + par, 1024, 2048),
                                               in_=ps["T"][:, 1024:2048], func=AF.Copy), "s")
                            p.wait(T, "s")
                    ck["tr_t"][par] = p.cnt["t"]
                    if par == 1:
                        p.wait(T, "v", upto=ck["oe_v"])
                        mm = None
                        for kc in range(16):
                            mm = T.matmul(ps["B"][b0:b0 + 64, 0:256], v4s(kc, hh),
                                          pts_pv(kc), start=(kc == 0), stop=(kc == 15))
                        p.inc(mm, "t")
                        p.wait(V, "t")
                        p.inc(V.tensor_copy(bau(sb, 2 + hh // 2, 256 * qp, 256 * qp + 256)
                                            [b0:b0 + 64, :], ps["B"][b0:b0 + 64, 0:256]), "v")
                        ck["oe_v"] = p.cnt["v"]

        p.wait(G, "t")
        p.dma(G, "dw", lambda: G.dma_start(out=bau(sb, 11, 0, 2048), in_=wo_slice(256 * g)))
        p.wait(T, "dw"); p.wait(T, "v"); p.wait(T, "s")
        for dm in range(8):
            for t4 in range(4):
                mm = None
                for j in range(2):
                    mm = T.matmul(ps["A"][:, 512 * t4:512 * t4 + 512],
                                  bau(sb, 11, 1024 * j + 128 * dm, 1024 * j + 128 * dm + 128),
                                  bau(sb, 2 + j, 512 * t4, 512 * t4 + 512),
                                  start=(j == 0), stop=(j == 1))
                p.inc(mm, "t")
            p.wait(V, "t")
            for t4 in range(4):
                c0 = 512 * t4
                if g == 0:
                    lastv = V.tensor_scalar(fts(sb, dm, 1 + c0, 1 + c0 + 512),
                                            ps["A"][:, c0:c0 + 512],
                                            sb["vt"][:, 12 * dm + 0:12 * dm + 1],
                                            None, ALU.add, ALU.bypass)
                else:
                    lastv = V.tensor_tensor(fts(sb, dm, 1 + c0, 1 + c0 + 512),
                                            fts(sb, dm, 1 + c0, 1 + c0 + 512),
                                            ps["A"][:, c0:c0 + 512], ALU.add)
            p.inc(lastv, "v")
            p.wait(T, "v")
    p.bar()



_NC_CACHE = {}


def kernel(**inputs):
    sh, percore = host_prep(inputs)
    key = PHASES
    if key not in _NC_CACHE:
        _NC_CACHE[key] = build_nc()
    nc = _NC_CACHE[key]
    in_maps = []
    for c in range(8):
        m = dict(sh)
        m.update(percore[c])
        in_maps.append(m)
    res = run_bass_kernel_spmd(nc, in_maps, core_ids=list(range(8)))
    out = np.stack([np.ascontiguousarray(np.asarray(res.results[c]["yt"]).T) for c in range(8)])
    return out.astype(np.float32)


# version 12
# speedup vs baseline: 1.0687x; 1.0687x over previous
import math
import numpy as np
from contextlib import ExitStack

import concourse.bass as bass
import concourse.mybir as mybir
from concourse.bass_utils import run_bass_kernel_spmd

F32 = mybir.dt.float32
BF16 = mybir.dt.bfloat16
AF = mybir.ActivationFunctionType
ALU = mybir.AluOpType
AX = mybir.AxisListType

N = 2048
D = 1024
NCHUNK = 8
CPAD = 2052
NEGBIG = -float(2.0 ** 127)
PI = math.pi
HID = 2730
NBLK = 22
BAU = 13

PHASES = ("l0ffn_a", "l0att", "l0ffn_c", "l1ffn_d", "l1att", "l1ffn_f")
L0FFN_DEPTH = 2
L0FFN_MLP = True


def _f32c(a):
    return np.ascontiguousarray(np.asarray(a, dtype=np.float32))



def host_prep(inp):
    sh = {}
    sh["l0a_wq"] = _f32c(inp["L0_attn_Wq"])
    sh["l0a_wkv"] = _f32c(inp["L0_attn_Wkv"])
    sh["l0a_wo"] = _f32c(inp["L0_attn_Wo"])
    sh["relemb"] = _f32c(inp["L0_attn_rel_emb"])

    def bucket(rel):
        n = rel
        nb = 16
        ret = 0
        if n < 0:
            ret = nb
        n = abs(n)
        max_exact = nb // 2
        if n < max_exact:
            return ret + n
        v = max_exact + int(
            math.log(max(n, 1) / max_exact) / math.log(128 / max_exact) * (nb - max_exact))
        return ret + min(v, nb - 1)

    m436 = np.zeros((436, 32), dtype=np.float32)
    for u in range(436):
        d = 217 - u
        m436[u, bucket(d)] += 1.0
        m436[u, 31] -= 1.0
    sh["m436t"] = _f32c(m436.T)
    ediff = np.zeros((32, 1), dtype=np.float32)
    ediff[15, 0] = 1.0
    ediff[31, 0] = -1.0
    sh["ediff"] = ediff

    sh["l0f_wqkv"] = _f32c(inp["L0_ffn_wqkv"])
    sh["l0f_wo"] = _f32c(inp["L0_ffn_wo"])
    w1 = _f32c(inp["L0_ffn_w1"]); w2 = _f32c(inp["L0_ffn_w2"]); b1 = _f32c(inp["L0_ffn_b1"])
    HIDP = 128 * NBLK
    w1a = np.zeros((2, 1024, HIDP), np.float32); w1a[:, :, :HID] = w1[:, :, :HID]
    w1g = np.zeros((2, 1024, HIDP), np.float32); w1g[:, :, :HID] = w1[:, :, HID:]
    w1pb = np.zeros((2, NBLK, 1024, 256), np.float32)
    for b in range(NBLK):
        w1pb[:, b, :, :128] = w1a[:, :, 128 * b:128 * b + 128]
        w1pb[:, b, :, 128:] = w1g[:, :, 128 * b:128 * b + 128]
    sh["w1pb"] = _f32c(w1pb)
    w2p = np.zeros((2, HIDP, 1024), np.float32); w2p[:, :HID, :] = w2
    sh["w2pb"] = _f32c(w2p.reshape(2, NBLK, 128, 1024))
    b1a = np.zeros((2, HIDP), np.float32); b1a[:, :HID] = b1[:, :HID]
    b1g = np.zeros((2, HIDP), np.float32); b1g[:, :HID] = b1[:, HID:]
    b1p = np.zeros((2, 44, 128), np.float32)
    for b in range(NBLK):
        b1p[:, 2 * b] = b1a[:, 128 * b:128 * b + 128]
        b1p[:, 2 * b + 1] = b1g[:, 128 * b:128 * b + 128]
    sh["b1p"] = _f32c(b1p.reshape(88, 128))

    sh["l1a_wqT"] = _f32c(np.asarray(inp["L1_attn_Wq"]).T)
    sh["l1a_wkT"] = _f32c(np.asarray(inp["L1_attn_Wk"]).T)
    sh["l1a_wvT"] = _f32c(np.asarray(inp["L1_attn_Wv"]).T)
    sh["l1a_woT"] = _f32c(np.asarray(inp["L1_attn_Wo"]).T)
    sh["l1a_pT"] = _f32c(np.transpose(np.asarray(inp["L1_attn_P"]), (0, 2, 1)))
    sh["l1f_wT"] = _f32c(np.asarray(inp["L1_ffn_weight"]).T)

    vecs = np.zeros((12, 1024), np.float32)
    vecs[0] = _f32c(inp["L0_attn_bo"])
    vecs[1] = _f32c(inp["L1_attn_bq"])
    vecs[2] = _f32c(inp["L1_attn_bk"])
    vecs[3] = _f32c(inp["L1_attn_bv"])
    vecs[4] = _f32c(inp["L1_attn_bo"])
    lng = _f32c(inp["L0_ffn_ln_g"]); lnb = _f32c(inp["L0_ffn_ln_b"]); b2 = _f32c(inp["L0_ffn_b2"])
    vecs[5], vecs[6], vecs[7] = lng[0], lnb[0], b2[0]
    vecs[8], vecs[9], vecs[10] = lng[1], lnb[1], b2[1]
    sh["vecs"] = vecs
    qs = _f32c(inp["L0_ffn_q_scale"]); ks = _f32c(inp["L0_ffn_k_scale"])
    sh["v2"] = _f32c(np.stack([np.tile(qs[0], 2), np.tile(ks[0], 2),
                               np.tile(qs[1], 2), np.tile(ks[1], 2)]))

    import jax
    import jax.numpy as jnp
    with jax.default_device(jax.devices("cpu")[0]):
        half_idx = jnp.arange(0, 64, 2, dtype=jnp.float32)
        inv_freq = 1.0 / 10000.0 ** (half_idx / 64)
        tpos = jnp.arange(2048, dtype=jnp.float32)
        freqs = tpos[:, None] * inv_freq[None, :]
        base = (half_idx + 0.4 * 64) / (1.4 * 64)
        power = (tpos - 1024) / float(256)
        rsc = base[None, :] ** power[:, None]
        rcos = jnp.cos(freqs)
        rsin = jnp.sin(freqs)
    rows = np.arange(128) % 32
    sh["rotc"] = _f32c(np.asarray(rcos).T[rows, :])
    sh["rots"] = _f32c(np.asarray(rsin).T[rows, :])
    sh["rotsc"] = _f32c(np.asarray(rsc).T[rows, :])

    ident = np.eye(128, dtype=np.float32)
    ones = np.ones((128, 128), dtype=np.float32)
    ones64 = np.zeros((128, 128), np.float32); ones64[:64, :64] = 1.0; ones64[64:, 64:] = 1.0
    pi_m = np.zeros((128, 128), np.float32)
    for blk in range(2):
        o = 64 * blk
        pi_m[o:o + 32, o + 32:o + 64] = -np.eye(32)
        pi_m[o + 32:o + 64, o:o + 32] = np.eye(32)
    r = np.arange(128)[:, None]; u = np.arange(128)[None, :]
    tri1 = np.where(u < r, np.float32(NEGBIG), np.float32(0.0)).astype(np.float32)
    tri2 = np.where(u > r, np.float32(NEGBIG), np.float32(0.0)).astype(np.float32)
    jrev = np.zeros((128, 128), np.float32)
    jrev[np.arange(128), 127 - np.arange(128)] = 1.0
    sh["aux7"] = np.stack([ident, ones, ones64, pi_m.T, tri1, tri2, jrev]).astype(np.float32)
    sh["aux_f32"] = np.stack([ident, ones]).astype(np.float32)
    sh["jcol"] = _f32c((np.arange(128) % 32).astype(np.float32)[:, None])

    x = np.asarray(inp["L0_attn_x"], dtype=np.float32)
    percore = [{"xt": _f32c(x[c].T)} for c in range(8)]
    return sh, percore



class P:
    def __init__(self, nc, stack):
        self.nc = nc
        self.T, self.V, self.S, self.G, self.Y = nc.tensor, nc.vector, nc.scalar, nc.gpsimd, nc.sync
        self.sems, self.cnt = {}, {}
        for name in ["dw", "dx", "t", "v", "s", "g", "b"]:
            self.sems[name] = stack.enter_context(nc.semaphore(name))
            self.cnt[name] = 0
        self.hwm = {"dw": 0, "dx": 0}
        self.issuer_waited = {"dw": 0, "dx": 0}

    def inc(self, inst, name, n=1):
        inst.then_inc(self.sems[name], n)
        self.cnt[name] += n

    def seq(self, eng, inst, name):
        self.inc(inst, name)
        eng.wait_ge(self.sems[name], self.cnt[name])

    def wait(self, eng, name, lag=0, upto=None):
        v = (self.cnt[name] - lag) if upto is None else upto
        if v > 0:
            eng.wait_ge(self.sems[name], v)
            if name in self.hwm and v > self.hwm[name]:
                self.hwm[name] = v

    def dma(self, eng, name, fn):
        if self.hwm[name] > self.issuer_waited[name]:
            eng.wait_ge(self.sems[name], self.hwm[name])
            self.issuer_waited[name] = self.hwm[name]
        self.inc(fn(), name, 16)

    def bar(self):
        for eng in (self.T, self.V, self.S, self.G):
            self.inc(eng.drain(), "b", 1)
        self.inc(self.Y.nop(), "b", 1)
        tgt = self.cnt["b"]
        for eng in (self.T, self.V, self.S, self.G, self.Y):
            eng.wait_ge(self.sems["b"], tgt)



def hbs(sb, c, c0, c1):
    return sb["hb"][:, c * CPAD + c0: c * CPAD + c1]


def fts(sb, c, c0, c1):
    return sb["ftile"][:, c * CPAD + c0: c * CPAD + c1]


def bau(sb, unit, c0, c1):
    return sb["ba"][:, unit * CPAD + c0: unit * CPAD + c1]



def build_nc():
    nc = bass.Bass("TRN2", target_bir_lowering=False)
    dr = {}

    def din(name, shape):
        dr[name] = nc.dram_tensor(name, list(shape), F32, kind="ExternalInput")

    din("xt", [D, N])
    din("l0a_wq", [D, D]); din("l0a_wkv", [D, 2 * D]); din("l0a_wo", [D, D])
    din("relemb", [32, 16]); din("m436t", [32, 436]); din("ediff", [32, 1])
    din("l0f_wqkv", [2, D, 1536]); din("l0f_wo", [2, 512, D])
    din("w1pb", [2, NBLK, D, 256]); din("w2pb", [2, NBLK, 128, D]); din("b1p", [88, 128])
    din("l1a_wqT", [D, D]); din("l1a_wkT", [D, D]); din("l1a_wvT", [D, D]); din("l1a_woT", [D, D])
    din("l1a_pT", [16, 64, 64]); din("l1f_wT", [D, D])
    din("vecs", [12, D]); din("v2", [4, 128]); din("aux7", [7, 128, 128])
    din("rotc", [128, N]); din("rots", [128, N]); din("rotsc", [128, N])
    din("aux_f32", [2, 128, 128]); din("jcol", [128, 1])
    dr["yt"] = nc.dram_tensor("yt", [D, N], F32, kind="ExternalOutput")
    dr["xm"] = nc.dram_tensor("xm", [D, N], F32)
    dr["gbuf"] = nc.dram_tensor("gbuf", [16, 576], F32)

    stack = ExitStack()
    sb = {}

    def sbuf(name, shape, dt):
        sb[name] = stack.enter_context(nc.sbuf_tensor(name, list(shape), dt))

    sbuf("hb", [128, NCHUNK * CPAD], BF16)
    sbuf("tbl", [128, 4 * N], BF16)
    sbuf("ftile", [128, NCHUNK * CPAD], F32)
    sbuf("xcb", [128, 2 * N], F32)
    sbuf("stat", [128, 2 * N], F32)
    sbuf("sc2", [128, 1024], F32)
    sbuf("ba", [128, BAU * CPAD], BF16)
    sbuf("vt", [128, 96], F32); sbuf("v2t", [128, 4], F32); sbuf("b1t", [128, 88], F32)
    sbuf("dct", [128, 16], F32); sbuf("sc1", [128, 64], F32)
    for nme in ["id_bf", "ones_bf", "o64_bf", "pit_bf", "tri1_bf", "tri2_bf", "j_bf"]:
        sbuf(nme, [128, 128], BF16)
    sbuf("id_f", [128, 128], F32); sbuf("ones_f", [128, 128], F32)
    sbuf("rel_sb", [32, 16], F32); sbuf("ediff_sb", [32, 1], F32)
    sbuf("jcol_sb", [128, 1], F32)
    sbuf("tmp16", [128, 16], F32)
    sbuf("pk_bf", [128, 1024], BF16)
    sbuf("stg", [128, 128], F32)

    ps = {}
    ps["A"] = stack.enter_context(nc.psum_tensor("psA", [128, 2048], F32))
    ps["B"] = stack.enter_context(nc.psum_tensor("psB", [128, 1024], F32))
    ps["T"] = stack.enter_context(nc.psum_tensor("psT", [128, 2048], BF16))

    p = P(nc, stack)
    emit_program(p, dr, sb, ps)
    stack.close()
    return nc


def emit_program(p, dr, sb, ps):
    emit_prologue(p, dr, sb, ps)
    on = lambda ph: ph in PHASES
    emit_boundary(p, dr, sb, ps, src="xt", dst="xm", delta=None, scale=0.0,
                  want_f=on("l0ffn_a"))
    if on("l0ffn_a"):
        emit_l0ffn(p, dr, sb, ps)
    emit_boundary(p, dr, sb, ps, src="xm", dst="xm",
                  delta=on("l0ffn_a"), scale=0.5, want_f=False)
    if on("l0att"):
        emit_l0att(p, dr, sb, ps)
    emit_boundary(p, dr, sb, ps, src="xm", dst="xm",
                  delta=on("l0att"), scale=1.0, want_f=on("l0ffn_c"))
    if on("l0ffn_c"):
        emit_l0ffn(p, dr, sb, ps)
    emit_boundary(p, dr, sb, ps, src="xm", dst="xm",
                  delta=on("l0ffn_c"), scale=0.5, want_f=False)
    if on("l1ffn_d") or on("l1ffn_f"):
        emit_l1quant(p, dr, sb, ps)
    if on("l1ffn_d"):
        emit_l1ffn(p, dr, sb, ps)
    emit_boundary(p, dr, sb, ps, src="xm", dst="xm",
                  delta=on("l1ffn_d"), scale=0.5, want_f=False)
    if on("l1att"):
        emit_l1att(p, dr, sb, ps)
    emit_boundary(p, dr, sb, ps, src="xm", dst="xm",
                  delta=on("l1att"), scale=1.0, want_f=False)
    if on("l1ffn_f"):
        emit_l1ffn(p, dr, sb, ps)
    emit_boundary(p, dr, sb, ps, src="xm", dst="yt",
                  delta=on("l1ffn_f"), scale=0.5, want_f=False, last=True)



def emit_prologue(p, dr, sb, ps):
    T, V, S, G, Y = p.T, p.V, p.S, p.G, p.Y

    for name, src in [("rel_sb", dr["relemb"]),
                      ("ediff_sb", dr["ediff"]), ("jcol_sb", dr["jcol"])]:
        p.dma(Y, "dx", lambda: Y.dma_start(out=sb[name][:, :], in_=src[:, :]))
    m436_sb = sb["stat"][0:32, 1536:1972]
    p.dma(Y, "dx", lambda: Y.dma_start(out=m436_sb, in_=dr["m436t"][:, :]))
    vload = sb["stat"][0:12, 0:1024]
    v2load = sb["stat"][0:4, 1024:1152]
    b1load = [sb["stat"][0:44, 1152 + 128 * l:1280 + 128 * l] for l in range(2)]
    p.dma(Y, "dx", lambda: Y.dma_start(out=vload, in_=dr["vecs"][:, :]))
    p.dma(Y, "dx", lambda: Y.dma_start(out=v2load, in_=dr["v2"][:, :]))
    for l in range(2):
        p.dma(Y, "dx", lambda: Y.dma_start(out=b1load[l], in_=dr["b1p"][44 * l:44 * l + 44, :]))
    p.dma(Y, "dx", lambda: Y.dma_start(out=sb["id_f"][:, :], in_=dr["aux_f32"][0, :, :]))
    p.dma(Y, "dx", lambda: Y.dma_start(out=sb["ones_f"][:, :], in_=dr["aux_f32"][1, :, :]))
    for i, name in enumerate(["id_bf", "ones_bf", "o64_bf", "pit_bf", "tri1_bf", "tri2_bf", "j_bf"]):
        p.dma(G, "dw", lambda: G.dma_start(out=sb[name][:, :], in_=dr["aux7"][i, :, :]))

    p.wait(T, "dx"); p.wait(T, "dw")

    def tr_small(dst_ap, src_ap):
        np_, nf = src_ap.shape[0], src_ap.shape[1]
        p.inc(T.transpose(ps["A"][0:nf, 0:np_], src_ap, sb["id_f"][0:np_, 0:np_]), "t")
        p.wait(V, "t")
        p.inc(V.tensor_copy(dst_ap, ps["A"][0:nf, 0:np_]), "v")
        p.wait(T, "v")

    for c in range(NCHUNK):
        tr_small(sb["vt"][:, 12 * c:12 * c + 12], vload[:, 128 * c:128 * c + 128])
    tr_small(sb["v2t"][:, :], v2load[:, :])
    for l in range(2):
        tr_small(sb["b1t"][:, 44 * l:44 * l + 44], b1load[l])

    for ch, ulen in enumerate([128, 128, 128, 52]):
        u0 = 128 * ch
        p.inc(T.matmul(ps["B"][0:ulen, 0:16], m436_sb[:, u0:u0 + ulen], sb["rel_sb"][:, :],
                       start=True, stop=True), "t")
        p.wait(V, "t")
        p.inc(V.tensor_copy(sb["tmp16"][0:ulen, 0:16], ps["B"][0:ulen, 0:16]), "v")
        p.wait(T, "v")
        p.inc(T.transpose(ps["B"][0:16, 128:128 + ulen], sb["tmp16"][0:ulen, 0:16],
                          sb["id_f"][0:ulen, 0:ulen]), "t")
        p.wait(V, "t"); p.wait(V, "dx")
        p.inc(V.tensor_copy(sb["stg"][0:16, 0:ulen], ps["B"][0:16, 128:128 + ulen]), "v")
        p.wait(Y, "v")
        p.dma(Y, "dx", lambda: Y.dma_start(out=dr["gbuf"][:, u0:u0 + ulen], in_=sb["stg"][0:16, 0:ulen]))
        p.wait(T, "v")

    p.inc(T.matmul(ps["B"][0:1, 0:16], sb["ediff_sb"][:, :], sb["rel_sb"][:, :],
                   start=True, stop=True), "t")
    p.wait(V, "t")
    p.inc(V.tensor_copy(sb["tmp16"][0:1, 0:16], ps["B"][0:1, 0:16]), "v")
    p.wait(T, "v")
    p.inc(T.matmul(ps["B"][:, 16:32], sb["ones_f"][0:1, :], sb["tmp16"][0:1, 0:16],
                   start=True, stop=True), "t")
    p.wait(V, "t")
    p.inc(V.tensor_copy(sb["dct"][:, :], ps["B"][:, 16:32]), "v")

    emit_rotary_tables(p, dr, sb, ps)
    p.wait(V, "v")
    for c in range(NCHUNK):
        V.memset(sb["hb"][:, c * CPAD:c * CPAD + 1], 0.0)
        last = V.memset(sb["ftile"][:, c * CPAD:c * CPAD + 1], 0.0)
    p.inc(last, "v")
    p.bar()


def emit_rotary_tables(p, dr, sb, ps):
    T, V, S, G, Y = p.T, p.V, p.S, p.G, p.Y
    cosb = sb["xcb"][:, 0:N]
    sinb = sb["xcb"][:, N:2 * N]
    scb = sb["stat"][:, 0:N]
    p.wait(Y, "v")
    p.dma(Y, "dx", lambda: Y.dma_start(out=cosb, in_=dr["rotc"][:, :]))
    p.dma(Y, "dx", lambda: Y.dma_start(out=sinb, in_=dr["rots"][:, :]))
    p.dma(Y, "dx", lambda: Y.dma_start(out=scb, in_=dr["rotsc"][:, :]))
    p.wait(V, "dx")
    tbl = sb["tbl"]
    p.seq(V, V.tensor_tensor(sb["sc2"][:, 0:1024], cosb[:, 0:1024], scb[:, 0:1024], ALU.mult), "v")
    p.seq(V, V.tensor_scalar(tbl[:, 0:1024], sb["sc2"][:, 0:1024], 8.0, None, ALU.mult, ALU.bypass), "v")
    p.seq(V, V.tensor_tensor(sb["sc2"][:, 0:1024], cosb[:, 1024:2048], scb[:, 1024:2048], ALU.mult), "v")
    p.seq(V, V.tensor_scalar(tbl[:, 1024:2048], sb["sc2"][:, 0:1024], 8.0, None, ALU.mult, ALU.bypass), "v")
    p.seq(V, V.tensor_tensor(sb["sc2"][:, 0:1024], sinb[:, 0:1024], scb[:, 0:1024], ALU.mult), "v")
    p.seq(V, V.tensor_scalar(tbl[:, N:N + 1024], sb["sc2"][:, 0:1024], 8.0, None, ALU.mult, ALU.bypass), "v")
    p.seq(V, V.tensor_tensor(sb["sc2"][:, 0:1024], sinb[:, 1024:2048], scb[:, 1024:2048], ALU.mult), "v")
    p.seq(V, V.tensor_scalar(tbl[:, N + 1024:2 * N], sb["sc2"][:, 0:1024], 8.0, None, ALU.mult, ALU.bypass), "v")
    p.seq(V, V.reciprocal(sb["sc2"][:, 0:1024], scb[:, 0:1024]), "v")
    p.seq(V, V.tensor_tensor(tbl[:, 2 * N:2 * N + 1024], cosb[:, 0:1024],
                             sb["sc2"][:, 0:1024], ALU.mult), "v")
    p.seq(V, V.tensor_tensor(tbl[:, 3 * N:3 * N + 1024], sinb[:, 0:1024],
                             sb["sc2"][:, 0:1024], ALU.mult), "v")
    p.seq(V, V.reciprocal(sb["sc2"][:, 0:1024], scb[:, 1024:2048]), "v")
    p.seq(V, V.tensor_tensor(tbl[:, 2 * N + 1024:3 * N], cosb[:, 1024:2048],
                             sb["sc2"][:, 0:1024], ALU.mult), "v")
    p.inc(V.tensor_tensor(tbl[:, 3 * N + 1024:4 * N], sinb[:, 1024:2048],
                          sb["sc2"][:, 0:1024], ALU.mult), "v")



def emit_boundary(p, dr, sb, ps, src, dst, delta, scale, want_f, last=False):
    T, V, S, G, Y = p.T, p.V, p.S, p.G, p.Y
    xsrc, xdst = dr[src], dr[dst]

    def xbc(c, c0, c1):
        return bau(sb, c, c0, c1)

    for c in range(NCHUNK):
        buf = sb["xcb"][:, (c % 2) * N:(c % 2) * N + N]
        p.wait(Y, "v"); p.wait(Y, "dx")
        p.dma(Y, "dx", lambda: Y.dma_start(out=buf, in_=xsrc[128 * c:128 * c + 128, :]))
        p.wait(V, "dx")
        lastv = None
        if delta:
            lastv = V.scalar_tensor_tensor(buf, fts(sb, c, 1, 1 + N), scale, buf, ALU.mult, ALU.add)
            if not last:
                p.seq(V, lastv, "v")
        if not last:
            lastv = V.tensor_copy(xbc(c, 0, N), buf)
        if lastv is not None:
            p.inc(lastv, "v")
        p.wait(Y, "v")
        if lastv is None:
            p.wait(Y, "dx")
        p.dma(Y, "dx", lambda: Y.dma_start(out=xdst[128 * c:128 * c + 128, :], in_=buf))
    if last:
        p.wait(Y, "dx")
        p.bar()
        return

    p.wait(S, "v")
    for tc in range(4):
        c0 = 512 * tc
        if tc > 0:
            p.wait(T, "v")
        for c in range(NCHUNK):
            par = c % 2
            sq = bau(sb, 8, 512 * par, 512 * par + 512)
            if tc > 0 or c > 1:
                p.wait(S, "t", lag=1)
            p.inc(S.activation(out=sq, in_=xbc(c, c0, c0 + 512), func=AF.Square), "s")
            p.wait(T, "s")
            T.matmul(ps["B"][:, 0:512], sb["ones_bf"][:, :], xbc(c, c0, c0 + 512),
                     start=(c == 0), stop=(c == NCHUNK - 1))
            p.inc(T.matmul(ps["B"][:, 512:1024], sb["ones_bf"][:, :], sq,
                           start=(c == 0), stop=(c == NCHUNK - 1)), "t")
        p.wait(V, "t")
        mean = sb["stat"][:, c0:c0 + 512]
        rstd = sb["stat"][:, N + c0:N + c0 + 512]
        p.seq(V, V.tensor_scalar(mean, ps["B"][:, 0:512], 1.0 / 1024.0, None,
                                 ALU.mult, ALU.bypass), "v")
        p.seq(V, V.tensor_scalar(rstd, ps["B"][:, 512:1024], 1.0 / 1024.0, None,
                                 ALU.mult, ALU.bypass), "v")
        p.seq(V, V.tensor_tensor(sb["sc2"][:, 0:512], mean, mean, ALU.mult), "v")
        p.seq(V, V.tensor_tensor(rstd, rstd, sb["sc2"][:, 0:512], ALU.subtract), "v")
        p.inc(V.tensor_scalar(rstd, rstd, 1e-5, None, ALU.add, ALU.bypass), "v")
        p.wait(S, "v")
        p.seq(S, S.activation(out=rstd, in_=rstd, func=AF.Ln), "s")
        p.inc(S.activation(out=rstd, in_=rstd, func=AF.Exp, scale=-0.5), "s")
    p.wait(V, "s")
    for c in range(NCHUNK):
        for tc in range(4):
            c0 = 512 * tc
            mean = sb["stat"][:, c0:c0 + 512]
            rstd = sb["stat"][:, N + c0:N + c0 + 512]
            out = fts(sb, c, 1 + c0, 1 + c0 + 512) if want_f else hbs(sb, c, 1 + c0, 1 + c0 + 512)
            p.seq(V, V.tensor_tensor(sb["sc2"][:, 512:1024], xbc(c, c0, c0 + 512),
                                     mean, ALU.subtract), "v")
            p.seq(V, V.tensor_tensor(out, sb["sc2"][:, 512:1024], rstd, ALU.mult), "v")
    p.inc(V.nop(), "b", 0) if False else None
    if want_f:
        emit_shiftcast(p, sb)
    p.bar()


def emit_shiftcast(p, sb):
    V = p.V
    p.wait(V, "v")
    for c in range(NCHUNK):
        if c < 4:
            lastv = V.tensor_copy(hbs(sb, c, 1, 1 + N), fts(sb, c, 1, 1 + N))
        else:
            lastv = V.tensor_copy(hbs(sb, c, 1, 1 + N), fts(sb, c, 0, N))
    p.inc(lastv, "v")



WQB_UNIT = 9


def wqbs(sb, ic, oc):
    base = WQB_UNIT * CPAD + 1024 * ic + 128 * oc
    return sb["ba"][:, base:base + 128]


def emit_l1quant(p, dr, sb, ps):
    T, V, S, G, Y = p.T, p.V, p.S, p.G, p.Y
    for c in range(NCHUNK):
        buf = sb["xcb"][:, (c % 2) * 1024:(c % 2) * 1024 + 1024]
        p.wait(Y, "v")
        p.dma(Y, "dx", lambda: Y.dma_start(out=buf, in_=dr["l1f_wT"][128 * c:128 * c + 128, :]))
        p.wait(V, "dx")
        p.inc(V.tensor_reduce(sb["sc1"][:, c:c + 1], buf, axis=AX.X, op=ALU.add,
                              apply_absolute_value=True), "v")
    p.wait(V, "v")
    p.inc(V.tensor_reduce(sb["sc1"][:, 8:9], sb["sc1"][:, 0:8], axis=AX.X, op=ALU.add), "v")
    p.wait(T, "v")
    p.inc(T.matmul(ps["B"][:, 0:1], sb["ones_f"][:, :], sb["sc1"][:, 8:9],
                   start=True, stop=True), "t")
    p.wait(V, "t")
    p.seq(V, V.tensor_scalar(sb["sc1"][:, 9:10], ps["B"][:, 0:1], 1.0 / (2.0 ** 20), 1e-5,
                             ALU.mult, ALU.max), "v")
    p.seq(V, V.tensor_scalar(sb["sc1"][:, 10:11], sb["sc1"][:, 9:10], 0.5, None,
                             ALU.mult, ALU.bypass), "v")
    p.inc(V.tensor_scalar(sb["sc1"][:, 11:12], sb["sc1"][:, 10:11], -1.0, None,
                          ALU.mult, ALU.bypass), "v")
    for c in range(NCHUNK):
        buf = sb["xcb"][:, (c % 2) * 1024:(c % 2) * 1024 + 1024]
        p.wait(Y, "v")
        p.dma(Y, "dx", lambda: Y.dma_start(out=buf, in_=dr["l1f_wT"][128 * c:128 * c + 128, :]))
        p.wait(V, "dx")
        ge = sb["sc2"][:, 0:1024]
        p.seq(V, V.tensor_scalar(ge, buf, sb["sc1"][:, 10:11], None, ALU.is_ge, ALU.bypass), "v")
        p.seq(V, V.tensor_scalar(buf, buf, sb["sc1"][:, 11:12], None, ALU.is_le, ALU.bypass), "v")
        p.seq(V, V.tensor_tensor(ge, ge, buf, ALU.subtract), "v")
        dst = sb["ba"][:, WQB_UNIT * CPAD + 1024 * c: WQB_UNIT * CPAD + 1024 * c + 1024]
        p.inc(V.tensor_scalar(dst, ge, sb["sc1"][:, 9:10], None, ALU.mult, ALU.bypass), "v")
    p.bar()


def emit_l1ffn(p, dr, sb, ps):
    T, V, S, G, Y = p.T, p.V, p.S, p.G, p.Y
    for oc in range(8):
        for t in range(4):
            for ic in range(NCHUNK):
                mm = T.matmul(ps["A"][:, 512 * t:512 * t + 512], wqbs(sb, ic, oc),
                              hbs(sb, ic, 1 + 512 * t, 1 + 512 * t + 512),
                              start=(ic == 0), stop=(ic == NCHUNK - 1))
            p.inc(mm, "t")
        p.wait(V, "t")
        for t in range(4):
            lastv = V.tensor_copy(fts(sb, oc, 1 + 512 * t, 1 + 512 * t + 512),
                                  ps["A"][:, 512 * t:512 * t + 512])
        p.inc(lastv, "v")
        p.wait(T, "v")
    p.bar()



def emit_l1att(p, dr, sb, ps):
    T, V, S, G, Y = p.T, p.V, p.S, p.G, p.Y

    for h in range(16):
        p.dma(G, "dw", lambda: G.dma_start(out=sb["pk_bf"][64 * (h % 2):64 * (h % 2) + 64,
                                          64 * (h // 2):64 * (h // 2) + 64],
                          in_=dr["l1a_pT"][h, :, :]))
    brow = bau(sb, 3, 0, 1024)
    p.dma(G, "dw", lambda: G.dma_start(out=brow[0:1, :], in_=dr["vecs"][3:4, :]))

    def wslice(t, col0, ncol):
        return bass.AP(t, col0, [[1024, 128], [131072, 8], [1, ncol]])
    woT = dr["l1a_woT"].ap().rearrange("(a b) c -> a b c", b=128)

    ptb = lambda h: sb["pk_bf"][64 * (h % 2):64 * (h % 2) + 64,
                                64 * (h // 2):64 * (h // 2) + 64]
    kvb = lambda h: sb["pk_bf"][64 * (h % 2):64 * (h % 2) + 64,
                                512 + 64 * (h // 2):512 + 64 * (h // 2) + 64]

    for hg in range(4):
        wvu = 4 + hg % 2
        p.wait(G, "t")
        p.dma(G, "dw", lambda: G.dma_start(out=bau(sb, wvu, 0, 2048),
                          in_=wslice(dr["l1a_wvT"], 256 * hg, 256)))
        p.wait(T, "dw")
        for tt in range(16):
            for ic in range(NCHUNK):
                T.matmul(ps["A"][:, 256 * (tt % 4):256 * (tt % 4) + 256],
                         hbs(sb, ic, 1 + 128 * tt, 1 + 128 * tt + 128),
                         bau(sb, wvu, 256 * ic, 256 * ic + 256),
                         start=(ic == 0), stop=False)
            p.inc(T.matmul(ps["A"][:, 256 * (tt % 4):256 * (tt % 4) + 256],
                           sb["ones_bf"][0:1, :], brow[0:1, 256 * hg:256 * hg + 256],
                           start=False, stop=True), "t")
            if tt % 4 == 3:
                p.wait(V, "t")
                for q in range(4):
                    tt2 = tt - 3 + q
                    lastv = V.tensor_copy(
                        bau(sb, 1 + tt2 // 8, 256 * (tt2 % 8), 256 * (tt2 % 8) + 256),
                        ps["A"][:, 256 * q:256 * q + 256])
                p.inc(lastv, "v")
                p.wait(T, "v")
        for hh in range(4):
            h = 4 * hg + hh
            b0 = 64 * (h % 2)
            wku = 1024 * (h % 2)
            p.wait(G, "t")
            p.dma(G, "dw", lambda: G.dma_start(out=bau(sb, 6, wku, wku + 512),
                              in_=wslice(dr["l1a_wkT"], 64 * h, 64)))
            p.wait(T, "dw")
            for t4 in range(4):
                mm = None
                for ic in range(NCHUNK):
                    mm = T.matmul(ps["A"][b0:b0 + 64, 512 * t4:512 * t4 + 512],
                                  bau(sb, 6, wku + 64 * ic, wku + 64 * ic + 64),
                                  hbs(sb, ic, 1 + 512 * t4, 1 + 512 * t4 + 512),
                                  start=(ic == 0), stop=(ic == NCHUNK - 1))
                p.inc(mm, "t")
            p.wait(S, "t")
            p.inc(S.activation(out=bau(sb, 8, 0, N)[b0:b0 + 64, :], in_=ps["A"][b0:b0 + 64, 0:N],
                               func=AF.Identity,
                               bias=sb["vt"][b0:b0 + 64, 12 * (h // 2) + 2:12 * (h // 2) + 3]), "s")
            p.wait(T, "s")
            for tt in range(16):
                p.inc(T.matmul(ps["B"][:, 64 * (tt % 8):64 * (tt % 8) + 64],
                               bau(sb, 8, 128 * tt, 128 * tt + 128)[b0:b0 + 64, :], ptb(h),
                               start=True, stop=True), "t")
                if tt % 8 == 7:
                    p.wait(S, "t")
                    for q in range(8):
                        tt2 = tt - 7 + q
                        lasts = S.activation(
                            out=bau(sb, 0, 256 * (tt2 % 8) + 64 * (tt2 // 8),
                                    256 * (tt2 % 8) + 64 * (tt2 // 8) + 64),
                            in_=ps["B"][:, 64 * q:64 * q + 64], func=AF.Relu, scale=0.125)
                    p.inc(lasts, "s")
                    p.wait(T, "s")
            for tt in range(16):
                mm = T.matmul(ps["B"][b0:b0 + 64, 512:576],
                              bau(sb, 0, 256 * (tt % 8) + 64 * (tt // 8),
                                  256 * (tt % 8) + 64 * (tt // 8) + 64),
                              bau(sb, 1 + tt // 8, 256 * (tt % 8) + 64 * hh,
                                  256 * (tt % 8) + 64 * hh + 64),
                              start=(tt == 0), stop=(tt == 15))
            p.inc(mm, "t")
            p.wait(V, "t")
            p.inc(V.tensor_copy(kvb(h), ps["B"][b0:b0 + 64, 512:576]), "v")
            p.wait(T, "v")

    p.wait(G, "t")
    woff = 4 * CPAD
    for ic in range(NCHUNK):
        p.dma(G, "dw", lambda: G.dma_start(out=sb["ba"][:, woff + 1024 * ic:woff + 1024 * ic + 1024],
                          in_=woT[ic, :, :]))
    p.wait(T, "dw")

    def wo(cc, cols):
        return sb["ba"][:, woff + 1024 * cc + cols[0]:woff + 1024 * cc + cols[1]]

    for tc in range(4):
        c0 = 512 * tc
        for h in range(16):
            pq = ps["B"] if h % 2 == 0 else ps["A"]
            b0 = 64 * (h % 2)
            wku = 1024 * (h % 2)
            if tc == 0 or True:
                p.wait(G, "t")
                p.dma(G, "dw", lambda: G.dma_start(out=bau(sb, 1, wku, wku + 512),
                                  in_=wslice(dr["l1a_wqT"], 64 * h, 64)))
                p.wait(T, "dw")
            mm = None
            for ic in range(NCHUNK):
                mm = T.matmul(pq[b0:b0 + 64, 0:512],
                              bau(sb, 1, wku + 64 * ic, wku + 64 * ic + 64),
                              hbs(sb, ic, 1 + c0, 1 + c0 + 512),
                              start=(ic == 0), stop=(ic == NCHUNK - 1))
            p.inc(mm, "t")
            p.wait(S, "t")
            p.inc(S.activation(out=bau(sb, 8, 0, 512)[b0:b0 + 64, :], in_=pq[b0:b0 + 64, 0:512],
                               func=AF.Identity,
                               bias=sb["vt"][b0:b0 + 64, 12 * (h // 2) + 1:12 * (h // 2) + 2]), "s")
            p.wait(T, "s")
            p.inc(T.matmul(pq[b0:b0 + 64, 512:1024], ptb(h), bau(sb, 8, 0, 512)[b0:b0 + 64, :],
                           start=True, stop=True), "t")
            p.wait(S, "t")
            p.inc(S.activation(out=bau(sb, 8, 512, 1024)[b0:b0 + 64, :],
                               in_=pq[b0:b0 + 64, 512:1024], func=AF.Relu, scale=0.125), "s")
            p.wait(T, "s")
            p.inc(T.matmul(pq[b0:b0 + 64, 0:512], kvb(h), bau(sb, 8, 512, 1024)[b0:b0 + 64, :],
                           start=True, stop=True), "t")
            p.wait(V, "t")
            cc = h // 2
            p.inc(V.tensor_copy(bau(sb, 2 + cc // 4, 512 * (cc % 4), 512 * (cc % 4) + 512)
                                [b0:b0 + 64, :], pq[b0:b0 + 64, 0:512]), "v")
            p.wait(T, "v")
        for dm in range(8):
            mm = None
            for cc in range(8):
                mm = T.matmul(ps["A"][:, 1024 + 512 * (dm % 2):1024 + 512 * (dm % 2) + 512],
                              wo(cc, (128 * dm, 128 * dm + 128)),
                              bau(sb, 2 + cc // 4, 512 * (cc % 4), 512 * (cc % 4) + 512),
                              start=(cc == 0), stop=(cc == 7))
            p.inc(mm, "t")
            p.wait(V, "t")
            p.inc(V.tensor_scalar(fts(sb, dm, 1 + c0, 1 + c0 + 512),
                                  ps["A"][:, 1024 + 512 * (dm % 2):1024 + 512 * (dm % 2) + 512],
                                  sb["vt"][:, 12 * dm + 4:12 * dm + 5], None, ALU.add, ALU.bypass), "v")
            p.wait(T, "v")
    p.bar()



def emit_l0ffn(p, dr, sb, ps):
    T, V, S, G, Y = p.T, p.V, p.S, p.G, p.Y
    tbl = sb["tbl"]

    def wqkv_slice(l, col0, ncol):
        return bass.AP(dr["l0f_wqkv"], 1572864 * l + col0, [[1536, 128], [196608, 8], [1, ncol]])

    def wo_slice(l, col0, ncol):
        return bass.AP(dr["l0f_wo"], 524288 * l + col0, [[1024, 128], [131072, 4], [1, ncol]])

    def vts(kidx, h):
        return bau(sb, kidx // 4, 512 * (kidx % 4) + 64 * h, 512 * (kidx % 4) + 64 * h + 64)

    for l in range(L0FFN_DEPTH):
        p.wait(G, "t"); p.wait(G, "s")
        p.dma(G, "dw", lambda: G.dma_start(out=bau(sb, 11, 0, 2048),
                                           in_=wqkv_slice(l, 1024, 256)))
        p.dma(G, "dw", lambda: G.dma_start(out=bau(sb, 12, 0, 2048),
                                           in_=wqkv_slice(l, 1280, 256)))
        p.wait(T, "dw"); p.wait(T, "v")
        vtm_ck = [p.cnt["v"]] * 2
        for tt in range(16):
            mm = None
            p.wait(T, "v", upto=vtm_ck[tt % 2])
            for half in range(2):
                u = 11 + half
                for ic in range(NCHUNK):
                    mm = T.matmul(ps["A"][:, 1024 * (tt % 2) + 256 * half:
                                          1024 * (tt % 2) + 256 * half + 256],
                                  hbs(sb, ic, 1 + 128 * tt, 1 + 128 * tt + 128),
                                  bau(sb, u, 256 * ic, 256 * ic + 256),
                                  start=(ic == 0), stop=(ic == NCHUNK - 1))
            p.inc(mm, "t")
            p.wait(V, "t")
            lastv = V.tensor_copy(bau(sb, tt // 4, 512 * (tt % 4), 512 * (tt % 4) + 512),
                                  ps["A"][:, 1024 * (tt % 2):1024 * (tt % 2) + 512])
            p.inc(lastv, "v")
            vtm_ck[tt % 2] = p.cnt["v"]
        for hp in range(4):
            p.wait(G, "t")
            p.dma(G, "dw", lambda: G.dma_start(out=bau(sb, 11, 0, 1024),
                                               in_=wqkv_slice(l, 128 * hp, 128)))
            p.dma(G, "dw", lambda: G.dma_start(out=bau(sb, 11, 1024, 2048),
                                               in_=wqkv_slice(l, 512 + 128 * hp, 128)))
            p.wait(T, "dw"); p.wait(T, "s"); p.wait(T, "v")
            qn_v_ck = [p.cnt["v"]] * 8
            ones_v_ck = [p.cnt["v"]] * 2
            pi_v_ck = [p.cnt["v"]] * 2
            sq_t_ck = [p.cnt["t"]] * 2
            qn_t_ck = [p.cnt["t"]] * 2
            sc2_s_ck = [p.cnt["s"]] * 2
            it = 0
            for which, unit, ct0, st0, scol in ((0, 4, 0, N, 2 * l), (1, 5, 2 * N, 3 * N, 2 * l + 1)):
                for tc in range(4):
                    c0 = 512 * tc
                    par = it % 2
                    it += 1
                    p.wait(T, "v", upto=qn_v_ck[4 * (1 - which) + tc])
                    mm = None
                    for ic in range(NCHUNK):
                        mm = T.matmul(ps["A"][:, c0:c0 + 512],
                                      bau(sb, 11, 1024 * which + 128 * ic,
                                          1024 * which + 128 * ic + 128),
                                      hbs(sb, ic, 1 + c0, 1 + c0 + 512),
                                      start=(ic == 0), stop=(ic == NCHUNK - 1))
                    p.inc(mm, "t")
                    p.wait(S, "t")
                    p.wait(S, "t", upto=sq_t_ck[par])
                    p.inc(S.activation(out=bau(sb, 12, 512 * par, 512 * par + 512),
                                       in_=ps["A"][:, c0:c0 + 512], func=AF.Square), "s")
                    p.wait(T, "s")
                    p.wait(T, "v", upto=ones_v_ck[par])
                    p.inc(T.matmul(ps["B"][:, 512 * par:512 * par + 512], sb["o64_bf"][:, :],
                                   bau(sb, 12, 512 * par, 512 * par + 512),
                                   start=True, stop=True), "t")
                    sq_t_ck[par] = p.cnt["t"]
                    p.wait(V, "t")
                    p.wait(V, "s", upto=sc2_s_ck[par])
                    p.seq(V, V.tensor_scalar(sb["sc2"][:, 512 * par:512 * par + 512],
                                             ps["B"][:, 512 * par:512 * par + 512],
                                             1e-12 ** 2, None, ALU.max, ALU.bypass), "v")
                    ones_v_ck[par] = p.cnt["v"]
                    p.wait(S, "v")
                    p.seq(S, S.activation(out=sb["sc2"][:, 512 * par:512 * par + 512],
                                          in_=sb["sc2"][:, 512 * par:512 * par + 512],
                                          func=AF.Ln), "s")
                    p.inc(S.activation(out=sb["sc2"][:, 512 * par:512 * par + 512],
                                       in_=sb["sc2"][:, 512 * par:512 * par + 512],
                                       func=AF.Exp, scale=-0.5), "s")
                    sc2_s_ck[par] = p.cnt["s"]
                    p.wait(V, "s")
                    p.seq(V, V.tensor_scalar(sb["sc2"][:, 512 * par:512 * par + 512],
                                             sb["sc2"][:, 512 * par:512 * par + 512],
                                             sb["v2t"][:, scol:scol + 1], None,
                                             ALU.mult, ALU.bypass), "v")
                    p.seq(V, V.tensor_tensor(bau(sb, 12, 1024 + 512 * par, 1536 + 512 * par),
                                             ps["A"][:, c0:c0 + 512],
                                             sb["sc2"][:, 512 * par:512 * par + 512],
                                             ALU.mult), "v")
                    v_qn = p.cnt["v"]
                    p.wait(T, "v", upto=v_qn)
                    p.inc(T.matmul(ps["A"][:, c0:c0 + 512], sb["pit_bf"][:, :],
                                   bau(sb, 12, 1024 + 512 * par, 1536 + 512 * par),
                                   start=True, stop=True), "t")
                    qn_t_ck[par] = p.cnt["t"]
                    p.wait(V, "t", upto=qn_t_ck[par])
                    p.seq(V, V.tensor_tensor(bau(sb, 12, 1024 + 512 * par, 1536 + 512 * par),
                                             bau(sb, 12, 1024 + 512 * par, 1536 + 512 * par),
                                             tbl[:, ct0 + c0:ct0 + c0 + 512], ALU.mult), "v")
                    p.seq(V, V.tensor_tensor(sb["sc2"][:, 512 * par:512 * par + 512],
                                             ps["A"][:, c0:c0 + 512],
                                             tbl[:, st0 + c0:st0 + c0 + 512], ALU.mult), "v")
                    p.inc(V.tensor_tensor(bau(sb, unit, c0, c0 + 512),
                                          bau(sb, 12, 1024 + 512 * par, 1536 + 512 * par),
                                          sb["sc2"][:, 512 * par:512 * par + 512], ALU.add), "v")
                    qn_v_ck[4 * which + tc] = p.cnt["v"]
            ck = p.__dict__.setdefault("_l0ffn_ck", {
                "exp_s": [0, 0], "tr_t": [0, 0], "pv_t": [0, 0], "norm_g": [0, 0],
                "oe_v": [0, 0]})
            for hh2 in range(2):
                h = 2 * hp + hh2
                b0 = 64 * (h % 2)

                def tile_geom(qt):
                    w, t = qt // 4, qt % 4
                    q0 = 512 * w + 128 * t
                    if w == 0:
                        return q0, 0, 128 * (t + 1), w, t
                    return q0, 512 * (w - 1) + 128 * t, 640, w, t

                def stage_a(qt):
                    q0, k0, nk, w, t = tile_geom(qt)
                    par = qt % 2
                    pbase = 1024 * par
                    p.wait(T, "s", upto=ck["exp_s"][par])
                    mm = None
                    if w == 0:
                        mm = T.matmul(ps["A"][:, pbase:pbase + nk],
                                      bau(sb, 4, q0, q0 + 128)[b0:b0 + 64, :],
                                      bau(sb, 5, k0, k0 + nk)[b0:b0 + 64, :],
                                      start=True, stop=False)
                        mm = T.matmul(ps["A"][:, pbase + 128 * t:pbase + 128 * t + 128],
                                      sb["id_bf"][:, :], sb["tri2_bf"][:, :],
                                      start=False, stop=True)
                    else:
                        T.matmul(ps["A"][:, pbase:pbase + 512],
                                 bau(sb, 4, q0, q0 + 128)[b0:b0 + 64, :],
                                 bau(sb, 5, k0, k0 + 512)[b0:b0 + 64, :],
                                 start=True, stop=False)
                        T.matmul(ps["A"][:, pbase + 512:pbase + 640],
                                 bau(sb, 4, q0, q0 + 128)[b0:b0 + 64, :],
                                 bau(sb, 5, k0 + 512, k0 + 640)[b0:b0 + 64, :],
                                 start=True, stop=False)
                        T.matmul(ps["A"][:, pbase:pbase + 128],
                                 sb["id_bf"][:, :], sb["tri1_bf"][:, :],
                                 start=False, stop=True)
                        mm = T.matmul(ps["A"][:, pbase + 512:pbase + 640],
                                      sb["id_bf"][:, :], sb["tri2_bf"][:, :],
                                      start=False, stop=True)
                    p.inc(mm, "t")
                    return p.cnt["t"]

                t_a = {}
                v_norm = {}
                t_a[0] = stage_a(0)
                for qt in range(16):
                    par = qt % 2
                    q0, k0, nk, w, t = tile_geom(qt)
                    if qt + 1 < 16:
                        t_a[qt + 1] = stage_a(qt + 1)
                    nm = sb["sc1"][:, 8 + 4 * par:9 + 4 * par]
                    rs = sb["sc1"][:, 9 + 4 * par:10 + 4 * par]
                    rc = sb["sc1"][:, 10 + 4 * par:11 + 4 * par]
                    pu = 6
                    pcol = 1024 * par
                    p.wait(V, "t", upto=t_a[qt])
                    p.wait(V, "s", upto=ck["exp_s"][par])
                    p.inc(V.reduce_max(nm, ps["A"][:, pcol:pcol + nk], axis=AX.X,
                                       negate=True), "v")
                    p.wait(S, "v")
                    p.wait(S, "t", upto=ck["tr_t"][par])
                    p.wait(S, "v", upto=ck["norm_g"][par])
                    p.inc(S.activation(out=bau(sb, pu, 640 * par, 640 * par + nk),
                                       in_=ps["A"][:, pcol:pcol + nk],
                                       func=AF.Exp, bias=nm, accum_out=rs), "s")
                    ck["exp_s"][par] = p.cnt["s"]
                    p.wait(V, "s")
                    p.seq(V, V.reciprocal(rc, rs), "v")
                    p.inc(V.tensor_scalar(bau(sb, pu, 640 * par, 640 * par + nk),
                                          bau(sb, pu, 640 * par, 640 * par + nk),
                                          rc, None, ALU.mult, ALU.bypass), "v")
                    v_norm_c = p.cnt["v"]
                    ck["norm_g"][par] = v_norm_c
                    nkb = nk // 128
                    p.wait(T, "v", upto=v_norm_c)
                    mm = None
                    for kb in range(nkb):
                        mm = T.transpose(ps["T"][:, 1024 * par + 128 * kb:
                                                 1024 * par + 128 * kb + 128],
                                         bau(sb, pu, 640 * par + 128 * kb,
                                             640 * par + 128 * kb + 128),
                                         sb["id_bf"][:, :])
                    p.inc(mm, "t")
                    ck["tr_t"][par] = p.cnt["t"]
                    t_tr = p.cnt["t"]
                    if par == 0:
                        p.wait(V, "t", upto=t_tr)
                        p.inc(V.tensor_copy(bau(sb, 12, 640 * par, 640 * par + nk),
                                            ps["T"][:, 1024 * par:1024 * par + nk]), "v")
                        pte = p.cnt["v"]
                        p.wait(T, "v", upto=pte)
                    else:
                        p.wait(S, "t", upto=t_tr)
                        p.inc(S.activation(out=bau(sb, 12, 640 * par, 640 * par + nk),
                                           in_=ps["T"][:, 1024 * par:1024 * par + nk],
                                           func=AF.Copy), "s")
                        pte = p.cnt["s"]
                        p.wait(T, "s", upto=pte)
                    p.wait(T, "v", upto=ck["oe_v"][par])
                    mm = None
                    for kb in range(nkb):
                        kidx = kb if w == 0 else 4 * (w - 1) + t + kb
                        mm = T.matmul(ps["B"][b0:b0 + 64, 512 * par:512 * par + 128],
                                      vts(kidx, h),
                                      bau(sb, 12, 640 * par + 128 * kb,
                                          640 * par + 128 * kb + 128),
                                      start=(kb == 0), stop=(kb == nkb - 1))
                    p.inc(mm, "t")
                    ck["pv_t"][par] = p.cnt["t"]
                    p.wait(V, "t", upto=p.cnt["t"])
                    p.inc(V.tensor_copy(bau(sb, 7 + h // 2, q0, q0 + 128)[b0:b0 + 64, :],
                                        ps["B"][b0:b0 + 64, 512 * par:512 * par + 128]), "v")
                    ck["oe_v"][par] = p.cnt["v"]
        for half in range(2):
            p.wait(G, "t")
            p.dma(G, "dw", lambda: G.dma_start(out=bau(sb, 11, 0, 2048),
                                               in_=wo_slice(l, 512 * half, 512)))
            p.wait(T, "dw"); p.wait(T, "v")
            for dm in range(4):
                dmg = 4 * half + dm
                for tc in range(4):
                    mm = None
                    for cc in range(4):
                        mm = T.matmul(ps["A"][:, 512 * tc:512 * tc + 512],
                                      bau(sb, 11, 512 * cc + 128 * dm, 512 * cc + 128 * dm + 128),
                                      bau(sb, 7 + cc, 512 * tc, 512 * tc + 512),
                                      start=(cc == 0), stop=(cc == 3))
                    p.inc(mm, "t")
                p.wait(V, "t")
                for tc in range(4):
                    c0 = 512 * tc
                    lastv = V.tensor_tensor(fts(sb, dmg, 1 + c0, 1 + c0 + 512),
                                            fts(sb, dmg, 1 + c0, 1 + c0 + 512),
                                            ps["A"][:, c0:c0 + 512], ALU.add)
                p.inc(lastv, "v")
                p.wait(T, "v")
        if L0FFN_MLP:
            emit_shiftcast(p, sb)
            emit_inner_ln(p, dr, sb, ps, l)
            emit_l0mlp(p, dr, sb, ps, l)
        if l == 0 and L0FFN_DEPTH == 2:
            emit_shiftcast(p, sb)
    p.bar()


def emit_inner_ln(p, dr, sb, ps, l):
    T, V, S, G, Y = p.T, p.V, p.S, p.G, p.Y
    p.wait(S, "v")
    for tc in range(4):
        c0 = 512 * tc
        if tc > 0:
            p.wait(T, "v")
        for c in range(NCHUNK):
            par = c % 2
            sq = bau(sb, 12, 512 * par, 512 * par + 512)
            if tc > 0 or c > 1:
                p.wait(S, "t", lag=1)
            p.inc(S.activation(out=sq, in_=hbs(sb, c, 1 + c0, 1 + c0 + 512), func=AF.Square), "s")
            p.wait(T, "s")
            T.matmul(ps["B"][:, 0:512], sb["ones_bf"][:, :], hbs(sb, c, 1 + c0, 1 + c0 + 512),
                     start=(c == 0), stop=(c == NCHUNK - 1))
            p.inc(T.matmul(ps["B"][:, 512:1024], sb["ones_bf"][:, :], sq,
                           start=(c == 0), stop=(c == NCHUNK - 1)), "t")
        p.wait(V, "t")
        mean = sb["stat"][:, c0:c0 + 512]
        rstd = sb["stat"][:, N + c0:N + c0 + 512]
        p.seq(V, V.tensor_scalar(mean, ps["B"][:, 0:512], 1.0 / 1024.0, None,
                                 ALU.mult, ALU.bypass), "v")
        p.seq(V, V.tensor_scalar(rstd, ps["B"][:, 512:1024], 1.0 / 1024.0, None,
                                 ALU.mult, ALU.bypass), "v")
        p.seq(V, V.tensor_tensor(sb["sc2"][:, 0:512], mean, mean, ALU.mult), "v")
        p.seq(V, V.tensor_tensor(rstd, rstd, sb["sc2"][:, 0:512], ALU.subtract), "v")
        p.inc(V.tensor_scalar(rstd, rstd, 1e-5, None, ALU.add, ALU.bypass), "v")
        p.wait(S, "v")
        p.seq(S, S.activation(out=rstd, in_=rstd, func=AF.Ln), "s")
        p.inc(S.activation(out=rstd, in_=rstd, func=AF.Exp, scale=-0.5), "s")
    p.wait(V, "s")
    for c in range(NCHUNK):
        for tc in range(4):
            c0 = 512 * tc
            mean = sb["stat"][:, c0:c0 + 512]
            rstd = sb["stat"][:, N + c0:N + c0 + 512]
            p.seq(V, V.tensor_tensor(sb["sc2"][:, 512:1024], hbs(sb, c, 1 + c0, 1 + c0 + 512),
                                     mean, ALU.subtract), "v")
            p.seq(V, V.tensor_tensor(sb["sc2"][:, 512:1024], sb["sc2"][:, 512:1024],
                                     rstd, ALU.mult), "v")
            p.seq(V, V.tensor_scalar(hbs(sb, c, 1 + c0, 1 + c0 + 512), sb["sc2"][:, 512:1024],
                                     sb["vt"][:, 12 * c + (5 if l == 0 else 8):
                                              12 * c + (5 if l == 0 else 8) + 1],
                                     sb["vt"][:, 12 * c + (6 if l == 0 else 9):
                                              12 * c + (6 if l == 0 else 9) + 1],
                                     ALU.mult, ALU.add), "v")


def emit_l0mlp(p, dr, sb, ps, l):
    T, V, S, G, Y = p.T, p.V, p.S, p.G, p.Y
    groups = [(0, 6), (6, 12), (12, 18), (18, 22)]

    def w2s(bb, c0, c1):
        return sb["ba"][:, 8 * CPAD + 1024 * bb + c0: 8 * CPAD + 1024 * bb + c1]

    w1_ck = [p.cnt["t"]] * 2
    ag_s_ck = [p.cnt["s"]] * 2
    ag_v_ck = [p.cnt["v"]] * 2
    sc2_v_ck = [p.cnt["v"]] * 2
    w2_t_ck = p.cnt["t"]
    for gi, (g0, g1) in enumerate(groups):
        for b in range(g0, g1):
            bu = 6 + b % 2
            if gi == 0 and b == g0:
                p.wait(G, "t")
            p.wait(G, "t", upto=w1_ck[b % 2])
            if b == g0:
                p.wait(G, "t", upto=w2_t_ck)
            p.dma(G, "dw", lambda: G.dma_start(
                out=bau(sb, bu, 0, 2048),
                in_=bass.AP(dr["w1pb"], 262144 * (NBLK * l + b),
                            [[256, 128], [32768, 8], [1, 256]])))
            p.dma(G, "dw", lambda: G.dma_start(
                out=w2s(b - g0, 0, 1024),
                in_=bass.AP(dr["w2pb"], 131072 * (NBLK * l + b), [[1024, 128], [1, 1024]])))
            p.wait(T, "dw")
            for tc in range(4):
                c0 = 512 * tc
                k = 4 * (b - g0) + tc
                par = k % 2
                pa = ps["B"] if par == 0 else ps["A"]
                p.wait(T, "s", upto=ag_s_ck[par])
                p.wait(T, "v", upto=ag_v_ck[par])
                mm = None
                for half in range(2):
                    for ic in range(NCHUNK):
                        mm = T.matmul(pa[:, 512 * half:512 * half + 512],
                                      bau(sb, bu, 256 * ic + 128 * half,
                                          256 * ic + 128 * half + 128),
                                      hbs(sb, ic, 1 + c0, 1 + c0 + 512),
                                      start=(ic == 0), stop=(ic == NCHUNK - 1))
                p.inc(mm, "t")
                t_after_mm = p.cnt["t"]
                p.wait(S, "t")
                p.wait(S, "v", upto=sc2_v_ck[par])
                p.inc(S.activation(out=sb["sc2"][:, 512 * par:512 * par + 512],
                                   in_=pa[:, 512:1024], func=AF.Gelu,
                                   bias=sb["b1t"][:, 44 * l + 2 * b + 1:44 * l + 2 * b + 2]), "s")
                ag_s_ck[par] = p.cnt["s"]
                p.wait(V, "s")
                if tc == 0:
                    p.wait(V, "t", upto=w2_t_ck)
                p.inc(V.scalar_tensor_tensor(bau(sb, b - g0, c0, c0 + 512), pa[:, 0:512],
                                             sb["b1t"][:, 44 * l + 2 * b:44 * l + 2 * b + 1],
                                             sb["sc2"][:, 512 * par:512 * par + 512],
                                             ALU.add, ALU.mult), "v")
                ag_v_ck[par] = p.cnt["v"]
                sc2_v_ck[par] = p.cnt["v"]
            w1_ck[b % 2] = t_after_mm
        p.wait(T, "v")
        rot_v_ck = [p.cnt["v"]] * 3
        banks = [(ps["A"], 0), (ps["A"], 1024), (ps["B"], 0)]
        for tc in range(4):
            c0 = 512 * tc
            for dq in range(4):
                rot = (4 * tc + dq) % 3
                pa, po = banks[rot]
                p.wait(T, "v", upto=rot_v_ck[rot])
                mm = None
                for b in range(g0, g1):
                    for dm in range(2):
                        dmg = 2 * dq + dm
                        mm = T.matmul(pa[:, po + 512 * dm:po + 512 * dm + 512],
                                      w2s(b - g0, 128 * dmg, 128 * dmg + 128),
                                      bau(sb, b - g0, c0, c0 + 512),
                                      start=(b == g0), stop=(b == g1 - 1))
                p.inc(mm, "t")
                p.wait(V, "t")
                for dm in range(2):
                    dmg = 2 * dq + dm
                    if g0 == 0:
                        lastv = V.scalar_tensor_tensor(
                            fts(sb, dmg, 1 + c0, 1 + c0 + 512), pa[:, po + 512 * dm:po + 512 * dm + 512],
                            sb["vt"][:, 12 * dmg + (7 if l == 0 else 10):
                                     12 * dmg + (7 if l == 0 else 10) + 1],
                            fts(sb, dmg, 1 + c0, 1 + c0 + 512), ALU.add, ALU.add)
                    else:
                        lastv = V.tensor_tensor(fts(sb, dmg, 1 + c0, 1 + c0 + 512),
                                                fts(sb, dmg, 1 + c0, 1 + c0 + 512),
                                                pa[:, po + 512 * dm:po + 512 * dm + 512], ALU.add)
                p.inc(lastv, "v")
                rot_v_ck[rot] = p.cnt["v"]
        w2_t_ck = p.cnt["t"]
        ag_v_ck = [p.cnt["v"], p.cnt["v"]]


def emit_l0att(p, dr, sb, ps):
    T, V, S, G, Y = p.T, p.V, p.S, p.G, p.Y
    p.wait(V, "dx")

    def wkv_slice(col0, ncol):
        return bass.AP(dr["l0a_wkv"], col0, [[2048, 128], [262144, 8], [1, ncol]])

    def wq_slice(col0, ncol):
        return bass.AP(dr["l0a_wq"], col0, [[1024, 128], [131072, 8], [1, ncol]])

    def wo_slice(row0):
        return bass.AP(dr["l0a_wo"], row0 * 1024, [[1024, 128], [131072, 2], [1, 1024]])

    def v4s(kc, hh):
        return bau(sb, 0 + kc // 8, 256 * (kc % 8) + 64 * hh, 256 * (kc % 8) + 64 * hh + 64)

    def pts_pv(kc):
        return bass.AP(sb["ba"], 8 * CPAD + 128 * kc, [[BAU * CPAD, 128], [CPAD, 2], [1, 128]])

    for g in range(4):
        p.wait(G, "t")
        p.dma(G, "dw", lambda: G.dma_start(out=bau(sb, 12, 0, 2048),
                                           in_=wkv_slice(1024 + 256 * g, 256)))
        p.wait(T, "dw")
        for tt in range(16):
            mm = None
            for ic in range(NCHUNK):
                mm = T.matmul(ps["A"][:, 256 * (tt % 4):256 * (tt % 4) + 256],
                              hbs(sb, ic, 1 + 128 * tt, 1 + 128 * tt + 128),
                              bau(sb, 12, 256 * ic, 256 * ic + 256),
                              start=(ic == 0), stop=(ic == NCHUNK - 1))
            p.inc(mm, "t")
            if tt % 4 == 3:
                p.wait(V, "t")
                for q in range(4):
                    tt2 = tt - 3 + q
                    lastv = V.tensor_copy(
                        bau(sb, tt2 // 8, 256 * (tt2 % 8), 256 * (tt2 % 8) + 256),
                        ps["A"][:, 256 * q:256 * q + 256])
                p.inc(lastv, "v")
                p.wait(T, "v")
        for hp in range(2):
            h0 = 4 * g + 2 * hp
            p.wait(G, "t")
            p.dma(G, "dw", lambda: G.dma_start(out=bau(sb, 10, 0, 1024),
                                               in_=wq_slice(64 * h0, 128)))
            p.dma(G, "dw", lambda: G.dma_start(out=bau(sb, 10, 1024, 2048),
                                               in_=wkv_slice(64 * h0, 128)))
            p.wait(T, "dw"); p.wait(T, "s"); p.wait(T, "v")
            for which, unit, scl in ((0, 4, 0.125), (1, 5, 1.0)):
                for tc in range(4):
                    c0 = 512 * tc
                    mm = None
                    for ic in range(NCHUNK):
                        mm = T.matmul(ps["B"][:, 512 * (tc % 2):512 * (tc % 2) + 512],
                                      bau(sb, 10, 1024 * which + 128 * ic,
                                          1024 * which + 128 * ic + 128),
                                      hbs(sb, ic, 1 + c0, 1 + c0 + 512),
                                      start=(ic == 0), stop=(ic == NCHUNK - 1))
                    p.inc(mm, "t")
                    p.wait(S, "t")
                    p.inc(S.activation(out=bau(sb, unit, c0, c0 + 512),
                                       in_=ps["B"][:, 512 * (tc % 2):512 * (tc % 2) + 512],
                                       func=AF.Identity, scale=scl), "s")
                    p.wait(T, "s", lag=1)
            p.wait(T, "s")
            for hh2 in range(2):
                h = h0 + hh2
                hh = 2 * hp + hh2
                b0 = 64 * (h % 2)
                hkoff = 448 * (h % 2)
                p.wait(G, "t")
                p.dma(G, "dw", lambda: G.dma_start(
                    out=bau(sb, 12, hkoff, hkoff + 308),
                    in_=bass.AP(dr["gbuf"], 576 * h, [[1, 128], [1, 308]])))
                p.wait(T, "dw")
                ck = p.__dict__.setdefault("_l0att_ck", {
                    "ttr_v": [0, 0], "exp_s": [0, 0, 0, 0], "tr_t": [0, 0],
                    "oe_v": 0, "nm_s": [0, 0], "exp_par_s": [0, 0], "pv_t": [0, 0],
                    "pte": [0, 0]})

                def scbuf(k):
                    j = k % 4
                    t_ = sb["stat"] if j < 2 else sb["xcb"]
                    return t_[:, 2048 * (j % 2):2048 * (j % 2) + 2048]

                def mxcol(k, which):
                    return sb["sc1"][:, 32 + 8 * (k % 2) + which:33 + 8 * (k % 2) + which]

                def nmc(k, j):
                    return sb["sc1"][:, 16 + 8 * (k % 2) + j:17 + 8 * (k % 2) + j]

                t_sc = {}

                def st_t1(qt):
                    i0 = 128 * qt
                    sb0 = i0 - 90
                    slo = max(0, sb0)
                    shi = min(N, sb0 + 308)
                    for half in range(2):
                        hlo = 1024 * half
                        p.wait(T, "v", upto=ck["ttr_v"][half])
                        mm = None
                        for nb in range(2):
                            b_lo = hlo + 512 * nb
                            bank_strip = max(slo, b_lo) < min(shi, b_lo + 512)
                            mm = T.matmul(ps["A"][:, b_lo:b_lo + 512],
                                          bau(sb, 4, i0, i0 + 128)[b0:b0 + 64, :],
                                          bau(sb, 5, b_lo, b_lo + 512)[b0:b0 + 64, :],
                                          start=True, stop=not bank_strip)
                            if bank_strip:
                                a, b = max(slo, b_lo), min(shi, b_lo + 512)
                                mm = T.matmul(ps["A"][:, a:b], sb["j_bf"][:, :],
                                              bau(sb, 12, hkoff + a - sb0, hkoff + b - sb0),
                                              start=False, stop=True)
                        p.inc(mm, "t")
                        t_sc[(qt, half)] = p.cnt["t"]

                def st_v1(qt):
                    for half in range(2):
                        hlo = 1024 * half
                        p.wait(V, "t", upto=t_sc.pop((qt, half)))
                        p.wait(V, "s", upto=ck["exp_s"][qt % 4])
                        p.wait(V, "s", upto=ck["nm_s"][qt % 2])
                        p.inc(V.tensor_scalar(scbuf(qt)[:, hlo:hlo + 1024],
                                              ps["A"][:, hlo:hlo + 1024], 0.0, None,
                                              ALU.add, ALU.max, accum_out=mxcol(qt, half)), "v")
                        ck["ttr_v"][half] = p.cnt["v"]
                    nm, bl = nmc(qt, 0), nmc(qt, 2)
                    p.wait(V, "v")
                    p.seq(V, V.tensor_tensor(nm, mxcol(qt, 0), mxcol(qt, 1), ALU.max), "v")
                    p.seq(V, V.tensor_scalar(nm, nm, -1.0, None, ALU.mult, ALU.bypass), "v")
                    p.inc(V.tensor_scalar(bl, nm, sb["dct"][:, h:h + 1], None,
                                          ALU.add, ALU.bypass), "v")
                    ck.setdefault("bias_v", {})[qt] = p.cnt["v"]

                def st_s(qt):
                    par = qt % 2
                    lw = max(0, 128 * qt - 90)
                    pu = 6 + par
                    sums = [sb["sc1"][:, 19 + 8 * par + j:20 + 8 * par + j] for j in range(2)]
                    cuts = ([(0, lw, nmc(qt, 2))] if lw > 0 else []) + [(lw, N, nmc(qt, 0))]
                    p.wait(S, "v", upto=ck["bias_v"].pop(qt))
                    p.wait(S, "t", upto=ck["tr_t"][par])
                    p.wait(S, "v", upto=ck.setdefault("sums_v", [0, 0])[par])
                    lasts = None
                    for j, (a, b, bias) in enumerate(cuts):
                        lasts = S.activation(out=bau(sb, pu, a, b), in_=scbuf(qt)[:, a:b],
                                             func=AF.Exp, bias=bias, accum_out=sums[j])
                    p.inc(lasts, "s")
                    ck["exp_s"][qt % 4] = p.cnt["s"]
                    ck["nm_s"][par] = p.cnt["s"]
                    ck["exp_par_s"][par] = p.cnt["s"]
                    ck.setdefault("ncuts", {})[qt] = len(cuts)

                def st_v2(qt):
                    par = qt % 2
                    pu = 6 + par
                    sums = [sb["sc1"][:, 19 + 8 * par + j:20 + 8 * par + j] for j in range(2)]
                    rc = sb["sc1"][:, 22 + 8 * par:23 + 8 * par]
                    p.wait(V, "s", upto=ck["exp_par_s"][par])
                    if ck["ncuts"].pop(qt) > 1:
                        p.seq(V, V.tensor_tensor(sums[0], sums[0], sums[1], ALU.add), "v")
                    p.seq(V, V.reciprocal(rc, sums[0]), "v")
                    ck.setdefault("sums_v", [0, 0])[par] = p.cnt["v"]
                    p.inc(V.tensor_scalar(bau(sb, pu, 0, N), bau(sb, pu, 0, N), rc,
                                          None, ALU.mult, ALU.bypass), "v")
                    ck.setdefault("norm_v", {})[qt] = p.cnt["v"]

                def st_t2(qt):
                    par = qt % 2
                    qp = qt // 2
                    pu = 6 + par
                    p.wait(T, "v", upto=ck["norm_v"].pop(qt))
                    for bt in range(2):
                        mm = None
                        for k8 in range(8):
                            kc = 8 * bt + k8
                            mm = T.transpose(ps["T"][:, 1024 * bt + 128 * k8:
                                                     1024 * bt + 128 * k8 + 128],
                                             bau(sb, pu, 128 * kc, 128 * kc + 128),
                                             sb["id_bf"][:, :])
                        p.inc(mm, "t")
                        t_tr = p.cnt["t"]
                        if bt == 0:
                            p.wait(V, "t", upto=t_tr)
                            p.wait(V, "t", upto=ck["pv_t"][par])
                            p.inc(V.tensor_copy(bau(sb, 8 + par, 0, 1024),
                                                ps["T"][:, 0:1024]), "v")
                            p.wait(T, "v", upto=p.cnt["v"])
                        else:
                            p.wait(S, "t", upto=t_tr)
                            p.wait(S, "t", upto=ck["pv_t"][par])
                            p.inc(S.activation(out=bau(sb, 8 + par, 1024, 2048),
                                               in_=ps["T"][:, 1024:2048], func=AF.Copy), "s")
                            p.wait(T, "s", upto=p.cnt["s"])
                    ck["tr_t"][par] = p.cnt["t"]
                    if par == 1:
                        p.wait(T, "v", upto=ck["oe_v"])
                        mm = None
                        for kc in range(16):
                            mm = T.matmul(ps["B"][b0:b0 + 64, 0:256], v4s(kc, hh),
                                          pts_pv(kc), start=(kc == 0), stop=(kc == 15))
                        p.inc(mm, "t")
                        ck["pv_t"][0] = p.cnt["t"]
                        ck["pv_t"][1] = p.cnt["t"]
                        p.wait(V, "t", upto=p.cnt["t"])
                        p.inc(V.tensor_copy(bau(sb, 2 + hh // 2, 256 * qp, 256 * qp + 256)
                                            [b0:b0 + 64, :], ps["B"][b0:b0 + 64, 0:256]), "v")
                        ck["oe_v"] = p.cnt["v"]

                st_t1(0); st_v1(0); st_s(0); st_t1(1); st_v1(1)
                for qt in range(16):
                    if qt + 2 < 16:
                        st_t1(qt + 2)
                    st_v2(qt)
                    st_t2(qt)
                    if qt + 2 < 16:
                        st_v1(qt + 2)
                    if qt + 1 < 16:
                        st_s(qt + 1)

        p.wait(G, "t")
        p.dma(G, "dw", lambda: G.dma_start(out=bau(sb, 11, 0, 2048), in_=wo_slice(256 * g)))
        p.wait(T, "dw"); p.wait(T, "v"); p.wait(T, "s")
        for dm in range(8):
            for t4 in range(4):
                mm = None
                for j in range(2):
                    mm = T.matmul(ps["A"][:, 512 * t4:512 * t4 + 512],
                                  bau(sb, 11, 1024 * j + 128 * dm, 1024 * j + 128 * dm + 128),
                                  bau(sb, 2 + j, 512 * t4, 512 * t4 + 512),
                                  start=(j == 0), stop=(j == 1))
                p.inc(mm, "t")
            p.wait(V, "t")
            for t4 in range(4):
                c0 = 512 * t4
                if g == 0:
                    lastv = V.tensor_scalar(fts(sb, dm, 1 + c0, 1 + c0 + 512),
                                            ps["A"][:, c0:c0 + 512],
                                            sb["vt"][:, 12 * dm + 0:12 * dm + 1],
                                            None, ALU.add, ALU.bypass)
                else:
                    lastv = V.tensor_tensor(fts(sb, dm, 1 + c0, 1 + c0 + 512),
                                            fts(sb, dm, 1 + c0, 1 + c0 + 512),
                                            ps["A"][:, c0:c0 + 512], ALU.add)
            p.inc(lastv, "v")
            p.wait(T, "v")
    p.bar()



_NC_CACHE = {}


def kernel(**inputs):
    sh, percore = host_prep(inputs)
    key = PHASES
    if key not in _NC_CACHE:
        _NC_CACHE[key] = build_nc()
    nc = _NC_CACHE[key]
    in_maps = []
    for c in range(8):
        m = dict(sh)
        m.update(percore[c])
        in_maps.append(m)
    res = run_bass_kernel_spmd(nc, in_maps, core_ids=list(range(8)))
    out = np.stack([np.ascontiguousarray(np.asarray(res.results[c]["yt"]).T) for c in range(8)])
    return out.astype(np.float32)


# version 13
# speedup vs baseline: 1.0939x; 1.0236x over previous
import math
import numpy as np
from contextlib import ExitStack

import concourse.bass as bass
import concourse.mybir as mybir
from concourse.bass_utils import run_bass_kernel_spmd

F32 = mybir.dt.float32
BF16 = mybir.dt.bfloat16
AF = mybir.ActivationFunctionType
ALU = mybir.AluOpType
AX = mybir.AxisListType

N = 2048
D = 1024
NCHUNK = 8
CPAD = 2052
NEGBIG = -float(2.0 ** 127)
PI = math.pi
HID = 2730
NBLK = 22
BAU = 13

PHASES = ("l0ffn_a", "l0att", "l0ffn_c", "l1ffn_d", "l1att", "l1ffn_f")
L0FFN_DEPTH = 2
L0FFN_MLP = True


def _f32c(a):
    return np.ascontiguousarray(np.asarray(a, dtype=np.float32))



def host_prep(inp):
    sh = {}
    sh["l0a_wq"] = _f32c(inp["L0_attn_Wq"])
    sh["l0a_wkv"] = _f32c(inp["L0_attn_Wkv"])
    sh["l0a_wo"] = _f32c(inp["L0_attn_Wo"])
    sh["relemb"] = _f32c(inp["L0_attn_rel_emb"])

    def bucket(rel):
        n = rel
        nb = 16
        ret = 0
        if n < 0:
            ret = nb
        n = abs(n)
        max_exact = nb // 2
        if n < max_exact:
            return ret + n
        v = max_exact + int(
            math.log(max(n, 1) / max_exact) / math.log(128 / max_exact) * (nb - max_exact))
        return ret + min(v, nb - 1)

    m436 = np.zeros((436, 32), dtype=np.float32)
    for u in range(436):
        d = 217 - u
        m436[u, bucket(d)] += 1.0
        m436[u, 31] -= 1.0
    sh["m436t"] = _f32c(m436.T)
    ediff = np.zeros((32, 1), dtype=np.float32)
    ediff[15, 0] = 1.0
    ediff[31, 0] = -1.0
    sh["ediff"] = ediff

    sh["l0f_wqkv"] = _f32c(inp["L0_ffn_wqkv"])
    sh["l0f_wo"] = _f32c(inp["L0_ffn_wo"])
    w1 = _f32c(inp["L0_ffn_w1"]); w2 = _f32c(inp["L0_ffn_w2"]); b1 = _f32c(inp["L0_ffn_b1"])
    HIDP = 128 * NBLK
    w1a = np.zeros((2, 1024, HIDP), np.float32); w1a[:, :, :HID] = w1[:, :, :HID]
    w1g = np.zeros((2, 1024, HIDP), np.float32); w1g[:, :, :HID] = w1[:, :, HID:]
    w1pb = np.zeros((2, NBLK, 1024, 256), np.float32)
    for b in range(NBLK):
        w1pb[:, b, :, :128] = w1a[:, :, 128 * b:128 * b + 128]
        w1pb[:, b, :, 128:] = w1g[:, :, 128 * b:128 * b + 128]
    sh["w1pb"] = _f32c(w1pb)
    w2p = np.zeros((2, HIDP, 1024), np.float32); w2p[:, :HID, :] = w2
    sh["w2pb"] = _f32c(w2p.reshape(2, NBLK, 128, 1024))
    b1a = np.zeros((2, HIDP), np.float32); b1a[:, :HID] = b1[:, :HID]
    b1g = np.zeros((2, HIDP), np.float32); b1g[:, :HID] = b1[:, HID:]
    b1p = np.zeros((2, 44, 128), np.float32)
    for b in range(NBLK):
        b1p[:, 2 * b] = b1a[:, 128 * b:128 * b + 128]
        b1p[:, 2 * b + 1] = b1g[:, 128 * b:128 * b + 128]
    sh["b1p"] = _f32c(b1p.reshape(88, 128))

    sh["l1a_wqT"] = _f32c(np.asarray(inp["L1_attn_Wq"]).T)
    sh["l1a_wkT"] = _f32c(np.asarray(inp["L1_attn_Wk"]).T)
    sh["l1a_wvT"] = _f32c(np.asarray(inp["L1_attn_Wv"]).T)
    sh["l1a_woT"] = _f32c(np.asarray(inp["L1_attn_Wo"]).T)
    sh["l1a_pT"] = _f32c(np.transpose(np.asarray(inp["L1_attn_P"]), (0, 2, 1)))
    sh["l1f_wT"] = _f32c(np.asarray(inp["L1_ffn_weight"]).T)

    vecs = np.zeros((12, 1024), np.float32)
    vecs[0] = _f32c(inp["L0_attn_bo"])
    vecs[1] = _f32c(inp["L1_attn_bq"])
    vecs[2] = _f32c(inp["L1_attn_bk"])
    vecs[3] = _f32c(inp["L1_attn_bv"])
    vecs[4] = _f32c(inp["L1_attn_bo"])
    lng = _f32c(inp["L0_ffn_ln_g"]); lnb = _f32c(inp["L0_ffn_ln_b"]); b2 = _f32c(inp["L0_ffn_b2"])
    vecs[5], vecs[6], vecs[7] = lng[0], lnb[0], b2[0]
    vecs[8], vecs[9], vecs[10] = lng[1], lnb[1], b2[1]
    sh["vecs"] = vecs
    qs = _f32c(inp["L0_ffn_q_scale"]); ks = _f32c(inp["L0_ffn_k_scale"])
    sh["v2"] = _f32c(np.stack([np.tile(qs[0], 2), np.tile(ks[0], 2),
                               np.tile(qs[1], 2), np.tile(ks[1], 2)]))

    import jax
    import jax.numpy as jnp
    with jax.default_device(jax.devices("cpu")[0]):
        half_idx = jnp.arange(0, 64, 2, dtype=jnp.float32)
        inv_freq = 1.0 / 10000.0 ** (half_idx / 64)
        tpos = jnp.arange(2048, dtype=jnp.float32)
        freqs = tpos[:, None] * inv_freq[None, :]
        base = (half_idx + 0.4 * 64) / (1.4 * 64)
        power = (tpos - 1024) / float(256)
        rsc = base[None, :] ** power[:, None]
        rcos = jnp.cos(freqs)
        rsin = jnp.sin(freqs)
    rows = np.arange(128) % 32
    sh["rotc"] = _f32c(np.asarray(rcos).T[rows, :])
    sh["rots"] = _f32c(np.asarray(rsin).T[rows, :])
    sh["rotsc"] = _f32c(np.asarray(rsc).T[rows, :])

    ident = np.eye(128, dtype=np.float32)
    ones = np.ones((128, 128), dtype=np.float32)
    ones64 = np.zeros((128, 128), np.float32); ones64[:64, :64] = 1.0; ones64[64:, 64:] = 1.0
    pi_m = np.zeros((128, 128), np.float32)
    for blk in range(2):
        o = 64 * blk
        pi_m[o:o + 32, o + 32:o + 64] = -np.eye(32)
        pi_m[o + 32:o + 64, o:o + 32] = np.eye(32)
    r = np.arange(128)[:, None]; u = np.arange(128)[None, :]
    tri1 = np.where(u < r, np.float32(NEGBIG), np.float32(0.0)).astype(np.float32)
    tri2 = np.where(u > r, np.float32(NEGBIG), np.float32(0.0)).astype(np.float32)
    jrev = np.zeros((128, 128), np.float32)
    jrev[np.arange(128), 127 - np.arange(128)] = 1.0
    sh["aux7"] = np.stack([ident, ones, ones64, pi_m.T, tri1, tri2, jrev]).astype(np.float32)
    sh["aux_f32"] = np.stack([ident, ones]).astype(np.float32)
    sh["jcol"] = _f32c((np.arange(128) % 32).astype(np.float32)[:, None])

    x = np.asarray(inp["L0_attn_x"], dtype=np.float32)
    percore = [{"xt": _f32c(x[c].T)} for c in range(8)]
    return sh, percore



class P:
    def __init__(self, nc, stack):
        self.nc = nc
        self.T, self.V, self.S, self.G, self.Y = nc.tensor, nc.vector, nc.scalar, nc.gpsimd, nc.sync
        self.sems, self.cnt = {}, {}
        for name in ["dw", "dx", "t", "v", "s", "g", "b"]:
            self.sems[name] = stack.enter_context(nc.semaphore(name))
            self.cnt[name] = 0
        self.hwm = {"dw": 0, "dx": 0}
        self.issuer_waited = {"dw": 0, "dx": 0}

    def inc(self, inst, name, n=1):
        inst.then_inc(self.sems[name], n)
        self.cnt[name] += n

    def seq(self, eng, inst, name):
        self.inc(inst, name)
        eng.wait_ge(self.sems[name], self.cnt[name])

    def wait(self, eng, name, lag=0, upto=None):
        v = (self.cnt[name] - lag) if upto is None else upto
        if v > 0:
            eng.wait_ge(self.sems[name], v)
            if name in self.hwm and v > self.hwm[name]:
                self.hwm[name] = v

    def dma(self, eng, name, fn):
        if self.hwm[name] > self.issuer_waited[name]:
            eng.wait_ge(self.sems[name], self.hwm[name])
            self.issuer_waited[name] = self.hwm[name]
        self.inc(fn(), name, 16)

    def bar(self):
        for eng in (self.T, self.V, self.S, self.G):
            self.inc(eng.drain(), "b", 1)
        self.inc(self.Y.nop(), "b", 1)
        tgt = self.cnt["b"]
        for eng in (self.T, self.V, self.S, self.G, self.Y):
            eng.wait_ge(self.sems["b"], tgt)



def hbs(sb, c, c0, c1):
    return sb["hb"][:, c * CPAD + c0: c * CPAD + c1]


def fts(sb, c, c0, c1):
    return sb["ftile"][:, c * CPAD + c0: c * CPAD + c1]


def bau(sb, unit, c0, c1):
    return sb["ba"][:, unit * CPAD + c0: unit * CPAD + c1]



def build_nc():
    nc = bass.Bass("TRN2", target_bir_lowering=False)
    dr = {}

    def din(name, shape):
        dr[name] = nc.dram_tensor(name, list(shape), F32, kind="ExternalInput")

    din("xt", [D, N])
    din("l0a_wq", [D, D]); din("l0a_wkv", [D, 2 * D]); din("l0a_wo", [D, D])
    din("relemb", [32, 16]); din("m436t", [32, 436]); din("ediff", [32, 1])
    din("l0f_wqkv", [2, D, 1536]); din("l0f_wo", [2, 512, D])
    din("w1pb", [2, NBLK, D, 256]); din("w2pb", [2, NBLK, 128, D]); din("b1p", [88, 128])
    din("l1a_wqT", [D, D]); din("l1a_wkT", [D, D]); din("l1a_wvT", [D, D]); din("l1a_woT", [D, D])
    din("l1a_pT", [16, 64, 64]); din("l1f_wT", [D, D])
    din("vecs", [12, D]); din("v2", [4, 128]); din("aux7", [7, 128, 128])
    din("rotc", [128, N]); din("rots", [128, N]); din("rotsc", [128, N])
    din("aux_f32", [2, 128, 128]); din("jcol", [128, 1])
    dr["yt"] = nc.dram_tensor("yt", [D, N], F32, kind="ExternalOutput")
    dr["xm"] = nc.dram_tensor("xm", [D, N], F32)
    dr["gbuf"] = nc.dram_tensor("gbuf", [16, 576], F32)

    stack = ExitStack()
    sb = {}

    def sbuf(name, shape, dt):
        sb[name] = stack.enter_context(nc.sbuf_tensor(name, list(shape), dt))

    sbuf("hb", [128, NCHUNK * CPAD], BF16)
    sbuf("tbl", [128, 4 * N], BF16)
    sbuf("ftile", [128, NCHUNK * CPAD], F32)
    sbuf("xcb", [128, 2 * N], F32)
    sbuf("stat", [128, 2 * N], F32)
    sbuf("sc2", [128, 1024], F32)
    sbuf("ba", [128, BAU * CPAD], BF16)
    sbuf("vt", [128, 96], F32); sbuf("v2t", [128, 4], F32); sbuf("b1t", [128, 88], F32)
    sbuf("dct", [128, 16], F32); sbuf("sc1", [128, 64], F32)
    for nme in ["id_bf", "ones_bf", "o64_bf", "pit_bf", "tri1_bf", "tri2_bf", "j_bf"]:
        sbuf(nme, [128, 128], BF16)
    sbuf("id_f", [128, 128], F32); sbuf("ones_f", [128, 128], F32)
    sbuf("rel_sb", [32, 16], F32); sbuf("ediff_sb", [32, 1], F32)
    sbuf("jcol_sb", [128, 1], F32)
    sbuf("tmp16", [128, 16], F32)
    sbuf("pk_bf", [128, 1024], BF16)
    sbuf("stg", [128, 128], F32)

    ps = {}
    ps["A"] = stack.enter_context(nc.psum_tensor("psA", [128, 2048], F32))
    ps["B"] = stack.enter_context(nc.psum_tensor("psB", [128, 1024], F32))
    ps["T"] = stack.enter_context(nc.psum_tensor("psT", [128, 2048], BF16))

    p = P(nc, stack)
    emit_program(p, dr, sb, ps)
    stack.close()
    return nc


def emit_program(p, dr, sb, ps):
    emit_prologue(p, dr, sb, ps)
    on = lambda ph: ph in PHASES
    emit_boundary(p, dr, sb, ps, src="xt", dst="xm", delta=None, scale=0.0,
                  want_f=on("l0ffn_a"))
    if on("l0ffn_a"):
        emit_l0ffn(p, dr, sb, ps)
    emit_boundary(p, dr, sb, ps, src="xm", dst="xm",
                  delta=on("l0ffn_a"), scale=0.5, want_f=False)
    if on("l0att"):
        emit_l0att(p, dr, sb, ps)
    emit_boundary(p, dr, sb, ps, src="xm", dst="xm",
                  delta=on("l0att"), scale=1.0, want_f=on("l0ffn_c"))
    if on("l0ffn_c"):
        emit_l0ffn(p, dr, sb, ps)
    emit_boundary(p, dr, sb, ps, src="xm", dst="xm",
                  delta=on("l0ffn_c"), scale=0.5, want_f=False)
    if on("l1ffn_d") or on("l1ffn_f"):
        emit_l1quant(p, dr, sb, ps)
    if on("l1ffn_d"):
        emit_l1ffn(p, dr, sb, ps)
    emit_boundary(p, dr, sb, ps, src="xm", dst="xm",
                  delta=on("l1ffn_d"), scale=0.5, want_f=False)
    if on("l1att"):
        emit_l1att(p, dr, sb, ps)
    emit_boundary(p, dr, sb, ps, src="xm", dst="xm",
                  delta=on("l1att"), scale=1.0, want_f=False)
    if on("l1ffn_f"):
        emit_l1ffn(p, dr, sb, ps)
    emit_boundary(p, dr, sb, ps, src="xm", dst="yt",
                  delta=on("l1ffn_f"), scale=0.5, want_f=False, last=True)



def emit_prologue(p, dr, sb, ps):
    T, V, S, G, Y = p.T, p.V, p.S, p.G, p.Y

    for name, src in [("rel_sb", dr["relemb"]),
                      ("ediff_sb", dr["ediff"]), ("jcol_sb", dr["jcol"])]:
        p.dma(Y, "dx", lambda: Y.dma_start(out=sb[name][:, :], in_=src[:, :]))
    m436_sb = sb["stat"][0:32, 1536:1972]
    p.dma(Y, "dx", lambda: Y.dma_start(out=m436_sb, in_=dr["m436t"][:, :]))
    vload = sb["stat"][0:12, 0:1024]
    v2load = sb["stat"][0:4, 1024:1152]
    b1load = [sb["stat"][0:44, 1152 + 128 * l:1280 + 128 * l] for l in range(2)]
    p.dma(Y, "dx", lambda: Y.dma_start(out=vload, in_=dr["vecs"][:, :]))
    p.dma(Y, "dx", lambda: Y.dma_start(out=v2load, in_=dr["v2"][:, :]))
    for l in range(2):
        p.dma(Y, "dx", lambda: Y.dma_start(out=b1load[l], in_=dr["b1p"][44 * l:44 * l + 44, :]))
    p.dma(Y, "dx", lambda: Y.dma_start(out=sb["id_f"][:, :], in_=dr["aux_f32"][0, :, :]))
    p.dma(Y, "dx", lambda: Y.dma_start(out=sb["ones_f"][:, :], in_=dr["aux_f32"][1, :, :]))
    for i, name in enumerate(["id_bf", "ones_bf", "o64_bf", "pit_bf", "tri1_bf", "tri2_bf", "j_bf"]):
        p.dma(G, "dw", lambda: G.dma_start(out=sb[name][:, :], in_=dr["aux7"][i, :, :]))

    p.wait(T, "dx"); p.wait(T, "dw")

    def tr_small(dst_ap, src_ap):
        np_, nf = src_ap.shape[0], src_ap.shape[1]
        p.inc(T.transpose(ps["A"][0:nf, 0:np_], src_ap, sb["id_f"][0:np_, 0:np_]), "t")
        p.wait(V, "t")
        p.inc(V.tensor_copy(dst_ap, ps["A"][0:nf, 0:np_]), "v")
        p.wait(T, "v")

    for c in range(NCHUNK):
        tr_small(sb["vt"][:, 12 * c:12 * c + 12], vload[:, 128 * c:128 * c + 128])
    tr_small(sb["v2t"][:, :], v2load[:, :])
    for l in range(2):
        tr_small(sb["b1t"][:, 44 * l:44 * l + 44], b1load[l])

    for ch, ulen in enumerate([128, 128, 128, 52]):
        u0 = 128 * ch
        p.inc(T.matmul(ps["B"][0:ulen, 0:16], m436_sb[:, u0:u0 + ulen], sb["rel_sb"][:, :],
                       start=True, stop=True), "t")
        p.wait(V, "t")
        p.inc(V.tensor_copy(sb["tmp16"][0:ulen, 0:16], ps["B"][0:ulen, 0:16]), "v")
        p.wait(T, "v")
        p.inc(T.transpose(ps["B"][0:16, 128:128 + ulen], sb["tmp16"][0:ulen, 0:16],
                          sb["id_f"][0:ulen, 0:ulen]), "t")
        p.wait(V, "t"); p.wait(V, "dx")
        p.inc(V.tensor_copy(sb["stg"][0:16, 0:ulen], ps["B"][0:16, 128:128 + ulen]), "v")
        p.wait(Y, "v")
        p.dma(Y, "dx", lambda: Y.dma_start(out=dr["gbuf"][:, u0:u0 + ulen], in_=sb["stg"][0:16, 0:ulen]))
        p.wait(T, "v")

    p.inc(T.matmul(ps["B"][0:1, 0:16], sb["ediff_sb"][:, :], sb["rel_sb"][:, :],
                   start=True, stop=True), "t")
    p.wait(V, "t")
    p.inc(V.tensor_copy(sb["tmp16"][0:1, 0:16], ps["B"][0:1, 0:16]), "v")
    p.wait(T, "v")
    p.inc(T.matmul(ps["B"][:, 16:32], sb["ones_f"][0:1, :], sb["tmp16"][0:1, 0:16],
                   start=True, stop=True), "t")
    p.wait(V, "t")
    p.inc(V.tensor_copy(sb["dct"][:, :], ps["B"][:, 16:32]), "v")

    emit_rotary_tables(p, dr, sb, ps)
    p.wait(V, "v")
    for c in range(NCHUNK):
        V.memset(sb["hb"][:, c * CPAD:c * CPAD + 1], 0.0)
        last = V.memset(sb["ftile"][:, c * CPAD:c * CPAD + 1], 0.0)
    p.inc(last, "v")
    p.bar()


def emit_rotary_tables(p, dr, sb, ps):
    T, V, S, G, Y = p.T, p.V, p.S, p.G, p.Y
    cosb = sb["xcb"][:, 0:N]
    sinb = sb["xcb"][:, N:2 * N]
    scb = sb["stat"][:, 0:N]
    p.wait(Y, "v")
    p.dma(Y, "dx", lambda: Y.dma_start(out=cosb, in_=dr["rotc"][:, :]))
    p.dma(Y, "dx", lambda: Y.dma_start(out=sinb, in_=dr["rots"][:, :]))
    p.dma(Y, "dx", lambda: Y.dma_start(out=scb, in_=dr["rotsc"][:, :]))
    p.wait(V, "dx")
    tbl = sb["tbl"]
    p.seq(V, V.tensor_tensor(sb["sc2"][:, 0:1024], cosb[:, 0:1024], scb[:, 0:1024], ALU.mult), "v")
    p.seq(V, V.tensor_scalar(tbl[:, 0:1024], sb["sc2"][:, 0:1024], 8.0, None, ALU.mult, ALU.bypass), "v")
    p.seq(V, V.tensor_tensor(sb["sc2"][:, 0:1024], cosb[:, 1024:2048], scb[:, 1024:2048], ALU.mult), "v")
    p.seq(V, V.tensor_scalar(tbl[:, 1024:2048], sb["sc2"][:, 0:1024], 8.0, None, ALU.mult, ALU.bypass), "v")
    p.seq(V, V.tensor_tensor(sb["sc2"][:, 0:1024], sinb[:, 0:1024], scb[:, 0:1024], ALU.mult), "v")
    p.seq(V, V.tensor_scalar(tbl[:, N:N + 1024], sb["sc2"][:, 0:1024], 8.0, None, ALU.mult, ALU.bypass), "v")
    p.seq(V, V.tensor_tensor(sb["sc2"][:, 0:1024], sinb[:, 1024:2048], scb[:, 1024:2048], ALU.mult), "v")
    p.seq(V, V.tensor_scalar(tbl[:, N + 1024:2 * N], sb["sc2"][:, 0:1024], 8.0, None, ALU.mult, ALU.bypass), "v")
    p.seq(V, V.reciprocal(sb["sc2"][:, 0:1024], scb[:, 0:1024]), "v")
    p.seq(V, V.tensor_tensor(tbl[:, 2 * N:2 * N + 1024], cosb[:, 0:1024],
                             sb["sc2"][:, 0:1024], ALU.mult), "v")
    p.seq(V, V.tensor_tensor(tbl[:, 3 * N:3 * N + 1024], sinb[:, 0:1024],
                             sb["sc2"][:, 0:1024], ALU.mult), "v")
    p.seq(V, V.reciprocal(sb["sc2"][:, 0:1024], scb[:, 1024:2048]), "v")
    p.seq(V, V.tensor_tensor(tbl[:, 2 * N + 1024:3 * N], cosb[:, 1024:2048],
                             sb["sc2"][:, 0:1024], ALU.mult), "v")
    p.inc(V.tensor_tensor(tbl[:, 3 * N + 1024:4 * N], sinb[:, 1024:2048],
                          sb["sc2"][:, 0:1024], ALU.mult), "v")



def emit_boundary(p, dr, sb, ps, src, dst, delta, scale, want_f, last=False):
    T, V, S, G, Y = p.T, p.V, p.S, p.G, p.Y
    xsrc, xdst = dr[src], dr[dst]

    def xbc(c, c0, c1):
        return bau(sb, c, c0, c1)

    v_ck = [p.cnt["v"]] * 2
    st_ck = [p.cnt["dx"]] * 2
    for c in range(NCHUNK):
        par = c % 2
        buf = sb["xcb"][:, par * N:par * N + N]
        p.wait(Y, "v", upto=v_ck[par])
        p.wait(Y, "dx", upto=st_ck[par])
        p.dma(Y, "dx", lambda: Y.dma_start(out=buf, in_=xsrc[128 * c:128 * c + 128, :]))
        ld_dx = p.cnt["dx"]
        p.wait(V, "dx", upto=ld_dx)
        lastv = None
        if delta:
            lastv = V.scalar_tensor_tensor(buf, fts(sb, c, 1, 1 + N), scale, buf, ALU.mult, ALU.add)
            if not last:
                p.seq(V, lastv, "v")
        if not last:
            lastv = V.tensor_copy(xbc(c, 0, N), buf)
        if lastv is not None:
            p.inc(lastv, "v")
        v_ck[par] = p.cnt["v"]
        p.wait(Y, "v", upto=v_ck[par])
        if lastv is None:
            p.wait(Y, "dx")
        p.dma(Y, "dx", lambda: Y.dma_start(out=xdst[128 * c:128 * c + 128, :], in_=buf))
        st_ck[par] = p.cnt["dx"]
    if last:
        p.wait(Y, "dx")
        p.bar()
        return

    p.wait(S, "v")
    for tc in range(4):
        c0 = 512 * tc
        if tc > 0:
            p.wait(T, "v")
        for c in range(NCHUNK):
            par = c % 2
            sq = bau(sb, 8, 512 * par, 512 * par + 512)
            if tc > 0 or c > 1:
                p.wait(S, "t", lag=1)
            p.inc(S.activation(out=sq, in_=xbc(c, c0, c0 + 512), func=AF.Square), "s")
            p.wait(T, "s")
            T.matmul(ps["B"][:, 0:512], sb["ones_bf"][:, :], xbc(c, c0, c0 + 512),
                     start=(c == 0), stop=(c == NCHUNK - 1))
            p.inc(T.matmul(ps["B"][:, 512:1024], sb["ones_bf"][:, :], sq,
                           start=(c == 0), stop=(c == NCHUNK - 1)), "t")
        p.wait(V, "t")
        mean = sb["stat"][:, c0:c0 + 512]
        rstd = sb["stat"][:, N + c0:N + c0 + 512]
        p.seq(V, V.tensor_scalar(mean, ps["B"][:, 0:512], 1.0 / 1024.0, None,
                                 ALU.mult, ALU.bypass), "v")
        p.seq(V, V.tensor_scalar(rstd, ps["B"][:, 512:1024], 1.0 / 1024.0, None,
                                 ALU.mult, ALU.bypass), "v")
        p.seq(V, V.tensor_tensor(sb["sc2"][:, 0:512], mean, mean, ALU.mult), "v")
        p.seq(V, V.tensor_tensor(rstd, rstd, sb["sc2"][:, 0:512], ALU.subtract), "v")
        p.inc(V.tensor_scalar(rstd, rstd, 1e-5, None, ALU.add, ALU.bypass), "v")
        p.wait(S, "v")
        p.seq(S, S.activation(out=rstd, in_=rstd, func=AF.Ln), "s")
        p.inc(S.activation(out=rstd, in_=rstd, func=AF.Exp, scale=-0.5), "s")
    p.wait(V, "s")
    for c in range(NCHUNK):
        for tc in range(4):
            c0 = 512 * tc
            mean = sb["stat"][:, c0:c0 + 512]
            rstd = sb["stat"][:, N + c0:N + c0 + 512]
            out = fts(sb, c, 1 + c0, 1 + c0 + 512) if want_f else hbs(sb, c, 1 + c0, 1 + c0 + 512)
            p.seq(V, V.tensor_tensor(sb["sc2"][:, 512:1024], xbc(c, c0, c0 + 512),
                                     mean, ALU.subtract), "v")
            p.seq(V, V.tensor_tensor(out, sb["sc2"][:, 512:1024], rstd, ALU.mult), "v")
    p.inc(V.nop(), "b", 0) if False else None
    if want_f:
        emit_shiftcast(p, sb)
    p.bar()


def emit_shiftcast(p, sb):
    V = p.V
    p.wait(V, "v")
    for c in range(NCHUNK):
        if c < 4:
            lastv = V.tensor_copy(hbs(sb, c, 1, 1 + N), fts(sb, c, 1, 1 + N))
        else:
            lastv = V.tensor_copy(hbs(sb, c, 1, 1 + N), fts(sb, c, 0, N))
    p.inc(lastv, "v")



WQB_UNIT = 9


def wqbs(sb, ic, oc):
    base = WQB_UNIT * CPAD + 1024 * ic + 128 * oc
    return sb["ba"][:, base:base + 128]


def emit_l1quant(p, dr, sb, ps):
    T, V, S, G, Y = p.T, p.V, p.S, p.G, p.Y
    for c in range(NCHUNK):
        buf = sb["xcb"][:, (c % 2) * 1024:(c % 2) * 1024 + 1024]
        p.wait(Y, "v")
        p.dma(Y, "dx", lambda: Y.dma_start(out=buf, in_=dr["l1f_wT"][128 * c:128 * c + 128, :]))
        p.wait(V, "dx")
        p.inc(V.tensor_reduce(sb["sc1"][:, c:c + 1], buf, axis=AX.X, op=ALU.add,
                              apply_absolute_value=True), "v")
    p.wait(V, "v")
    p.inc(V.tensor_reduce(sb["sc1"][:, 8:9], sb["sc1"][:, 0:8], axis=AX.X, op=ALU.add), "v")
    p.wait(T, "v")
    p.inc(T.matmul(ps["B"][:, 0:1], sb["ones_f"][:, :], sb["sc1"][:, 8:9],
                   start=True, stop=True), "t")
    p.wait(V, "t")
    p.seq(V, V.tensor_scalar(sb["sc1"][:, 9:10], ps["B"][:, 0:1], 1.0 / (2.0 ** 20), 1e-5,
                             ALU.mult, ALU.max), "v")
    p.seq(V, V.tensor_scalar(sb["sc1"][:, 10:11], sb["sc1"][:, 9:10], 0.5, None,
                             ALU.mult, ALU.bypass), "v")
    p.inc(V.tensor_scalar(sb["sc1"][:, 11:12], sb["sc1"][:, 10:11], -1.0, None,
                          ALU.mult, ALU.bypass), "v")
    for c in range(NCHUNK):
        buf = sb["xcb"][:, (c % 2) * 1024:(c % 2) * 1024 + 1024]
        p.wait(Y, "v")
        p.dma(Y, "dx", lambda: Y.dma_start(out=buf, in_=dr["l1f_wT"][128 * c:128 * c + 128, :]))
        p.wait(V, "dx")
        ge = sb["sc2"][:, 0:1024]
        p.seq(V, V.tensor_scalar(ge, buf, sb["sc1"][:, 10:11], None, ALU.is_ge, ALU.bypass), "v")
        p.seq(V, V.tensor_scalar(buf, buf, sb["sc1"][:, 11:12], None, ALU.is_le, ALU.bypass), "v")
        p.seq(V, V.tensor_tensor(ge, ge, buf, ALU.subtract), "v")
        dst = sb["ba"][:, WQB_UNIT * CPAD + 1024 * c: WQB_UNIT * CPAD + 1024 * c + 1024]
        p.inc(V.tensor_scalar(dst, ge, sb["sc1"][:, 9:10], None, ALU.mult, ALU.bypass), "v")
    p.bar()


def emit_l1ffn(p, dr, sb, ps):
    T, V, S, G, Y = p.T, p.V, p.S, p.G, p.Y
    ev_ck = [p.cnt["v"]] * 4
    for oc in range(8):
        for t in range(4):
            p.wait(T, "v", upto=ev_ck[t])
            for ic in range(NCHUNK):
                mm = T.matmul(ps["A"][:, 512 * t:512 * t + 512], wqbs(sb, ic, oc),
                              hbs(sb, ic, 1 + 512 * t, 1 + 512 * t + 512),
                              start=(ic == 0), stop=(ic == NCHUNK - 1))
            p.inc(mm, "t")
            p.wait(V, "t")
            p.inc(V.tensor_copy(fts(sb, oc, 1 + 512 * t, 1 + 512 * t + 512),
                                ps["A"][:, 512 * t:512 * t + 512]), "v")
            ev_ck[t] = p.cnt["v"]
    p.bar()



def emit_l1att(p, dr, sb, ps):
    T, V, S, G, Y = p.T, p.V, p.S, p.G, p.Y

    for h in range(16):
        p.dma(G, "dw", lambda: G.dma_start(out=sb["pk_bf"][64 * (h % 2):64 * (h % 2) + 64,
                                          64 * (h // 2):64 * (h // 2) + 64],
                          in_=dr["l1a_pT"][h, :, :]))
    brow = bau(sb, 3, 0, 1024)
    p.dma(G, "dw", lambda: G.dma_start(out=brow[0:1, :], in_=dr["vecs"][3:4, :]))

    def wslice(t, col0, ncol):
        return bass.AP(t, col0, [[1024, 128], [131072, 8], [1, ncol]])
    woT = dr["l1a_woT"].ap().rearrange("(a b) c -> a b c", b=128)

    ptb = lambda h: sb["pk_bf"][64 * (h % 2):64 * (h % 2) + 64,
                                64 * (h // 2):64 * (h // 2) + 64]
    kvb = lambda h: sb["pk_bf"][64 * (h % 2):64 * (h % 2) + 64,
                                512 + 64 * (h // 2):512 + 64 * (h // 2) + 64]

    for hg in range(4):
        wvu = 4 + hg % 2
        p.wait(G, "t")
        p.dma(G, "dw", lambda: G.dma_start(out=bau(sb, wvu, 0, 2048),
                          in_=wslice(dr["l1a_wvT"], 256 * hg, 256)))
        p.wait(T, "dw")
        for tt in range(16):
            for ic in range(NCHUNK):
                T.matmul(ps["A"][:, 256 * (tt % 4):256 * (tt % 4) + 256],
                         hbs(sb, ic, 1 + 128 * tt, 1 + 128 * tt + 128),
                         bau(sb, wvu, 256 * ic, 256 * ic + 256),
                         start=(ic == 0), stop=False)
            p.inc(T.matmul(ps["A"][:, 256 * (tt % 4):256 * (tt % 4) + 256],
                           sb["ones_bf"][0:1, :], brow[0:1, 256 * hg:256 * hg + 256],
                           start=False, stop=True), "t")
            if tt % 4 == 3:
                p.wait(V, "t")
                for q in range(4):
                    tt2 = tt - 3 + q
                    lastv = V.tensor_copy(
                        bau(sb, 1 + tt2 // 8, 256 * (tt2 % 8), 256 * (tt2 % 8) + 256),
                        ps["A"][:, 256 * q:256 * q + 256])
                p.inc(lastv, "v")
                p.wait(T, "v")
        for hh in range(4):
            h = 4 * hg + hh
            b0 = 64 * (h % 2)
            wku = 1024 * (h % 2)
            p.wait(G, "t")
            p.dma(G, "dw", lambda: G.dma_start(out=bau(sb, 6, wku, wku + 512),
                              in_=wslice(dr["l1a_wkT"], 64 * h, 64)))
            p.wait(T, "dw")
            for t4 in range(4):
                mm = None
                for ic in range(NCHUNK):
                    mm = T.matmul(ps["A"][b0:b0 + 64, 512 * t4:512 * t4 + 512],
                                  bau(sb, 6, wku + 64 * ic, wku + 64 * ic + 64),
                                  hbs(sb, ic, 1 + 512 * t4, 1 + 512 * t4 + 512),
                                  start=(ic == 0), stop=(ic == NCHUNK - 1))
                p.inc(mm, "t")
            p.wait(S, "t")
            p.inc(S.activation(out=bau(sb, 8, 0, N)[b0:b0 + 64, :], in_=ps["A"][b0:b0 + 64, 0:N],
                               func=AF.Identity,
                               bias=sb["vt"][b0:b0 + 64, 12 * (h // 2) + 2:12 * (h // 2) + 3]), "s")
            p.wait(T, "s")
            for tt in range(16):
                p.inc(T.matmul(ps["B"][:, 64 * (tt % 8):64 * (tt % 8) + 64],
                               bau(sb, 8, 128 * tt, 128 * tt + 128)[b0:b0 + 64, :], ptb(h),
                               start=True, stop=True), "t")
                if tt % 8 == 7:
                    p.wait(S, "t")
                    for q in range(8):
                        tt2 = tt - 7 + q
                        lasts = S.activation(
                            out=bau(sb, 0, 256 * (tt2 % 8) + 64 * (tt2 // 8),
                                    256 * (tt2 % 8) + 64 * (tt2 // 8) + 64),
                            in_=ps["B"][:, 64 * q:64 * q + 64], func=AF.Relu, scale=0.125)
                    p.inc(lasts, "s")
                    p.wait(T, "s")
            for tt in range(16):
                mm = T.matmul(ps["B"][b0:b0 + 64, 512:576],
                              bau(sb, 0, 256 * (tt % 8) + 64 * (tt // 8),
                                  256 * (tt % 8) + 64 * (tt // 8) + 64),
                              bau(sb, 1 + tt // 8, 256 * (tt % 8) + 64 * hh,
                                  256 * (tt % 8) + 64 * hh + 64),
                              start=(tt == 0), stop=(tt == 15))
            p.inc(mm, "t")
            p.wait(V, "t")
            p.inc(V.tensor_copy(kvb(h), ps["B"][b0:b0 + 64, 512:576]), "v")
            p.wait(T, "v")

    p.wait(G, "t")
    woff = 4 * CPAD
    for ic in range(NCHUNK):
        p.dma(G, "dw", lambda: G.dma_start(out=sb["ba"][:, woff + 1024 * ic:woff + 1024 * ic + 1024],
                          in_=woT[ic, :, :]))
    p.wait(T, "dw")

    def wo(cc, cols):
        return sb["ba"][:, woff + 1024 * cc + cols[0]:woff + 1024 * cc + cols[1]]

    for tc in range(4):
        c0 = 512 * tc
        for h in range(16):
            pq = ps["B"] if h % 2 == 0 else ps["A"]
            b0 = 64 * (h % 2)
            wku = 1024 * (h % 2)
            if tc == 0 or True:
                p.wait(G, "t")
                p.dma(G, "dw", lambda: G.dma_start(out=bau(sb, 1, wku, wku + 512),
                                  in_=wslice(dr["l1a_wqT"], 64 * h, 64)))
                p.wait(T, "dw")
            mm = None
            for ic in range(NCHUNK):
                mm = T.matmul(pq[b0:b0 + 64, 0:512],
                              bau(sb, 1, wku + 64 * ic, wku + 64 * ic + 64),
                              hbs(sb, ic, 1 + c0, 1 + c0 + 512),
                              start=(ic == 0), stop=(ic == NCHUNK - 1))
            p.inc(mm, "t")
            p.wait(S, "t")
            p.inc(S.activation(out=bau(sb, 8, 0, 512)[b0:b0 + 64, :], in_=pq[b0:b0 + 64, 0:512],
                               func=AF.Identity,
                               bias=sb["vt"][b0:b0 + 64, 12 * (h // 2) + 1:12 * (h // 2) + 2]), "s")
            p.wait(T, "s")
            p.inc(T.matmul(pq[b0:b0 + 64, 512:1024], ptb(h), bau(sb, 8, 0, 512)[b0:b0 + 64, :],
                           start=True, stop=True), "t")
            p.wait(S, "t")
            p.inc(S.activation(out=bau(sb, 8, 512, 1024)[b0:b0 + 64, :],
                               in_=pq[b0:b0 + 64, 512:1024], func=AF.Relu, scale=0.125), "s")
            p.wait(T, "s")
            p.inc(T.matmul(pq[b0:b0 + 64, 0:512], kvb(h), bau(sb, 8, 512, 1024)[b0:b0 + 64, :],
                           start=True, stop=True), "t")
            p.wait(V, "t")
            cc = h // 2
            p.inc(V.tensor_copy(bau(sb, 2 + cc // 4, 512 * (cc % 4), 512 * (cc % 4) + 512)
                                [b0:b0 + 64, :], pq[b0:b0 + 64, 0:512]), "v")
            p.wait(T, "v")
        for dm in range(8):
            mm = None
            for cc in range(8):
                mm = T.matmul(ps["A"][:, 1024 + 512 * (dm % 2):1024 + 512 * (dm % 2) + 512],
                              wo(cc, (128 * dm, 128 * dm + 128)),
                              bau(sb, 2 + cc // 4, 512 * (cc % 4), 512 * (cc % 4) + 512),
                              start=(cc == 0), stop=(cc == 7))
            p.inc(mm, "t")
            p.wait(V, "t")
            p.inc(V.tensor_scalar(fts(sb, dm, 1 + c0, 1 + c0 + 512),
                                  ps["A"][:, 1024 + 512 * (dm % 2):1024 + 512 * (dm % 2) + 512],
                                  sb["vt"][:, 12 * dm + 4:12 * dm + 5], None, ALU.add, ALU.bypass), "v")
            p.wait(T, "v")
    p.bar()



def emit_l0ffn(p, dr, sb, ps):
    T, V, S, G, Y = p.T, p.V, p.S, p.G, p.Y
    tbl = sb["tbl"]

    def wqkv_slice(l, col0, ncol):
        return bass.AP(dr["l0f_wqkv"], 1572864 * l + col0, [[1536, 128], [196608, 8], [1, ncol]])

    def wo_slice(l, col0, ncol):
        return bass.AP(dr["l0f_wo"], 524288 * l + col0, [[1024, 128], [131072, 4], [1, ncol]])

    def vts(kidx, h):
        return bau(sb, kidx // 4, 512 * (kidx % 4) + 64 * h, 512 * (kidx % 4) + 64 * h + 64)

    for l in range(L0FFN_DEPTH):
        p.wait(G, "t"); p.wait(G, "s")
        p.dma(G, "dw", lambda: G.dma_start(out=bau(sb, 11, 0, 2048),
                                           in_=wqkv_slice(l, 1024, 256)))
        p.dma(G, "dw", lambda: G.dma_start(out=bau(sb, 12, 0, 2048),
                                           in_=wqkv_slice(l, 1280, 256)))
        p.wait(T, "dw"); p.wait(T, "v")
        vtm_ck = [p.cnt["v"]] * 2
        for tt in range(16):
            mm = None
            p.wait(T, "v", upto=vtm_ck[tt % 2])
            for half in range(2):
                u = 11 + half
                for ic in range(NCHUNK):
                    mm = T.matmul(ps["A"][:, 1024 * (tt % 2) + 256 * half:
                                          1024 * (tt % 2) + 256 * half + 256],
                                  hbs(sb, ic, 1 + 128 * tt, 1 + 128 * tt + 128),
                                  bau(sb, u, 256 * ic, 256 * ic + 256),
                                  start=(ic == 0), stop=(ic == NCHUNK - 1))
            p.inc(mm, "t")
            p.wait(V, "t")
            lastv = V.tensor_copy(bau(sb, tt // 4, 512 * (tt % 4), 512 * (tt % 4) + 512),
                                  ps["A"][:, 1024 * (tt % 2):1024 * (tt % 2) + 512])
            p.inc(lastv, "v")
            vtm_ck[tt % 2] = p.cnt["v"]
        for hp in range(4):
            p.wait(G, "t")
            p.dma(G, "dw", lambda: G.dma_start(out=bau(sb, 11, 0, 1024),
                                               in_=wqkv_slice(l, 128 * hp, 128)))
            p.dma(G, "dw", lambda: G.dma_start(out=bau(sb, 11, 1024, 2048),
                                               in_=wqkv_slice(l, 512 + 128 * hp, 128)))
            p.wait(T, "dw"); p.wait(T, "s"); p.wait(T, "v")
            qn_v_ck = [p.cnt["v"]] * 8
            ones_v_ck = [p.cnt["v"]] * 2
            pi_v_ck = [p.cnt["v"]] * 2
            sq_t_ck = [p.cnt["t"]] * 2
            qn_t_ck = [p.cnt["t"]] * 2
            sc2_s_ck = [p.cnt["s"]] * 2
            it = 0
            for which, unit, ct0, st0, scol in ((0, 4, 0, N, 2 * l), (1, 5, 2 * N, 3 * N, 2 * l + 1)):
                for tc in range(4):
                    c0 = 512 * tc
                    par = it % 2
                    it += 1
                    p.wait(T, "v", upto=qn_v_ck[4 * (1 - which) + tc])
                    mm = None
                    for ic in range(NCHUNK):
                        mm = T.matmul(ps["A"][:, c0:c0 + 512],
                                      bau(sb, 11, 1024 * which + 128 * ic,
                                          1024 * which + 128 * ic + 128),
                                      hbs(sb, ic, 1 + c0, 1 + c0 + 512),
                                      start=(ic == 0), stop=(ic == NCHUNK - 1))
                    p.inc(mm, "t")
                    p.wait(S, "t")
                    p.wait(S, "t", upto=sq_t_ck[par])
                    p.inc(S.activation(out=bau(sb, 12, 512 * par, 512 * par + 512),
                                       in_=ps["A"][:, c0:c0 + 512], func=AF.Square), "s")
                    p.wait(T, "s")
                    p.wait(T, "v", upto=ones_v_ck[par])
                    p.inc(T.matmul(ps["B"][:, 512 * par:512 * par + 512], sb["o64_bf"][:, :],
                                   bau(sb, 12, 512 * par, 512 * par + 512),
                                   start=True, stop=True), "t")
                    sq_t_ck[par] = p.cnt["t"]
                    p.wait(V, "t")
                    p.wait(V, "s", upto=sc2_s_ck[par])
                    p.seq(V, V.tensor_scalar(sb["sc2"][:, 512 * par:512 * par + 512],
                                             ps["B"][:, 512 * par:512 * par + 512],
                                             1e-12 ** 2, None, ALU.max, ALU.bypass), "v")
                    ones_v_ck[par] = p.cnt["v"]
                    p.wait(S, "v")
                    p.seq(S, S.activation(out=sb["sc2"][:, 512 * par:512 * par + 512],
                                          in_=sb["sc2"][:, 512 * par:512 * par + 512],
                                          func=AF.Ln), "s")
                    p.inc(S.activation(out=sb["sc2"][:, 512 * par:512 * par + 512],
                                       in_=sb["sc2"][:, 512 * par:512 * par + 512],
                                       func=AF.Exp, scale=-0.5), "s")
                    sc2_s_ck[par] = p.cnt["s"]
                    p.wait(V, "s")
                    p.seq(V, V.tensor_scalar(sb["sc2"][:, 512 * par:512 * par + 512],
                                             sb["sc2"][:, 512 * par:512 * par + 512],
                                             sb["v2t"][:, scol:scol + 1], None,
                                             ALU.mult, ALU.bypass), "v")
                    p.seq(V, V.tensor_tensor(bau(sb, 12, 1024 + 512 * par, 1536 + 512 * par),
                                             ps["A"][:, c0:c0 + 512],
                                             sb["sc2"][:, 512 * par:512 * par + 512],
                                             ALU.mult), "v")
                    v_qn = p.cnt["v"]
                    p.wait(T, "v", upto=v_qn)
                    p.inc(T.matmul(ps["A"][:, c0:c0 + 512], sb["pit_bf"][:, :],
                                   bau(sb, 12, 1024 + 512 * par, 1536 + 512 * par),
                                   start=True, stop=True), "t")
                    qn_t_ck[par] = p.cnt["t"]
                    p.wait(V, "t", upto=qn_t_ck[par])
                    p.seq(V, V.tensor_tensor(bau(sb, 12, 1024 + 512 * par, 1536 + 512 * par),
                                             bau(sb, 12, 1024 + 512 * par, 1536 + 512 * par),
                                             tbl[:, ct0 + c0:ct0 + c0 + 512], ALU.mult), "v")
                    p.seq(V, V.tensor_tensor(sb["sc2"][:, 512 * par:512 * par + 512],
                                             ps["A"][:, c0:c0 + 512],
                                             tbl[:, st0 + c0:st0 + c0 + 512], ALU.mult), "v")
                    p.inc(V.tensor_tensor(bau(sb, unit, c0, c0 + 512),
                                          bau(sb, 12, 1024 + 512 * par, 1536 + 512 * par),
                                          sb["sc2"][:, 512 * par:512 * par + 512], ALU.add), "v")
                    qn_v_ck[4 * which + tc] = p.cnt["v"]
            ck = p.__dict__.setdefault("_l0ffn_ck", {
                "exp_s": [0, 0], "tr_t": [0, 0], "pv_t": [0, 0], "norm_g": [0, 0],
                "oe_v": [0, 0]})
            for hh2 in range(2):
                h = 2 * hp + hh2
                b0 = 64 * (h % 2)

                def tile_geom(qt):
                    w, t = qt // 4, qt % 4
                    q0 = 512 * w + 128 * t
                    if w == 0:
                        return q0, 0, 128 * (t + 1), w, t
                    return q0, 512 * (w - 1) + 128 * t, 640, w, t

                def stage_a(qt):
                    q0, k0, nk, w, t = tile_geom(qt)
                    par = qt % 2
                    pbase = 1024 * par
                    p.wait(T, "s", upto=ck["exp_s"][par])
                    mm = None
                    if w == 0:
                        mm = T.matmul(ps["A"][:, pbase:pbase + nk],
                                      bau(sb, 4, q0, q0 + 128)[b0:b0 + 64, :],
                                      bau(sb, 5, k0, k0 + nk)[b0:b0 + 64, :],
                                      start=True, stop=False)
                        mm = T.matmul(ps["A"][:, pbase + 128 * t:pbase + 128 * t + 128],
                                      sb["id_bf"][:, :], sb["tri2_bf"][:, :],
                                      start=False, stop=True)
                    else:
                        T.matmul(ps["A"][:, pbase:pbase + 512],
                                 bau(sb, 4, q0, q0 + 128)[b0:b0 + 64, :],
                                 bau(sb, 5, k0, k0 + 512)[b0:b0 + 64, :],
                                 start=True, stop=False)
                        T.matmul(ps["A"][:, pbase + 512:pbase + 640],
                                 bau(sb, 4, q0, q0 + 128)[b0:b0 + 64, :],
                                 bau(sb, 5, k0 + 512, k0 + 640)[b0:b0 + 64, :],
                                 start=True, stop=False)
                        T.matmul(ps["A"][:, pbase:pbase + 128],
                                 sb["id_bf"][:, :], sb["tri1_bf"][:, :],
                                 start=False, stop=True)
                        mm = T.matmul(ps["A"][:, pbase + 512:pbase + 640],
                                      sb["id_bf"][:, :], sb["tri2_bf"][:, :],
                                      start=False, stop=True)
                    p.inc(mm, "t")
                    return p.cnt["t"]

                t_a = {}
                v_norm = {}
                t_a[0] = stage_a(0)
                for qt in range(16):
                    par = qt % 2
                    q0, k0, nk, w, t = tile_geom(qt)
                    if qt + 1 < 16:
                        t_a[qt + 1] = stage_a(qt + 1)
                    nm = sb["sc1"][:, 8 + 4 * par:9 + 4 * par]
                    rs = sb["sc1"][:, 9 + 4 * par:10 + 4 * par]
                    rc = sb["sc1"][:, 10 + 4 * par:11 + 4 * par]
                    pu = 6
                    pcol = 1024 * par
                    p.wait(V, "t", upto=t_a[qt])
                    p.wait(V, "s", upto=ck["exp_s"][par])
                    p.inc(V.reduce_max(nm, ps["A"][:, pcol:pcol + nk], axis=AX.X,
                                       negate=True), "v")
                    p.wait(S, "v")
                    p.wait(S, "t", upto=ck["tr_t"][par])
                    p.wait(S, "v", upto=ck["norm_g"][par])
                    p.inc(S.activation(out=bau(sb, pu, 640 * par, 640 * par + nk),
                                       in_=ps["A"][:, pcol:pcol + nk],
                                       func=AF.Exp, bias=nm, accum_out=rs), "s")
                    ck["exp_s"][par] = p.cnt["s"]
                    p.wait(V, "s")
                    p.seq(V, V.reciprocal(rc, rs), "v")
                    p.inc(V.tensor_scalar(bau(sb, pu, 640 * par, 640 * par + nk),
                                          bau(sb, pu, 640 * par, 640 * par + nk),
                                          rc, None, ALU.mult, ALU.bypass), "v")
                    v_norm_c = p.cnt["v"]
                    ck["norm_g"][par] = v_norm_c
                    nkb = nk // 128
                    p.wait(T, "v", upto=v_norm_c)
                    mm = None
                    for kb in range(nkb):
                        mm = T.transpose(ps["T"][:, 1024 * par + 128 * kb:
                                                 1024 * par + 128 * kb + 128],
                                         bau(sb, pu, 640 * par + 128 * kb,
                                             640 * par + 128 * kb + 128),
                                         sb["id_bf"][:, :])
                    p.inc(mm, "t")
                    ck["tr_t"][par] = p.cnt["t"]
                    t_tr = p.cnt["t"]
                    if par == 0:
                        p.wait(V, "t", upto=t_tr)
                        p.inc(V.tensor_copy(bau(sb, 12, 640 * par, 640 * par + nk),
                                            ps["T"][:, 1024 * par:1024 * par + nk]), "v")
                        pte = p.cnt["v"]
                        p.wait(T, "v", upto=pte)
                    else:
                        p.wait(S, "t", upto=t_tr)
                        p.inc(S.activation(out=bau(sb, 12, 640 * par, 640 * par + nk),
                                           in_=ps["T"][:, 1024 * par:1024 * par + nk],
                                           func=AF.Copy), "s")
                        pte = p.cnt["s"]
                        p.wait(T, "s", upto=pte)
                    p.wait(T, "v", upto=ck["oe_v"][par])
                    mm = None
                    for kb in range(nkb):
                        kidx = kb if w == 0 else 4 * (w - 1) + t + kb
                        mm = T.matmul(ps["B"][b0:b0 + 64, 512 * par:512 * par + 128],
                                      vts(kidx, h),
                                      bau(sb, 12, 640 * par + 128 * kb,
                                          640 * par + 128 * kb + 128),
                                      start=(kb == 0), stop=(kb == nkb - 1))
                    p.inc(mm, "t")
                    ck["pv_t"][par] = p.cnt["t"]
                    p.wait(V, "t", upto=p.cnt["t"])
                    p.inc(V.tensor_copy(bau(sb, 7 + h // 2, q0, q0 + 128)[b0:b0 + 64, :],
                                        ps["B"][b0:b0 + 64, 512 * par:512 * par + 128]), "v")
                    ck["oe_v"][par] = p.cnt["v"]
        for half in range(2):
            p.wait(G, "t")
            p.dma(G, "dw", lambda: G.dma_start(out=bau(sb, 11, 0, 2048),
                                               in_=wo_slice(l, 512 * half, 512)))
            p.wait(T, "dw"); p.wait(T, "v")
            for dm in range(4):
                dmg = 4 * half + dm
                for tc in range(4):
                    mm = None
                    for cc in range(4):
                        mm = T.matmul(ps["A"][:, 512 * tc:512 * tc + 512],
                                      bau(sb, 11, 512 * cc + 128 * dm, 512 * cc + 128 * dm + 128),
                                      bau(sb, 7 + cc, 512 * tc, 512 * tc + 512),
                                      start=(cc == 0), stop=(cc == 3))
                    p.inc(mm, "t")
                p.wait(V, "t")
                for tc in range(4):
                    c0 = 512 * tc
                    lastv = V.tensor_tensor(fts(sb, dmg, 1 + c0, 1 + c0 + 512),
                                            fts(sb, dmg, 1 + c0, 1 + c0 + 512),
                                            ps["A"][:, c0:c0 + 512], ALU.add)
                p.inc(lastv, "v")
                p.wait(T, "v")
        if L0FFN_MLP:
            emit_shiftcast(p, sb)
            emit_inner_ln(p, dr, sb, ps, l)
            emit_l0mlp(p, dr, sb, ps, l)
        if l == 0 and L0FFN_DEPTH == 2:
            emit_shiftcast(p, sb)
    p.bar()


def emit_inner_ln(p, dr, sb, ps, l):
    T, V, S, G, Y = p.T, p.V, p.S, p.G, p.Y
    p.wait(S, "v")
    for tc in range(4):
        c0 = 512 * tc
        if tc > 0:
            p.wait(T, "v")
        for c in range(NCHUNK):
            par = c % 2
            sq = bau(sb, 12, 512 * par, 512 * par + 512)
            if tc > 0 or c > 1:
                p.wait(S, "t", lag=1)
            p.inc(S.activation(out=sq, in_=hbs(sb, c, 1 + c0, 1 + c0 + 512), func=AF.Square), "s")
            p.wait(T, "s")
            T.matmul(ps["B"][:, 0:512], sb["ones_bf"][:, :], hbs(sb, c, 1 + c0, 1 + c0 + 512),
                     start=(c == 0), stop=(c == NCHUNK - 1))
            p.inc(T.matmul(ps["B"][:, 512:1024], sb["ones_bf"][:, :], sq,
                           start=(c == 0), stop=(c == NCHUNK - 1)), "t")
        p.wait(V, "t")
        mean = sb["stat"][:, c0:c0 + 512]
        rstd = sb["stat"][:, N + c0:N + c0 + 512]
        p.seq(V, V.tensor_scalar(mean, ps["B"][:, 0:512], 1.0 / 1024.0, None,
                                 ALU.mult, ALU.bypass), "v")
        p.seq(V, V.tensor_scalar(rstd, ps["B"][:, 512:1024], 1.0 / 1024.0, None,
                                 ALU.mult, ALU.bypass), "v")
        p.seq(V, V.tensor_tensor(sb["sc2"][:, 0:512], mean, mean, ALU.mult), "v")
        p.seq(V, V.tensor_tensor(rstd, rstd, sb["sc2"][:, 0:512], ALU.subtract), "v")
        p.inc(V.tensor_scalar(rstd, rstd, 1e-5, None, ALU.add, ALU.bypass), "v")
        p.wait(S, "v")
        p.seq(S, S.activation(out=rstd, in_=rstd, func=AF.Ln), "s")
        p.inc(S.activation(out=rstd, in_=rstd, func=AF.Exp, scale=-0.5), "s")
    p.wait(V, "s")
    for c in range(NCHUNK):
        for tc in range(4):
            c0 = 512 * tc
            mean = sb["stat"][:, c0:c0 + 512]
            rstd = sb["stat"][:, N + c0:N + c0 + 512]
            p.seq(V, V.tensor_tensor(sb["sc2"][:, 512:1024], hbs(sb, c, 1 + c0, 1 + c0 + 512),
                                     mean, ALU.subtract), "v")
            p.seq(V, V.tensor_tensor(sb["sc2"][:, 512:1024], sb["sc2"][:, 512:1024],
                                     rstd, ALU.mult), "v")
            p.seq(V, V.tensor_scalar(hbs(sb, c, 1 + c0, 1 + c0 + 512), sb["sc2"][:, 512:1024],
                                     sb["vt"][:, 12 * c + (5 if l == 0 else 8):
                                              12 * c + (5 if l == 0 else 8) + 1],
                                     sb["vt"][:, 12 * c + (6 if l == 0 else 9):
                                              12 * c + (6 if l == 0 else 9) + 1],
                                     ALU.mult, ALU.add), "v")


def emit_l0mlp(p, dr, sb, ps, l):
    T, V, S, G, Y = p.T, p.V, p.S, p.G, p.Y
    groups = [(0, 6), (6, 12), (12, 18), (18, 22)]

    def w2s(bb, c0, c1):
        return sb["ba"][:, 8 * CPAD + 1024 * bb + c0: 8 * CPAD + 1024 * bb + c1]

    w1_ck = [p.cnt["t"]] * 2
    ag_s_ck = [p.cnt["s"]] * 2
    ag_v_ck = [p.cnt["v"]] * 2
    sc2_v_ck = [p.cnt["v"]] * 2
    w2_t_ck = p.cnt["t"]
    for gi, (g0, g1) in enumerate(groups):
        for b in range(g0, g1):
            bu = 6 + b % 2
            if gi == 0 and b == g0:
                p.wait(G, "t")
            p.wait(G, "t", upto=w1_ck[b % 2])
            if b == g0:
                p.wait(G, "t", upto=w2_t_ck)
            p.dma(G, "dw", lambda: G.dma_start(
                out=bau(sb, bu, 0, 2048),
                in_=bass.AP(dr["w1pb"], 262144 * (NBLK * l + b),
                            [[256, 128], [32768, 8], [1, 256]])))
            p.dma(G, "dw", lambda: G.dma_start(
                out=w2s(b - g0, 0, 1024),
                in_=bass.AP(dr["w2pb"], 131072 * (NBLK * l + b), [[1024, 128], [1, 1024]])))
            p.wait(T, "dw")
            for tc in range(4):
                c0 = 512 * tc
                k = 4 * (b - g0) + tc
                par = k % 2
                pa = ps["B"] if par == 0 else ps["A"]
                p.wait(T, "s", upto=ag_s_ck[par])
                p.wait(T, "v", upto=ag_v_ck[par])
                mm = None
                for half in range(2):
                    for ic in range(NCHUNK):
                        mm = T.matmul(pa[:, 512 * half:512 * half + 512],
                                      bau(sb, bu, 256 * ic + 128 * half,
                                          256 * ic + 128 * half + 128),
                                      hbs(sb, ic, 1 + c0, 1 + c0 + 512),
                                      start=(ic == 0), stop=(ic == NCHUNK - 1))
                p.inc(mm, "t")
                t_after_mm = p.cnt["t"]
                p.wait(S, "t")
                p.wait(S, "v", upto=sc2_v_ck[par])
                p.inc(S.activation(out=sb["sc2"][:, 512 * par:512 * par + 512],
                                   in_=pa[:, 512:1024], func=AF.Gelu,
                                   bias=sb["b1t"][:, 44 * l + 2 * b + 1:44 * l + 2 * b + 2]), "s")
                ag_s_ck[par] = p.cnt["s"]
                p.wait(V, "s")
                if tc == 0:
                    p.wait(V, "t", upto=w2_t_ck)
                p.inc(V.scalar_tensor_tensor(bau(sb, b - g0, c0, c0 + 512), pa[:, 0:512],
                                             sb["b1t"][:, 44 * l + 2 * b:44 * l + 2 * b + 1],
                                             sb["sc2"][:, 512 * par:512 * par + 512],
                                             ALU.add, ALU.mult), "v")
                ag_v_ck[par] = p.cnt["v"]
                sc2_v_ck[par] = p.cnt["v"]
            w1_ck[b % 2] = t_after_mm
        p.wait(T, "v")
        rot_v_ck = [p.cnt["v"]] * 3
        banks = [(ps["A"], 0), (ps["A"], 1024), (ps["B"], 0)]
        for tc in range(4):
            c0 = 512 * tc
            for dq in range(4):
                rot = (4 * tc + dq) % 3
                pa, po = banks[rot]
                p.wait(T, "v", upto=rot_v_ck[rot])
                mm = None
                for b in range(g0, g1):
                    for dm in range(2):
                        dmg = 2 * dq + dm
                        mm = T.matmul(pa[:, po + 512 * dm:po + 512 * dm + 512],
                                      w2s(b - g0, 128 * dmg, 128 * dmg + 128),
                                      bau(sb, b - g0, c0, c0 + 512),
                                      start=(b == g0), stop=(b == g1 - 1))
                p.inc(mm, "t")
                p.wait(V, "t")
                for dm in range(2):
                    dmg = 2 * dq + dm
                    if g0 == 0:
                        lastv = V.scalar_tensor_tensor(
                            fts(sb, dmg, 1 + c0, 1 + c0 + 512), pa[:, po + 512 * dm:po + 512 * dm + 512],
                            sb["vt"][:, 12 * dmg + (7 if l == 0 else 10):
                                     12 * dmg + (7 if l == 0 else 10) + 1],
                            fts(sb, dmg, 1 + c0, 1 + c0 + 512), ALU.add, ALU.add)
                    else:
                        lastv = V.tensor_tensor(fts(sb, dmg, 1 + c0, 1 + c0 + 512),
                                                fts(sb, dmg, 1 + c0, 1 + c0 + 512),
                                                pa[:, po + 512 * dm:po + 512 * dm + 512], ALU.add)
                p.inc(lastv, "v")
                rot_v_ck[rot] = p.cnt["v"]
        w2_t_ck = p.cnt["t"]
        ag_v_ck = [p.cnt["v"], p.cnt["v"]]


def emit_l0att(p, dr, sb, ps):
    T, V, S, G, Y = p.T, p.V, p.S, p.G, p.Y
    p.wait(V, "dx")

    def wkv_slice(col0, ncol):
        return bass.AP(dr["l0a_wkv"], col0, [[2048, 128], [262144, 8], [1, ncol]])

    def wq_slice(col0, ncol):
        return bass.AP(dr["l0a_wq"], col0, [[1024, 128], [131072, 8], [1, ncol]])

    def wo_slice(row0):
        return bass.AP(dr["l0a_wo"], row0 * 1024, [[1024, 128], [131072, 2], [1, 1024]])

    def v4s(kc, hh):
        return bau(sb, 0 + kc // 8, 256 * (kc % 8) + 64 * hh, 256 * (kc % 8) + 64 * hh + 64)

    def pts_pv(kc):
        return bass.AP(sb["ba"], 8 * CPAD + 128 * kc, [[BAU * CPAD, 128], [CPAD, 2], [1, 128]])

    for g in range(4):
        p.wait(G, "t")
        p.dma(G, "dw", lambda: G.dma_start(out=bau(sb, 12, 0, 2048),
                                           in_=wkv_slice(1024 + 256 * g, 256)))
        p.wait(T, "dw")
        v4_ck = [p.cnt["v"]] * 4
        for tt in range(16):
            mm = None
            p.wait(T, "v", upto=v4_ck[tt % 4])
            for ic in range(NCHUNK):
                mm = T.matmul(ps["A"][:, 512 * (tt % 4):512 * (tt % 4) + 256],
                              hbs(sb, ic, 1 + 128 * tt, 1 + 128 * tt + 128),
                              bau(sb, 12, 256 * ic, 256 * ic + 256),
                              start=(ic == 0), stop=(ic == NCHUNK - 1))
            p.inc(mm, "t")
            p.wait(V, "t")
            p.inc(V.tensor_copy(bau(sb, tt // 8, 256 * (tt % 8), 256 * (tt % 8) + 256),
                                ps["A"][:, 512 * (tt % 4):512 * (tt % 4) + 256]), "v")
            v4_ck[tt % 4] = p.cnt["v"]
        for hp in range(2):
            h0 = 4 * g + 2 * hp
            p.wait(G, "t")
            p.dma(G, "dw", lambda: G.dma_start(out=bau(sb, 10, 0, 1024),
                                               in_=wq_slice(64 * h0, 128)))
            p.dma(G, "dw", lambda: G.dma_start(out=bau(sb, 10, 1024, 2048),
                                               in_=wkv_slice(64 * h0, 128)))
            p.wait(T, "dw"); p.wait(T, "s"); p.wait(T, "v")
            for which, unit, scl in ((0, 4, 0.125), (1, 5, 1.0)):
                for tc in range(4):
                    c0 = 512 * tc
                    mm = None
                    for ic in range(NCHUNK):
                        mm = T.matmul(ps["B"][:, 512 * (tc % 2):512 * (tc % 2) + 512],
                                      bau(sb, 10, 1024 * which + 128 * ic,
                                          1024 * which + 128 * ic + 128),
                                      hbs(sb, ic, 1 + c0, 1 + c0 + 512),
                                      start=(ic == 0), stop=(ic == NCHUNK - 1))
                    p.inc(mm, "t")
                    p.wait(S, "t")
                    p.inc(S.activation(out=bau(sb, unit, c0, c0 + 512),
                                       in_=ps["B"][:, 512 * (tc % 2):512 * (tc % 2) + 512],
                                       func=AF.Identity, scale=scl), "s")
                    p.wait(T, "s", lag=1)
            p.wait(T, "s")
            for hh2 in range(2):
                h = h0 + hh2
                hh = 2 * hp + hh2
                b0 = 64 * (h % 2)
                hkoff = 448 * (h % 2)
                p.wait(G, "t")
                p.dma(G, "dw", lambda: G.dma_start(
                    out=bau(sb, 12, hkoff, hkoff + 308),
                    in_=bass.AP(dr["gbuf"], 576 * h, [[1, 128], [1, 308]])))
                p.wait(T, "dw")
                ck = p.__dict__.setdefault("_l0att_ck", {
                    "ttr_v": [0, 0], "exp_s": [0, 0, 0, 0], "tr_t": [0, 0],
                    "oe_v": 0, "nm_s": [0, 0], "exp_par_s": [0, 0], "pv_t": [0, 0],
                    "pte": [0, 0]})

                def scbuf(k):
                    j = k % 4
                    t_ = sb["stat"] if j < 2 else sb["xcb"]
                    return t_[:, 2048 * (j % 2):2048 * (j % 2) + 2048]

                def mxcol(k, which):
                    return sb["sc1"][:, 32 + 8 * (k % 2) + which:33 + 8 * (k % 2) + which]

                def nmc(k, j):
                    return sb["sc1"][:, 16 + 8 * (k % 2) + j:17 + 8 * (k % 2) + j]

                t_sc = {}

                def st_t1(qt):
                    i0 = 128 * qt
                    sb0 = i0 - 90
                    slo = max(0, sb0)
                    shi = min(N, sb0 + 308)
                    for half in range(2):
                        hlo = 1024 * half
                        p.wait(T, "v", upto=ck["ttr_v"][half])
                        mm = None
                        for nb in range(2):
                            b_lo = hlo + 512 * nb
                            bank_strip = max(slo, b_lo) < min(shi, b_lo + 512)
                            mm = T.matmul(ps["A"][:, b_lo:b_lo + 512],
                                          bau(sb, 4, i0, i0 + 128)[b0:b0 + 64, :],
                                          bau(sb, 5, b_lo, b_lo + 512)[b0:b0 + 64, :],
                                          start=True, stop=not bank_strip)
                            if bank_strip:
                                a, b = max(slo, b_lo), min(shi, b_lo + 512)
                                mm = T.matmul(ps["A"][:, a:b], sb["j_bf"][:, :],
                                              bau(sb, 12, hkoff + a - sb0, hkoff + b - sb0),
                                              start=False, stop=True)
                        p.inc(mm, "t")
                        t_sc[(qt, half)] = p.cnt["t"]

                def st_v1(qt):
                    for half in range(2):
                        hlo = 1024 * half
                        p.wait(V, "t", upto=t_sc.pop((qt, half)))
                        p.wait(V, "s", upto=ck["exp_s"][qt % 4])
                        p.wait(V, "s", upto=ck["nm_s"][qt % 2])
                        p.inc(V.tensor_scalar(scbuf(qt)[:, hlo:hlo + 1024],
                                              ps["A"][:, hlo:hlo + 1024], 0.0, None,
                                              ALU.add, ALU.max, accum_out=mxcol(qt, half)), "v")
                        ck["ttr_v"][half] = p.cnt["v"]
                    nm, bl = nmc(qt, 0), nmc(qt, 2)
                    p.wait(V, "v")
                    p.seq(V, V.tensor_tensor(nm, mxcol(qt, 0), mxcol(qt, 1), ALU.max), "v")
                    p.seq(V, V.tensor_scalar(nm, nm, -1.0, None, ALU.mult, ALU.bypass), "v")
                    p.inc(V.tensor_scalar(bl, nm, sb["dct"][:, h:h + 1], None,
                                          ALU.add, ALU.bypass), "v")
                    ck.setdefault("bias_v", {})[qt] = p.cnt["v"]

                def st_s(qt):
                    par = qt % 2
                    lw = max(0, 128 * qt - 90)
                    pu = 6 + par
                    sums = [sb["sc1"][:, 19 + 8 * par + j:20 + 8 * par + j] for j in range(2)]
                    cuts = ([(0, lw, nmc(qt, 2))] if lw > 0 else []) + [(lw, N, nmc(qt, 0))]
                    p.wait(S, "v", upto=ck["bias_v"].pop(qt))
                    p.wait(S, "t", upto=ck["tr_t"][par])
                    p.wait(S, "v", upto=ck.setdefault("sums_v", [0, 0])[par])
                    lasts = None
                    for j, (a, b, bias) in enumerate(cuts):
                        lasts = S.activation(out=bau(sb, pu, a, b), in_=scbuf(qt)[:, a:b],
                                             func=AF.Exp, bias=bias, accum_out=sums[j])
                    p.inc(lasts, "s")
                    ck["exp_s"][qt % 4] = p.cnt["s"]
                    ck["nm_s"][par] = p.cnt["s"]
                    ck["exp_par_s"][par] = p.cnt["s"]
                    ck.setdefault("ncuts", {})[qt] = len(cuts)

                def st_v2(qt):
                    par = qt % 2
                    pu = 6 + par
                    sums = [sb["sc1"][:, 19 + 8 * par + j:20 + 8 * par + j] for j in range(2)]
                    rc = sb["sc1"][:, 22 + 8 * par:23 + 8 * par]
                    p.wait(V, "s", upto=ck["exp_par_s"][par])
                    if ck["ncuts"].pop(qt) > 1:
                        p.seq(V, V.tensor_tensor(sums[0], sums[0], sums[1], ALU.add), "v")
                    p.seq(V, V.reciprocal(rc, sums[0]), "v")
                    ck.setdefault("sums_v", [0, 0])[par] = p.cnt["v"]
                    p.inc(V.tensor_scalar(bau(sb, pu, 0, N), bau(sb, pu, 0, N), rc,
                                          None, ALU.mult, ALU.bypass), "v")
                    ck.setdefault("norm_v", {})[qt] = p.cnt["v"]

                def st_t2(qt):
                    par = qt % 2
                    qp = qt // 2
                    pu = 6 + par
                    p.wait(T, "v", upto=ck["norm_v"].pop(qt))
                    for bt in range(2):
                        mm = None
                        for k8 in range(8):
                            kc = 8 * bt + k8
                            mm = T.transpose(ps["T"][:, 1024 * bt + 128 * k8:
                                                     1024 * bt + 128 * k8 + 128],
                                             bau(sb, pu, 128 * kc, 128 * kc + 128),
                                             sb["id_bf"][:, :])
                        p.inc(mm, "t")
                        t_tr = p.cnt["t"]
                        if bt == 0:
                            p.wait(V, "t", upto=t_tr)
                            p.wait(V, "t", upto=ck["pv_t"][par])
                            p.inc(V.tensor_copy(bau(sb, 8 + par, 0, 1024),
                                                ps["T"][:, 0:1024]), "v")
                            p.wait(T, "v", upto=p.cnt["v"])
                        else:
                            p.wait(S, "t", upto=t_tr)
                            p.wait(S, "t", upto=ck["pv_t"][par])
                            p.inc(S.activation(out=bau(sb, 8 + par, 1024, 2048),
                                               in_=ps["T"][:, 1024:2048], func=AF.Copy), "s")
                            p.wait(T, "s", upto=p.cnt["s"])
                    ck["tr_t"][par] = p.cnt["t"]
                    if par == 1:
                        p.wait(T, "v", upto=ck["oe_v"])
                        mm = None
                        for kc in range(16):
                            mm = T.matmul(ps["B"][b0:b0 + 64, 0:256], v4s(kc, hh),
                                          pts_pv(kc), start=(kc == 0), stop=(kc == 15))
                        p.inc(mm, "t")
                        ck["pv_t"][0] = p.cnt["t"]
                        ck["pv_t"][1] = p.cnt["t"]
                        p.wait(V, "t", upto=p.cnt["t"])
                        p.inc(V.tensor_copy(bau(sb, 2 + hh // 2, 256 * qp, 256 * qp + 256)
                                            [b0:b0 + 64, :], ps["B"][b0:b0 + 64, 0:256]), "v")
                        ck["oe_v"] = p.cnt["v"]

                st_t1(0); st_v1(0); st_s(0); st_t1(1); st_v1(1)
                for qt in range(16):
                    if qt + 2 < 16:
                        st_t1(qt + 2)
                    st_v2(qt)
                    st_t2(qt)
                    if qt + 2 < 16:
                        st_v1(qt + 2)
                    if qt + 1 < 16:
                        st_s(qt + 1)

        p.wait(G, "t")
        p.dma(G, "dw", lambda: G.dma_start(out=bau(sb, 11, 0, 2048), in_=wo_slice(256 * g)))
        p.wait(T, "dw"); p.wait(T, "v"); p.wait(T, "s")
        wo_ck = [p.cnt["v"]] * 4
        for dm in range(8):
            for t4 in range(4):
                c0 = 512 * t4
                p.wait(T, "v", upto=wo_ck[t4])
                mm = None
                for j in range(2):
                    mm = T.matmul(ps["A"][:, c0:c0 + 512],
                                  bau(sb, 11, 1024 * j + 128 * dm, 1024 * j + 128 * dm + 128),
                                  bau(sb, 2 + j, c0, c0 + 512),
                                  start=(j == 0), stop=(j == 1))
                p.inc(mm, "t")
                p.wait(V, "t")
                if g == 0:
                    lastv = V.tensor_scalar(fts(sb, dm, 1 + c0, 1 + c0 + 512),
                                            ps["A"][:, c0:c0 + 512],
                                            sb["vt"][:, 12 * dm + 0:12 * dm + 1],
                                            None, ALU.add, ALU.bypass)
                else:
                    lastv = V.tensor_tensor(fts(sb, dm, 1 + c0, 1 + c0 + 512),
                                            fts(sb, dm, 1 + c0, 1 + c0 + 512),
                                            ps["A"][:, c0:c0 + 512], ALU.add)
                p.inc(lastv, "v")
                wo_ck[t4] = p.cnt["v"]
    p.bar()



_NC_CACHE = {}


def kernel(**inputs):
    sh, percore = host_prep(inputs)
    key = PHASES
    if key not in _NC_CACHE:
        _NC_CACHE[key] = build_nc()
    nc = _NC_CACHE[key]
    in_maps = []
    for c in range(8):
        m = dict(sh)
        m.update(percore[c])
        in_maps.append(m)
    res = run_bass_kernel_spmd(nc, in_maps, core_ids=list(range(8)))
    out = np.stack([np.ascontiguousarray(np.asarray(res.results[c]["yt"]).T) for c in range(8)])
    return out.astype(np.float32)


# version 14
# speedup vs baseline: 1.1078x; 1.0127x over previous
import math
import numpy as np
from contextlib import ExitStack

import concourse.bass as bass
import concourse.mybir as mybir
from concourse.bass_utils import run_bass_kernel_spmd

F32 = mybir.dt.float32
BF16 = mybir.dt.bfloat16
AF = mybir.ActivationFunctionType
ALU = mybir.AluOpType
AX = mybir.AxisListType

N = 2048
D = 1024
NCHUNK = 8
CPAD = 2052
NEGBIG = -float(2.0 ** 127)
PI = math.pi
HID = 2730
NBLK = 22
BAU = 13

PHASES = ("l0ffn_a", "l0att", "l0ffn_c", "l1ffn_d", "l1att", "l1ffn_f")
L0FFN_DEPTH = 2
L0FFN_MLP = True


def _f32c(a):
    return np.ascontiguousarray(np.asarray(a, dtype=np.float32))



def host_prep(inp):
    sh = {}
    sh["l0a_wq"] = _f32c(inp["L0_attn_Wq"])
    sh["l0a_wkv"] = _f32c(inp["L0_attn_Wkv"])
    sh["l0a_wo"] = _f32c(inp["L0_attn_Wo"])
    sh["relemb"] = _f32c(inp["L0_attn_rel_emb"])

    def bucket(rel):
        n = rel
        nb = 16
        ret = 0
        if n < 0:
            ret = nb
        n = abs(n)
        max_exact = nb // 2
        if n < max_exact:
            return ret + n
        v = max_exact + int(
            math.log(max(n, 1) / max_exact) / math.log(128 / max_exact) * (nb - max_exact))
        return ret + min(v, nb - 1)

    m436 = np.zeros((436, 32), dtype=np.float32)
    for u in range(436):
        d = 217 - u
        m436[u, bucket(d)] += 1.0
        m436[u, 31] -= 1.0
    sh["m436t"] = _f32c(m436.T)
    ediff = np.zeros((32, 1), dtype=np.float32)
    ediff[15, 0] = 1.0
    ediff[31, 0] = -1.0
    sh["ediff"] = ediff

    sh["l0f_wqkv"] = _f32c(inp["L0_ffn_wqkv"])
    sh["l0f_wo"] = _f32c(inp["L0_ffn_wo"])
    w1 = _f32c(inp["L0_ffn_w1"]); w2 = _f32c(inp["L0_ffn_w2"]); b1 = _f32c(inp["L0_ffn_b1"])
    HIDP = 128 * NBLK
    w1a = np.zeros((2, 1024, HIDP), np.float32); w1a[:, :, :HID] = w1[:, :, :HID]
    w1g = np.zeros((2, 1024, HIDP), np.float32); w1g[:, :, :HID] = w1[:, :, HID:]
    w1pb = np.zeros((2, NBLK, 1024, 256), np.float32)
    for b in range(NBLK):
        w1pb[:, b, :, :128] = w1a[:, :, 128 * b:128 * b + 128]
        w1pb[:, b, :, 128:] = w1g[:, :, 128 * b:128 * b + 128]
    sh["w1pb"] = _f32c(w1pb)
    w2p = np.zeros((2, HIDP, 1024), np.float32); w2p[:, :HID, :] = w2
    sh["w2pb"] = _f32c(w2p.reshape(2, NBLK, 128, 1024))
    b1a = np.zeros((2, HIDP), np.float32); b1a[:, :HID] = b1[:, :HID]
    b1g = np.zeros((2, HIDP), np.float32); b1g[:, :HID] = b1[:, HID:]
    b1p = np.zeros((2, 44, 128), np.float32)
    for b in range(NBLK):
        b1p[:, 2 * b] = b1a[:, 128 * b:128 * b + 128]
        b1p[:, 2 * b + 1] = b1g[:, 128 * b:128 * b + 128]
    sh["b1p"] = _f32c(b1p.reshape(88, 128))

    sh["l1a_wqT"] = _f32c(np.asarray(inp["L1_attn_Wq"]).T)
    sh["l1a_wkT"] = _f32c(np.asarray(inp["L1_attn_Wk"]).T)
    sh["l1a_wvT"] = _f32c(np.asarray(inp["L1_attn_Wv"]).T)
    sh["l1a_woT"] = _f32c(np.asarray(inp["L1_attn_Wo"]).T)
    sh["l1a_pT"] = _f32c(np.transpose(np.asarray(inp["L1_attn_P"]), (0, 2, 1)))
    sh["l1f_wT"] = _f32c(np.asarray(inp["L1_ffn_weight"]).T)

    vecs = np.zeros((12, 1024), np.float32)
    vecs[0] = _f32c(inp["L0_attn_bo"])
    vecs[1] = _f32c(inp["L1_attn_bq"])
    vecs[2] = _f32c(inp["L1_attn_bk"])
    vecs[3] = _f32c(inp["L1_attn_bv"])
    vecs[4] = _f32c(inp["L1_attn_bo"])
    lng = _f32c(inp["L0_ffn_ln_g"]); lnb = _f32c(inp["L0_ffn_ln_b"]); b2 = _f32c(inp["L0_ffn_b2"])
    vecs[5], vecs[6], vecs[7] = lng[0], lnb[0], b2[0]
    vecs[8], vecs[9], vecs[10] = lng[1], lnb[1], b2[1]
    sh["vecs"] = vecs
    qs = _f32c(inp["L0_ffn_q_scale"]); ks = _f32c(inp["L0_ffn_k_scale"])
    sh["v2"] = _f32c(np.stack([np.tile(qs[0], 2), np.tile(ks[0], 2),
                               np.tile(qs[1], 2), np.tile(ks[1], 2)]))

    import jax
    import jax.numpy as jnp
    with jax.default_device(jax.devices("cpu")[0]):
        half_idx = jnp.arange(0, 64, 2, dtype=jnp.float32)
        inv_freq = 1.0 / 10000.0 ** (half_idx / 64)
        tpos = jnp.arange(2048, dtype=jnp.float32)
        freqs = tpos[:, None] * inv_freq[None, :]
        base = (half_idx + 0.4 * 64) / (1.4 * 64)
        power = (tpos - 1024) / float(256)
        rsc = base[None, :] ** power[:, None]
        rcos = jnp.cos(freqs)
        rsin = jnp.sin(freqs)
    rows = np.arange(128) % 32
    sh["rotc"] = _f32c(np.asarray(rcos).T[rows, :])
    sh["rots"] = _f32c(np.asarray(rsin).T[rows, :])
    sh["rotsc"] = _f32c(np.asarray(rsc).T[rows, :])

    ident = np.eye(128, dtype=np.float32)
    ones = np.ones((128, 128), dtype=np.float32)
    ones64 = np.zeros((128, 128), np.float32); ones64[:64, :64] = 1.0; ones64[64:, 64:] = 1.0
    pi_m = np.zeros((128, 128), np.float32)
    for blk in range(2):
        o = 64 * blk
        pi_m[o:o + 32, o + 32:o + 64] = -np.eye(32)
        pi_m[o + 32:o + 64, o:o + 32] = np.eye(32)
    r = np.arange(128)[:, None]; u = np.arange(128)[None, :]
    tri1 = np.where(u < r, np.float32(NEGBIG), np.float32(0.0)).astype(np.float32)
    tri2 = np.where(u > r, np.float32(NEGBIG), np.float32(0.0)).astype(np.float32)
    jrev = np.zeros((128, 128), np.float32)
    jrev[np.arange(128), 127 - np.arange(128)] = 1.0
    sh["aux7"] = np.stack([ident, ones, ones64, pi_m.T, tri1, tri2, jrev]).astype(np.float32)
    sh["aux_f32"] = np.stack([ident, ones]).astype(np.float32)
    sh["jcol"] = _f32c((np.arange(128) % 32).astype(np.float32)[:, None])

    x = np.asarray(inp["L0_attn_x"], dtype=np.float32)
    percore = [{"xt": _f32c(x[c].T)} for c in range(8)]
    return sh, percore



class P:
    def __init__(self, nc, stack):
        self.nc = nc
        self.T, self.V, self.S, self.G, self.Y = nc.tensor, nc.vector, nc.scalar, nc.gpsimd, nc.sync
        self.sems, self.cnt = {}, {}
        for name in ["dw", "dx", "t", "v", "s", "g", "b"]:
            self.sems[name] = stack.enter_context(nc.semaphore(name))
            self.cnt[name] = 0
        self.hwm = {"dw": 0, "dx": 0}
        self.issuer_waited = {"dw": 0, "dx": 0}

    def inc(self, inst, name, n=1):
        inst.then_inc(self.sems[name], n)
        self.cnt[name] += n

    def seq(self, eng, inst, name):
        self.inc(inst, name)
        eng.wait_ge(self.sems[name], self.cnt[name])

    def wait(self, eng, name, lag=0, upto=None):
        v = (self.cnt[name] - lag) if upto is None else upto
        if v > 0:
            eng.wait_ge(self.sems[name], v)
            if name in self.hwm and v > self.hwm[name]:
                self.hwm[name] = v

    def dma(self, eng, name, fn):
        if self.hwm[name] > self.issuer_waited[name]:
            eng.wait_ge(self.sems[name], self.hwm[name])
            self.issuer_waited[name] = self.hwm[name]
        self.inc(fn(), name, 16)

    def bar(self):
        for eng in (self.T, self.V, self.S, self.G):
            self.inc(eng.drain(), "b", 1)
        self.inc(self.Y.nop(), "b", 1)
        tgt = self.cnt["b"]
        for eng in (self.T, self.V, self.S, self.G, self.Y):
            eng.wait_ge(self.sems["b"], tgt)



def hbs(sb, c, c0, c1):
    return sb["hb"][:, c * CPAD + c0: c * CPAD + c1]


def fts(sb, c, c0, c1):
    return sb["ftile"][:, c * CPAD + c0: c * CPAD + c1]


def bau(sb, unit, c0, c1):
    return sb["ba"][:, unit * CPAD + c0: unit * CPAD + c1]



def build_nc():
    nc = bass.Bass("TRN2", target_bir_lowering=False)
    dr = {}

    def din(name, shape):
        dr[name] = nc.dram_tensor(name, list(shape), F32, kind="ExternalInput")

    din("xt", [D, N])
    din("l0a_wq", [D, D]); din("l0a_wkv", [D, 2 * D]); din("l0a_wo", [D, D])
    din("relemb", [32, 16]); din("m436t", [32, 436]); din("ediff", [32, 1])
    din("l0f_wqkv", [2, D, 1536]); din("l0f_wo", [2, 512, D])
    din("w1pb", [2, NBLK, D, 256]); din("w2pb", [2, NBLK, 128, D]); din("b1p", [88, 128])
    din("l1a_wqT", [D, D]); din("l1a_wkT", [D, D]); din("l1a_wvT", [D, D]); din("l1a_woT", [D, D])
    din("l1a_pT", [16, 64, 64]); din("l1f_wT", [D, D])
    din("vecs", [12, D]); din("v2", [4, 128]); din("aux7", [7, 128, 128])
    din("rotc", [128, N]); din("rots", [128, N]); din("rotsc", [128, N])
    din("aux_f32", [2, 128, 128]); din("jcol", [128, 1])
    dr["yt"] = nc.dram_tensor("yt", [D, N], F32, kind="ExternalOutput")
    dr["xm"] = nc.dram_tensor("xm", [D, N], F32)
    dr["gbuf"] = nc.dram_tensor("gbuf", [16, 576], F32)

    stack = ExitStack()
    sb = {}

    def sbuf(name, shape, dt):
        sb[name] = stack.enter_context(nc.sbuf_tensor(name, list(shape), dt))

    sbuf("hb", [128, NCHUNK * CPAD], BF16)
    sbuf("tbl", [128, 4 * N], BF16)
    sbuf("ftile", [128, NCHUNK * CPAD], F32)
    sbuf("xcb", [128, 2 * N], F32)
    sbuf("stat", [128, 2 * N], F32)
    sbuf("sc2", [128, 1024], F32)
    sbuf("ba", [128, BAU * CPAD], BF16)
    sbuf("vt", [128, 96], F32); sbuf("v2t", [128, 4], F32); sbuf("b1t", [128, 88], F32)
    sbuf("dct", [128, 16], F32); sbuf("sc1", [128, 64], F32)
    for nme in ["id_bf", "ones_bf", "o64_bf", "pit_bf", "tri1_bf", "tri2_bf", "j_bf"]:
        sbuf(nme, [128, 128], BF16)
    sbuf("id_f", [128, 128], F32); sbuf("ones_f", [128, 128], F32)
    sbuf("rel_sb", [32, 16], F32); sbuf("ediff_sb", [32, 1], F32)
    sbuf("jcol_sb", [128, 1], F32)
    sbuf("tmp16", [128, 16], F32)
    sbuf("pk_bf", [128, 1024], BF16)
    sbuf("stg", [128, 128], F32)

    ps = {}
    ps["A"] = stack.enter_context(nc.psum_tensor("psA", [128, 2048], F32))
    ps["B"] = stack.enter_context(nc.psum_tensor("psB", [128, 1024], F32))
    ps["T"] = stack.enter_context(nc.psum_tensor("psT", [128, 2048], BF16))

    p = P(nc, stack)
    emit_program(p, dr, sb, ps)
    stack.close()
    return nc


def emit_program(p, dr, sb, ps):
    emit_prologue(p, dr, sb, ps)
    on = lambda ph: ph in PHASES
    emit_boundary(p, dr, sb, ps, src="xt", dst="xm", delta=None, scale=0.0,
                  want_f=on("l0ffn_a"))
    if on("l0ffn_a"):
        emit_l0ffn(p, dr, sb, ps)
    emit_boundary(p, dr, sb, ps, src="xm", dst="xm",
                  delta=on("l0ffn_a"), scale=0.5, want_f=False)
    if on("l0att"):
        emit_l0att(p, dr, sb, ps)
    emit_boundary(p, dr, sb, ps, src="xm", dst="xm",
                  delta=on("l0att"), scale=1.0, want_f=on("l0ffn_c"))
    if on("l0ffn_c"):
        emit_l0ffn(p, dr, sb, ps)
    emit_boundary(p, dr, sb, ps, src="xm", dst="xm",
                  delta=on("l0ffn_c"), scale=0.5, want_f=False)
    if on("l1ffn_d") or on("l1ffn_f"):
        emit_l1quant(p, dr, sb, ps)
    if on("l1ffn_d"):
        emit_l1ffn(p, dr, sb, ps)
    emit_boundary(p, dr, sb, ps, src="xm", dst="xm",
                  delta=on("l1ffn_d"), scale=0.5, want_f=False)
    if on("l1att"):
        emit_l1att(p, dr, sb, ps)
    emit_boundary(p, dr, sb, ps, src="xm", dst="xm",
                  delta=on("l1att"), scale=1.0, want_f=False)
    if on("l1ffn_f"):
        emit_l1ffn(p, dr, sb, ps)
    emit_boundary(p, dr, sb, ps, src="xm", dst="yt",
                  delta=on("l1ffn_f"), scale=0.5, want_f=False, last=True)



def emit_prologue(p, dr, sb, ps):
    T, V, S, G, Y = p.T, p.V, p.S, p.G, p.Y

    for name, src in [("rel_sb", dr["relemb"]),
                      ("ediff_sb", dr["ediff"]), ("jcol_sb", dr["jcol"])]:
        p.dma(Y, "dx", lambda: Y.dma_start(out=sb[name][:, :], in_=src[:, :]))
    m436_sb = sb["stat"][0:32, 1536:1972]
    p.dma(Y, "dx", lambda: Y.dma_start(out=m436_sb, in_=dr["m436t"][:, :]))
    vload = sb["stat"][0:12, 0:1024]
    v2load = sb["stat"][0:4, 1024:1152]
    b1load = [sb["stat"][0:44, 1152 + 128 * l:1280 + 128 * l] for l in range(2)]
    p.dma(Y, "dx", lambda: Y.dma_start(out=vload, in_=dr["vecs"][:, :]))
    p.dma(Y, "dx", lambda: Y.dma_start(out=v2load, in_=dr["v2"][:, :]))
    for l in range(2):
        p.dma(Y, "dx", lambda: Y.dma_start(out=b1load[l], in_=dr["b1p"][44 * l:44 * l + 44, :]))
    p.dma(Y, "dx", lambda: Y.dma_start(out=sb["id_f"][:, :], in_=dr["aux_f32"][0, :, :]))
    p.dma(Y, "dx", lambda: Y.dma_start(out=sb["ones_f"][:, :], in_=dr["aux_f32"][1, :, :]))
    for i, name in enumerate(["id_bf", "ones_bf", "o64_bf", "pit_bf", "tri1_bf", "tri2_bf", "j_bf"]):
        p.dma(G, "dw", lambda: G.dma_start(out=sb[name][:, :], in_=dr["aux7"][i, :, :]))

    p.wait(T, "dx"); p.wait(T, "dw")

    def tr_small(dst_ap, src_ap):
        np_, nf = src_ap.shape[0], src_ap.shape[1]
        p.inc(T.transpose(ps["A"][0:nf, 0:np_], src_ap, sb["id_f"][0:np_, 0:np_]), "t")
        p.wait(V, "t")
        p.inc(V.tensor_copy(dst_ap, ps["A"][0:nf, 0:np_]), "v")
        p.wait(T, "v")

    for c in range(NCHUNK):
        tr_small(sb["vt"][:, 12 * c:12 * c + 12], vload[:, 128 * c:128 * c + 128])
    tr_small(sb["v2t"][:, :], v2load[:, :])
    for l in range(2):
        tr_small(sb["b1t"][:, 44 * l:44 * l + 44], b1load[l])

    for ch, ulen in enumerate([128, 128, 128, 52]):
        u0 = 128 * ch
        p.inc(T.matmul(ps["B"][0:ulen, 0:16], m436_sb[:, u0:u0 + ulen], sb["rel_sb"][:, :],
                       start=True, stop=True), "t")
        p.wait(V, "t")
        p.inc(V.tensor_copy(sb["tmp16"][0:ulen, 0:16], ps["B"][0:ulen, 0:16]), "v")
        p.wait(T, "v")
        p.inc(T.transpose(ps["B"][0:16, 128:128 + ulen], sb["tmp16"][0:ulen, 0:16],
                          sb["id_f"][0:ulen, 0:ulen]), "t")
        p.wait(V, "t"); p.wait(V, "dx")
        p.inc(V.tensor_copy(sb["stg"][0:16, 0:ulen], ps["B"][0:16, 128:128 + ulen]), "v")
        p.wait(Y, "v")
        p.dma(Y, "dx", lambda: Y.dma_start(out=dr["gbuf"][:, u0:u0 + ulen], in_=sb["stg"][0:16, 0:ulen]))
        p.wait(T, "v")

    p.inc(T.matmul(ps["B"][0:1, 0:16], sb["ediff_sb"][:, :], sb["rel_sb"][:, :],
                   start=True, stop=True), "t")
    p.wait(V, "t")
    p.inc(V.tensor_copy(sb["tmp16"][0:1, 0:16], ps["B"][0:1, 0:16]), "v")
    p.wait(T, "v")
    p.inc(T.matmul(ps["B"][:, 16:32], sb["ones_f"][0:1, :], sb["tmp16"][0:1, 0:16],
                   start=True, stop=True), "t")
    p.wait(V, "t")
    p.inc(V.tensor_copy(sb["dct"][:, :], ps["B"][:, 16:32]), "v")

    emit_rotary_tables(p, dr, sb, ps)
    p.wait(V, "v")
    for c in range(NCHUNK):
        V.memset(sb["hb"][:, c * CPAD:c * CPAD + 1], 0.0)
        last = V.memset(sb["ftile"][:, c * CPAD:c * CPAD + 1], 0.0)
    p.inc(last, "v")
    p.bar()


def emit_rotary_tables(p, dr, sb, ps):
    T, V, S, G, Y = p.T, p.V, p.S, p.G, p.Y
    cosb = sb["xcb"][:, 0:N]
    sinb = sb["xcb"][:, N:2 * N]
    scb = sb["stat"][:, 0:N]
    p.wait(Y, "v")
    p.dma(Y, "dx", lambda: Y.dma_start(out=cosb, in_=dr["rotc"][:, :]))
    p.dma(Y, "dx", lambda: Y.dma_start(out=sinb, in_=dr["rots"][:, :]))
    p.dma(Y, "dx", lambda: Y.dma_start(out=scb, in_=dr["rotsc"][:, :]))
    p.wait(V, "dx")
    tbl = sb["tbl"]
    p.seq(V, V.tensor_tensor(sb["sc2"][:, 0:1024], cosb[:, 0:1024], scb[:, 0:1024], ALU.mult), "v")
    p.seq(V, V.tensor_scalar(tbl[:, 0:1024], sb["sc2"][:, 0:1024], 8.0, None, ALU.mult, ALU.bypass), "v")
    p.seq(V, V.tensor_tensor(sb["sc2"][:, 0:1024], cosb[:, 1024:2048], scb[:, 1024:2048], ALU.mult), "v")
    p.seq(V, V.tensor_scalar(tbl[:, 1024:2048], sb["sc2"][:, 0:1024], 8.0, None, ALU.mult, ALU.bypass), "v")
    p.seq(V, V.tensor_tensor(sb["sc2"][:, 0:1024], sinb[:, 0:1024], scb[:, 0:1024], ALU.mult), "v")
    p.seq(V, V.tensor_scalar(tbl[:, N:N + 1024], sb["sc2"][:, 0:1024], 8.0, None, ALU.mult, ALU.bypass), "v")
    p.seq(V, V.tensor_tensor(sb["sc2"][:, 0:1024], sinb[:, 1024:2048], scb[:, 1024:2048], ALU.mult), "v")
    p.seq(V, V.tensor_scalar(tbl[:, N + 1024:2 * N], sb["sc2"][:, 0:1024], 8.0, None, ALU.mult, ALU.bypass), "v")
    p.seq(V, V.reciprocal(sb["sc2"][:, 0:1024], scb[:, 0:1024]), "v")
    p.seq(V, V.tensor_tensor(tbl[:, 2 * N:2 * N + 1024], cosb[:, 0:1024],
                             sb["sc2"][:, 0:1024], ALU.mult), "v")
    p.seq(V, V.tensor_tensor(tbl[:, 3 * N:3 * N + 1024], sinb[:, 0:1024],
                             sb["sc2"][:, 0:1024], ALU.mult), "v")
    p.seq(V, V.reciprocal(sb["sc2"][:, 0:1024], scb[:, 1024:2048]), "v")
    p.seq(V, V.tensor_tensor(tbl[:, 2 * N + 1024:3 * N], cosb[:, 1024:2048],
                             sb["sc2"][:, 0:1024], ALU.mult), "v")
    p.inc(V.tensor_tensor(tbl[:, 3 * N + 1024:4 * N], sinb[:, 1024:2048],
                          sb["sc2"][:, 0:1024], ALU.mult), "v")



def emit_boundary(p, dr, sb, ps, src, dst, delta, scale, want_f, last=False):
    T, V, S, G, Y = p.T, p.V, p.S, p.G, p.Y
    xsrc, xdst = dr[src], dr[dst]

    def xbc(c, c0, c1):
        return bau(sb, c, c0, c1)

    v_ck = [p.cnt["v"]] * 2
    st_ck = [p.cnt["dx"]] * 2
    for c in range(NCHUNK):
        par = c % 2
        buf = sb["xcb"][:, par * N:par * N + N]
        p.wait(Y, "v", upto=v_ck[par])
        if c >= 2:
            p.wait(Y, "dx")
        p.dma(Y, "dx", lambda: Y.dma_start(out=buf, in_=xsrc[128 * c:128 * c + 128, :]))
        ld_dx = p.cnt["dx"]
        p.wait(V, "dx", upto=ld_dx)
        lastv = None
        if delta:
            lastv = V.scalar_tensor_tensor(buf, fts(sb, c, 1, 1 + N), scale, buf, ALU.mult, ALU.add)
            if not last:
                p.seq(V, lastv, "v")
        if not last:
            lastv = V.tensor_copy(xbc(c, 0, N), buf)
        if lastv is not None:
            p.inc(lastv, "v")
        v_ck[par] = p.cnt["v"]
        p.wait(Y, "v", upto=v_ck[par])
        if lastv is None:
            p.wait(Y, "dx")
        p.dma(Y, "dx", lambda: Y.dma_start(out=xdst[128 * c:128 * c + 128, :], in_=buf))
        st_ck[par] = p.cnt["dx"]
    if last:
        p.wait(Y, "dx")
        p.bar()
        return

    p.wait(S, "v")
    for tc in range(4):
        c0 = 512 * tc
        if tc > 0:
            p.wait(T, "v")
        for c in range(NCHUNK):
            par = c % 2
            sq = bau(sb, 8, 512 * par, 512 * par + 512)
            if tc > 0 or c > 1:
                p.wait(S, "t", lag=1)
            p.inc(S.activation(out=sq, in_=xbc(c, c0, c0 + 512), func=AF.Square), "s")
            p.wait(T, "s")
            T.matmul(ps["B"][:, 0:512], sb["ones_bf"][:, :], xbc(c, c0, c0 + 512),
                     start=(c == 0), stop=(c == NCHUNK - 1))
            p.inc(T.matmul(ps["B"][:, 512:1024], sb["ones_bf"][:, :], sq,
                           start=(c == 0), stop=(c == NCHUNK - 1)), "t")
        p.wait(V, "t")
        mean = sb["stat"][:, c0:c0 + 512]
        rstd = sb["stat"][:, N + c0:N + c0 + 512]
        p.seq(V, V.tensor_scalar(mean, ps["B"][:, 0:512], 1.0 / 1024.0, None,
                                 ALU.mult, ALU.bypass), "v")
        p.seq(V, V.tensor_scalar(rstd, ps["B"][:, 512:1024], 1.0 / 1024.0, None,
                                 ALU.mult, ALU.bypass), "v")
        p.seq(V, V.tensor_tensor(sb["sc2"][:, 0:512], mean, mean, ALU.mult), "v")
        p.seq(V, V.tensor_tensor(rstd, rstd, sb["sc2"][:, 0:512], ALU.subtract), "v")
        p.inc(V.tensor_scalar(rstd, rstd, 1e-5, None, ALU.add, ALU.bypass), "v")
        p.wait(S, "v")
        p.seq(S, S.activation(out=rstd, in_=rstd, func=AF.Ln), "s")
        p.inc(S.activation(out=rstd, in_=rstd, func=AF.Exp, scale=-0.5), "s")
    p.wait(V, "s")
    for c in range(NCHUNK):
        for tc in range(4):
            c0 = 512 * tc
            mean = sb["stat"][:, c0:c0 + 512]
            rstd = sb["stat"][:, N + c0:N + c0 + 512]
            out = fts(sb, c, 1 + c0, 1 + c0 + 512) if want_f else hbs(sb, c, 1 + c0, 1 + c0 + 512)
            p.seq(V, V.tensor_tensor(sb["sc2"][:, 512:1024], xbc(c, c0, c0 + 512),
                                     mean, ALU.subtract), "v")
            p.seq(V, V.tensor_tensor(out, sb["sc2"][:, 512:1024], rstd, ALU.mult), "v")
    p.inc(V.nop(), "b", 0) if False else None
    if want_f:
        emit_shiftcast(p, sb)
    p.bar()


def emit_shiftcast(p, sb):
    V = p.V
    p.wait(V, "v")
    for c in range(NCHUNK):
        if c < 4:
            lastv = V.tensor_copy(hbs(sb, c, 1, 1 + N), fts(sb, c, 1, 1 + N))
        else:
            lastv = V.tensor_copy(hbs(sb, c, 1, 1 + N), fts(sb, c, 0, N))
    p.inc(lastv, "v")



WQB_UNIT = 9


def wqbs(sb, ic, oc):
    base = WQB_UNIT * CPAD + 1024 * ic + 128 * oc
    return sb["ba"][:, base:base + 128]


def emit_l1quant(p, dr, sb, ps):
    T, V, S, G, Y = p.T, p.V, p.S, p.G, p.Y
    for c in range(NCHUNK):
        buf = sb["xcb"][:, (c % 2) * 1024:(c % 2) * 1024 + 1024]
        p.wait(Y, "v")
        p.dma(Y, "dx", lambda: Y.dma_start(out=buf, in_=dr["l1f_wT"][128 * c:128 * c + 128, :]))
        p.wait(V, "dx")
        p.inc(V.tensor_reduce(sb["sc1"][:, c:c + 1], buf, axis=AX.X, op=ALU.add,
                              apply_absolute_value=True), "v")
    p.wait(V, "v")
    p.inc(V.tensor_reduce(sb["sc1"][:, 8:9], sb["sc1"][:, 0:8], axis=AX.X, op=ALU.add), "v")
    p.wait(T, "v")
    p.inc(T.matmul(ps["B"][:, 0:1], sb["ones_f"][:, :], sb["sc1"][:, 8:9],
                   start=True, stop=True), "t")
    p.wait(V, "t")
    p.seq(V, V.tensor_scalar(sb["sc1"][:, 9:10], ps["B"][:, 0:1], 1.0 / (2.0 ** 20), 1e-5,
                             ALU.mult, ALU.max), "v")
    p.seq(V, V.tensor_scalar(sb["sc1"][:, 10:11], sb["sc1"][:, 9:10], 0.5, None,
                             ALU.mult, ALU.bypass), "v")
    p.inc(V.tensor_scalar(sb["sc1"][:, 11:12], sb["sc1"][:, 10:11], -1.0, None,
                          ALU.mult, ALU.bypass), "v")
    for c in range(NCHUNK):
        buf = sb["xcb"][:, (c % 2) * 1024:(c % 2) * 1024 + 1024]
        p.wait(Y, "v")
        p.dma(Y, "dx", lambda: Y.dma_start(out=buf, in_=dr["l1f_wT"][128 * c:128 * c + 128, :]))
        p.wait(V, "dx")
        ge = sb["sc2"][:, 0:1024]
        p.seq(V, V.tensor_scalar(ge, buf, sb["sc1"][:, 10:11], None, ALU.is_ge, ALU.bypass), "v")
        p.seq(V, V.tensor_scalar(buf, buf, sb["sc1"][:, 11:12], None, ALU.is_le, ALU.bypass), "v")
        p.seq(V, V.tensor_tensor(ge, ge, buf, ALU.subtract), "v")
        dst = sb["ba"][:, WQB_UNIT * CPAD + 1024 * c: WQB_UNIT * CPAD + 1024 * c + 1024]
        p.inc(V.tensor_scalar(dst, ge, sb["sc1"][:, 9:10], None, ALU.mult, ALU.bypass), "v")
    p.bar()


def emit_l1ffn(p, dr, sb, ps):
    T, V, S, G, Y = p.T, p.V, p.S, p.G, p.Y
    ev_ck = [p.cnt["v"]] * 4
    for oc in range(8):
        for t in range(4):
            p.wait(T, "v", upto=ev_ck[t])
            for ic in range(NCHUNK):
                mm = T.matmul(ps["A"][:, 512 * t:512 * t + 512], wqbs(sb, ic, oc),
                              hbs(sb, ic, 1 + 512 * t, 1 + 512 * t + 512),
                              start=(ic == 0), stop=(ic == NCHUNK - 1))
            p.inc(mm, "t")
            p.wait(V, "t")
            p.inc(V.tensor_copy(fts(sb, oc, 1 + 512 * t, 1 + 512 * t + 512),
                                ps["A"][:, 512 * t:512 * t + 512]), "v")
            ev_ck[t] = p.cnt["v"]
    p.bar()



def emit_l1att(p, dr, sb, ps):
    T, V, S, G, Y = p.T, p.V, p.S, p.G, p.Y

    for h in range(16):
        p.dma(G, "dw", lambda: G.dma_start(out=sb["pk_bf"][64 * (h % 2):64 * (h % 2) + 64,
                                          64 * (h // 2):64 * (h // 2) + 64],
                          in_=dr["l1a_pT"][h, :, :]))
    brow = bau(sb, 3, 0, 1024)
    p.dma(G, "dw", lambda: G.dma_start(out=brow[0:1, :], in_=dr["vecs"][3:4, :]))

    def wslice(t, col0, ncol):
        return bass.AP(t, col0, [[1024, 128], [131072, 8], [1, ncol]])
    woT = dr["l1a_woT"].ap().rearrange("(a b) c -> a b c", b=128)

    ptb = lambda h: sb["pk_bf"][64 * (h % 2):64 * (h % 2) + 64,
                                64 * (h // 2):64 * (h // 2) + 64]
    kvb = lambda h: sb["pk_bf"][64 * (h % 2):64 * (h % 2) + 64,
                                512 + 64 * (h // 2):512 + 64 * (h // 2) + 64]

    for hg in range(4):
        wvu = 4 + hg % 2
        p.wait(G, "t")
        p.dma(G, "dw", lambda: G.dma_start(out=bau(sb, wvu, 0, 2048),
                          in_=wslice(dr["l1a_wvT"], 256 * hg, 256)))
        p.wait(T, "dw")
        for tt in range(16):
            for ic in range(NCHUNK):
                T.matmul(ps["A"][:, 256 * (tt % 4):256 * (tt % 4) + 256],
                         hbs(sb, ic, 1 + 128 * tt, 1 + 128 * tt + 128),
                         bau(sb, wvu, 256 * ic, 256 * ic + 256),
                         start=(ic == 0), stop=False)
            p.inc(T.matmul(ps["A"][:, 256 * (tt % 4):256 * (tt % 4) + 256],
                           sb["ones_bf"][0:1, :], brow[0:1, 256 * hg:256 * hg + 256],
                           start=False, stop=True), "t")
            if tt % 4 == 3:
                p.wait(V, "t")
                for q in range(4):
                    tt2 = tt - 3 + q
                    lastv = V.tensor_copy(
                        bau(sb, 1 + tt2 // 8, 256 * (tt2 % 8), 256 * (tt2 % 8) + 256),
                        ps["A"][:, 256 * q:256 * q + 256])
                p.inc(lastv, "v")
                p.wait(T, "v")
        for hh in range(4):
            h = 4 * hg + hh
            b0 = 64 * (h % 2)
            wku = 1024 * (h % 2)
            p.wait(G, "t")
            p.dma(G, "dw", lambda: G.dma_start(out=bau(sb, 6, wku, wku + 512),
                              in_=wslice(dr["l1a_wkT"], 64 * h, 64)))
            p.wait(T, "dw")
            for t4 in range(4):
                mm = None
                for ic in range(NCHUNK):
                    mm = T.matmul(ps["A"][b0:b0 + 64, 512 * t4:512 * t4 + 512],
                                  bau(sb, 6, wku + 64 * ic, wku + 64 * ic + 64),
                                  hbs(sb, ic, 1 + 512 * t4, 1 + 512 * t4 + 512),
                                  start=(ic == 0), stop=(ic == NCHUNK - 1))
                p.inc(mm, "t")
            p.wait(S, "t")
            p.inc(S.activation(out=bau(sb, 8, 0, N)[b0:b0 + 64, :], in_=ps["A"][b0:b0 + 64, 0:N],
                               func=AF.Identity,
                               bias=sb["vt"][b0:b0 + 64, 12 * (h // 2) + 2:12 * (h // 2) + 3]), "s")
            p.wait(T, "s")
            for tt in range(16):
                p.inc(T.matmul(ps["B"][:, 64 * (tt % 8):64 * (tt % 8) + 64],
                               bau(sb, 8, 128 * tt, 128 * tt + 128)[b0:b0 + 64, :], ptb(h),
                               start=True, stop=True), "t")
                if tt % 8 == 7:
                    p.wait(S, "t")
                    for q in range(8):
                        tt2 = tt - 7 + q
                        lasts = S.activation(
                            out=bau(sb, 0, 256 * (tt2 % 8) + 64 * (tt2 // 8),
                                    256 * (tt2 % 8) + 64 * (tt2 // 8) + 64),
                            in_=ps["B"][:, 64 * q:64 * q + 64], func=AF.Relu, scale=0.125)
                    p.inc(lasts, "s")
                    p.wait(T, "s")
            for tt in range(16):
                mm = T.matmul(ps["B"][b0:b0 + 64, 512:576],
                              bau(sb, 0, 256 * (tt % 8) + 64 * (tt // 8),
                                  256 * (tt % 8) + 64 * (tt // 8) + 64),
                              bau(sb, 1 + tt // 8, 256 * (tt % 8) + 64 * hh,
                                  256 * (tt % 8) + 64 * hh + 64),
                              start=(tt == 0), stop=(tt == 15))
            p.inc(mm, "t")
            p.wait(V, "t")
            p.inc(V.tensor_copy(kvb(h), ps["B"][b0:b0 + 64, 512:576]), "v")
            p.wait(T, "v")

    p.wait(G, "t")
    woff = 4 * CPAD
    for ic in range(NCHUNK):
        p.dma(G, "dw", lambda: G.dma_start(out=sb["ba"][:, woff + 1024 * ic:woff + 1024 * ic + 1024],
                          in_=woT[ic, :, :]))
    p.wait(T, "dw")

    def wo(cc, cols):
        return sb["ba"][:, woff + 1024 * cc + cols[0]:woff + 1024 * cc + cols[1]]

    for tc in range(4):
        c0 = 512 * tc
        st2 = {"oev": [p.cnt["v"]] * 2, "sq": [p.cnt["s"]] * 2, "sf": [p.cnt["s"]] * 2,
               "qmm": [p.cnt["t"]] * 2, "fmm": {}, "amm": {}, "qt": {}, "sqv": {}, "sfv": {}}

        def l1_q(h):
            pq = ps["B"] if h % 2 == 0 else ps["A"]
            b0 = 64 * (h % 2)
            wku = 1024 * (h % 2)
            p.wait(G, "t", upto=st2["qmm"][h % 2])
            p.dma(G, "dw", lambda: G.dma_start(out=bau(sb, 1, wku, wku + 512),
                              in_=wslice(dr["l1a_wqT"], 64 * h, 64)))
            p.wait(T, "dw")
            p.wait(T, "v", upto=st2["oev"][h % 2])
            p.wait(T, "s", upto=st2["sq"][h % 2])
            mm = None
            for ic in range(NCHUNK):
                mm = T.matmul(pq[b0:b0 + 64, 0:512],
                              bau(sb, 1, wku + 64 * ic, wku + 64 * ic + 64),
                              hbs(sb, ic, 1 + c0, 1 + c0 + 512),
                              start=(ic == 0), stop=(ic == NCHUNK - 1))
            p.inc(mm, "t")
            st2["qmm"][h % 2] = p.cnt["t"]
            st2["qt"][h] = p.cnt["t"]

        def l1_mid(h):
            pq = ps["B"] if h % 2 == 0 else ps["A"]
            b0 = 64 * (h % 2)
            p.wait(S, "t", upto=st2["qt"][h])
            p.inc(S.activation(out=bau(sb, 8, 0, 512)[b0:b0 + 64, :], in_=pq[b0:b0 + 64, 0:512],
                               func=AF.Identity,
                               bias=sb["vt"][b0:b0 + 64, 12 * (h // 2) + 1:12 * (h // 2) + 2]), "s")
            st2["sq"][h % 2] = p.cnt["s"]
            st2["sqv"][h] = p.cnt["s"]
            p.wait(T, "s", upto=st2["sqv"][h])
            p.inc(T.matmul(pq[b0:b0 + 64, 512:1024], ptb(h), bau(sb, 8, 0, 512)[b0:b0 + 64, :],
                           start=True, stop=True), "t")
            st2["fmm"][h] = p.cnt["t"]
            p.wait(S, "t", upto=st2["fmm"][h])
            p.inc(S.activation(out=bau(sb, 8, 512, 1024)[b0:b0 + 64, :],
                               in_=pq[b0:b0 + 64, 512:1024], func=AF.Relu, scale=0.125), "s")
            st2["sf"][h % 2] = p.cnt["s"]
            st2["sfv"][h] = p.cnt["s"]
            p.wait(T, "s", upto=st2["sfv"][h])
            p.inc(T.matmul(pq[b0:b0 + 64, 0:512], kvb(h), bau(sb, 8, 512, 1024)[b0:b0 + 64, :],
                           start=True, stop=True), "t")
            st2["amm"][h] = p.cnt["t"]

        def l1_oe(h):
            pq = ps["B"] if h % 2 == 0 else ps["A"]
            b0 = 64 * (h % 2)
            cc = h // 2
            p.wait(V, "t", upto=st2["amm"][h])
            p.inc(V.tensor_copy(bau(sb, 2 + cc // 4, 512 * (cc % 4), 512 * (cc % 4) + 512)
                                [b0:b0 + 64, :], pq[b0:b0 + 64, 0:512]), "v")
            st2["oev"][h % 2] = p.cnt["v"]

        l1_q(0)
        for h in range(16):
            if h + 1 < 16:
                l1_q(h + 1)
            l1_mid(h)
            l1_oe(h)
        p.wait(T, "v")
        for dm in range(8):
            mm = None
            for cc in range(8):
                mm = T.matmul(ps["A"][:, 1024 + 512 * (dm % 2):1024 + 512 * (dm % 2) + 512],
                              wo(cc, (128 * dm, 128 * dm + 128)),
                              bau(sb, 2 + cc // 4, 512 * (cc % 4), 512 * (cc % 4) + 512),
                              start=(cc == 0), stop=(cc == 7))
            p.inc(mm, "t")
            p.wait(V, "t")
            p.inc(V.tensor_scalar(fts(sb, dm, 1 + c0, 1 + c0 + 512),
                                  ps["A"][:, 1024 + 512 * (dm % 2):1024 + 512 * (dm % 2) + 512],
                                  sb["vt"][:, 12 * dm + 4:12 * dm + 5], None, ALU.add, ALU.bypass), "v")
            p.wait(T, "v")
    p.bar()



def emit_l0ffn(p, dr, sb, ps):
    T, V, S, G, Y = p.T, p.V, p.S, p.G, p.Y
    tbl = sb["tbl"]

    def wqkv_slice(l, col0, ncol):
        return bass.AP(dr["l0f_wqkv"], 1572864 * l + col0, [[1536, 128], [196608, 8], [1, ncol]])

    def wo_slice(l, col0, ncol):
        return bass.AP(dr["l0f_wo"], 524288 * l + col0, [[1024, 128], [131072, 4], [1, ncol]])

    def vts(kidx, h):
        return bau(sb, kidx // 4, 512 * (kidx % 4) + 64 * h, 512 * (kidx % 4) + 64 * h + 64)

    for l in range(L0FFN_DEPTH):
        p.wait(G, "t"); p.wait(G, "s")
        p.dma(G, "dw", lambda: G.dma_start(out=bau(sb, 11, 0, 2048),
                                           in_=wqkv_slice(l, 1024, 256)))
        p.dma(G, "dw", lambda: G.dma_start(out=bau(sb, 12, 0, 2048),
                                           in_=wqkv_slice(l, 1280, 256)))
        p.wait(T, "dw"); p.wait(T, "v")
        vtm_ck = [p.cnt["v"]] * 2
        for tt in range(16):
            mm = None
            p.wait(T, "v", upto=vtm_ck[tt % 2])
            for half in range(2):
                u = 11 + half
                for ic in range(NCHUNK):
                    mm = T.matmul(ps["A"][:, 1024 * (tt % 2) + 256 * half:
                                          1024 * (tt % 2) + 256 * half + 256],
                                  hbs(sb, ic, 1 + 128 * tt, 1 + 128 * tt + 128),
                                  bau(sb, u, 256 * ic, 256 * ic + 256),
                                  start=(ic == 0), stop=(ic == NCHUNK - 1))
            p.inc(mm, "t")
            p.wait(V, "t")
            lastv = V.tensor_copy(bau(sb, tt // 4, 512 * (tt % 4), 512 * (tt % 4) + 512),
                                  ps["A"][:, 1024 * (tt % 2):1024 * (tt % 2) + 512])
            p.inc(lastv, "v")
            vtm_ck[tt % 2] = p.cnt["v"]
        for hp in range(4):
            p.wait(G, "t")
            p.dma(G, "dw", lambda: G.dma_start(out=bau(sb, 11, 0, 1024),
                                               in_=wqkv_slice(l, 128 * hp, 128)))
            p.dma(G, "dw", lambda: G.dma_start(out=bau(sb, 11, 1024, 2048),
                                               in_=wqkv_slice(l, 512 + 128 * hp, 128)))
            p.wait(T, "dw"); p.wait(T, "s"); p.wait(T, "v")
            qn_v_ck = [p.cnt["v"]] * 8
            ones_v_ck = [p.cnt["v"]] * 2
            pi_v_ck = [p.cnt["v"]] * 2
            sq_t_ck = [p.cnt["t"]] * 2
            qn_t_ck = [p.cnt["t"]] * 2
            sc2_s_ck = [p.cnt["s"]] * 2
            it = 0
            for which, unit, ct0, st0, scol in ((0, 4, 0, N, 2 * l), (1, 5, 2 * N, 3 * N, 2 * l + 1)):
                for tc in range(4):
                    c0 = 512 * tc
                    par = it % 2
                    it += 1
                    p.wait(T, "v", upto=qn_v_ck[4 * (1 - which) + tc])
                    mm = None
                    for ic in range(NCHUNK):
                        mm = T.matmul(ps["A"][:, c0:c0 + 512],
                                      bau(sb, 11, 1024 * which + 128 * ic,
                                          1024 * which + 128 * ic + 128),
                                      hbs(sb, ic, 1 + c0, 1 + c0 + 512),
                                      start=(ic == 0), stop=(ic == NCHUNK - 1))
                    p.inc(mm, "t")
                    p.wait(S, "t")
                    p.wait(S, "t", upto=sq_t_ck[par])
                    p.inc(S.activation(out=bau(sb, 12, 512 * par, 512 * par + 512),
                                       in_=ps["A"][:, c0:c0 + 512], func=AF.Square), "s")
                    p.wait(T, "s")
                    p.wait(T, "v", upto=ones_v_ck[par])
                    p.inc(T.matmul(ps["B"][:, 512 * par:512 * par + 512], sb["o64_bf"][:, :],
                                   bau(sb, 12, 512 * par, 512 * par + 512),
                                   start=True, stop=True), "t")
                    sq_t_ck[par] = p.cnt["t"]
                    p.wait(V, "t")
                    p.wait(V, "s", upto=sc2_s_ck[par])
                    p.seq(V, V.tensor_scalar(sb["sc2"][:, 512 * par:512 * par + 512],
                                             ps["B"][:, 512 * par:512 * par + 512],
                                             1e-12 ** 2, None, ALU.max, ALU.bypass), "v")
                    ones_v_ck[par] = p.cnt["v"]
                    p.wait(S, "v")
                    p.seq(S, S.activation(out=sb["sc2"][:, 512 * par:512 * par + 512],
                                          in_=sb["sc2"][:, 512 * par:512 * par + 512],
                                          func=AF.Ln), "s")
                    p.inc(S.activation(out=sb["sc2"][:, 512 * par:512 * par + 512],
                                       in_=sb["sc2"][:, 512 * par:512 * par + 512],
                                       func=AF.Exp, scale=-0.5), "s")
                    sc2_s_ck[par] = p.cnt["s"]
                    p.wait(V, "s")
                    p.seq(V, V.tensor_scalar(sb["sc2"][:, 512 * par:512 * par + 512],
                                             sb["sc2"][:, 512 * par:512 * par + 512],
                                             sb["v2t"][:, scol:scol + 1], None,
                                             ALU.mult, ALU.bypass), "v")
                    p.seq(V, V.tensor_tensor(bau(sb, 12, 1024 + 512 * par, 1536 + 512 * par),
                                             ps["A"][:, c0:c0 + 512],
                                             sb["sc2"][:, 512 * par:512 * par + 512],
                                             ALU.mult), "v")
                    v_qn = p.cnt["v"]
                    p.wait(T, "v", upto=v_qn)
                    p.inc(T.matmul(ps["A"][:, c0:c0 + 512], sb["pit_bf"][:, :],
                                   bau(sb, 12, 1024 + 512 * par, 1536 + 512 * par),
                                   start=True, stop=True), "t")
                    qn_t_ck[par] = p.cnt["t"]
                    p.wait(V, "t", upto=qn_t_ck[par])
                    p.seq(V, V.tensor_tensor(bau(sb, 12, 1024 + 512 * par, 1536 + 512 * par),
                                             bau(sb, 12, 1024 + 512 * par, 1536 + 512 * par),
                                             tbl[:, ct0 + c0:ct0 + c0 + 512], ALU.mult), "v")
                    p.seq(V, V.tensor_tensor(sb["sc2"][:, 512 * par:512 * par + 512],
                                             ps["A"][:, c0:c0 + 512],
                                             tbl[:, st0 + c0:st0 + c0 + 512], ALU.mult), "v")
                    p.inc(V.tensor_tensor(bau(sb, unit, c0, c0 + 512),
                                          bau(sb, 12, 1024 + 512 * par, 1536 + 512 * par),
                                          sb["sc2"][:, 512 * par:512 * par + 512], ALU.add), "v")
                    qn_v_ck[4 * which + tc] = p.cnt["v"]
            ck = p.__dict__.setdefault("_l0ffn_ck", {
                "exp_s": [0, 0], "tr_t": [0, 0], "pv_t": [0, 0], "norm_g": [0, 0],
                "oe_v": [0, 0]})
            for hh2 in range(2):
                h = 2 * hp + hh2
                b0 = 64 * (h % 2)

                def tile_geom(qt):
                    w, t = qt // 4, qt % 4
                    q0 = 512 * w + 128 * t
                    if w == 0:
                        return q0, 0, 128 * (t + 1), w, t
                    return q0, 512 * (w - 1) + 128 * t, 640, w, t

                def stage_a(qt):
                    q0, k0, nk, w, t = tile_geom(qt)
                    par = qt % 2
                    pbase = 1024 * par
                    p.wait(T, "s", upto=ck["exp_s"][par])
                    mm = None
                    if w == 0:
                        mm = T.matmul(ps["A"][:, pbase:pbase + nk],
                                      bau(sb, 4, q0, q0 + 128)[b0:b0 + 64, :],
                                      bau(sb, 5, k0, k0 + nk)[b0:b0 + 64, :],
                                      start=True, stop=False)
                        mm = T.matmul(ps["A"][:, pbase + 128 * t:pbase + 128 * t + 128],
                                      sb["id_bf"][:, :], sb["tri2_bf"][:, :],
                                      start=False, stop=True)
                    else:
                        T.matmul(ps["A"][:, pbase:pbase + 512],
                                 bau(sb, 4, q0, q0 + 128)[b0:b0 + 64, :],
                                 bau(sb, 5, k0, k0 + 512)[b0:b0 + 64, :],
                                 start=True, stop=False)
                        T.matmul(ps["A"][:, pbase + 512:pbase + 640],
                                 bau(sb, 4, q0, q0 + 128)[b0:b0 + 64, :],
                                 bau(sb, 5, k0 + 512, k0 + 640)[b0:b0 + 64, :],
                                 start=True, stop=False)
                        T.matmul(ps["A"][:, pbase:pbase + 128],
                                 sb["id_bf"][:, :], sb["tri1_bf"][:, :],
                                 start=False, stop=True)
                        mm = T.matmul(ps["A"][:, pbase + 512:pbase + 640],
                                      sb["id_bf"][:, :], sb["tri2_bf"][:, :],
                                      start=False, stop=True)
                    p.inc(mm, "t")
                    return p.cnt["t"]

                t_a = {}
                v_norm = {}
                t_a[0] = stage_a(0)
                for qt in range(16):
                    par = qt % 2
                    q0, k0, nk, w, t = tile_geom(qt)
                    if qt + 1 < 16:
                        t_a[qt + 1] = stage_a(qt + 1)
                    nm = sb["sc1"][:, 8 + 4 * par:9 + 4 * par]
                    rs = sb["sc1"][:, 9 + 4 * par:10 + 4 * par]
                    rc = sb["sc1"][:, 10 + 4 * par:11 + 4 * par]
                    pu = 6
                    pcol = 1024 * par
                    p.wait(V, "t", upto=t_a[qt])
                    p.wait(V, "s", upto=ck["exp_s"][par])
                    p.inc(V.reduce_max(nm, ps["A"][:, pcol:pcol + nk], axis=AX.X,
                                       negate=True), "v")
                    p.wait(S, "v")
                    p.wait(S, "t", upto=ck["tr_t"][par])
                    p.wait(S, "v", upto=ck["norm_g"][par])
                    p.inc(S.activation(out=bau(sb, pu, 640 * par, 640 * par + nk),
                                       in_=ps["A"][:, pcol:pcol + nk],
                                       func=AF.Exp, bias=nm, accum_out=rs), "s")
                    ck["exp_s"][par] = p.cnt["s"]
                    p.wait(V, "s")
                    p.seq(V, V.reciprocal(rc, rs), "v")
                    p.inc(V.tensor_scalar(bau(sb, pu, 640 * par, 640 * par + nk),
                                          bau(sb, pu, 640 * par, 640 * par + nk),
                                          rc, None, ALU.mult, ALU.bypass), "v")
                    v_norm_c = p.cnt["v"]
                    ck["norm_g"][par] = v_norm_c
                    nkb = nk // 128
                    p.wait(T, "v", upto=v_norm_c)
                    mm = None
                    for kb in range(nkb):
                        mm = T.transpose(ps["T"][:, 1024 * par + 128 * kb:
                                                 1024 * par + 128 * kb + 128],
                                         bau(sb, pu, 640 * par + 128 * kb,
                                             640 * par + 128 * kb + 128),
                                         sb["id_bf"][:, :])
                    p.inc(mm, "t")
                    ck["tr_t"][par] = p.cnt["t"]
                    t_tr = p.cnt["t"]
                    if par == 0:
                        p.wait(V, "t", upto=t_tr)
                        p.inc(V.tensor_copy(bau(sb, 12, 640 * par, 640 * par + nk),
                                            ps["T"][:, 1024 * par:1024 * par + nk]), "v")
                        pte = p.cnt["v"]
                        p.wait(T, "v", upto=pte)
                    else:
                        p.wait(S, "t", upto=t_tr)
                        p.inc(S.activation(out=bau(sb, 12, 640 * par, 640 * par + nk),
                                           in_=ps["T"][:, 1024 * par:1024 * par + nk],
                                           func=AF.Copy), "s")
                        pte = p.cnt["s"]
                        p.wait(T, "s", upto=pte)
                    p.wait(T, "v", upto=ck["oe_v"][par])
                    mm = None
                    for kb in range(nkb):
                        kidx = kb if w == 0 else 4 * (w - 1) + t + kb
                        mm = T.matmul(ps["B"][b0:b0 + 64, 512 * par:512 * par + 128],
                                      vts(kidx, h),
                                      bau(sb, 12, 640 * par + 128 * kb,
                                          640 * par + 128 * kb + 128),
                                      start=(kb == 0), stop=(kb == nkb - 1))
                    p.inc(mm, "t")
                    ck["pv_t"][par] = p.cnt["t"]
                    p.wait(V, "t", upto=p.cnt["t"])
                    p.inc(V.tensor_copy(bau(sb, 7 + h // 2, q0, q0 + 128)[b0:b0 + 64, :],
                                        ps["B"][b0:b0 + 64, 512 * par:512 * par + 128]), "v")
                    ck["oe_v"][par] = p.cnt["v"]
        for half in range(2):
            p.wait(G, "t")
            p.dma(G, "dw", lambda: G.dma_start(out=bau(sb, 11, 0, 2048),
                                               in_=wo_slice(l, 512 * half, 512)))
            p.wait(T, "dw"); p.wait(T, "v")
            for dm in range(4):
                dmg = 4 * half + dm
                for tc in range(4):
                    mm = None
                    for cc in range(4):
                        mm = T.matmul(ps["A"][:, 512 * tc:512 * tc + 512],
                                      bau(sb, 11, 512 * cc + 128 * dm, 512 * cc + 128 * dm + 128),
                                      bau(sb, 7 + cc, 512 * tc, 512 * tc + 512),
                                      start=(cc == 0), stop=(cc == 3))
                    p.inc(mm, "t")
                p.wait(V, "t")
                for tc in range(4):
                    c0 = 512 * tc
                    lastv = V.tensor_tensor(fts(sb, dmg, 1 + c0, 1 + c0 + 512),
                                            fts(sb, dmg, 1 + c0, 1 + c0 + 512),
                                            ps["A"][:, c0:c0 + 512], ALU.add)
                p.inc(lastv, "v")
                p.wait(T, "v")
        if L0FFN_MLP:
            emit_shiftcast(p, sb)
            emit_inner_ln(p, dr, sb, ps, l)
            emit_l0mlp(p, dr, sb, ps, l)
        if l == 0 and L0FFN_DEPTH == 2:
            emit_shiftcast(p, sb)
    p.bar()


def emit_inner_ln(p, dr, sb, ps, l):
    T, V, S, G, Y = p.T, p.V, p.S, p.G, p.Y
    p.wait(S, "v")
    for tc in range(4):
        c0 = 512 * tc
        if tc > 0:
            p.wait(T, "v")
        for c in range(NCHUNK):
            par = c % 2
            sq = bau(sb, 12, 512 * par, 512 * par + 512)
            if tc > 0 or c > 1:
                p.wait(S, "t", lag=1)
            p.inc(S.activation(out=sq, in_=hbs(sb, c, 1 + c0, 1 + c0 + 512), func=AF.Square), "s")
            p.wait(T, "s")
            T.matmul(ps["B"][:, 0:512], sb["ones_bf"][:, :], hbs(sb, c, 1 + c0, 1 + c0 + 512),
                     start=(c == 0), stop=(c == NCHUNK - 1))
            p.inc(T.matmul(ps["B"][:, 512:1024], sb["ones_bf"][:, :], sq,
                           start=(c == 0), stop=(c == NCHUNK - 1)), "t")
        p.wait(V, "t")
        mean = sb["stat"][:, c0:c0 + 512]
        rstd = sb["stat"][:, N + c0:N + c0 + 512]
        p.seq(V, V.tensor_scalar(mean, ps["B"][:, 0:512], 1.0 / 1024.0, None,
                                 ALU.mult, ALU.bypass), "v")
        p.seq(V, V.tensor_scalar(rstd, ps["B"][:, 512:1024], 1.0 / 1024.0, None,
                                 ALU.mult, ALU.bypass), "v")
        p.seq(V, V.tensor_tensor(sb["sc2"][:, 0:512], mean, mean, ALU.mult), "v")
        p.seq(V, V.tensor_tensor(rstd, rstd, sb["sc2"][:, 0:512], ALU.subtract), "v")
        p.inc(V.tensor_scalar(rstd, rstd, 1e-5, None, ALU.add, ALU.bypass), "v")
        p.wait(S, "v")
        p.seq(S, S.activation(out=rstd, in_=rstd, func=AF.Ln), "s")
        p.inc(S.activation(out=rstd, in_=rstd, func=AF.Exp, scale=-0.5), "s")
    p.wait(V, "s")
    for c in range(NCHUNK):
        for tc in range(4):
            c0 = 512 * tc
            mean = sb["stat"][:, c0:c0 + 512]
            rstd = sb["stat"][:, N + c0:N + c0 + 512]
            p.seq(V, V.tensor_tensor(sb["sc2"][:, 512:1024], hbs(sb, c, 1 + c0, 1 + c0 + 512),
                                     mean, ALU.subtract), "v")
            p.seq(V, V.tensor_tensor(sb["sc2"][:, 512:1024], sb["sc2"][:, 512:1024],
                                     rstd, ALU.mult), "v")
            p.seq(V, V.tensor_scalar(hbs(sb, c, 1 + c0, 1 + c0 + 512), sb["sc2"][:, 512:1024],
                                     sb["vt"][:, 12 * c + (5 if l == 0 else 8):
                                              12 * c + (5 if l == 0 else 8) + 1],
                                     sb["vt"][:, 12 * c + (6 if l == 0 else 9):
                                              12 * c + (6 if l == 0 else 9) + 1],
                                     ALU.mult, ALU.add), "v")


def emit_l0mlp(p, dr, sb, ps, l):
    T, V, S, G, Y = p.T, p.V, p.S, p.G, p.Y
    groups = [(0, 6), (6, 12), (12, 18), (18, 22)]

    def w2s(bb, c0, c1):
        return sb["ba"][:, 8 * CPAD + 1024 * bb + c0: 8 * CPAD + 1024 * bb + c1]

    w1_ck = [p.cnt["t"]] * 2
    ag_s_ck = [p.cnt["s"]] * 2
    ag_v_ck = [p.cnt["v"]] * 2
    sc2_v_ck = [p.cnt["v"]] * 2
    w2_t_ck = p.cnt["t"]
    for gi, (g0, g1) in enumerate(groups):
        for b in range(g0, g1):
            bu = 6 + b % 2
            if gi == 0 and b == g0:
                p.wait(G, "t")
            p.wait(G, "t", upto=w1_ck[b % 2])
            if b == g0:
                p.wait(G, "t", upto=w2_t_ck)
            p.dma(G, "dw", lambda: G.dma_start(
                out=bau(sb, bu, 0, 2048),
                in_=bass.AP(dr["w1pb"], 262144 * (NBLK * l + b),
                            [[256, 128], [32768, 8], [1, 256]])))
            p.dma(G, "dw", lambda: G.dma_start(
                out=w2s(b - g0, 0, 1024),
                in_=bass.AP(dr["w2pb"], 131072 * (NBLK * l + b), [[1024, 128], [1, 1024]])))
            p.wait(T, "dw")
            for tc in range(4):
                c0 = 512 * tc
                k = 4 * (b - g0) + tc
                par = k % 2
                pa = ps["B"] if par == 0 else ps["A"]
                p.wait(T, "s", upto=ag_s_ck[par])
                p.wait(T, "v", upto=ag_v_ck[par])
                mm = None
                for half in range(2):
                    for ic in range(NCHUNK):
                        mm = T.matmul(pa[:, 512 * half:512 * half + 512],
                                      bau(sb, bu, 256 * ic + 128 * half,
                                          256 * ic + 128 * half + 128),
                                      hbs(sb, ic, 1 + c0, 1 + c0 + 512),
                                      start=(ic == 0), stop=(ic == NCHUNK - 1))
                p.inc(mm, "t")
                t_after_mm = p.cnt["t"]
                p.wait(S, "t")
                p.wait(S, "v", upto=sc2_v_ck[par])
                p.inc(S.activation(out=sb["sc2"][:, 512 * par:512 * par + 512],
                                   in_=pa[:, 512:1024], func=AF.Gelu,
                                   bias=sb["b1t"][:, 44 * l + 2 * b + 1:44 * l + 2 * b + 2]), "s")
                ag_s_ck[par] = p.cnt["s"]
                p.wait(V, "s")
                if tc == 0:
                    p.wait(V, "t", upto=w2_t_ck)
                p.inc(V.scalar_tensor_tensor(bau(sb, b - g0, c0, c0 + 512), pa[:, 0:512],
                                             sb["b1t"][:, 44 * l + 2 * b:44 * l + 2 * b + 1],
                                             sb["sc2"][:, 512 * par:512 * par + 512],
                                             ALU.add, ALU.mult), "v")
                ag_v_ck[par] = p.cnt["v"]
                sc2_v_ck[par] = p.cnt["v"]
            w1_ck[b % 2] = t_after_mm
        p.wait(T, "v")
        rot_v_ck = [p.cnt["v"]] * 3
        banks = [(ps["A"], 0), (ps["A"], 1024), (ps["B"], 0)]
        for tc in range(4):
            c0 = 512 * tc
            for dq in range(4):
                rot = (4 * tc + dq) % 3
                pa, po = banks[rot]
                p.wait(T, "v", upto=rot_v_ck[rot])
                mm = None
                for b in range(g0, g1):
                    for dm in range(2):
                        dmg = 2 * dq + dm
                        mm = T.matmul(pa[:, po + 512 * dm:po + 512 * dm + 512],
                                      w2s(b - g0, 128 * dmg, 128 * dmg + 128),
                                      bau(sb, b - g0, c0, c0 + 512),
                                      start=(b == g0), stop=(b == g1 - 1))
                p.inc(mm, "t")
                p.wait(V, "t")
                for dm in range(2):
                    dmg = 2 * dq + dm
                    if g0 == 0:
                        lastv = V.scalar_tensor_tensor(
                            fts(sb, dmg, 1 + c0, 1 + c0 + 512), pa[:, po + 512 * dm:po + 512 * dm + 512],
                            sb["vt"][:, 12 * dmg + (7 if l == 0 else 10):
                                     12 * dmg + (7 if l == 0 else 10) + 1],
                            fts(sb, dmg, 1 + c0, 1 + c0 + 512), ALU.add, ALU.add)
                    else:
                        lastv = V.tensor_tensor(fts(sb, dmg, 1 + c0, 1 + c0 + 512),
                                                fts(sb, dmg, 1 + c0, 1 + c0 + 512),
                                                pa[:, po + 512 * dm:po + 512 * dm + 512], ALU.add)
                p.inc(lastv, "v")
                rot_v_ck[rot] = p.cnt["v"]
        w2_t_ck = p.cnt["t"]
        ag_v_ck = [p.cnt["v"], p.cnt["v"]]


def emit_l0att(p, dr, sb, ps):
    T, V, S, G, Y = p.T, p.V, p.S, p.G, p.Y
    p.wait(V, "dx")

    def wkv_slice(col0, ncol):
        return bass.AP(dr["l0a_wkv"], col0, [[2048, 128], [262144, 8], [1, ncol]])

    def wq_slice(col0, ncol):
        return bass.AP(dr["l0a_wq"], col0, [[1024, 128], [131072, 8], [1, ncol]])

    def wo_slice(row0):
        return bass.AP(dr["l0a_wo"], row0 * 1024, [[1024, 128], [131072, 2], [1, 1024]])

    def v4s(kc, hh):
        return bau(sb, 0 + kc // 8, 256 * (kc % 8) + 64 * hh, 256 * (kc % 8) + 64 * hh + 64)

    def pts_pv(kc):
        return bass.AP(sb["ba"], 8 * CPAD + 128 * kc, [[BAU * CPAD, 128], [CPAD, 2], [1, 128]])

    for g in range(4):
        p.wait(G, "t")
        p.dma(G, "dw", lambda: G.dma_start(out=bau(sb, 12, 0, 2048),
                                           in_=wkv_slice(1024 + 256 * g, 256)))
        p.wait(T, "dw")
        v4_ck = [p.cnt["v"]] * 4
        for tt in range(16):
            mm = None
            p.wait(T, "v", upto=v4_ck[tt % 4])
            for ic in range(NCHUNK):
                mm = T.matmul(ps["A"][:, 512 * (tt % 4):512 * (tt % 4) + 256],
                              hbs(sb, ic, 1 + 128 * tt, 1 + 128 * tt + 128),
                              bau(sb, 12, 256 * ic, 256 * ic + 256),
                              start=(ic == 0), stop=(ic == NCHUNK - 1))
            p.inc(mm, "t")
            p.wait(V, "t")
            p.inc(V.tensor_copy(bau(sb, tt // 8, 256 * (tt % 8), 256 * (tt % 8) + 256),
                                ps["A"][:, 512 * (tt % 4):512 * (tt % 4) + 256]), "v")
            v4_ck[tt % 4] = p.cnt["v"]
        for hp in range(2):
            h0 = 4 * g + 2 * hp
            p.wait(G, "t")
            p.dma(G, "dw", lambda: G.dma_start(out=bau(sb, 10, 0, 1024),
                                               in_=wq_slice(64 * h0, 128)))
            p.dma(G, "dw", lambda: G.dma_start(out=bau(sb, 10, 1024, 2048),
                                               in_=wkv_slice(64 * h0, 128)))
            p.wait(T, "dw"); p.wait(T, "s"); p.wait(T, "v")
            for which, unit, scl in ((0, 4, 0.125), (1, 5, 1.0)):
                for tc in range(4):
                    c0 = 512 * tc
                    mm = None
                    for ic in range(NCHUNK):
                        mm = T.matmul(ps["B"][:, 512 * (tc % 2):512 * (tc % 2) + 512],
                                      bau(sb, 10, 1024 * which + 128 * ic,
                                          1024 * which + 128 * ic + 128),
                                      hbs(sb, ic, 1 + c0, 1 + c0 + 512),
                                      start=(ic == 0), stop=(ic == NCHUNK - 1))
                    p.inc(mm, "t")
                    p.wait(S, "t")
                    p.inc(S.activation(out=bau(sb, unit, c0, c0 + 512),
                                       in_=ps["B"][:, 512 * (tc % 2):512 * (tc % 2) + 512],
                                       func=AF.Identity, scale=scl), "s")
                    p.wait(T, "s", lag=1)
            p.wait(T, "s")
            for hh2 in range(2):
                h = h0 + hh2
                hh = 2 * hp + hh2
                b0 = 64 * (h % 2)
                hkoff = 448 * (h % 2)
                p.wait(G, "t")
                p.dma(G, "dw", lambda: G.dma_start(
                    out=bau(sb, 12, hkoff, hkoff + 308),
                    in_=bass.AP(dr["gbuf"], 576 * h, [[1, 128], [1, 308]])))
                p.wait(T, "dw")
                ck = p.__dict__.setdefault("_l0att_ck", {
                    "ttr_v": [0, 0], "exp_s": [0, 0, 0, 0], "tr_t": [0, 0],
                    "oe_v": 0, "nm_s": [0, 0], "exp_par_s": [0, 0], "pv_t": [0, 0],
                    "pte": [0, 0]})

                def scbuf(k):
                    j = k % 4
                    t_ = sb["stat"] if j < 2 else sb["xcb"]
                    return t_[:, 2048 * (j % 2):2048 * (j % 2) + 2048]

                def mxcol(k, which):
                    return sb["sc1"][:, 32 + 8 * (k % 2) + which:33 + 8 * (k % 2) + which]

                def nmc(k, j):
                    return sb["sc1"][:, 16 + 8 * (k % 2) + j:17 + 8 * (k % 2) + j]

                t_sc = {}

                def st_t1(qt):
                    i0 = 128 * qt
                    sb0 = i0 - 90
                    slo = max(0, sb0)
                    shi = min(N, sb0 + 308)
                    for half in range(2):
                        hlo = 1024 * half
                        p.wait(T, "v", upto=ck["ttr_v"][half])
                        mm = None
                        for nb in range(2):
                            b_lo = hlo + 512 * nb
                            bank_strip = max(slo, b_lo) < min(shi, b_lo + 512)
                            mm = T.matmul(ps["A"][:, b_lo:b_lo + 512],
                                          bau(sb, 4, i0, i0 + 128)[b0:b0 + 64, :],
                                          bau(sb, 5, b_lo, b_lo + 512)[b0:b0 + 64, :],
                                          start=True, stop=not bank_strip)
                            if bank_strip:
                                a, b = max(slo, b_lo), min(shi, b_lo + 512)
                                mm = T.matmul(ps["A"][:, a:b], sb["j_bf"][:, :],
                                              bau(sb, 12, hkoff + a - sb0, hkoff + b - sb0),
                                              start=False, stop=True)
                        p.inc(mm, "t")
                        t_sc[(qt, half)] = p.cnt["t"]

                def st_v1(qt):
                    for half in range(2):
                        hlo = 1024 * half
                        p.wait(V, "t", upto=t_sc.pop((qt, half)))
                        p.wait(V, "s", upto=ck["exp_s"][qt % 4])
                        p.wait(V, "s", upto=ck["nm_s"][qt % 2])
                        p.inc(V.tensor_scalar(scbuf(qt)[:, hlo:hlo + 1024],
                                              ps["A"][:, hlo:hlo + 1024], 0.0, None,
                                              ALU.add, ALU.max, accum_out=mxcol(qt, half)), "v")
                        ck["ttr_v"][half] = p.cnt["v"]
                    nm, bl = nmc(qt, 0), nmc(qt, 2)
                    p.wait(V, "v")
                    p.seq(V, V.tensor_tensor(nm, mxcol(qt, 0), mxcol(qt, 1), ALU.max), "v")
                    p.seq(V, V.tensor_scalar(nm, nm, -1.0, None, ALU.mult, ALU.bypass), "v")
                    p.inc(V.tensor_scalar(bl, nm, sb["dct"][:, h:h + 1], None,
                                          ALU.add, ALU.bypass), "v")
                    ck.setdefault("bias_v", {})[qt] = p.cnt["v"]

                def st_s(qt):
                    par = qt % 2
                    lw = max(0, 128 * qt - 90)
                    pu = 6 + par
                    sums = [sb["sc1"][:, 19 + 8 * par + j:20 + 8 * par + j] for j in range(2)]
                    cuts = ([(0, lw, nmc(qt, 2))] if lw > 0 else []) + [(lw, N, nmc(qt, 0))]
                    p.wait(S, "v", upto=ck["bias_v"].pop(qt))
                    p.wait(S, "t", upto=ck["tr_t"][par])
                    p.wait(S, "v", upto=ck.setdefault("sums_v", [0, 0])[par])
                    lasts = None
                    for j, (a, b, bias) in enumerate(cuts):
                        lasts = S.activation(out=bau(sb, pu, a, b), in_=scbuf(qt)[:, a:b],
                                             func=AF.Exp, bias=bias, accum_out=sums[j])
                    p.inc(lasts, "s")
                    ck["exp_s"][qt % 4] = p.cnt["s"]
                    ck["nm_s"][par] = p.cnt["s"]
                    ck["exp_par_s"][par] = p.cnt["s"]
                    ck.setdefault("ncuts", {})[qt] = len(cuts)

                def st_v2(qt):
                    par = qt % 2
                    pu = 6 + par
                    sums = [sb["sc1"][:, 19 + 8 * par + j:20 + 8 * par + j] for j in range(2)]
                    rc = sb["sc1"][:, 22 + 8 * par:23 + 8 * par]
                    p.wait(V, "s", upto=ck["exp_par_s"][par])
                    if ck["ncuts"].pop(qt) > 1:
                        p.seq(V, V.tensor_tensor(sums[0], sums[0], sums[1], ALU.add), "v")
                    p.seq(V, V.reciprocal(rc, sums[0]), "v")
                    ck.setdefault("sums_v", [0, 0])[par] = p.cnt["v"]
                    p.inc(V.tensor_scalar(bau(sb, pu, 0, N), bau(sb, pu, 0, N), rc,
                                          None, ALU.mult, ALU.bypass), "v")
                    ck.setdefault("norm_v", {})[qt] = p.cnt["v"]

                def st_t2(qt):
                    par = qt % 2
                    qp = qt // 2
                    pu = 6 + par
                    p.wait(T, "v", upto=ck["norm_v"].pop(qt))
                    for bt in range(2):
                        mm = None
                        for k8 in range(8):
                            kc = 8 * bt + k8
                            mm = T.transpose(ps["T"][:, 1024 * bt + 128 * k8:
                                                     1024 * bt + 128 * k8 + 128],
                                             bau(sb, pu, 128 * kc, 128 * kc + 128),
                                             sb["id_bf"][:, :])
                        p.inc(mm, "t")
                        t_tr = p.cnt["t"]
                        if bt == 0:
                            p.wait(V, "t", upto=t_tr)
                            p.wait(V, "t", upto=ck["pv_t"][par])
                            p.inc(V.tensor_copy(bau(sb, 8 + par, 0, 1024),
                                                ps["T"][:, 0:1024]), "v")
                            p.wait(T, "v", upto=p.cnt["v"])
                        else:
                            p.wait(S, "t", upto=t_tr)
                            p.wait(S, "t", upto=ck["pv_t"][par])
                            p.inc(S.activation(out=bau(sb, 8 + par, 1024, 2048),
                                               in_=ps["T"][:, 1024:2048], func=AF.Copy), "s")
                            p.wait(T, "s", upto=p.cnt["s"])
                    ck["tr_t"][par] = p.cnt["t"]
                    if par == 1:
                        p.wait(T, "v", upto=ck["oe_v"])
                        mm = None
                        for kc in range(16):
                            mm = T.matmul(ps["B"][b0:b0 + 64, 0:256], v4s(kc, hh),
                                          pts_pv(kc), start=(kc == 0), stop=(kc == 15))
                        p.inc(mm, "t")
                        ck["pv_t"][0] = p.cnt["t"]
                        ck["pv_t"][1] = p.cnt["t"]
                        p.wait(V, "t", upto=p.cnt["t"])
                        p.inc(V.tensor_copy(bau(sb, 2 + hh // 2, 256 * qp, 256 * qp + 256)
                                            [b0:b0 + 64, :], ps["B"][b0:b0 + 64, 0:256]), "v")
                        ck["oe_v"] = p.cnt["v"]

                st_t1(0); st_v1(0); st_s(0); st_t1(1); st_v1(1)
                for qt in range(16):
                    if qt + 2 < 16:
                        st_t1(qt + 2)
                    st_v2(qt)
                    st_t2(qt)
                    if qt + 2 < 16:
                        st_v1(qt + 2)
                    if qt + 1 < 16:
                        st_s(qt + 1)

        p.wait(G, "t")
        p.dma(G, "dw", lambda: G.dma_start(out=bau(sb, 11, 0, 2048), in_=wo_slice(256 * g)))
        p.wait(T, "dw"); p.wait(T, "v"); p.wait(T, "s")
        wo_ck = [p.cnt["v"]] * 4
        for dm in range(8):
            for t4 in range(4):
                c0 = 512 * t4
                p.wait(T, "v", upto=wo_ck[t4])
                mm = None
                for j in range(2):
                    mm = T.matmul(ps["A"][:, c0:c0 + 512],
                                  bau(sb, 11, 1024 * j + 128 * dm, 1024 * j + 128 * dm + 128),
                                  bau(sb, 2 + j, c0, c0 + 512),
                                  start=(j == 0), stop=(j == 1))
                p.inc(mm, "t")
                p.wait(V, "t")
                if g == 0:
                    lastv = V.tensor_scalar(fts(sb, dm, 1 + c0, 1 + c0 + 512),
                                            ps["A"][:, c0:c0 + 512],
                                            sb["vt"][:, 12 * dm + 0:12 * dm + 1],
                                            None, ALU.add, ALU.bypass)
                else:
                    lastv = V.tensor_tensor(fts(sb, dm, 1 + c0, 1 + c0 + 512),
                                            fts(sb, dm, 1 + c0, 1 + c0 + 512),
                                            ps["A"][:, c0:c0 + 512], ALU.add)
                p.inc(lastv, "v")
                wo_ck[t4] = p.cnt["v"]
    p.bar()



_NC_CACHE = {}


def kernel(**inputs):
    sh, percore = host_prep(inputs)
    key = PHASES
    if key not in _NC_CACHE:
        _NC_CACHE[key] = build_nc()
    nc = _NC_CACHE[key]
    in_maps = []
    for c in range(8):
        m = dict(sh)
        m.update(percore[c])
        in_maps.append(m)
    res = run_bass_kernel_spmd(nc, in_maps, core_ids=list(range(8)))
    out = np.stack([np.ascontiguousarray(np.asarray(res.results[c]["yt"]).T) for c in range(8)])
    return out.astype(np.float32)


# version 15
# speedup vs baseline: 1.1149x; 1.0064x over previous
import math
import numpy as np
from contextlib import ExitStack

import concourse.bass as bass
import concourse.mybir as mybir
from concourse.bass_utils import run_bass_kernel_spmd

F32 = mybir.dt.float32
BF16 = mybir.dt.bfloat16
AF = mybir.ActivationFunctionType
ALU = mybir.AluOpType
AX = mybir.AxisListType

N = 2048
D = 1024
NCHUNK = 8
CPAD = 2052
NEGBIG = -float(2.0 ** 127)
PI = math.pi
HID = 2730
NBLK = 22
BAU = 13

PHASES = ("l0ffn_a", "l0att", "l0ffn_c", "l1ffn_d", "l1att", "l1ffn_f")
L0FFN_DEPTH = 2
L0FFN_MLP = True


def _f32c(a):
    return np.ascontiguousarray(np.asarray(a, dtype=np.float32))



def host_prep(inp):
    sh = {}
    sh["l0a_wq"] = _f32c(inp["L0_attn_Wq"])
    sh["l0a_wkv"] = _f32c(inp["L0_attn_Wkv"])
    sh["l0a_wo"] = _f32c(inp["L0_attn_Wo"])
    sh["relemb"] = _f32c(inp["L0_attn_rel_emb"])

    def bucket(rel):
        n = rel
        nb = 16
        ret = 0
        if n < 0:
            ret = nb
        n = abs(n)
        max_exact = nb // 2
        if n < max_exact:
            return ret + n
        v = max_exact + int(
            math.log(max(n, 1) / max_exact) / math.log(128 / max_exact) * (nb - max_exact))
        return ret + min(v, nb - 1)

    m436 = np.zeros((436, 32), dtype=np.float32)
    for u in range(436):
        d = 217 - u
        m436[u, bucket(d)] += 1.0
        m436[u, 31] -= 1.0
    sh["m436t"] = _f32c(m436.T)
    ediff = np.zeros((32, 1), dtype=np.float32)
    ediff[15, 0] = 1.0
    ediff[31, 0] = -1.0
    sh["ediff"] = ediff

    sh["l0f_wqkv"] = _f32c(inp["L0_ffn_wqkv"])
    sh["l0f_wo"] = _f32c(inp["L0_ffn_wo"])
    w1 = _f32c(inp["L0_ffn_w1"]); w2 = _f32c(inp["L0_ffn_w2"]); b1 = _f32c(inp["L0_ffn_b1"])
    HIDP = 128 * NBLK
    w1a = np.zeros((2, 1024, HIDP), np.float32); w1a[:, :, :HID] = w1[:, :, :HID]
    w1g = np.zeros((2, 1024, HIDP), np.float32); w1g[:, :, :HID] = w1[:, :, HID:]
    w1pb = np.zeros((2, NBLK, 1024, 256), np.float32)
    for b in range(NBLK):
        w1pb[:, b, :, :128] = w1a[:, :, 128 * b:128 * b + 128]
        w1pb[:, b, :, 128:] = w1g[:, :, 128 * b:128 * b + 128]
    sh["w1pb"] = _f32c(w1pb)
    w2p = np.zeros((2, HIDP, 1024), np.float32); w2p[:, :HID, :] = w2
    sh["w2pb"] = _f32c(w2p.reshape(2, NBLK, 128, 1024))
    b1a = np.zeros((2, HIDP), np.float32); b1a[:, :HID] = b1[:, :HID]
    b1g = np.zeros((2, HIDP), np.float32); b1g[:, :HID] = b1[:, HID:]
    b1p = np.zeros((2, 44, 128), np.float32)
    for b in range(NBLK):
        b1p[:, 2 * b] = b1a[:, 128 * b:128 * b + 128]
        b1p[:, 2 * b + 1] = b1g[:, 128 * b:128 * b + 128]
    sh["b1p"] = _f32c(b1p.reshape(88, 128))

    sh["l1a_wqT"] = _f32c(np.asarray(inp["L1_attn_Wq"]).T)
    sh["l1a_wkT"] = _f32c(np.asarray(inp["L1_attn_Wk"]).T)
    sh["l1a_wvT"] = _f32c(np.asarray(inp["L1_attn_Wv"]).T)
    sh["l1a_woT"] = _f32c(np.asarray(inp["L1_attn_Wo"]).T)
    sh["l1a_pT"] = _f32c(np.transpose(np.asarray(inp["L1_attn_P"]), (0, 2, 1)))
    sh["l1f_wT"] = _f32c(np.asarray(inp["L1_ffn_weight"]).T)

    vecs = np.zeros((12, 1024), np.float32)
    vecs[0] = _f32c(inp["L0_attn_bo"])
    vecs[1] = _f32c(inp["L1_attn_bq"])
    vecs[2] = _f32c(inp["L1_attn_bk"])
    vecs[3] = _f32c(inp["L1_attn_bv"])
    vecs[4] = _f32c(inp["L1_attn_bo"])
    lng = _f32c(inp["L0_ffn_ln_g"]); lnb = _f32c(inp["L0_ffn_ln_b"]); b2 = _f32c(inp["L0_ffn_b2"])
    vecs[5], vecs[6], vecs[7] = lng[0], lnb[0], b2[0]
    vecs[8], vecs[9], vecs[10] = lng[1], lnb[1], b2[1]
    sh["vecs"] = vecs
    qs = _f32c(inp["L0_ffn_q_scale"]); ks = _f32c(inp["L0_ffn_k_scale"])
    sh["v2"] = _f32c(np.stack([np.tile(qs[0], 2), np.tile(ks[0], 2),
                               np.tile(qs[1], 2), np.tile(ks[1], 2)]))

    import jax
    import jax.numpy as jnp
    with jax.default_device(jax.devices("cpu")[0]):
        half_idx = jnp.arange(0, 64, 2, dtype=jnp.float32)
        inv_freq = 1.0 / 10000.0 ** (half_idx / 64)
        tpos = jnp.arange(2048, dtype=jnp.float32)
        freqs = tpos[:, None] * inv_freq[None, :]
        base = (half_idx + 0.4 * 64) / (1.4 * 64)
        power = (tpos - 1024) / float(256)
        rsc = base[None, :] ** power[:, None]
        rcos = jnp.cos(freqs)
        rsin = jnp.sin(freqs)
    rows = np.arange(128) % 32
    sh["rotc"] = _f32c(np.asarray(rcos).T[rows, :])
    sh["rots"] = _f32c(np.asarray(rsin).T[rows, :])
    sh["rotsc"] = _f32c(np.asarray(rsc).T[rows, :])

    ident = np.eye(128, dtype=np.float32)
    ones = np.ones((128, 128), dtype=np.float32)
    ones64 = np.zeros((128, 128), np.float32); ones64[:64, :64] = 1.0; ones64[64:, 64:] = 1.0
    pi_m = np.zeros((128, 128), np.float32)
    for blk in range(2):
        o = 64 * blk
        pi_m[o:o + 32, o + 32:o + 64] = -np.eye(32)
        pi_m[o + 32:o + 64, o:o + 32] = np.eye(32)
    r = np.arange(128)[:, None]; u = np.arange(128)[None, :]
    tri1 = np.where(u < r, np.float32(NEGBIG), np.float32(0.0)).astype(np.float32)
    tri2 = np.where(u > r, np.float32(NEGBIG), np.float32(0.0)).astype(np.float32)
    jrev = np.zeros((128, 128), np.float32)
    jrev[np.arange(128), 127 - np.arange(128)] = 1.0
    sh["aux7"] = np.stack([ident, ones, ones64, pi_m.T, tri1, tri2, jrev]).astype(np.float32)
    sh["aux_f32"] = np.stack([ident, ones]).astype(np.float32)
    sh["jcol"] = _f32c((np.arange(128) % 32).astype(np.float32)[:, None])

    x = np.asarray(inp["L0_attn_x"], dtype=np.float32)
    percore = [{"xt": _f32c(x[c].T)} for c in range(8)]
    return sh, percore



class P:
    def __init__(self, nc, stack):
        self.nc = nc
        self.T, self.V, self.S, self.G, self.Y = nc.tensor, nc.vector, nc.scalar, nc.gpsimd, nc.sync
        self.sems, self.cnt = {}, {}
        for name in ["dw", "dx", "t", "v", "s", "g", "b"]:
            self.sems[name] = stack.enter_context(nc.semaphore(name))
            self.cnt[name] = 0
        self.hwm = {"dw": 0, "dx": 0}
        self.issuer_waited = {"dw": 0, "dx": 0}

    def inc(self, inst, name, n=1):
        inst.then_inc(self.sems[name], n)
        self.cnt[name] += n

    def seq(self, eng, inst, name):
        self.inc(inst, name)
        eng.wait_ge(self.sems[name], self.cnt[name])

    def wait(self, eng, name, lag=0, upto=None):
        v = (self.cnt[name] - lag) if upto is None else upto
        if v > 0:
            eng.wait_ge(self.sems[name], v)
            if name in self.hwm and v > self.hwm[name]:
                self.hwm[name] = v

    def dma(self, eng, name, fn):
        if self.hwm[name] > self.issuer_waited[name]:
            eng.wait_ge(self.sems[name], self.hwm[name])
            self.issuer_waited[name] = self.hwm[name]
        self.inc(fn(), name, 16)

    def bar(self):
        for eng in (self.T, self.V, self.S, self.G):
            self.inc(eng.drain(), "b", 1)
        self.inc(self.Y.nop(), "b", 1)
        tgt = self.cnt["b"]
        for eng in (self.T, self.V, self.S, self.G, self.Y):
            eng.wait_ge(self.sems["b"], tgt)



def hbs(sb, c, c0, c1):
    return sb["hb"][:, c * CPAD + c0: c * CPAD + c1]


def fts(sb, c, c0, c1):
    return sb["ftile"][:, c * CPAD + c0: c * CPAD + c1]


def bau(sb, unit, c0, c1):
    return sb["ba"][:, unit * CPAD + c0: unit * CPAD + c1]



def build_nc():
    nc = bass.Bass("TRN2", target_bir_lowering=False)
    dr = {}

    def din(name, shape):
        dr[name] = nc.dram_tensor(name, list(shape), F32, kind="ExternalInput")

    din("xt", [D, N])
    din("l0a_wq", [D, D]); din("l0a_wkv", [D, 2 * D]); din("l0a_wo", [D, D])
    din("relemb", [32, 16]); din("m436t", [32, 436]); din("ediff", [32, 1])
    din("l0f_wqkv", [2, D, 1536]); din("l0f_wo", [2, 512, D])
    din("w1pb", [2, NBLK, D, 256]); din("w2pb", [2, NBLK, 128, D]); din("b1p", [88, 128])
    din("l1a_wqT", [D, D]); din("l1a_wkT", [D, D]); din("l1a_wvT", [D, D]); din("l1a_woT", [D, D])
    din("l1a_pT", [16, 64, 64]); din("l1f_wT", [D, D])
    din("vecs", [12, D]); din("v2", [4, 128]); din("aux7", [7, 128, 128])
    din("rotc", [128, N]); din("rots", [128, N]); din("rotsc", [128, N])
    din("aux_f32", [2, 128, 128]); din("jcol", [128, 1])
    dr["yt"] = nc.dram_tensor("yt", [D, N], F32, kind="ExternalOutput")
    dr["xm"] = nc.dram_tensor("xm", [D, N], F32)
    dr["gbuf"] = nc.dram_tensor("gbuf", [16, 576], F32)

    stack = ExitStack()
    sb = {}

    def sbuf(name, shape, dt):
        sb[name] = stack.enter_context(nc.sbuf_tensor(name, list(shape), dt))

    sbuf("hb", [128, NCHUNK * CPAD], BF16)
    sbuf("tbl", [128, 4 * N], BF16)
    sbuf("ftile", [128, NCHUNK * CPAD], F32)
    sbuf("xcb", [128, 2 * N], F32)
    sbuf("stat", [128, 2 * N], F32)
    sbuf("sc2", [128, 1024], F32)
    sbuf("ba", [128, BAU * CPAD], BF16)
    sbuf("vt", [128, 96], F32); sbuf("v2t", [128, 4], F32); sbuf("b1t", [128, 88], F32)
    sbuf("dct", [128, 16], F32); sbuf("sc1", [128, 64], F32)
    for nme in ["id_bf", "ones_bf", "o64_bf", "pit_bf", "tri1_bf", "tri2_bf", "j_bf"]:
        sbuf(nme, [128, 128], BF16)
    sbuf("id_f", [128, 128], F32); sbuf("ones_f", [128, 128], F32)
    sbuf("rel_sb", [32, 16], F32); sbuf("ediff_sb", [32, 1], F32)
    sbuf("jcol_sb", [128, 1], F32)
    sbuf("tmp16", [128, 16], F32)
    sbuf("pk_bf", [128, 1024], BF16)
    sbuf("stg", [128, 128], F32)

    ps = {}
    ps["A"] = stack.enter_context(nc.psum_tensor("psA", [128, 2048], F32))
    ps["B"] = stack.enter_context(nc.psum_tensor("psB", [128, 1024], F32))
    ps["T"] = stack.enter_context(nc.psum_tensor("psT", [128, 2048], BF16))

    p = P(nc, stack)
    emit_program(p, dr, sb, ps)
    stack.close()
    return nc


def emit_program(p, dr, sb, ps):
    emit_prologue(p, dr, sb, ps)
    on = lambda ph: ph in PHASES
    emit_boundary(p, dr, sb, ps, src="xt", dst="xm", delta=None, scale=0.0,
                  want_f=on("l0ffn_a"))
    if on("l0ffn_a"):
        emit_l0ffn(p, dr, sb, ps)
    emit_boundary(p, dr, sb, ps, src="xm", dst="xm",
                  delta=on("l0ffn_a"), scale=0.5, want_f=False)
    if on("l0att"):
        emit_l0att(p, dr, sb, ps)
    emit_boundary(p, dr, sb, ps, src="xm", dst="xm",
                  delta=on("l0att"), scale=1.0, want_f=on("l0ffn_c"))
    if on("l0ffn_c"):
        emit_l0ffn(p, dr, sb, ps)
    emit_boundary(p, dr, sb, ps, src="xm", dst="xm",
                  delta=on("l0ffn_c"), scale=0.5, want_f=False)
    if on("l1ffn_d") or on("l1ffn_f"):
        emit_l1quant(p, dr, sb, ps)
    if on("l1ffn_d"):
        emit_l1ffn(p, dr, sb, ps)
    emit_boundary(p, dr, sb, ps, src="xm", dst="xm",
                  delta=on("l1ffn_d"), scale=0.5, want_f=False)
    if on("l1att"):
        emit_l1att(p, dr, sb, ps)
    emit_boundary(p, dr, sb, ps, src="xm", dst="xm",
                  delta=on("l1att"), scale=1.0, want_f=False)
    if on("l1ffn_f"):
        emit_l1ffn(p, dr, sb, ps)
    emit_boundary(p, dr, sb, ps, src="xm", dst="yt",
                  delta=on("l1ffn_f"), scale=0.5, want_f=False, last=True)



def emit_prologue(p, dr, sb, ps):
    T, V, S, G, Y = p.T, p.V, p.S, p.G, p.Y

    for name, src in [("rel_sb", dr["relemb"]),
                      ("ediff_sb", dr["ediff"]), ("jcol_sb", dr["jcol"])]:
        p.dma(Y, "dx", lambda: Y.dma_start(out=sb[name][:, :], in_=src[:, :]))
    m436_sb = sb["stat"][0:32, 1536:1972]
    p.dma(Y, "dx", lambda: Y.dma_start(out=m436_sb, in_=dr["m436t"][:, :]))
    vload = sb["stat"][0:12, 0:1024]
    v2load = sb["stat"][0:4, 1024:1152]
    b1load = [sb["stat"][0:44, 1152 + 128 * l:1280 + 128 * l] for l in range(2)]
    p.dma(Y, "dx", lambda: Y.dma_start(out=vload, in_=dr["vecs"][:, :]))
    p.dma(Y, "dx", lambda: Y.dma_start(out=v2load, in_=dr["v2"][:, :]))
    for l in range(2):
        p.dma(Y, "dx", lambda: Y.dma_start(out=b1load[l], in_=dr["b1p"][44 * l:44 * l + 44, :]))
    p.dma(Y, "dx", lambda: Y.dma_start(out=sb["id_f"][:, :], in_=dr["aux_f32"][0, :, :]))
    p.dma(Y, "dx", lambda: Y.dma_start(out=sb["ones_f"][:, :], in_=dr["aux_f32"][1, :, :]))
    for i, name in enumerate(["id_bf", "ones_bf", "o64_bf", "pit_bf", "tri1_bf", "tri2_bf", "j_bf"]):
        p.dma(G, "dw", lambda: G.dma_start(out=sb[name][:, :], in_=dr["aux7"][i, :, :]))

    p.wait(T, "dx"); p.wait(T, "dw")

    def tr_small(dst_ap, src_ap):
        np_, nf = src_ap.shape[0], src_ap.shape[1]
        p.inc(T.transpose(ps["A"][0:nf, 0:np_], src_ap, sb["id_f"][0:np_, 0:np_]), "t")
        p.wait(V, "t")
        p.inc(V.tensor_copy(dst_ap, ps["A"][0:nf, 0:np_]), "v")
        p.wait(T, "v")

    for c in range(NCHUNK):
        tr_small(sb["vt"][:, 12 * c:12 * c + 12], vload[:, 128 * c:128 * c + 128])
    tr_small(sb["v2t"][:, :], v2load[:, :])
    for l in range(2):
        tr_small(sb["b1t"][:, 44 * l:44 * l + 44], b1load[l])

    for ch, ulen in enumerate([128, 128, 128, 52]):
        u0 = 128 * ch
        p.inc(T.matmul(ps["B"][0:ulen, 0:16], m436_sb[:, u0:u0 + ulen], sb["rel_sb"][:, :],
                       start=True, stop=True), "t")
        p.wait(V, "t")
        p.inc(V.tensor_copy(sb["tmp16"][0:ulen, 0:16], ps["B"][0:ulen, 0:16]), "v")
        p.wait(T, "v")
        p.inc(T.transpose(ps["B"][0:16, 128:128 + ulen], sb["tmp16"][0:ulen, 0:16],
                          sb["id_f"][0:ulen, 0:ulen]), "t")
        p.wait(V, "t"); p.wait(V, "dx")
        p.inc(V.tensor_copy(sb["stg"][0:16, 0:ulen], ps["B"][0:16, 128:128 + ulen]), "v")
        p.wait(Y, "v")
        p.dma(Y, "dx", lambda: Y.dma_start(out=dr["gbuf"][:, u0:u0 + ulen], in_=sb["stg"][0:16, 0:ulen]))
        p.wait(T, "v")

    p.inc(T.matmul(ps["B"][0:1, 0:16], sb["ediff_sb"][:, :], sb["rel_sb"][:, :],
                   start=True, stop=True), "t")
    p.wait(V, "t")
    p.inc(V.tensor_copy(sb["tmp16"][0:1, 0:16], ps["B"][0:1, 0:16]), "v")
    p.wait(T, "v")
    p.inc(T.matmul(ps["B"][:, 16:32], sb["ones_f"][0:1, :], sb["tmp16"][0:1, 0:16],
                   start=True, stop=True), "t")
    p.wait(V, "t")
    p.inc(V.tensor_copy(sb["dct"][:, :], ps["B"][:, 16:32]), "v")

    emit_rotary_tables(p, dr, sb, ps)
    p.wait(V, "v")
    for c in range(NCHUNK):
        V.memset(sb["hb"][:, c * CPAD:c * CPAD + 1], 0.0)
        last = V.memset(sb["ftile"][:, c * CPAD:c * CPAD + 1], 0.0)
    p.inc(last, "v")
    p.bar()


def emit_rotary_tables(p, dr, sb, ps):
    T, V, S, G, Y = p.T, p.V, p.S, p.G, p.Y
    cosb = sb["xcb"][:, 0:N]
    sinb = sb["xcb"][:, N:2 * N]
    scb = sb["stat"][:, 0:N]
    p.wait(Y, "v")
    p.dma(Y, "dx", lambda: Y.dma_start(out=cosb, in_=dr["rotc"][:, :]))
    p.dma(Y, "dx", lambda: Y.dma_start(out=sinb, in_=dr["rots"][:, :]))
    p.dma(Y, "dx", lambda: Y.dma_start(out=scb, in_=dr["rotsc"][:, :]))
    p.wait(V, "dx")
    tbl = sb["tbl"]
    p.seq(V, V.tensor_tensor(sb["sc2"][:, 0:1024], cosb[:, 0:1024], scb[:, 0:1024], ALU.mult), "v")
    p.seq(V, V.tensor_scalar(tbl[:, 0:1024], sb["sc2"][:, 0:1024], 8.0, None, ALU.mult, ALU.bypass), "v")
    p.seq(V, V.tensor_tensor(sb["sc2"][:, 0:1024], cosb[:, 1024:2048], scb[:, 1024:2048], ALU.mult), "v")
    p.seq(V, V.tensor_scalar(tbl[:, 1024:2048], sb["sc2"][:, 0:1024], 8.0, None, ALU.mult, ALU.bypass), "v")
    p.seq(V, V.tensor_tensor(sb["sc2"][:, 0:1024], sinb[:, 0:1024], scb[:, 0:1024], ALU.mult), "v")
    p.seq(V, V.tensor_scalar(tbl[:, N:N + 1024], sb["sc2"][:, 0:1024], 8.0, None, ALU.mult, ALU.bypass), "v")
    p.seq(V, V.tensor_tensor(sb["sc2"][:, 0:1024], sinb[:, 1024:2048], scb[:, 1024:2048], ALU.mult), "v")
    p.seq(V, V.tensor_scalar(tbl[:, N + 1024:2 * N], sb["sc2"][:, 0:1024], 8.0, None, ALU.mult, ALU.bypass), "v")
    p.seq(V, V.reciprocal(sb["sc2"][:, 0:1024], scb[:, 0:1024]), "v")
    p.seq(V, V.tensor_tensor(tbl[:, 2 * N:2 * N + 1024], cosb[:, 0:1024],
                             sb["sc2"][:, 0:1024], ALU.mult), "v")
    p.seq(V, V.tensor_tensor(tbl[:, 3 * N:3 * N + 1024], sinb[:, 0:1024],
                             sb["sc2"][:, 0:1024], ALU.mult), "v")
    p.seq(V, V.reciprocal(sb["sc2"][:, 0:1024], scb[:, 1024:2048]), "v")
    p.seq(V, V.tensor_tensor(tbl[:, 2 * N + 1024:3 * N], cosb[:, 1024:2048],
                             sb["sc2"][:, 0:1024], ALU.mult), "v")
    p.inc(V.tensor_tensor(tbl[:, 3 * N + 1024:4 * N], sinb[:, 1024:2048],
                          sb["sc2"][:, 0:1024], ALU.mult), "v")



def emit_boundary(p, dr, sb, ps, src, dst, delta, scale, want_f, last=False):
    T, V, S, G, Y = p.T, p.V, p.S, p.G, p.Y
    xsrc, xdst = dr[src], dr[dst]

    def xbc(c, c0, c1):
        return bau(sb, c, c0, c1)

    v_ck = [p.cnt["v"]] * 4
    for c in range(NCHUNK):
        par = c % 4
        bt = sb["xcb"] if par < 2 else sb["stat"]
        buf = bt[:, (par % 2) * N:(par % 2) * N + N]
        p.wait(Y, "v", upto=v_ck[par])
        if c >= 4:
            p.wait(Y, "dx")
        p.dma(Y, "dx", lambda: Y.dma_start(out=buf, in_=xsrc[128 * c:128 * c + 128, :]))
        ld_dx = p.cnt["dx"]
        p.wait(V, "dx", upto=ld_dx)
        lastv = None
        if delta:
            lastv = V.scalar_tensor_tensor(buf, fts(sb, c, 1, 1 + N), scale, buf, ALU.mult, ALU.add)
            if not last:
                p.seq(V, lastv, "v")
        if not last:
            lastv = V.tensor_copy(xbc(c, 0, N), buf)
        if lastv is not None:
            p.inc(lastv, "v")
        v_ck[par] = p.cnt["v"]
        p.wait(Y, "v", upto=v_ck[par])
        if lastv is None:
            p.wait(Y, "dx")
        p.dma(Y, "dx", lambda: Y.dma_start(out=xdst[128 * c:128 * c + 128, :], in_=buf))
    if last:
        p.wait(Y, "dx")
        p.bar()
        return

    p.wait(S, "v"); p.wait(V, "dx")
    for tc in range(4):
        c0 = 512 * tc
        if tc > 0:
            p.wait(T, "v")
        for c in range(NCHUNK):
            par = c % 2
            sq = bau(sb, 8, 512 * par, 512 * par + 512)
            if tc > 0 or c > 1:
                p.wait(S, "t", lag=1)
            p.inc(S.activation(out=sq, in_=xbc(c, c0, c0 + 512), func=AF.Square), "s")
            p.wait(T, "s")
            T.matmul(ps["B"][:, 0:512], sb["ones_bf"][:, :], xbc(c, c0, c0 + 512),
                     start=(c == 0), stop=(c == NCHUNK - 1))
            p.inc(T.matmul(ps["B"][:, 512:1024], sb["ones_bf"][:, :], sq,
                           start=(c == 0), stop=(c == NCHUNK - 1)), "t")
        p.wait(V, "t")
        mean = sb["stat"][:, c0:c0 + 512]
        rstd = sb["stat"][:, N + c0:N + c0 + 512]
        p.seq(V, V.tensor_scalar(mean, ps["B"][:, 0:512], 1.0 / 1024.0, None,
                                 ALU.mult, ALU.bypass), "v")
        p.seq(V, V.tensor_scalar(rstd, ps["B"][:, 512:1024], 1.0 / 1024.0, None,
                                 ALU.mult, ALU.bypass), "v")
        p.seq(V, V.tensor_tensor(sb["sc2"][:, 0:512], mean, mean, ALU.mult), "v")
        p.seq(V, V.tensor_tensor(rstd, rstd, sb["sc2"][:, 0:512], ALU.subtract), "v")
        p.inc(V.tensor_scalar(rstd, rstd, 1e-5, None, ALU.add, ALU.bypass), "v")
        p.wait(S, "v")
        p.seq(S, S.activation(out=rstd, in_=rstd, func=AF.Ln), "s")
        p.inc(S.activation(out=rstd, in_=rstd, func=AF.Exp, scale=-0.5), "s")
    p.wait(V, "s")
    for c in range(NCHUNK):
        for tc in range(4):
            c0 = 512 * tc
            mean = sb["stat"][:, c0:c0 + 512]
            rstd = sb["stat"][:, N + c0:N + c0 + 512]
            out = fts(sb, c, 1 + c0, 1 + c0 + 512) if want_f else hbs(sb, c, 1 + c0, 1 + c0 + 512)
            p.seq(V, V.tensor_tensor(sb["sc2"][:, 512:1024], xbc(c, c0, c0 + 512),
                                     mean, ALU.subtract), "v")
            p.seq(V, V.tensor_tensor(out, sb["sc2"][:, 512:1024], rstd, ALU.mult), "v")
    p.inc(V.nop(), "b", 0) if False else None
    if want_f:
        emit_shiftcast(p, sb)
    p.bar()


def emit_shiftcast(p, sb):
    V = p.V
    p.wait(V, "v")
    for c in range(NCHUNK):
        if c < 4:
            lastv = V.tensor_copy(hbs(sb, c, 1, 1 + N), fts(sb, c, 1, 1 + N))
        else:
            lastv = V.tensor_copy(hbs(sb, c, 1, 1 + N), fts(sb, c, 0, N))
    p.inc(lastv, "v")



WQB_UNIT = 9


def wqbs(sb, ic, oc):
    base = WQB_UNIT * CPAD + 1024 * ic + 128 * oc
    return sb["ba"][:, base:base + 128]


def emit_l1quant(p, dr, sb, ps):
    T, V, S, G, Y = p.T, p.V, p.S, p.G, p.Y
    for c in range(NCHUNK):
        buf = sb["xcb"][:, (c % 2) * 1024:(c % 2) * 1024 + 1024]
        p.wait(Y, "v")
        p.dma(Y, "dx", lambda: Y.dma_start(out=buf, in_=dr["l1f_wT"][128 * c:128 * c + 128, :]))
        p.wait(V, "dx")
        p.inc(V.tensor_reduce(sb["sc1"][:, c:c + 1], buf, axis=AX.X, op=ALU.add,
                              apply_absolute_value=True), "v")
    p.wait(V, "v")
    p.inc(V.tensor_reduce(sb["sc1"][:, 8:9], sb["sc1"][:, 0:8], axis=AX.X, op=ALU.add), "v")
    p.wait(T, "v")
    p.inc(T.matmul(ps["B"][:, 0:1], sb["ones_f"][:, :], sb["sc1"][:, 8:9],
                   start=True, stop=True), "t")
    p.wait(V, "t")
    p.seq(V, V.tensor_scalar(sb["sc1"][:, 9:10], ps["B"][:, 0:1], 1.0 / (2.0 ** 20), 1e-5,
                             ALU.mult, ALU.max), "v")
    p.seq(V, V.tensor_scalar(sb["sc1"][:, 10:11], sb["sc1"][:, 9:10], 0.5, None,
                             ALU.mult, ALU.bypass), "v")
    p.inc(V.tensor_scalar(sb["sc1"][:, 11:12], sb["sc1"][:, 10:11], -1.0, None,
                          ALU.mult, ALU.bypass), "v")
    for c in range(NCHUNK):
        buf = sb["xcb"][:, (c % 2) * 1024:(c % 2) * 1024 + 1024]
        p.wait(Y, "v")
        p.dma(Y, "dx", lambda: Y.dma_start(out=buf, in_=dr["l1f_wT"][128 * c:128 * c + 128, :]))
        p.wait(V, "dx")
        ge = sb["sc2"][:, 0:1024]
        p.seq(V, V.tensor_scalar(ge, buf, sb["sc1"][:, 10:11], None, ALU.is_ge, ALU.bypass), "v")
        p.seq(V, V.tensor_scalar(buf, buf, sb["sc1"][:, 11:12], None, ALU.is_le, ALU.bypass), "v")
        p.seq(V, V.tensor_tensor(ge, ge, buf, ALU.subtract), "v")
        dst = sb["ba"][:, WQB_UNIT * CPAD + 1024 * c: WQB_UNIT * CPAD + 1024 * c + 1024]
        p.inc(V.tensor_scalar(dst, ge, sb["sc1"][:, 9:10], None, ALU.mult, ALU.bypass), "v")
    p.bar()


def emit_l1ffn(p, dr, sb, ps):
    T, V, S, G, Y = p.T, p.V, p.S, p.G, p.Y
    ev_ck = [p.cnt["v"]] * 4
    for oc in range(8):
        for t in range(4):
            p.wait(T, "v", upto=ev_ck[t])
            for ic in range(NCHUNK):
                mm = T.matmul(ps["A"][:, 512 * t:512 * t + 512], wqbs(sb, ic, oc),
                              hbs(sb, ic, 1 + 512 * t, 1 + 512 * t + 512),
                              start=(ic == 0), stop=(ic == NCHUNK - 1))
            p.inc(mm, "t")
            p.wait(V, "t")
            p.inc(V.tensor_copy(fts(sb, oc, 1 + 512 * t, 1 + 512 * t + 512),
                                ps["A"][:, 512 * t:512 * t + 512]), "v")
            ev_ck[t] = p.cnt["v"]
    p.bar()



def emit_l1att(p, dr, sb, ps):
    T, V, S, G, Y = p.T, p.V, p.S, p.G, p.Y

    for h in range(16):
        p.dma(G, "dw", lambda: G.dma_start(out=sb["pk_bf"][64 * (h % 2):64 * (h % 2) + 64,
                                          64 * (h // 2):64 * (h // 2) + 64],
                          in_=dr["l1a_pT"][h, :, :]))
    brow = bau(sb, 3, 0, 1024)
    p.dma(G, "dw", lambda: G.dma_start(out=brow[0:1, :], in_=dr["vecs"][3:4, :]))

    def wslice(t, col0, ncol):
        return bass.AP(t, col0, [[1024, 128], [131072, 8], [1, ncol]])
    woT = dr["l1a_woT"].ap().rearrange("(a b) c -> a b c", b=128)

    ptb = lambda h: sb["pk_bf"][64 * (h % 2):64 * (h % 2) + 64,
                                64 * (h // 2):64 * (h // 2) + 64]
    kvb = lambda h: sb["pk_bf"][64 * (h % 2):64 * (h % 2) + 64,
                                512 + 64 * (h // 2):512 + 64 * (h // 2) + 64]

    for hg in range(4):
        wvu = 4 + hg % 2
        p.wait(G, "t")
        p.dma(G, "dw", lambda: G.dma_start(out=bau(sb, wvu, 0, 2048),
                          in_=wslice(dr["l1a_wvT"], 256 * hg, 256)))
        p.wait(T, "dw")
        for tt in range(16):
            for ic in range(NCHUNK):
                T.matmul(ps["A"][:, 256 * (tt % 4):256 * (tt % 4) + 256],
                         hbs(sb, ic, 1 + 128 * tt, 1 + 128 * tt + 128),
                         bau(sb, wvu, 256 * ic, 256 * ic + 256),
                         start=(ic == 0), stop=False)
            p.inc(T.matmul(ps["A"][:, 256 * (tt % 4):256 * (tt % 4) + 256],
                           sb["ones_bf"][0:1, :], brow[0:1, 256 * hg:256 * hg + 256],
                           start=False, stop=True), "t")
            if tt % 4 == 3:
                p.wait(V, "t")
                for q in range(4):
                    tt2 = tt - 3 + q
                    lastv = V.tensor_copy(
                        bau(sb, 1 + tt2 // 8, 256 * (tt2 % 8), 256 * (tt2 % 8) + 256),
                        ps["A"][:, 256 * q:256 * q + 256])
                p.inc(lastv, "v")
                p.wait(T, "v")
        for hh in range(4):
            h = 4 * hg + hh
            b0 = 64 * (h % 2)
            wku = 1024 * (h % 2)
            p.wait(G, "t")
            p.dma(G, "dw", lambda: G.dma_start(out=bau(sb, 6, wku, wku + 512),
                              in_=wslice(dr["l1a_wkT"], 64 * h, 64)))
            p.wait(T, "dw")
            for t4 in range(4):
                mm = None
                for ic in range(NCHUNK):
                    mm = T.matmul(ps["A"][b0:b0 + 64, 512 * t4:512 * t4 + 512],
                                  bau(sb, 6, wku + 64 * ic, wku + 64 * ic + 64),
                                  hbs(sb, ic, 1 + 512 * t4, 1 + 512 * t4 + 512),
                                  start=(ic == 0), stop=(ic == NCHUNK - 1))
                p.inc(mm, "t")
            p.wait(S, "t")
            p.inc(S.activation(out=bau(sb, 8, 0, N)[b0:b0 + 64, :], in_=ps["A"][b0:b0 + 64, 0:N],
                               func=AF.Identity,
                               bias=sb["vt"][b0:b0 + 64, 12 * (h // 2) + 2:12 * (h // 2) + 3]), "s")
            p.wait(T, "s")
            for tt in range(16):
                p.inc(T.matmul(ps["B"][:, 64 * (tt % 8):64 * (tt % 8) + 64],
                               bau(sb, 8, 128 * tt, 128 * tt + 128)[b0:b0 + 64, :], ptb(h),
                               start=True, stop=True), "t")
                if tt % 8 == 7:
                    p.wait(S, "t")
                    for q in range(8):
                        tt2 = tt - 7 + q
                        lasts = S.activation(
                            out=bau(sb, 0, 256 * (tt2 % 8) + 64 * (tt2 // 8),
                                    256 * (tt2 % 8) + 64 * (tt2 // 8) + 64),
                            in_=ps["B"][:, 64 * q:64 * q + 64], func=AF.Relu, scale=0.125)
                    p.inc(lasts, "s")
                    p.wait(T, "s")
            for tt in range(16):
                mm = T.matmul(ps["B"][b0:b0 + 64, 512:576],
                              bau(sb, 0, 256 * (tt % 8) + 64 * (tt // 8),
                                  256 * (tt % 8) + 64 * (tt // 8) + 64),
                              bau(sb, 1 + tt // 8, 256 * (tt % 8) + 64 * hh,
                                  256 * (tt % 8) + 64 * hh + 64),
                              start=(tt == 0), stop=(tt == 15))
            p.inc(mm, "t")
            p.wait(V, "t")
            p.inc(V.tensor_copy(kvb(h), ps["B"][b0:b0 + 64, 512:576]), "v")
            p.wait(T, "v")

    p.wait(G, "t")
    woff = 4 * CPAD
    for ic in range(NCHUNK):
        p.dma(G, "dw", lambda: G.dma_start(out=sb["ba"][:, woff + 1024 * ic:woff + 1024 * ic + 1024],
                          in_=woT[ic, :, :]))
    p.wait(T, "dw")

    def wo(cc, cols):
        return sb["ba"][:, woff + 1024 * cc + cols[0]:woff + 1024 * cc + cols[1]]

    for tc in range(4):
        c0 = 512 * tc
        st2 = {"oev": [p.cnt["v"]] * 2, "sq": [p.cnt["s"]] * 2, "sf": [p.cnt["s"]] * 2,
               "qmm": [p.cnt["t"]] * 2, "fmm": {}, "amm": {}, "qt": {}, "sqv": {}, "sfv": {}}

        def l1_q(h):
            pq = ps["B"] if h % 2 == 0 else ps["A"]
            b0 = 64 * (h % 2)
            wku = 1024 * (h % 2)
            p.wait(G, "t", upto=st2["qmm"][h % 2])
            p.dma(G, "dw", lambda: G.dma_start(out=bau(sb, 1, wku, wku + 512),
                              in_=wslice(dr["l1a_wqT"], 64 * h, 64)))
            p.wait(T, "dw")
            p.wait(T, "v", upto=st2["oev"][h % 2])
            p.wait(T, "s", upto=st2["sq"][h % 2])
            mm = None
            for ic in range(NCHUNK):
                mm = T.matmul(pq[b0:b0 + 64, 0:512],
                              bau(sb, 1, wku + 64 * ic, wku + 64 * ic + 64),
                              hbs(sb, ic, 1 + c0, 1 + c0 + 512),
                              start=(ic == 0), stop=(ic == NCHUNK - 1))
            p.inc(mm, "t")
            st2["qmm"][h % 2] = p.cnt["t"]
            st2["qt"][h] = p.cnt["t"]

        def l1_mid(h):
            pq = ps["B"] if h % 2 == 0 else ps["A"]
            b0 = 64 * (h % 2)
            p.wait(S, "t", upto=st2["qt"][h])
            p.inc(S.activation(out=bau(sb, 8, 0, 512)[b0:b0 + 64, :], in_=pq[b0:b0 + 64, 0:512],
                               func=AF.Identity,
                               bias=sb["vt"][b0:b0 + 64, 12 * (h // 2) + 1:12 * (h // 2) + 2]), "s")
            st2["sq"][h % 2] = p.cnt["s"]
            st2["sqv"][h] = p.cnt["s"]
            p.wait(T, "s", upto=st2["sqv"][h])
            p.inc(T.matmul(pq[b0:b0 + 64, 512:1024], ptb(h), bau(sb, 8, 0, 512)[b0:b0 + 64, :],
                           start=True, stop=True), "t")
            st2["fmm"][h] = p.cnt["t"]
            p.wait(S, "t", upto=st2["fmm"][h])
            p.inc(S.activation(out=bau(sb, 8, 512, 1024)[b0:b0 + 64, :],
                               in_=pq[b0:b0 + 64, 512:1024], func=AF.Relu, scale=0.125), "s")
            st2["sf"][h % 2] = p.cnt["s"]
            st2["sfv"][h] = p.cnt["s"]
            p.wait(T, "s", upto=st2["sfv"][h])
            p.inc(T.matmul(pq[b0:b0 + 64, 0:512], kvb(h), bau(sb, 8, 512, 1024)[b0:b0 + 64, :],
                           start=True, stop=True), "t")
            st2["amm"][h] = p.cnt["t"]

        def l1_oe(h):
            pq = ps["B"] if h % 2 == 0 else ps["A"]
            b0 = 64 * (h % 2)
            cc = h // 2
            p.wait(V, "t", upto=st2["amm"][h])
            p.inc(V.tensor_copy(bau(sb, 2 + cc // 4, 512 * (cc % 4), 512 * (cc % 4) + 512)
                                [b0:b0 + 64, :], pq[b0:b0 + 64, 0:512]), "v")
            st2["oev"][h % 2] = p.cnt["v"]

        l1_q(0)
        for h in range(16):
            if h + 1 < 16:
                l1_q(h + 1)
            l1_mid(h)
            l1_oe(h)
        p.wait(T, "v")
        for dm in range(8):
            mm = None
            for cc in range(8):
                mm = T.matmul(ps["A"][:, 1024 + 512 * (dm % 2):1024 + 512 * (dm % 2) + 512],
                              wo(cc, (128 * dm, 128 * dm + 128)),
                              bau(sb, 2 + cc // 4, 512 * (cc % 4), 512 * (cc % 4) + 512),
                              start=(cc == 0), stop=(cc == 7))
            p.inc(mm, "t")
            p.wait(V, "t")
            p.inc(V.tensor_scalar(fts(sb, dm, 1 + c0, 1 + c0 + 512),
                                  ps["A"][:, 1024 + 512 * (dm % 2):1024 + 512 * (dm % 2) + 512],
                                  sb["vt"][:, 12 * dm + 4:12 * dm + 5], None, ALU.add, ALU.bypass), "v")
            p.wait(T, "v")
    p.bar()



def emit_l0ffn(p, dr, sb, ps):
    T, V, S, G, Y = p.T, p.V, p.S, p.G, p.Y
    tbl = sb["tbl"]

    def wqkv_slice(l, col0, ncol):
        return bass.AP(dr["l0f_wqkv"], 1572864 * l + col0, [[1536, 128], [196608, 8], [1, ncol]])

    def wo_slice(l, col0, ncol):
        return bass.AP(dr["l0f_wo"], 524288 * l + col0, [[1024, 128], [131072, 4], [1, ncol]])

    def vts(kidx, h):
        return bau(sb, kidx // 4, 512 * (kidx % 4) + 64 * h, 512 * (kidx % 4) + 64 * h + 64)

    for l in range(L0FFN_DEPTH):
        p.wait(G, "t"); p.wait(G, "s")
        p.dma(G, "dw", lambda: G.dma_start(out=bau(sb, 11, 0, 2048),
                                           in_=wqkv_slice(l, 1024, 256)))
        p.dma(G, "dw", lambda: G.dma_start(out=bau(sb, 12, 0, 2048),
                                           in_=wqkv_slice(l, 1280, 256)))
        p.wait(T, "dw"); p.wait(T, "v")
        vtm_ck = [p.cnt["v"]] * 2
        for tt in range(16):
            mm = None
            p.wait(T, "v", upto=vtm_ck[tt % 2])
            for half in range(2):
                u = 11 + half
                for ic in range(NCHUNK):
                    mm = T.matmul(ps["A"][:, 1024 * (tt % 2) + 256 * half:
                                          1024 * (tt % 2) + 256 * half + 256],
                                  hbs(sb, ic, 1 + 128 * tt, 1 + 128 * tt + 128),
                                  bau(sb, u, 256 * ic, 256 * ic + 256),
                                  start=(ic == 0), stop=(ic == NCHUNK - 1))
            p.inc(mm, "t")
            p.wait(V, "t")
            lastv = V.tensor_copy(bau(sb, tt // 4, 512 * (tt % 4), 512 * (tt % 4) + 512),
                                  ps["A"][:, 1024 * (tt % 2):1024 * (tt % 2) + 512])
            p.inc(lastv, "v")
            vtm_ck[tt % 2] = p.cnt["v"]
        for hp in range(4):
            p.wait(G, "t")
            p.dma(G, "dw", lambda: G.dma_start(out=bau(sb, 11, 0, 1024),
                                               in_=wqkv_slice(l, 128 * hp, 128)))
            p.dma(G, "dw", lambda: G.dma_start(out=bau(sb, 11, 1024, 2048),
                                               in_=wqkv_slice(l, 512 + 128 * hp, 128)))
            p.wait(T, "dw"); p.wait(T, "s"); p.wait(T, "v")
            qn_v_ck = [p.cnt["v"]] * 8
            ones_v_ck = [p.cnt["v"]] * 2
            pi_v_ck = [p.cnt["v"]] * 2
            sq_t_ck = [p.cnt["t"]] * 2
            qn_t_ck = [p.cnt["t"]] * 2
            sc2_s_ck = [p.cnt["s"]] * 2
            it = 0
            for which, unit, ct0, st0, scol in ((0, 4, 0, N, 2 * l), (1, 5, 2 * N, 3 * N, 2 * l + 1)):
                for tc in range(4):
                    c0 = 512 * tc
                    par = it % 2
                    it += 1
                    p.wait(T, "v", upto=qn_v_ck[4 * (1 - which) + tc])
                    mm = None
                    for ic in range(NCHUNK):
                        mm = T.matmul(ps["A"][:, c0:c0 + 512],
                                      bau(sb, 11, 1024 * which + 128 * ic,
                                          1024 * which + 128 * ic + 128),
                                      hbs(sb, ic, 1 + c0, 1 + c0 + 512),
                                      start=(ic == 0), stop=(ic == NCHUNK - 1))
                    p.inc(mm, "t")
                    p.wait(S, "t")
                    p.wait(S, "t", upto=sq_t_ck[par])
                    p.inc(S.activation(out=bau(sb, 12, 512 * par, 512 * par + 512),
                                       in_=ps["A"][:, c0:c0 + 512], func=AF.Square), "s")
                    p.wait(T, "s")
                    p.wait(T, "v", upto=ones_v_ck[par])
                    p.inc(T.matmul(ps["B"][:, 512 * par:512 * par + 512], sb["o64_bf"][:, :],
                                   bau(sb, 12, 512 * par, 512 * par + 512),
                                   start=True, stop=True), "t")
                    sq_t_ck[par] = p.cnt["t"]
                    p.wait(V, "t")
                    p.wait(V, "s", upto=sc2_s_ck[par])
                    p.seq(V, V.tensor_scalar(sb["sc2"][:, 512 * par:512 * par + 512],
                                             ps["B"][:, 512 * par:512 * par + 512],
                                             1e-12 ** 2, None, ALU.max, ALU.bypass), "v")
                    ones_v_ck[par] = p.cnt["v"]
                    p.wait(S, "v")
                    p.seq(S, S.activation(out=sb["sc2"][:, 512 * par:512 * par + 512],
                                          in_=sb["sc2"][:, 512 * par:512 * par + 512],
                                          func=AF.Ln), "s")
                    p.inc(S.activation(out=sb["sc2"][:, 512 * par:512 * par + 512],
                                       in_=sb["sc2"][:, 512 * par:512 * par + 512],
                                       func=AF.Exp, scale=-0.5), "s")
                    sc2_s_ck[par] = p.cnt["s"]
                    p.wait(V, "s")
                    p.seq(V, V.tensor_scalar(sb["sc2"][:, 512 * par:512 * par + 512],
                                             sb["sc2"][:, 512 * par:512 * par + 512],
                                             sb["v2t"][:, scol:scol + 1], None,
                                             ALU.mult, ALU.bypass), "v")
                    p.seq(V, V.tensor_tensor(bau(sb, 12, 1024 + 512 * par, 1536 + 512 * par),
                                             ps["A"][:, c0:c0 + 512],
                                             sb["sc2"][:, 512 * par:512 * par + 512],
                                             ALU.mult), "v")
                    v_qn = p.cnt["v"]
                    p.wait(T, "v", upto=v_qn)
                    p.inc(T.matmul(ps["A"][:, c0:c0 + 512], sb["pit_bf"][:, :],
                                   bau(sb, 12, 1024 + 512 * par, 1536 + 512 * par),
                                   start=True, stop=True), "t")
                    qn_t_ck[par] = p.cnt["t"]
                    p.wait(V, "t", upto=qn_t_ck[par])
                    p.seq(V, V.tensor_tensor(bau(sb, 12, 1024 + 512 * par, 1536 + 512 * par),
                                             bau(sb, 12, 1024 + 512 * par, 1536 + 512 * par),
                                             tbl[:, ct0 + c0:ct0 + c0 + 512], ALU.mult), "v")
                    p.seq(V, V.tensor_tensor(sb["sc2"][:, 512 * par:512 * par + 512],
                                             ps["A"][:, c0:c0 + 512],
                                             tbl[:, st0 + c0:st0 + c0 + 512], ALU.mult), "v")
                    p.inc(V.tensor_tensor(bau(sb, unit, c0, c0 + 512),
                                          bau(sb, 12, 1024 + 512 * par, 1536 + 512 * par),
                                          sb["sc2"][:, 512 * par:512 * par + 512], ALU.add), "v")
                    qn_v_ck[4 * which + tc] = p.cnt["v"]
            ck = p.__dict__.setdefault("_l0ffn_ck", {
                "exp_s": [0, 0], "tr_t": [0, 0], "pv_t": [0, 0], "norm_g": [0, 0],
                "oe_v": [0, 0]})
            for hh2 in range(2):
                h = 2 * hp + hh2
                b0 = 64 * (h % 2)

                def tile_geom(qt):
                    w, t = qt // 4, qt % 4
                    q0 = 512 * w + 128 * t
                    if w == 0:
                        return q0, 0, 128 * (t + 1), w, t
                    return q0, 512 * (w - 1) + 128 * t, 640, w, t

                def stage_a(qt):
                    q0, k0, nk, w, t = tile_geom(qt)
                    par = qt % 2
                    pbase = 1024 * par
                    p.wait(T, "s", upto=ck["exp_s"][par])
                    mm = None
                    if w == 0:
                        mm = T.matmul(ps["A"][:, pbase:pbase + nk],
                                      bau(sb, 4, q0, q0 + 128)[b0:b0 + 64, :],
                                      bau(sb, 5, k0, k0 + nk)[b0:b0 + 64, :],
                                      start=True, stop=False)
                        mm = T.matmul(ps["A"][:, pbase + 128 * t:pbase + 128 * t + 128],
                                      sb["id_bf"][:, :], sb["tri2_bf"][:, :],
                                      start=False, stop=True)
                    else:
                        T.matmul(ps["A"][:, pbase:pbase + 512],
                                 bau(sb, 4, q0, q0 + 128)[b0:b0 + 64, :],
                                 bau(sb, 5, k0, k0 + 512)[b0:b0 + 64, :],
                                 start=True, stop=False)
                        T.matmul(ps["A"][:, pbase + 512:pbase + 640],
                                 bau(sb, 4, q0, q0 + 128)[b0:b0 + 64, :],
                                 bau(sb, 5, k0 + 512, k0 + 640)[b0:b0 + 64, :],
                                 start=True, stop=False)
                        T.matmul(ps["A"][:, pbase:pbase + 128],
                                 sb["id_bf"][:, :], sb["tri1_bf"][:, :],
                                 start=False, stop=True)
                        mm = T.matmul(ps["A"][:, pbase + 512:pbase + 640],
                                      sb["id_bf"][:, :], sb["tri2_bf"][:, :],
                                      start=False, stop=True)
                    p.inc(mm, "t")
                    return p.cnt["t"]

                t_a = {}
                v_norm = {}
                t_a[0] = stage_a(0)
                for qt in range(16):
                    par = qt % 2
                    q0, k0, nk, w, t = tile_geom(qt)
                    if qt + 1 < 16:
                        t_a[qt + 1] = stage_a(qt + 1)
                    nm = sb["sc1"][:, 8 + 4 * par:9 + 4 * par]
                    rs = sb["sc1"][:, 9 + 4 * par:10 + 4 * par]
                    rc = sb["sc1"][:, 10 + 4 * par:11 + 4 * par]
                    pu = 6
                    pcol = 1024 * par
                    p.wait(V, "t", upto=t_a[qt])
                    p.wait(V, "s", upto=ck["exp_s"][par])
                    p.inc(V.reduce_max(nm, ps["A"][:, pcol:pcol + nk], axis=AX.X,
                                       negate=True), "v")
                    p.wait(S, "v")
                    p.wait(S, "t", upto=ck["tr_t"][par])
                    p.wait(S, "v", upto=ck["norm_g"][par])
                    p.inc(S.activation(out=bau(sb, pu, 640 * par, 640 * par + nk),
                                       in_=ps["A"][:, pcol:pcol + nk],
                                       func=AF.Exp, bias=nm, accum_out=rs), "s")
                    ck["exp_s"][par] = p.cnt["s"]
                    p.wait(V, "s")
                    p.seq(V, V.reciprocal(rc, rs), "v")
                    p.inc(V.tensor_scalar(bau(sb, pu, 640 * par, 640 * par + nk),
                                          bau(sb, pu, 640 * par, 640 * par + nk),
                                          rc, None, ALU.mult, ALU.bypass), "v")
                    v_norm_c = p.cnt["v"]
                    ck["norm_g"][par] = v_norm_c
                    nkb = nk // 128
                    p.wait(T, "v", upto=v_norm_c)
                    mm = None
                    for kb in range(nkb):
                        mm = T.transpose(ps["T"][:, 1024 * par + 128 * kb:
                                                 1024 * par + 128 * kb + 128],
                                         bau(sb, pu, 640 * par + 128 * kb,
                                             640 * par + 128 * kb + 128),
                                         sb["id_bf"][:, :])
                    p.inc(mm, "t")
                    ck["tr_t"][par] = p.cnt["t"]
                    t_tr = p.cnt["t"]
                    if par == 0:
                        p.wait(V, "t", upto=t_tr)
                        p.inc(V.tensor_copy(bau(sb, 12, 640 * par, 640 * par + nk),
                                            ps["T"][:, 1024 * par:1024 * par + nk]), "v")
                        pte = p.cnt["v"]
                        p.wait(T, "v", upto=pte)
                    else:
                        p.wait(S, "t", upto=t_tr)
                        p.inc(S.activation(out=bau(sb, 12, 640 * par, 640 * par + nk),
                                           in_=ps["T"][:, 1024 * par:1024 * par + nk],
                                           func=AF.Copy), "s")
                        pte = p.cnt["s"]
                        p.wait(T, "s", upto=pte)
                    p.wait(T, "v", upto=ck["oe_v"][par])
                    mm = None
                    for kb in range(nkb):
                        kidx = kb if w == 0 else 4 * (w - 1) + t + kb
                        mm = T.matmul(ps["B"][b0:b0 + 64, 512 * par:512 * par + 128],
                                      vts(kidx, h),
                                      bau(sb, 12, 640 * par + 128 * kb,
                                          640 * par + 128 * kb + 128),
                                      start=(kb == 0), stop=(kb == nkb - 1))
                    p.inc(mm, "t")
                    ck["pv_t"][par] = p.cnt["t"]
                    p.wait(V, "t", upto=p.cnt["t"])
                    p.inc(V.tensor_copy(bau(sb, 7 + h // 2, q0, q0 + 128)[b0:b0 + 64, :],
                                        ps["B"][b0:b0 + 64, 512 * par:512 * par + 128]), "v")
                    ck["oe_v"][par] = p.cnt["v"]
        for half in range(2):
            p.wait(G, "t")
            p.dma(G, "dw", lambda: G.dma_start(out=bau(sb, 11, 0, 2048),
                                               in_=wo_slice(l, 512 * half, 512)))
            p.wait(T, "dw"); p.wait(T, "v")
            for dm in range(4):
                dmg = 4 * half + dm
                for tc in range(4):
                    mm = None
                    for cc in range(4):
                        mm = T.matmul(ps["A"][:, 512 * tc:512 * tc + 512],
                                      bau(sb, 11, 512 * cc + 128 * dm, 512 * cc + 128 * dm + 128),
                                      bau(sb, 7 + cc, 512 * tc, 512 * tc + 512),
                                      start=(cc == 0), stop=(cc == 3))
                    p.inc(mm, "t")
                p.wait(V, "t")
                for tc in range(4):
                    c0 = 512 * tc
                    lastv = V.tensor_tensor(fts(sb, dmg, 1 + c0, 1 + c0 + 512),
                                            fts(sb, dmg, 1 + c0, 1 + c0 + 512),
                                            ps["A"][:, c0:c0 + 512], ALU.add)
                p.inc(lastv, "v")
                p.wait(T, "v")
        p._oproj_t_ck = p.cnt["t"]
        if L0FFN_MLP:
            emit_shiftcast(p, sb)
            emit_inner_ln(p, dr, sb, ps, l)
            emit_l0mlp(p, dr, sb, ps, l)
        if l == 0 and L0FFN_DEPTH == 2:
            emit_shiftcast(p, sb)
    p.bar()


def emit_inner_ln(p, dr, sb, ps, l):
    T, V, S, G, Y = p.T, p.V, p.S, p.G, p.Y
    p.wait(S, "v")
    for tc in range(4):
        c0 = 512 * tc
        if tc > 0:
            p.wait(T, "v")
        for c in range(NCHUNK):
            par = c % 2
            sq = bau(sb, 12, 512 * par, 512 * par + 512)
            if tc > 0 or c > 1:
                p.wait(S, "t", lag=1)
            p.inc(S.activation(out=sq, in_=hbs(sb, c, 1 + c0, 1 + c0 + 512), func=AF.Square), "s")
            p.wait(T, "s")
            T.matmul(ps["B"][:, 0:512], sb["ones_bf"][:, :], hbs(sb, c, 1 + c0, 1 + c0 + 512),
                     start=(c == 0), stop=(c == NCHUNK - 1))
            p.inc(T.matmul(ps["B"][:, 512:1024], sb["ones_bf"][:, :], sq,
                           start=(c == 0), stop=(c == NCHUNK - 1)), "t")
        p.wait(V, "t")
        mean = sb["stat"][:, c0:c0 + 512]
        rstd = sb["stat"][:, N + c0:N + c0 + 512]
        p.seq(V, V.tensor_scalar(mean, ps["B"][:, 0:512], 1.0 / 1024.0, None,
                                 ALU.mult, ALU.bypass), "v")
        p.seq(V, V.tensor_scalar(rstd, ps["B"][:, 512:1024], 1.0 / 1024.0, None,
                                 ALU.mult, ALU.bypass), "v")
        p.seq(V, V.tensor_tensor(sb["sc2"][:, 0:512], mean, mean, ALU.mult), "v")
        p.seq(V, V.tensor_tensor(rstd, rstd, sb["sc2"][:, 0:512], ALU.subtract), "v")
        p.inc(V.tensor_scalar(rstd, rstd, 1e-5, None, ALU.add, ALU.bypass), "v")
        p.wait(S, "v")
        p.seq(S, S.activation(out=rstd, in_=rstd, func=AF.Ln), "s")
        p.inc(S.activation(out=rstd, in_=rstd, func=AF.Exp, scale=-0.5), "s")
    p.wait(V, "s")
    for c in range(NCHUNK):
        for tc in range(4):
            c0 = 512 * tc
            mean = sb["stat"][:, c0:c0 + 512]
            rstd = sb["stat"][:, N + c0:N + c0 + 512]
            p.seq(V, V.tensor_tensor(sb["sc2"][:, 512:1024], hbs(sb, c, 1 + c0, 1 + c0 + 512),
                                     mean, ALU.subtract), "v")
            p.seq(V, V.tensor_tensor(sb["sc2"][:, 512:1024], sb["sc2"][:, 512:1024],
                                     rstd, ALU.mult), "v")
            p.seq(V, V.tensor_scalar(hbs(sb, c, 1 + c0, 1 + c0 + 512), sb["sc2"][:, 512:1024],
                                     sb["vt"][:, 12 * c + (5 if l == 0 else 8):
                                              12 * c + (5 if l == 0 else 8) + 1],
                                     sb["vt"][:, 12 * c + (6 if l == 0 else 9):
                                              12 * c + (6 if l == 0 else 9) + 1],
                                     ALU.mult, ALU.add), "v")


def emit_l0mlp(p, dr, sb, ps, l):
    T, V, S, G, Y = p.T, p.V, p.S, p.G, p.Y
    groups = [(0, 6), (6, 12), (12, 18), (18, 22)]

    def w2s(bb, c0, c1):
        return sb["ba"][:, 8 * CPAD + 1024 * bb + c0: 8 * CPAD + 1024 * bb + c1]

    w1_ck = [p.cnt["t"]] * 2
    ag_s_ck = [p.cnt["s"]] * 2
    ag_v_ck = [p.cnt["v"]] * 2
    sc2_v_ck = [p.cnt["v"]] * 2
    w2_t_ck = p.cnt["t"]
    for gi, (g0, g1) in enumerate(groups):
        for b in range(g0, g1):
            bu = 6 + b % 2
            if gi == 0 and b == g0:
                p.wait(G, "t", upto=getattr(p, "_oproj_t_ck", p.cnt["t"]))
            p.wait(G, "t", upto=w1_ck[b % 2])
            if b == g0:
                p.wait(G, "t", upto=w2_t_ck)
            p.dma(G, "dw", lambda: G.dma_start(
                out=bau(sb, bu, 0, 2048),
                in_=bass.AP(dr["w1pb"], 262144 * (NBLK * l + b),
                            [[256, 128], [32768, 8], [1, 256]])))
            p.dma(G, "dw", lambda: G.dma_start(
                out=w2s(b - g0, 0, 1024),
                in_=bass.AP(dr["w2pb"], 131072 * (NBLK * l + b), [[1024, 128], [1, 1024]])))
            p.wait(T, "dw")
            for tc in range(4):
                c0 = 512 * tc
                k = 4 * (b - g0) + tc
                par = k % 2
                pa = ps["B"] if par == 0 else ps["A"]
                p.wait(T, "s", upto=ag_s_ck[par])
                p.wait(T, "v", upto=ag_v_ck[par])
                mm = None
                for half in range(2):
                    for ic in range(NCHUNK):
                        mm = T.matmul(pa[:, 512 * half:512 * half + 512],
                                      bau(sb, bu, 256 * ic + 128 * half,
                                          256 * ic + 128 * half + 128),
                                      hbs(sb, ic, 1 + c0, 1 + c0 + 512),
                                      start=(ic == 0), stop=(ic == NCHUNK - 1))
                p.inc(mm, "t")
                t_after_mm = p.cnt["t"]
                p.wait(S, "t")
                p.wait(S, "v", upto=sc2_v_ck[par])
                p.inc(S.activation(out=sb["sc2"][:, 512 * par:512 * par + 512],
                                   in_=pa[:, 512:1024], func=AF.Gelu,
                                   bias=sb["b1t"][:, 44 * l + 2 * b + 1:44 * l + 2 * b + 2]), "s")
                ag_s_ck[par] = p.cnt["s"]
                p.wait(V, "s")
                if tc == 0:
                    p.wait(V, "t", upto=w2_t_ck)
                p.inc(V.scalar_tensor_tensor(bau(sb, b - g0, c0, c0 + 512), pa[:, 0:512],
                                             sb["b1t"][:, 44 * l + 2 * b:44 * l + 2 * b + 1],
                                             sb["sc2"][:, 512 * par:512 * par + 512],
                                             ALU.add, ALU.mult), "v")
                ag_v_ck[par] = p.cnt["v"]
                sc2_v_ck[par] = p.cnt["v"]
            w1_ck[b % 2] = t_after_mm
        p.wait(T, "v")
        rot_v_ck = [p.cnt["v"]] * 3
        banks = [(ps["A"], 0), (ps["A"], 1024), (ps["B"], 0)]
        for tc in range(4):
            c0 = 512 * tc
            for dq in range(4):
                rot = (4 * tc + dq) % 3
                pa, po = banks[rot]
                p.wait(T, "v", upto=rot_v_ck[rot])
                mm = None
                for b in range(g0, g1):
                    for dm in range(2):
                        dmg = 2 * dq + dm
                        mm = T.matmul(pa[:, po + 512 * dm:po + 512 * dm + 512],
                                      w2s(b - g0, 128 * dmg, 128 * dmg + 128),
                                      bau(sb, b - g0, c0, c0 + 512),
                                      start=(b == g0), stop=(b == g1 - 1))
                p.inc(mm, "t")
                p.wait(V, "t")
                for dm in range(2):
                    dmg = 2 * dq + dm
                    if g0 == 0:
                        lastv = V.scalar_tensor_tensor(
                            fts(sb, dmg, 1 + c0, 1 + c0 + 512), pa[:, po + 512 * dm:po + 512 * dm + 512],
                            sb["vt"][:, 12 * dmg + (7 if l == 0 else 10):
                                     12 * dmg + (7 if l == 0 else 10) + 1],
                            fts(sb, dmg, 1 + c0, 1 + c0 + 512), ALU.add, ALU.add)
                    else:
                        lastv = V.tensor_tensor(fts(sb, dmg, 1 + c0, 1 + c0 + 512),
                                                fts(sb, dmg, 1 + c0, 1 + c0 + 512),
                                                pa[:, po + 512 * dm:po + 512 * dm + 512], ALU.add)
                p.inc(lastv, "v")
                rot_v_ck[rot] = p.cnt["v"]
        w2_t_ck = p.cnt["t"]
        ag_v_ck = [p.cnt["v"], p.cnt["v"]]


def emit_l0att(p, dr, sb, ps):
    T, V, S, G, Y = p.T, p.V, p.S, p.G, p.Y
    p.wait(V, "dx")

    def wkv_slice(col0, ncol):
        return bass.AP(dr["l0a_wkv"], col0, [[2048, 128], [262144, 8], [1, ncol]])

    def wq_slice(col0, ncol):
        return bass.AP(dr["l0a_wq"], col0, [[1024, 128], [131072, 8], [1, ncol]])

    def wo_slice(row0):
        return bass.AP(dr["l0a_wo"], row0 * 1024, [[1024, 128], [131072, 2], [1, 1024]])

    def v4s(kc, hh):
        return bau(sb, 0 + kc // 8, 256 * (kc % 8) + 64 * hh, 256 * (kc % 8) + 64 * hh + 64)

    def pts_pv(kc):
        return bass.AP(sb["ba"], 8 * CPAD + 128 * kc, [[BAU * CPAD, 128], [CPAD, 2], [1, 128]])

    for g in range(4):
        p.wait(G, "t")
        p.dma(G, "dw", lambda: G.dma_start(out=bau(sb, 12, 0, 2048),
                                           in_=wkv_slice(1024 + 256 * g, 256)))
        p.wait(T, "dw")
        v4_ck = [p.cnt["v"]] * 4
        for tt in range(16):
            mm = None
            p.wait(T, "v", upto=v4_ck[tt % 4])
            for ic in range(NCHUNK):
                mm = T.matmul(ps["A"][:, 512 * (tt % 4):512 * (tt % 4) + 256],
                              hbs(sb, ic, 1 + 128 * tt, 1 + 128 * tt + 128),
                              bau(sb, 12, 256 * ic, 256 * ic + 256),
                              start=(ic == 0), stop=(ic == NCHUNK - 1))
            p.inc(mm, "t")
            p.wait(V, "t")
            p.inc(V.tensor_copy(bau(sb, tt // 8, 256 * (tt % 8), 256 * (tt % 8) + 256),
                                ps["A"][:, 512 * (tt % 4):512 * (tt % 4) + 256]), "v")
            v4_ck[tt % 4] = p.cnt["v"]
        for hp in range(2):
            h0 = 4 * g + 2 * hp
            p.wait(G, "t")
            p.dma(G, "dw", lambda: G.dma_start(out=bau(sb, 10, 0, 1024),
                                               in_=wq_slice(64 * h0, 128)))
            p.dma(G, "dw", lambda: G.dma_start(out=bau(sb, 10, 1024, 2048),
                                               in_=wkv_slice(64 * h0, 128)))
            p.wait(T, "dw"); p.wait(T, "s"); p.wait(T, "v")
            for which, unit, scl in ((0, 4, 0.125), (1, 5, 1.0)):
                for tc in range(4):
                    c0 = 512 * tc
                    mm = None
                    for ic in range(NCHUNK):
                        mm = T.matmul(ps["B"][:, 512 * (tc % 2):512 * (tc % 2) + 512],
                                      bau(sb, 10, 1024 * which + 128 * ic,
                                          1024 * which + 128 * ic + 128),
                                      hbs(sb, ic, 1 + c0, 1 + c0 + 512),
                                      start=(ic == 0), stop=(ic == NCHUNK - 1))
                    p.inc(mm, "t")
                    p.wait(S, "t")
                    p.inc(S.activation(out=bau(sb, unit, c0, c0 + 512),
                                       in_=ps["B"][:, 512 * (tc % 2):512 * (tc % 2) + 512],
                                       func=AF.Identity, scale=scl), "s")
                    p.wait(T, "s", lag=1)
            p.wait(T, "s")
            for hh2 in range(2):
                h = h0 + hh2
                hh = 2 * hp + hh2
                b0 = 64 * (h % 2)
                hkoff = 448 * (h % 2)
                p.wait(G, "t")
                p.dma(G, "dw", lambda: G.dma_start(
                    out=bau(sb, 12, hkoff, hkoff + 308),
                    in_=bass.AP(dr["gbuf"], 576 * h, [[1, 128], [1, 308]])))
                p.wait(T, "dw")
                ck = p.__dict__.setdefault("_l0att_ck", {
                    "ttr_v": [0, 0], "exp_s": [0, 0, 0, 0], "tr_t": [0, 0],
                    "oe_v": 0, "nm_s": [0, 0], "exp_par_s": [0, 0], "pv_t": [0, 0],
                    "pte": [0, 0]})

                def scbuf(k):
                    j = k % 4
                    t_ = sb["stat"] if j < 2 else sb["xcb"]
                    return t_[:, 2048 * (j % 2):2048 * (j % 2) + 2048]

                def mxcol(k, which):
                    return sb["sc1"][:, 32 + 8 * (k % 2) + which:33 + 8 * (k % 2) + which]

                def nmc(k, j):
                    return sb["sc1"][:, 16 + 8 * (k % 2) + j:17 + 8 * (k % 2) + j]

                t_sc = {}

                def st_t1(qt):
                    i0 = 128 * qt
                    sb0 = i0 - 90
                    slo = max(0, sb0)
                    shi = min(N, sb0 + 308)
                    for half in range(2):
                        hlo = 1024 * half
                        p.wait(T, "v", upto=ck["ttr_v"][half])
                        mm = None
                        for nb in range(2):
                            b_lo = hlo + 512 * nb
                            bank_strip = max(slo, b_lo) < min(shi, b_lo + 512)
                            mm = T.matmul(ps["A"][:, b_lo:b_lo + 512],
                                          bau(sb, 4, i0, i0 + 128)[b0:b0 + 64, :],
                                          bau(sb, 5, b_lo, b_lo + 512)[b0:b0 + 64, :],
                                          start=True, stop=not bank_strip)
                            if bank_strip:
                                a, b = max(slo, b_lo), min(shi, b_lo + 512)
                                mm = T.matmul(ps["A"][:, a:b], sb["j_bf"][:, :],
                                              bau(sb, 12, hkoff + a - sb0, hkoff + b - sb0),
                                              start=False, stop=True)
                        p.inc(mm, "t")
                        t_sc[(qt, half)] = p.cnt["t"]

                def st_v1(qt):
                    for half in range(2):
                        hlo = 1024 * half
                        p.wait(V, "t", upto=t_sc.pop((qt, half)))
                        p.wait(V, "s", upto=ck["exp_s"][qt % 4])
                        p.wait(V, "s", upto=ck["nm_s"][qt % 2])
                        p.inc(V.tensor_scalar(scbuf(qt)[:, hlo:hlo + 1024],
                                              ps["A"][:, hlo:hlo + 1024], 0.0, None,
                                              ALU.add, ALU.max, accum_out=mxcol(qt, half)), "v")
                        ck["ttr_v"][half] = p.cnt["v"]
                    nm, bl = nmc(qt, 0), nmc(qt, 2)
                    p.wait(V, "v")
                    p.seq(V, V.tensor_tensor(nm, mxcol(qt, 0), mxcol(qt, 1), ALU.max), "v")
                    p.seq(V, V.tensor_scalar(nm, nm, -1.0, None, ALU.mult, ALU.bypass), "v")
                    p.inc(V.tensor_scalar(bl, nm, sb["dct"][:, h:h + 1], None,
                                          ALU.add, ALU.bypass), "v")
                    ck.setdefault("bias_v", {})[qt] = p.cnt["v"]

                def st_s(qt):
                    par = qt % 2
                    lw = max(0, 128 * qt - 90)
                    pu = 6 + par
                    sums = [sb["sc1"][:, 19 + 8 * par + j:20 + 8 * par + j] for j in range(2)]
                    cuts = ([(0, lw, nmc(qt, 2))] if lw > 0 else []) + [(lw, N, nmc(qt, 0))]
                    p.wait(S, "v", upto=ck["bias_v"].pop(qt))
                    p.wait(S, "t", upto=ck["tr_t"][par])
                    p.wait(S, "v", upto=ck.setdefault("sums_v", [0, 0])[par])
                    lasts = None
                    for j, (a, b, bias) in enumerate(cuts):
                        lasts = S.activation(out=bau(sb, pu, a, b), in_=scbuf(qt)[:, a:b],
                                             func=AF.Exp, bias=bias, accum_out=sums[j])
                    p.inc(lasts, "s")
                    ck["exp_s"][qt % 4] = p.cnt["s"]
                    ck["nm_s"][par] = p.cnt["s"]
                    ck["exp_par_s"][par] = p.cnt["s"]
                    ck.setdefault("ncuts", {})[qt] = len(cuts)

                def st_v2(qt):
                    par = qt % 2
                    pu = 6 + par
                    sums = [sb["sc1"][:, 19 + 8 * par + j:20 + 8 * par + j] for j in range(2)]
                    rc = sb["sc1"][:, 22 + 8 * par:23 + 8 * par]
                    p.wait(V, "s", upto=ck["exp_par_s"][par])
                    if ck["ncuts"].pop(qt) > 1:
                        p.seq(V, V.tensor_tensor(sums[0], sums[0], sums[1], ALU.add), "v")
                    p.seq(V, V.reciprocal(rc, sums[0]), "v")
                    ck.setdefault("sums_v", [0, 0])[par] = p.cnt["v"]
                    p.inc(V.tensor_scalar(bau(sb, pu, 0, N), bau(sb, pu, 0, N), rc,
                                          None, ALU.mult, ALU.bypass), "v")
                    ck.setdefault("norm_v", {})[qt] = p.cnt["v"]

                def st_t2(qt):
                    par = qt % 2
                    qp = qt // 2
                    pu = 6 + par
                    p.wait(T, "v", upto=ck["norm_v"].pop(qt))
                    for bt in range(2):
                        mm = None
                        for k8 in range(8):
                            kc = 8 * bt + k8
                            mm = T.transpose(ps["T"][:, 1024 * bt + 128 * k8:
                                                     1024 * bt + 128 * k8 + 128],
                                             bau(sb, pu, 128 * kc, 128 * kc + 128),
                                             sb["id_bf"][:, :])
                        p.inc(mm, "t")
                        t_tr = p.cnt["t"]
                        if bt == 0:
                            p.wait(V, "t", upto=t_tr)
                            p.wait(V, "t", upto=ck["pv_t"][par])
                            p.inc(V.tensor_copy(bau(sb, 8 + par, 0, 1024),
                                                ps["T"][:, 0:1024]), "v")
                            p.wait(T, "v", upto=p.cnt["v"])
                        else:
                            p.wait(S, "t", upto=t_tr)
                            p.wait(S, "t", upto=ck["pv_t"][par])
                            p.inc(S.activation(out=bau(sb, 8 + par, 1024, 2048),
                                               in_=ps["T"][:, 1024:2048], func=AF.Copy), "s")
                            p.wait(T, "s", upto=p.cnt["s"])
                    ck["tr_t"][par] = p.cnt["t"]
                    if par == 1:
                        p.wait(T, "v", upto=ck["oe_v"])
                        mm = None
                        for kc in range(16):
                            mm = T.matmul(ps["B"][b0:b0 + 64, 0:256], v4s(kc, hh),
                                          pts_pv(kc), start=(kc == 0), stop=(kc == 15))
                        p.inc(mm, "t")
                        ck["pv_t"][0] = p.cnt["t"]
                        ck["pv_t"][1] = p.cnt["t"]
                        p.wait(V, "t", upto=p.cnt["t"])
                        p.inc(V.tensor_copy(bau(sb, 2 + hh // 2, 256 * qp, 256 * qp + 256)
                                            [b0:b0 + 64, :], ps["B"][b0:b0 + 64, 0:256]), "v")
                        ck["oe_v"] = p.cnt["v"]

                st_t1(0); st_v1(0); st_s(0); st_t1(1); st_v1(1)
                for qt in range(16):
                    if qt + 2 < 16:
                        st_t1(qt + 2)
                    st_v2(qt)
                    st_t2(qt)
                    if qt + 2 < 16:
                        st_v1(qt + 2)
                    if qt + 1 < 16:
                        st_s(qt + 1)

        p.wait(G, "t")
        p.dma(G, "dw", lambda: G.dma_start(out=bau(sb, 11, 0, 2048), in_=wo_slice(256 * g)))
        p.wait(T, "dw"); p.wait(T, "v"); p.wait(T, "s")
        wo_ck = [p.cnt["v"]] * 4
        for dm in range(8):
            for t4 in range(4):
                c0 = 512 * t4
                p.wait(T, "v", upto=wo_ck[t4])
                mm = None
                for j in range(2):
                    mm = T.matmul(ps["A"][:, c0:c0 + 512],
                                  bau(sb, 11, 1024 * j + 128 * dm, 1024 * j + 128 * dm + 128),
                                  bau(sb, 2 + j, c0, c0 + 512),
                                  start=(j == 0), stop=(j == 1))
                p.inc(mm, "t")
                p.wait(V, "t")
                if g == 0:
                    lastv = V.tensor_scalar(fts(sb, dm, 1 + c0, 1 + c0 + 512),
                                            ps["A"][:, c0:c0 + 512],
                                            sb["vt"][:, 12 * dm + 0:12 * dm + 1],
                                            None, ALU.add, ALU.bypass)
                else:
                    lastv = V.tensor_tensor(fts(sb, dm, 1 + c0, 1 + c0 + 512),
                                            fts(sb, dm, 1 + c0, 1 + c0 + 512),
                                            ps["A"][:, c0:c0 + 512], ALU.add)
                p.inc(lastv, "v")
                wo_ck[t4] = p.cnt["v"]
    p.bar()



_NC_CACHE = {}


def kernel(**inputs):
    sh, percore = host_prep(inputs)
    key = PHASES
    if key not in _NC_CACHE:
        _NC_CACHE[key] = build_nc()
    nc = _NC_CACHE[key]
    in_maps = []
    for c in range(8):
        m = dict(sh)
        m.update(percore[c])
        in_maps.append(m)
    res = run_bass_kernel_spmd(nc, in_maps, core_ids=list(range(8)))
    out = np.stack([np.ascontiguousarray(np.asarray(res.results[c]["yt"]).T) for c in range(8)])
    return out.astype(np.float32)


# version 16 (final)
# speedup vs baseline: 1.1515x; 1.0328x over previous
import math
import numpy as np
from contextlib import ExitStack

import concourse.bass as bass
import concourse.mybir as mybir
from concourse.bass_utils import run_bass_kernel_spmd

F32 = mybir.dt.float32
BF16 = mybir.dt.bfloat16
AF = mybir.ActivationFunctionType
ALU = mybir.AluOpType
AX = mybir.AxisListType

N = 2048
D = 1024
NCHUNK = 8
CPAD = 2052
NEGBIG = -float(2.0 ** 127)
PI = math.pi
HID = 2730
NBLK = 22
BAU = 13

PHASES = ("l0ffn_a", "l0att", "l0ffn_c", "l1ffn_d", "l1att", "l1ffn_f")
L0FFN_DEPTH = 2
L0FFN_MLP = True


def _f32c(a):
    return np.ascontiguousarray(np.asarray(a, dtype=np.float32))



def host_prep(inp):
    sh = {}
    sh["l0a_wq"] = _f32c(inp["L0_attn_Wq"])
    sh["l0a_wkv"] = _f32c(inp["L0_attn_Wkv"])
    sh["l0a_wo"] = _f32c(inp["L0_attn_Wo"])
    sh["relemb"] = _f32c(inp["L0_attn_rel_emb"])

    def bucket(rel):
        n = rel
        nb = 16
        ret = 0
        if n < 0:
            ret = nb
        n = abs(n)
        max_exact = nb // 2
        if n < max_exact:
            return ret + n
        v = max_exact + int(
            math.log(max(n, 1) / max_exact) / math.log(128 / max_exact) * (nb - max_exact))
        return ret + min(v, nb - 1)

    m436 = np.zeros((436, 32), dtype=np.float32)
    for u in range(436):
        d = 217 - u
        m436[u, bucket(d)] += 1.0
        m436[u, 31] -= 1.0
    sh["m436t"] = _f32c(m436.T)
    ediff = np.zeros((32, 1), dtype=np.float32)
    ediff[15, 0] = 1.0
    ediff[31, 0] = -1.0
    sh["ediff"] = ediff

    sh["l0f_wqkv"] = _f32c(inp["L0_ffn_wqkv"])
    sh["l0f_wo"] = _f32c(inp["L0_ffn_wo"])
    w1 = _f32c(inp["L0_ffn_w1"]); w2 = _f32c(inp["L0_ffn_w2"]); b1 = _f32c(inp["L0_ffn_b1"])
    HIDP = 128 * NBLK
    w1a = np.zeros((2, 1024, HIDP), np.float32); w1a[:, :, :HID] = w1[:, :, :HID]
    w1g = np.zeros((2, 1024, HIDP), np.float32); w1g[:, :, :HID] = w1[:, :, HID:]
    w1pb = np.zeros((2, NBLK, 1024, 256), np.float32)
    for b in range(NBLK):
        w1pb[:, b, :, :128] = w1a[:, :, 128 * b:128 * b + 128]
        w1pb[:, b, :, 128:] = w1g[:, :, 128 * b:128 * b + 128]
    sh["w1pb"] = _f32c(w1pb)
    w2p = np.zeros((2, HIDP, 1024), np.float32); w2p[:, :HID, :] = w2
    sh["w2pb"] = _f32c(w2p.reshape(2, NBLK, 128, 1024))
    b1a = np.zeros((2, HIDP), np.float32); b1a[:, :HID] = b1[:, :HID]
    b1g = np.zeros((2, HIDP), np.float32); b1g[:, :HID] = b1[:, HID:]
    b1p = np.zeros((2, 44, 128), np.float32)
    for b in range(NBLK):
        b1p[:, 2 * b] = b1a[:, 128 * b:128 * b + 128]
        b1p[:, 2 * b + 1] = b1g[:, 128 * b:128 * b + 128]
    sh["b1p"] = _f32c(b1p.reshape(88, 128))

    sh["l1a_wqT"] = _f32c(np.asarray(inp["L1_attn_Wq"]).T)
    sh["l1a_wkT"] = _f32c(np.asarray(inp["L1_attn_Wk"]).T)
    sh["l1a_wvT"] = _f32c(np.asarray(inp["L1_attn_Wv"]).T)
    sh["l1a_woT"] = _f32c(np.asarray(inp["L1_attn_Wo"]).T)
    sh["l1a_pT"] = _f32c(np.transpose(np.asarray(inp["L1_attn_P"]), (0, 2, 1)))
    sh["l1f_wT"] = _f32c(np.asarray(inp["L1_ffn_weight"]).T)

    vecs = np.zeros((12, 1024), np.float32)
    vecs[0] = _f32c(inp["L0_attn_bo"])
    vecs[1] = _f32c(inp["L1_attn_bq"])
    vecs[2] = _f32c(inp["L1_attn_bk"])
    vecs[3] = _f32c(inp["L1_attn_bv"])
    vecs[4] = _f32c(inp["L1_attn_bo"])
    lng = _f32c(inp["L0_ffn_ln_g"]); lnb = _f32c(inp["L0_ffn_ln_b"]); b2 = _f32c(inp["L0_ffn_b2"])
    vecs[5], vecs[6], vecs[7] = lng[0], lnb[0], b2[0]
    vecs[8], vecs[9], vecs[10] = lng[1], lnb[1], b2[1]
    sh["vecs"] = vecs
    qs = _f32c(inp["L0_ffn_q_scale"]); ks = _f32c(inp["L0_ffn_k_scale"])
    sh["v2"] = _f32c(np.stack([np.tile(qs[0], 2), np.tile(ks[0], 2),
                               np.tile(qs[1], 2), np.tile(ks[1], 2)]))

    import jax
    import jax.numpy as jnp
    with jax.default_device(jax.devices("cpu")[0]):
        half_idx = jnp.arange(0, 64, 2, dtype=jnp.float32)
        inv_freq = 1.0 / 10000.0 ** (half_idx / 64)
        tpos = jnp.arange(2048, dtype=jnp.float32)
        freqs = tpos[:, None] * inv_freq[None, :]
        base = (half_idx + 0.4 * 64) / (1.4 * 64)
        power = (tpos - 1024) / float(256)
        rsc = base[None, :] ** power[:, None]
        rcos = jnp.cos(freqs)
        rsin = jnp.sin(freqs)
    rows = np.arange(128) % 32
    sh["rotc"] = _f32c(np.asarray(rcos).T[rows, :])
    sh["rots"] = _f32c(np.asarray(rsin).T[rows, :])
    sh["rotsc"] = _f32c(np.asarray(rsc).T[rows, :])

    ident = np.eye(128, dtype=np.float32)
    ones = np.ones((128, 128), dtype=np.float32)
    ones64 = np.zeros((128, 128), np.float32); ones64[:64, :64] = 1.0; ones64[64:, 64:] = 1.0
    pi_m = np.zeros((128, 128), np.float32)
    for blk in range(2):
        o = 64 * blk
        pi_m[o:o + 32, o + 32:o + 64] = -np.eye(32)
        pi_m[o + 32:o + 64, o:o + 32] = np.eye(32)
    r = np.arange(128)[:, None]; u = np.arange(128)[None, :]
    tri1 = np.where(u < r, np.float32(NEGBIG), np.float32(0.0)).astype(np.float32)
    tri2 = np.where(u > r, np.float32(NEGBIG), np.float32(0.0)).astype(np.float32)
    jrev = np.zeros((128, 128), np.float32)
    jrev[np.arange(128), 127 - np.arange(128)] = 1.0
    sh["aux7"] = np.stack([ident, ones, ones64, pi_m.T, tri1, tri2, jrev]).astype(np.float32)
    sh["aux_f32"] = np.stack([ident, ones]).astype(np.float32)
    sh["jcol"] = _f32c((np.arange(128) % 32).astype(np.float32)[:, None])

    x = np.asarray(inp["L0_attn_x"], dtype=np.float32)
    percore = [{"xt": _f32c(x[c].T)} for c in range(8)]
    return sh, percore



class P:
    def __init__(self, nc, stack):
        self.nc = nc
        self.T, self.V, self.S, self.G, self.Y = nc.tensor, nc.vector, nc.scalar, nc.gpsimd, nc.sync
        self.sems, self.cnt = {}, {}
        for name in ["dw", "dx", "t", "v", "s", "g", "b"]:
            self.sems[name] = stack.enter_context(nc.semaphore(name))
            self.cnt[name] = 0
        self.hwm = {"dw": 0, "dx": 0}
        self.issuer_waited = {"dw": 0, "dx": 0}

    def inc(self, inst, name, n=1):
        inst.then_inc(self.sems[name], n)
        self.cnt[name] += n

    def seq(self, eng, inst, name):
        self.inc(inst, name)
        eng.wait_ge(self.sems[name], self.cnt[name])

    def wait(self, eng, name, lag=0, upto=None):
        v = (self.cnt[name] - lag) if upto is None else upto
        if v > 0:
            eng.wait_ge(self.sems[name], v)
            if name in self.hwm and v > self.hwm[name]:
                self.hwm[name] = v

    def dma(self, eng, name, fn):
        if self.hwm[name] > self.issuer_waited[name]:
            eng.wait_ge(self.sems[name], self.hwm[name])
            self.issuer_waited[name] = self.hwm[name]
        self.inc(fn(), name, 16)

    def bar(self):
        for eng in (self.T, self.V, self.S, self.G):
            self.inc(eng.drain(), "b", 1)
        self.inc(self.Y.nop(), "b", 1)
        tgt = self.cnt["b"]
        for eng in (self.T, self.V, self.S, self.G, self.Y):
            eng.wait_ge(self.sems["b"], tgt)



def hbs(sb, c, c0, c1):
    return sb["hb"][:, c * CPAD + c0: c * CPAD + c1]


def fts(sb, c, c0, c1):
    return sb["ftile"][:, c * CPAD + c0: c * CPAD + c1]


def bau(sb, unit, c0, c1):
    return sb["ba"][:, unit * CPAD + c0: unit * CPAD + c1]



def build_nc():
    nc = bass.Bass("TRN2", target_bir_lowering=False)
    dr = {}

    def din(name, shape):
        dr[name] = nc.dram_tensor(name, list(shape), F32, kind="ExternalInput")

    din("xt", [D, N])
    din("l0a_wq", [D, D]); din("l0a_wkv", [D, 2 * D]); din("l0a_wo", [D, D])
    din("relemb", [32, 16]); din("m436t", [32, 436]); din("ediff", [32, 1])
    din("l0f_wqkv", [2, D, 1536]); din("l0f_wo", [2, 512, D])
    din("w1pb", [2, NBLK, D, 256]); din("w2pb", [2, NBLK, 128, D]); din("b1p", [88, 128])
    din("l1a_wqT", [D, D]); din("l1a_wkT", [D, D]); din("l1a_wvT", [D, D]); din("l1a_woT", [D, D])
    din("l1a_pT", [16, 64, 64]); din("l1f_wT", [D, D])
    din("vecs", [12, D]); din("v2", [4, 128]); din("aux7", [7, 128, 128])
    din("rotc", [128, N]); din("rots", [128, N]); din("rotsc", [128, N])
    din("aux_f32", [2, 128, 128]); din("jcol", [128, 1])
    dr["yt"] = nc.dram_tensor("yt", [D, N], F32, kind="ExternalOutput")
    dr["xm"] = nc.dram_tensor("xm", [D, N], F32)
    dr["gbuf"] = nc.dram_tensor("gbuf", [16, 576], F32)

    stack = ExitStack()
    sb = {}

    def sbuf(name, shape, dt):
        sb[name] = stack.enter_context(nc.sbuf_tensor(name, list(shape), dt))

    sbuf("hb", [128, NCHUNK * CPAD], BF16)
    sbuf("tbl", [128, 4 * N], BF16)
    sbuf("ftile", [128, NCHUNK * CPAD], F32)
    sbuf("xcb", [128, 2 * N], F32)
    sbuf("stat", [128, 2 * N], F32)
    sbuf("sc2", [128, 1024], F32)
    sbuf("ba", [128, BAU * CPAD], BF16)
    sbuf("vt", [128, 96], F32); sbuf("v2t", [128, 4], F32); sbuf("b1t", [128, 88], F32)
    sbuf("dct", [128, 16], F32); sbuf("sc1", [128, 64], F32)
    for nme in ["id_bf", "ones_bf", "o64_bf", "pit_bf", "tri1_bf", "tri2_bf", "j_bf"]:
        sbuf(nme, [128, 128], BF16)
    sbuf("id_f", [128, 128], F32); sbuf("ones_f", [128, 128], F32)
    sbuf("rel_sb", [32, 16], F32); sbuf("ediff_sb", [32, 1], F32)
    sbuf("jcol_sb", [128, 1], F32)
    sbuf("tmp16", [128, 16], F32)
    sbuf("pk_bf", [128, 1024], BF16)
    sbuf("stg", [128, 128], F32)

    ps = {}
    ps["A"] = stack.enter_context(nc.psum_tensor("psA", [128, 2048], F32))
    ps["B"] = stack.enter_context(nc.psum_tensor("psB", [128, 1024], F32))
    ps["T"] = stack.enter_context(nc.psum_tensor("psT", [128, 2048], BF16))

    p = P(nc, stack)
    emit_program(p, dr, sb, ps)
    stack.close()
    return nc


def emit_program(p, dr, sb, ps):
    emit_prologue(p, dr, sb, ps)
    on = lambda ph: ph in PHASES
    emit_boundary(p, dr, sb, ps, src="xt", dst="xm", delta=None, scale=0.0,
                  want_f=on("l0ffn_a"))
    if on("l0ffn_a"):
        emit_l0ffn(p, dr, sb, ps)
    emit_boundary(p, dr, sb, ps, src="xm", dst="xm",
                  delta=on("l0ffn_a"), scale=0.5, want_f=False)
    if on("l0att"):
        emit_l0att(p, dr, sb, ps)
    emit_boundary(p, dr, sb, ps, src="xm", dst="xm",
                  delta=on("l0att"), scale=1.0, want_f=on("l0ffn_c"))
    if on("l0ffn_c"):
        emit_l0ffn(p, dr, sb, ps)
    emit_boundary(p, dr, sb, ps, src="xm", dst="xm",
                  delta=on("l0ffn_c"), scale=0.5, want_f=False)
    if on("l1ffn_d") or on("l1ffn_f"):
        emit_l1quant(p, dr, sb, ps)
    if on("l1ffn_d"):
        emit_l1ffn(p, dr, sb, ps)
    emit_boundary(p, dr, sb, ps, src="xm", dst="xm",
                  delta=on("l1ffn_d"), scale=0.5, want_f=False)
    if on("l1att"):
        emit_l1att(p, dr, sb, ps)
    emit_boundary(p, dr, sb, ps, src="xm", dst="xm",
                  delta=on("l1att"), scale=1.0, want_f=False)
    if on("l1ffn_f"):
        emit_l1ffn(p, dr, sb, ps)
    emit_boundary(p, dr, sb, ps, src="xm", dst="yt",
                  delta=on("l1ffn_f"), scale=0.5, want_f=False, last=True)



def emit_prologue(p, dr, sb, ps):
    T, V, S, G, Y = p.T, p.V, p.S, p.G, p.Y

    for name, src in [("rel_sb", dr["relemb"]),
                      ("ediff_sb", dr["ediff"]), ("jcol_sb", dr["jcol"])]:
        p.dma(Y, "dx", lambda: Y.dma_start(out=sb[name][:, :], in_=src[:, :]))
    m436_sb = sb["stat"][0:32, 1536:1972]
    p.dma(Y, "dx", lambda: Y.dma_start(out=m436_sb, in_=dr["m436t"][:, :]))
    vload = sb["stat"][0:12, 0:1024]
    v2load = sb["stat"][0:4, 1024:1152]
    b1load = [sb["stat"][0:44, 1152 + 128 * l:1280 + 128 * l] for l in range(2)]
    p.dma(Y, "dx", lambda: Y.dma_start(out=vload, in_=dr["vecs"][:, :]))
    p.dma(Y, "dx", lambda: Y.dma_start(out=v2load, in_=dr["v2"][:, :]))
    for l in range(2):
        p.dma(Y, "dx", lambda: Y.dma_start(out=b1load[l], in_=dr["b1p"][44 * l:44 * l + 44, :]))
    p.dma(Y, "dx", lambda: Y.dma_start(out=sb["id_f"][:, :], in_=dr["aux_f32"][0, :, :]))
    p.dma(Y, "dx", lambda: Y.dma_start(out=sb["ones_f"][:, :], in_=dr["aux_f32"][1, :, :]))
    for i, name in enumerate(["id_bf", "ones_bf", "o64_bf", "pit_bf", "tri1_bf", "tri2_bf", "j_bf"]):
        p.dma(G, "dw", lambda: G.dma_start(out=sb[name][:, :], in_=dr["aux7"][i, :, :]))

    p.wait(T, "dx"); p.wait(T, "dw")

    def tr_small(dst_ap, src_ap):
        np_, nf = src_ap.shape[0], src_ap.shape[1]
        p.inc(T.transpose(ps["A"][0:nf, 0:np_], src_ap, sb["id_f"][0:np_, 0:np_]), "t")
        p.wait(V, "t")
        p.inc(V.tensor_copy(dst_ap, ps["A"][0:nf, 0:np_]), "v")
        p.wait(T, "v")

    for c in range(NCHUNK):
        tr_small(sb["vt"][:, 12 * c:12 * c + 12], vload[:, 128 * c:128 * c + 128])
    tr_small(sb["v2t"][:, :], v2load[:, :])
    for l in range(2):
        tr_small(sb["b1t"][:, 44 * l:44 * l + 44], b1load[l])

    for ch, ulen in enumerate([128, 128, 128, 52]):
        u0 = 128 * ch
        p.inc(T.matmul(ps["B"][0:ulen, 0:16], m436_sb[:, u0:u0 + ulen], sb["rel_sb"][:, :],
                       start=True, stop=True), "t")
        p.wait(V, "t")
        p.inc(V.tensor_copy(sb["tmp16"][0:ulen, 0:16], ps["B"][0:ulen, 0:16]), "v")
        p.wait(T, "v")
        p.inc(T.transpose(ps["B"][0:16, 128:128 + ulen], sb["tmp16"][0:ulen, 0:16],
                          sb["id_f"][0:ulen, 0:ulen]), "t")
        p.wait(V, "t"); p.wait(V, "dx")
        p.inc(V.tensor_copy(sb["stg"][0:16, 0:ulen], ps["B"][0:16, 128:128 + ulen]), "v")
        p.wait(Y, "v")
        p.dma(Y, "dx", lambda: Y.dma_start(out=dr["gbuf"][:, u0:u0 + ulen], in_=sb["stg"][0:16, 0:ulen]))
        p.wait(T, "v")

    p.inc(T.matmul(ps["B"][0:1, 0:16], sb["ediff_sb"][:, :], sb["rel_sb"][:, :],
                   start=True, stop=True), "t")
    p.wait(V, "t")
    p.inc(V.tensor_copy(sb["tmp16"][0:1, 0:16], ps["B"][0:1, 0:16]), "v")
    p.wait(T, "v")
    p.inc(T.matmul(ps["B"][:, 16:32], sb["ones_f"][0:1, :], sb["tmp16"][0:1, 0:16],
                   start=True, stop=True), "t")
    p.wait(V, "t")
    p.inc(V.tensor_copy(sb["dct"][:, :], ps["B"][:, 16:32]), "v")

    emit_rotary_tables(p, dr, sb, ps)
    p.wait(V, "v")
    for c in range(NCHUNK):
        V.memset(sb["hb"][:, c * CPAD:c * CPAD + 1], 0.0)
        last = V.memset(sb["ftile"][:, c * CPAD:c * CPAD + 1], 0.0)
    p.inc(last, "v")
    p.bar()


def emit_rotary_tables(p, dr, sb, ps):
    T, V, S, G, Y = p.T, p.V, p.S, p.G, p.Y
    cosb = sb["xcb"][:, 0:N]
    sinb = sb["xcb"][:, N:2 * N]
    scb = sb["stat"][:, 0:N]
    p.wait(Y, "v")
    p.dma(Y, "dx", lambda: Y.dma_start(out=cosb, in_=dr["rotc"][:, :]))
    p.dma(Y, "dx", lambda: Y.dma_start(out=sinb, in_=dr["rots"][:, :]))
    p.dma(Y, "dx", lambda: Y.dma_start(out=scb, in_=dr["rotsc"][:, :]))
    p.wait(V, "dx")
    tbl = sb["tbl"]
    p.seq(V, V.tensor_tensor(sb["sc2"][:, 0:1024], cosb[:, 0:1024], scb[:, 0:1024], ALU.mult), "v")
    p.seq(V, V.tensor_scalar(tbl[:, 0:1024], sb["sc2"][:, 0:1024], 8.0, None, ALU.mult, ALU.bypass), "v")
    p.seq(V, V.tensor_tensor(sb["sc2"][:, 0:1024], cosb[:, 1024:2048], scb[:, 1024:2048], ALU.mult), "v")
    p.seq(V, V.tensor_scalar(tbl[:, 1024:2048], sb["sc2"][:, 0:1024], 8.0, None, ALU.mult, ALU.bypass), "v")
    p.seq(V, V.tensor_tensor(sb["sc2"][:, 0:1024], sinb[:, 0:1024], scb[:, 0:1024], ALU.mult), "v")
    p.seq(V, V.tensor_scalar(tbl[:, N:N + 1024], sb["sc2"][:, 0:1024], 8.0, None, ALU.mult, ALU.bypass), "v")
    p.seq(V, V.tensor_tensor(sb["sc2"][:, 0:1024], sinb[:, 1024:2048], scb[:, 1024:2048], ALU.mult), "v")
    p.seq(V, V.tensor_scalar(tbl[:, N + 1024:2 * N], sb["sc2"][:, 0:1024], 8.0, None, ALU.mult, ALU.bypass), "v")
    p.seq(V, V.reciprocal(sb["sc2"][:, 0:1024], scb[:, 0:1024]), "v")
    p.seq(V, V.tensor_tensor(tbl[:, 2 * N:2 * N + 1024], cosb[:, 0:1024],
                             sb["sc2"][:, 0:1024], ALU.mult), "v")
    p.seq(V, V.tensor_tensor(tbl[:, 3 * N:3 * N + 1024], sinb[:, 0:1024],
                             sb["sc2"][:, 0:1024], ALU.mult), "v")
    p.seq(V, V.reciprocal(sb["sc2"][:, 0:1024], scb[:, 1024:2048]), "v")
    p.seq(V, V.tensor_tensor(tbl[:, 2 * N + 1024:3 * N], cosb[:, 1024:2048],
                             sb["sc2"][:, 0:1024], ALU.mult), "v")
    p.inc(V.tensor_tensor(tbl[:, 3 * N + 1024:4 * N], sinb[:, 1024:2048],
                          sb["sc2"][:, 0:1024], ALU.mult), "v")



def emit_boundary(p, dr, sb, ps, src, dst, delta, scale, want_f, last=False):
    T, V, S, G, Y = p.T, p.V, p.S, p.G, p.Y
    xsrc, xdst = dr[src], dr[dst]

    def xbc(c, c0, c1):
        return bau(sb, c, c0, c1)

    v_ck = [p.cnt["v"]] * 4
    for c in range(NCHUNK):
        par = c % 4
        bt = sb["xcb"] if par < 2 else sb["stat"]
        buf = bt[:, (par % 2) * N:(par % 2) * N + N]
        p.wait(Y, "v", upto=v_ck[par])
        if c >= 4:
            p.wait(Y, "dx")
        p.dma(Y, "dx", lambda: Y.dma_start(out=buf, in_=xsrc[128 * c:128 * c + 128, :]))
        ld_dx = p.cnt["dx"]
        p.wait(V, "dx", upto=ld_dx)
        lastv = None
        if delta:
            lastv = V.scalar_tensor_tensor(buf, fts(sb, c, 1, 1 + N), scale, buf, ALU.mult, ALU.add)
            if not last:
                p.seq(V, lastv, "v")
        if not last:
            lastv = V.tensor_copy(xbc(c, 0, N), buf)
        if lastv is not None:
            p.inc(lastv, "v")
        v_ck[par] = p.cnt["v"]
        p.wait(Y, "v", upto=v_ck[par])
        if lastv is None:
            p.wait(Y, "dx")
        p.dma(Y, "dx", lambda: Y.dma_start(out=xdst[128 * c:128 * c + 128, :], in_=buf))
    if last:
        p.wait(Y, "dx")
        p.bar()
        return

    p.wait(S, "v"); p.wait(V, "dx")
    for tc in range(4):
        c0 = 512 * tc
        if tc > 0:
            p.wait(T, "v")
        for c in range(NCHUNK):
            par = c % 2
            sq = bau(sb, 8, 512 * par, 512 * par + 512)
            if tc > 0 or c > 1:
                p.wait(S, "t", lag=1)
            p.inc(S.activation(out=sq, in_=xbc(c, c0, c0 + 512), func=AF.Square), "s")
            p.wait(T, "s")
            T.matmul(ps["B"][:, 0:512], sb["ones_bf"][:, :], xbc(c, c0, c0 + 512),
                     start=(c == 0), stop=(c == NCHUNK - 1))
            p.inc(T.matmul(ps["B"][:, 512:1024], sb["ones_bf"][:, :], sq,
                           start=(c == 0), stop=(c == NCHUNK - 1)), "t")
        p.wait(V, "t")
        mean = sb["stat"][:, c0:c0 + 512]
        rstd = sb["stat"][:, N + c0:N + c0 + 512]
        p.seq(V, V.tensor_scalar(mean, ps["B"][:, 0:512], 1.0 / 1024.0, None,
                                 ALU.mult, ALU.bypass), "v")
        p.seq(V, V.tensor_scalar(rstd, ps["B"][:, 512:1024], 1.0 / 1024.0, None,
                                 ALU.mult, ALU.bypass), "v")
        p.seq(V, V.tensor_tensor(sb["sc2"][:, 0:512], mean, mean, ALU.mult), "v")
        p.seq(V, V.tensor_tensor(rstd, rstd, sb["sc2"][:, 0:512], ALU.subtract), "v")
        p.inc(V.tensor_scalar(rstd, rstd, 1e-5, None, ALU.add, ALU.bypass), "v")
        p.wait(S, "v")
        p.seq(S, S.activation(out=rstd, in_=rstd, func=AF.Ln), "s")
        p.inc(S.activation(out=rstd, in_=rstd, func=AF.Exp, scale=-0.5), "s")
    p.wait(V, "s")
    for c in range(NCHUNK):
        for tc in range(4):
            c0 = 512 * tc
            mean = sb["stat"][:, c0:c0 + 512]
            rstd = sb["stat"][:, N + c0:N + c0 + 512]
            out = fts(sb, c, 1 + c0, 1 + c0 + 512) if want_f else hbs(sb, c, 1 + c0, 1 + c0 + 512)
            p.seq(V, V.tensor_tensor(sb["sc2"][:, 512:1024], xbc(c, c0, c0 + 512),
                                     mean, ALU.subtract), "v")
            p.seq(V, V.tensor_tensor(out, sb["sc2"][:, 512:1024], rstd, ALU.mult), "v")
    p.inc(V.nop(), "b", 0) if False else None
    if want_f:
        emit_shiftcast(p, sb)
    p.bar()


def emit_shiftcast(p, sb):
    V = p.V
    p.wait(V, "v")
    for c in range(NCHUNK):
        if c < 4:
            lastv = V.tensor_copy(hbs(sb, c, 1, 1 + N), fts(sb, c, 1, 1 + N))
        else:
            lastv = V.tensor_copy(hbs(sb, c, 1, 1 + N), fts(sb, c, 0, N))
    p.inc(lastv, "v")



WQB_UNIT = 9


def wqbs(sb, ic, oc):
    base = WQB_UNIT * CPAD + 1024 * ic + 128 * oc
    return sb["ba"][:, base:base + 128]


def emit_l1quant(p, dr, sb, ps):
    T, V, S, G, Y = p.T, p.V, p.S, p.G, p.Y
    for c in range(NCHUNK):
        buf = sb["xcb"][:, (c % 2) * 1024:(c % 2) * 1024 + 1024]
        p.wait(Y, "v")
        p.dma(Y, "dx", lambda: Y.dma_start(out=buf, in_=dr["l1f_wT"][128 * c:128 * c + 128, :]))
        p.wait(V, "dx")
        p.inc(V.tensor_reduce(sb["sc1"][:, c:c + 1], buf, axis=AX.X, op=ALU.add,
                              apply_absolute_value=True), "v")
    p.wait(V, "v")
    p.inc(V.tensor_reduce(sb["sc1"][:, 8:9], sb["sc1"][:, 0:8], axis=AX.X, op=ALU.add), "v")
    p.wait(T, "v")
    p.inc(T.matmul(ps["B"][:, 0:1], sb["ones_f"][:, :], sb["sc1"][:, 8:9],
                   start=True, stop=True), "t")
    p.wait(V, "t")
    p.seq(V, V.tensor_scalar(sb["sc1"][:, 9:10], ps["B"][:, 0:1], 1.0 / (2.0 ** 20), 1e-5,
                             ALU.mult, ALU.max), "v")
    p.seq(V, V.tensor_scalar(sb["sc1"][:, 10:11], sb["sc1"][:, 9:10], 0.5, None,
                             ALU.mult, ALU.bypass), "v")
    p.inc(V.tensor_scalar(sb["sc1"][:, 11:12], sb["sc1"][:, 10:11], -1.0, None,
                          ALU.mult, ALU.bypass), "v")
    for c in range(NCHUNK):
        buf = sb["xcb"][:, (c % 2) * 1024:(c % 2) * 1024 + 1024]
        p.wait(Y, "v")
        p.dma(Y, "dx", lambda: Y.dma_start(out=buf, in_=dr["l1f_wT"][128 * c:128 * c + 128, :]))
        p.wait(V, "dx")
        ge = sb["sc2"][:, 0:1024]
        p.seq(V, V.tensor_scalar(ge, buf, sb["sc1"][:, 10:11], None, ALU.is_ge, ALU.bypass), "v")
        p.seq(V, V.tensor_scalar(buf, buf, sb["sc1"][:, 11:12], None, ALU.is_le, ALU.bypass), "v")
        p.seq(V, V.tensor_tensor(ge, ge, buf, ALU.subtract), "v")
        dst = sb["ba"][:, WQB_UNIT * CPAD + 1024 * c: WQB_UNIT * CPAD + 1024 * c + 1024]
        p.inc(V.tensor_scalar(dst, ge, sb["sc1"][:, 9:10], None, ALU.mult, ALU.bypass), "v")
    p.bar()


def emit_l1ffn(p, dr, sb, ps):
    T, V, S, G, Y = p.T, p.V, p.S, p.G, p.Y
    ev_ck = [p.cnt["v"]] * 4
    for oc in range(8):
        for t in range(4):
            p.wait(T, "v", upto=ev_ck[t])
            for ic in range(NCHUNK):
                mm = T.matmul(ps["A"][:, 512 * t:512 * t + 512], wqbs(sb, ic, oc),
                              hbs(sb, ic, 1 + 512 * t, 1 + 512 * t + 512),
                              start=(ic == 0), stop=(ic == NCHUNK - 1))
            p.inc(mm, "t")
            p.wait(V, "t")
            p.inc(V.tensor_copy(fts(sb, oc, 1 + 512 * t, 1 + 512 * t + 512),
                                ps["A"][:, 512 * t:512 * t + 512]), "v")
            ev_ck[t] = p.cnt["v"]
    p.bar()



def emit_l1att(p, dr, sb, ps):
    T, V, S, G, Y = p.T, p.V, p.S, p.G, p.Y

    for h in range(16):
        p.dma(G, "dw", lambda: G.dma_start(out=sb["pk_bf"][64 * (h % 2):64 * (h % 2) + 64,
                                          64 * (h // 2):64 * (h // 2) + 64],
                          in_=dr["l1a_pT"][h, :, :]))
    brow = bau(sb, 3, 0, 1024)
    p.dma(G, "dw", lambda: G.dma_start(out=brow[0:1, :], in_=dr["vecs"][3:4, :]))

    def wslice(t, col0, ncol):
        return bass.AP(t, col0, [[1024, 128], [131072, 8], [1, ncol]])
    woT = dr["l1a_woT"].ap().rearrange("(a b) c -> a b c", b=128)

    ptb = lambda h: sb["pk_bf"][64 * (h % 2):64 * (h % 2) + 64,
                                64 * (h // 2):64 * (h // 2) + 64]
    kvb = lambda h: sb["pk_bf"][64 * (h % 2):64 * (h % 2) + 64,
                                512 + 64 * (h // 2):512 + 64 * (h // 2) + 64]

    for hg in range(4):
        wvu = 4 + hg % 2
        p.wait(G, "t")
        p.dma(G, "dw", lambda: G.dma_start(out=bau(sb, wvu, 0, 2048),
                          in_=wslice(dr["l1a_wvT"], 256 * hg, 256)))
        p.wait(T, "dw")
        for tt in range(16):
            for ic in range(NCHUNK):
                T.matmul(ps["A"][:, 256 * (tt % 4):256 * (tt % 4) + 256],
                         hbs(sb, ic, 1 + 128 * tt, 1 + 128 * tt + 128),
                         bau(sb, wvu, 256 * ic, 256 * ic + 256),
                         start=(ic == 0), stop=False)
            p.inc(T.matmul(ps["A"][:, 256 * (tt % 4):256 * (tt % 4) + 256],
                           sb["ones_bf"][0:1, :], brow[0:1, 256 * hg:256 * hg + 256],
                           start=False, stop=True), "t")
            if tt % 4 == 3:
                p.wait(V, "t")
                for q in range(4):
                    tt2 = tt - 3 + q
                    lastv = V.tensor_copy(
                        bau(sb, 1 + tt2 // 8, 256 * (tt2 % 8), 256 * (tt2 % 8) + 256),
                        ps["A"][:, 256 * q:256 * q + 256])
                p.inc(lastv, "v")
                p.wait(T, "v")
        for hh in range(4):
            h = 4 * hg + hh
            b0 = 64 * (h % 2)
            wku = 1024 * (h % 2)
            p.wait(G, "t")
            p.dma(G, "dw", lambda: G.dma_start(out=bau(sb, 6, wku, wku + 512),
                              in_=wslice(dr["l1a_wkT"], 64 * h, 64)))
            p.wait(T, "dw")
            for t4 in range(4):
                mm = None
                for ic in range(NCHUNK):
                    mm = T.matmul(ps["A"][b0:b0 + 64, 512 * t4:512 * t4 + 512],
                                  bau(sb, 6, wku + 64 * ic, wku + 64 * ic + 64),
                                  hbs(sb, ic, 1 + 512 * t4, 1 + 512 * t4 + 512),
                                  start=(ic == 0), stop=(ic == NCHUNK - 1))
                p.inc(mm, "t")
            p.wait(S, "t")
            p.inc(S.activation(out=bau(sb, 8, 0, N)[b0:b0 + 64, :], in_=ps["A"][b0:b0 + 64, 0:N],
                               func=AF.Identity,
                               bias=sb["vt"][b0:b0 + 64, 12 * (h // 2) + 2:12 * (h // 2) + 3]), "s")
            p.wait(T, "s")
            for tt in range(16):
                p.inc(T.matmul(ps["B"][:, 64 * (tt % 8):64 * (tt % 8) + 64],
                               bau(sb, 8, 128 * tt, 128 * tt + 128)[b0:b0 + 64, :], ptb(h),
                               start=True, stop=True), "t")
                if tt % 8 == 7:
                    p.wait(S, "t")
                    for q in range(8):
                        tt2 = tt - 7 + q
                        lasts = S.activation(
                            out=bau(sb, 0, 256 * (tt2 % 8) + 64 * (tt2 // 8),
                                    256 * (tt2 % 8) + 64 * (tt2 // 8) + 64),
                            in_=ps["B"][:, 64 * q:64 * q + 64], func=AF.Relu, scale=0.125)
                    p.inc(lasts, "s")
                    p.wait(T, "s")
            for tt in range(16):
                mm = T.matmul(ps["B"][b0:b0 + 64, 512:576],
                              bau(sb, 0, 256 * (tt % 8) + 64 * (tt // 8),
                                  256 * (tt % 8) + 64 * (tt // 8) + 64),
                              bau(sb, 1 + tt // 8, 256 * (tt % 8) + 64 * hh,
                                  256 * (tt % 8) + 64 * hh + 64),
                              start=(tt == 0), stop=(tt == 15))
            p.inc(mm, "t")
            p.wait(V, "t")
            p.inc(V.tensor_copy(kvb(h), ps["B"][b0:b0 + 64, 512:576]), "v")
            p.wait(T, "v")

    p.wait(G, "t")
    woff = 4 * CPAD
    for ic in range(NCHUNK):
        p.dma(G, "dw", lambda: G.dma_start(out=sb["ba"][:, woff + 1024 * ic:woff + 1024 * ic + 1024],
                          in_=woT[ic, :, :]))
    p.wait(T, "dw")

    def wo(cc, cols):
        return sb["ba"][:, woff + 1024 * cc + cols[0]:woff + 1024 * cc + cols[1]]

    for tc in range(4):
        c0 = 512 * tc
        st2 = {"oev": [p.cnt["v"]] * 2, "sq": [p.cnt["s"]] * 2, "sf": [p.cnt["s"]] * 2,
               "qmm": [p.cnt["t"]] * 2, "fmm": {}, "amm": {}, "qt": {}, "sqv": {}, "sfv": {}}

        def l1_q(h):
            pq = ps["B"] if h % 2 == 0 else ps["A"]
            b0 = 64 * (h % 2)
            wku = 1024 * (h % 2)
            p.wait(G, "t", upto=st2["qmm"][h % 2])
            p.dma(G, "dw", lambda: G.dma_start(out=bau(sb, 1, wku, wku + 512),
                              in_=wslice(dr["l1a_wqT"], 64 * h, 64)))
            p.wait(T, "dw")
            p.wait(T, "v", upto=st2["oev"][h % 2])
            p.wait(T, "s", upto=st2["sq"][h % 2])
            mm = None
            for ic in range(NCHUNK):
                mm = T.matmul(pq[b0:b0 + 64, 0:512],
                              bau(sb, 1, wku + 64 * ic, wku + 64 * ic + 64),
                              hbs(sb, ic, 1 + c0, 1 + c0 + 512),
                              start=(ic == 0), stop=(ic == NCHUNK - 1))
            p.inc(mm, "t")
            st2["qmm"][h % 2] = p.cnt["t"]
            st2["qt"][h] = p.cnt["t"]

        def l1_mid(h):
            pq = ps["B"] if h % 2 == 0 else ps["A"]
            b0 = 64 * (h % 2)
            p.wait(S, "t", upto=st2["qt"][h])
            p.inc(S.activation(out=bau(sb, 8, 0, 512)[b0:b0 + 64, :], in_=pq[b0:b0 + 64, 0:512],
                               func=AF.Identity,
                               bias=sb["vt"][b0:b0 + 64, 12 * (h // 2) + 1:12 * (h // 2) + 2]), "s")
            st2["sq"][h % 2] = p.cnt["s"]
            st2["sqv"][h] = p.cnt["s"]
            p.wait(T, "s", upto=st2["sqv"][h])
            p.inc(T.matmul(pq[b0:b0 + 64, 512:1024], ptb(h), bau(sb, 8, 0, 512)[b0:b0 + 64, :],
                           start=True, stop=True), "t")
            st2["fmm"][h] = p.cnt["t"]
            p.wait(S, "t", upto=st2["fmm"][h])
            p.inc(S.activation(out=bau(sb, 8, 512, 1024)[b0:b0 + 64, :],
                               in_=pq[b0:b0 + 64, 512:1024], func=AF.Relu, scale=0.125), "s")
            st2["sf"][h % 2] = p.cnt["s"]
            st2["sfv"][h] = p.cnt["s"]
            p.wait(T, "s", upto=st2["sfv"][h])
            p.inc(T.matmul(pq[b0:b0 + 64, 0:512], kvb(h), bau(sb, 8, 512, 1024)[b0:b0 + 64, :],
                           start=True, stop=True), "t")
            st2["amm"][h] = p.cnt["t"]

        def l1_oe(h):
            pq = ps["B"] if h % 2 == 0 else ps["A"]
            b0 = 64 * (h % 2)
            cc = h // 2
            p.wait(V, "t", upto=st2["amm"][h])
            p.inc(V.tensor_copy(bau(sb, 2 + cc // 4, 512 * (cc % 4), 512 * (cc % 4) + 512)
                                [b0:b0 + 64, :], pq[b0:b0 + 64, 0:512]), "v")
            st2["oev"][h % 2] = p.cnt["v"]

        l1_q(0)
        for h in range(16):
            if h + 1 < 16:
                l1_q(h + 1)
            l1_mid(h)
            l1_oe(h)
        p.wait(T, "v")
        for dm in range(8):
            mm = None
            for cc in range(8):
                mm = T.matmul(ps["A"][:, 1024 + 512 * (dm % 2):1024 + 512 * (dm % 2) + 512],
                              wo(cc, (128 * dm, 128 * dm + 128)),
                              bau(sb, 2 + cc // 4, 512 * (cc % 4), 512 * (cc % 4) + 512),
                              start=(cc == 0), stop=(cc == 7))
            p.inc(mm, "t")
            p.wait(V, "t")
            p.inc(V.tensor_scalar(fts(sb, dm, 1 + c0, 1 + c0 + 512),
                                  ps["A"][:, 1024 + 512 * (dm % 2):1024 + 512 * (dm % 2) + 512],
                                  sb["vt"][:, 12 * dm + 4:12 * dm + 5], None, ALU.add, ALU.bypass), "v")
            p.wait(T, "v")
    p.bar()



def emit_l0ffn(p, dr, sb, ps):
    T, V, S, G, Y = p.T, p.V, p.S, p.G, p.Y
    tbl = sb["tbl"]

    def wqkv_slice(l, col0, ncol):
        return bass.AP(dr["l0f_wqkv"], 1572864 * l + col0, [[1536, 128], [196608, 8], [1, ncol]])

    def wo_slice(l, col0, ncol):
        return bass.AP(dr["l0f_wo"], 524288 * l + col0, [[1024, 128], [131072, 4], [1, ncol]])

    def vts(kidx, h):
        return bau(sb, kidx // 4, 512 * (kidx % 4) + 64 * h, 512 * (kidx % 4) + 64 * h + 64)

    for l in range(L0FFN_DEPTH):
        p.wait(G, "t"); p.wait(G, "s")
        p.dma(G, "dw", lambda: G.dma_start(out=bau(sb, 11, 0, 2048),
                                           in_=wqkv_slice(l, 1024, 256)))
        p.dma(G, "dw", lambda: G.dma_start(out=bau(sb, 12, 0, 2048),
                                           in_=wqkv_slice(l, 1280, 256)))
        p.wait(T, "dw"); p.wait(T, "v")
        vtm_ck = [p.cnt["v"]] * 2
        for tt in range(16):
            mm = None
            p.wait(T, "v", upto=vtm_ck[tt % 2])
            for half in range(2):
                u = 11 + half
                for ic in range(NCHUNK):
                    mm = T.matmul(ps["A"][:, 1024 * (tt % 2) + 256 * half:
                                          1024 * (tt % 2) + 256 * half + 256],
                                  hbs(sb, ic, 1 + 128 * tt, 1 + 128 * tt + 128),
                                  bau(sb, u, 256 * ic, 256 * ic + 256),
                                  start=(ic == 0), stop=(ic == NCHUNK - 1))
            p.inc(mm, "t")
            p.wait(V, "t")
            lastv = V.tensor_copy(bau(sb, tt // 4, 512 * (tt % 4), 512 * (tt % 4) + 512),
                                  ps["A"][:, 1024 * (tt % 2):1024 * (tt % 2) + 512])
            p.inc(lastv, "v")
            vtm_ck[tt % 2] = p.cnt["v"]
        for hp in range(4):
            p.wait(G, "t")
            p.dma(G, "dw", lambda: G.dma_start(out=bau(sb, 11, 0, 1024),
                                               in_=wqkv_slice(l, 128 * hp, 128)))
            p.dma(G, "dw", lambda: G.dma_start(out=bau(sb, 11, 1024, 2048),
                                               in_=wqkv_slice(l, 512 + 128 * hp, 128)))
            p.wait(T, "dw"); p.wait(T, "s"); p.wait(T, "v")
            qn_v_ck = [p.cnt["v"]] * 8
            ones_v_ck = [p.cnt["v"]] * 2
            pi_v_ck = [p.cnt["v"]] * 2
            sq_t_ck = [p.cnt["t"]] * 2
            qn_t_ck = [p.cnt["t"]] * 2
            sc2_s_ck = [p.cnt["s"]] * 2
            it = 0
            for which, unit, ct0, st0, scol in ((0, 4, 0, N, 2 * l), (1, 5, 2 * N, 3 * N, 2 * l + 1)):
                for tc in range(4):
                    c0 = 512 * tc
                    par = it % 2
                    it += 1
                    p.wait(T, "v", upto=qn_v_ck[4 * (1 - which) + tc])
                    mm = None
                    for ic in range(NCHUNK):
                        mm = T.matmul(ps["A"][:, c0:c0 + 512],
                                      bau(sb, 11, 1024 * which + 128 * ic,
                                          1024 * which + 128 * ic + 128),
                                      hbs(sb, ic, 1 + c0, 1 + c0 + 512),
                                      start=(ic == 0), stop=(ic == NCHUNK - 1))
                    p.inc(mm, "t")
                    p.wait(S, "t")
                    p.wait(S, "t", upto=sq_t_ck[par])
                    p.inc(S.activation(out=bau(sb, 12, 512 * par, 512 * par + 512),
                                       in_=ps["A"][:, c0:c0 + 512], func=AF.Square), "s")
                    p.wait(T, "s")
                    p.wait(T, "v", upto=ones_v_ck[par])
                    p.inc(T.matmul(ps["B"][:, 512 * par:512 * par + 512], sb["o64_bf"][:, :],
                                   bau(sb, 12, 512 * par, 512 * par + 512),
                                   start=True, stop=True), "t")
                    sq_t_ck[par] = p.cnt["t"]
                    p.wait(V, "t")
                    p.wait(V, "s", upto=sc2_s_ck[par])
                    p.seq(V, V.tensor_scalar(sb["sc2"][:, 512 * par:512 * par + 512],
                                             ps["B"][:, 512 * par:512 * par + 512],
                                             1e-12 ** 2, None, ALU.max, ALU.bypass), "v")
                    ones_v_ck[par] = p.cnt["v"]
                    p.wait(S, "v")
                    p.seq(S, S.activation(out=sb["sc2"][:, 512 * par:512 * par + 512],
                                          in_=sb["sc2"][:, 512 * par:512 * par + 512],
                                          func=AF.Ln), "s")
                    p.inc(S.activation(out=sb["sc2"][:, 512 * par:512 * par + 512],
                                       in_=sb["sc2"][:, 512 * par:512 * par + 512],
                                       func=AF.Exp, scale=-0.5), "s")
                    sc2_s_ck[par] = p.cnt["s"]
                    p.wait(V, "s")
                    p.seq(V, V.tensor_scalar(sb["sc2"][:, 512 * par:512 * par + 512],
                                             sb["sc2"][:, 512 * par:512 * par + 512],
                                             sb["v2t"][:, scol:scol + 1], None,
                                             ALU.mult, ALU.bypass), "v")
                    p.seq(V, V.tensor_tensor(bau(sb, 12, 1024 + 512 * par, 1536 + 512 * par),
                                             ps["A"][:, c0:c0 + 512],
                                             sb["sc2"][:, 512 * par:512 * par + 512],
                                             ALU.mult), "v")
                    v_qn = p.cnt["v"]
                    p.wait(T, "v", upto=v_qn)
                    p.inc(T.matmul(ps["A"][:, c0:c0 + 512], sb["pit_bf"][:, :],
                                   bau(sb, 12, 1024 + 512 * par, 1536 + 512 * par),
                                   start=True, stop=True), "t")
                    qn_t_ck[par] = p.cnt["t"]
                    p.wait(V, "t", upto=qn_t_ck[par])
                    p.seq(V, V.tensor_tensor(bau(sb, 12, 1024 + 512 * par, 1536 + 512 * par),
                                             bau(sb, 12, 1024 + 512 * par, 1536 + 512 * par),
                                             tbl[:, ct0 + c0:ct0 + c0 + 512], ALU.mult), "v")
                    p.seq(V, V.tensor_tensor(sb["sc2"][:, 512 * par:512 * par + 512],
                                             ps["A"][:, c0:c0 + 512],
                                             tbl[:, st0 + c0:st0 + c0 + 512], ALU.mult), "v")
                    p.inc(V.tensor_tensor(bau(sb, unit, c0, c0 + 512),
                                          bau(sb, 12, 1024 + 512 * par, 1536 + 512 * par),
                                          sb["sc2"][:, 512 * par:512 * par + 512], ALU.add), "v")
                    qn_v_ck[4 * which + tc] = p.cnt["v"]
            ck = p.__dict__.setdefault("_l0ffn_ck", {
                "exp_s": [0, 0], "tr_t": [0, 0], "pv_t": [0, 0], "norm_g": [0, 0],
                "oe_v": [0, 0]})
            for hh2 in range(2):
                h = 2 * hp + hh2
                b0 = 64 * (h % 2)

                def tile_geom(qt):
                    w, t = qt // 4, qt % 4
                    q0 = 512 * w + 128 * t
                    if w == 0:
                        return q0, 0, 128 * (t + 1), w, t
                    return q0, 512 * (w - 1) + 128 * t, 640, w, t

                def stage_a(qt):
                    q0, k0, nk, w, t = tile_geom(qt)
                    par = qt % 2
                    pbase = 1024 * par
                    p.wait(T, "s", upto=ck["exp_s"][par])
                    mm = None
                    if w == 0:
                        mm = T.matmul(ps["A"][:, pbase:pbase + nk],
                                      bau(sb, 4, q0, q0 + 128)[b0:b0 + 64, :],
                                      bau(sb, 5, k0, k0 + nk)[b0:b0 + 64, :],
                                      start=True, stop=False)
                        mm = T.matmul(ps["A"][:, pbase + 128 * t:pbase + 128 * t + 128],
                                      sb["id_bf"][:, :], sb["tri2_bf"][:, :],
                                      start=False, stop=True)
                    else:
                        T.matmul(ps["A"][:, pbase:pbase + 512],
                                 bau(sb, 4, q0, q0 + 128)[b0:b0 + 64, :],
                                 bau(sb, 5, k0, k0 + 512)[b0:b0 + 64, :],
                                 start=True, stop=False)
                        T.matmul(ps["A"][:, pbase + 512:pbase + 640],
                                 bau(sb, 4, q0, q0 + 128)[b0:b0 + 64, :],
                                 bau(sb, 5, k0 + 512, k0 + 640)[b0:b0 + 64, :],
                                 start=True, stop=False)
                        T.matmul(ps["A"][:, pbase:pbase + 128],
                                 sb["id_bf"][:, :], sb["tri1_bf"][:, :],
                                 start=False, stop=True)
                        mm = T.matmul(ps["A"][:, pbase + 512:pbase + 640],
                                      sb["id_bf"][:, :], sb["tri2_bf"][:, :],
                                      start=False, stop=True)
                    p.inc(mm, "t")
                    return p.cnt["t"]

                t_a = {}
                v_norm = {}
                t_a[0] = stage_a(0)
                for qt in range(16):
                    par = qt % 2
                    q0, k0, nk, w, t = tile_geom(qt)
                    if qt + 1 < 16:
                        t_a[qt + 1] = stage_a(qt + 1)
                    nm = sb["sc1"][:, 8 + 4 * par:9 + 4 * par]
                    rs = sb["sc1"][:, 9 + 4 * par:10 + 4 * par]
                    rc = sb["sc1"][:, 10 + 4 * par:11 + 4 * par]
                    pu = 6
                    pcol = 1024 * par
                    p.wait(V, "t", upto=t_a[qt])
                    p.wait(V, "s", upto=ck["exp_s"][par])
                    p.inc(V.reduce_max(nm, ps["A"][:, pcol:pcol + nk], axis=AX.X,
                                       negate=True), "v")
                    p.wait(S, "v")
                    p.wait(S, "t", upto=ck["tr_t"][par])
                    p.wait(S, "v", upto=ck["norm_g"][par])
                    p.inc(S.activation(out=bau(sb, pu, 640 * par, 640 * par + nk),
                                       in_=ps["A"][:, pcol:pcol + nk],
                                       func=AF.Exp, bias=nm, accum_out=rs), "s")
                    ck["exp_s"][par] = p.cnt["s"]
                    p.wait(V, "s")
                    p.seq(V, V.reciprocal(rc, rs), "v")
                    p.inc(V.tensor_scalar(bau(sb, pu, 640 * par, 640 * par + nk),
                                          bau(sb, pu, 640 * par, 640 * par + nk),
                                          rc, None, ALU.mult, ALU.bypass), "v")
                    v_norm_c = p.cnt["v"]
                    ck["norm_g"][par] = v_norm_c
                    nkb = nk // 128
                    p.wait(T, "v", upto=v_norm_c)
                    mm = None
                    for kb in range(nkb):
                        mm = T.transpose(ps["T"][:, 1024 * par + 128 * kb:
                                                 1024 * par + 128 * kb + 128],
                                         bau(sb, pu, 640 * par + 128 * kb,
                                             640 * par + 128 * kb + 128),
                                         sb["id_bf"][:, :])
                    p.inc(mm, "t")
                    ck["tr_t"][par] = p.cnt["t"]
                    t_tr = p.cnt["t"]
                    p.wait(S, "t", upto=t_tr)
                    p.inc(S.activation(out=bau(sb, 12, 640 * par, 640 * par + nk),
                                       in_=ps["T"][:, 1024 * par:1024 * par + nk],
                                       func=AF.Copy), "s")
                    pte = p.cnt["s"]
                    p.wait(T, "s", upto=pte)
                    p.wait(T, "s", upto=ck["oe_v"][par])
                    mm = None
                    for kb in range(nkb):
                        kidx = kb if w == 0 else 4 * (w - 1) + t + kb
                        mm = T.matmul(ps["B"][b0:b0 + 64, 512 * par:512 * par + 128],
                                      vts(kidx, h),
                                      bau(sb, 12, 640 * par + 128 * kb,
                                          640 * par + 128 * kb + 128),
                                      start=(kb == 0), stop=(kb == nkb - 1))
                    p.inc(mm, "t")
                    ck["pv_t"][par] = p.cnt["t"]
                    p.wait(S, "t", upto=p.cnt["t"])
                    p.inc(S.activation(out=bau(sb, 7 + h // 2, q0, q0 + 128)[b0:b0 + 64, :],
                                       in_=ps["B"][b0:b0 + 64, 512 * par:512 * par + 128],
                                       func=AF.Copy), "s")
                    ck["oe_v"][par] = p.cnt["s"]
        for half in range(2):
            p.wait(G, "t")
            p.dma(G, "dw", lambda: G.dma_start(out=bau(sb, 11, 0, 2048),
                                               in_=wo_slice(l, 512 * half, 512)))
            p.wait(T, "dw"); p.wait(T, "v"); p.wait(T, "s")
            for dm in range(4):
                dmg = 4 * half + dm
                for tc in range(4):
                    mm = None
                    for cc in range(4):
                        mm = T.matmul(ps["A"][:, 512 * tc:512 * tc + 512],
                                      bau(sb, 11, 512 * cc + 128 * dm, 512 * cc + 128 * dm + 128),
                                      bau(sb, 7 + cc, 512 * tc, 512 * tc + 512),
                                      start=(cc == 0), stop=(cc == 3))
                    p.inc(mm, "t")
                p.wait(V, "t")
                for tc in range(4):
                    c0 = 512 * tc
                    lastv = V.tensor_tensor(fts(sb, dmg, 1 + c0, 1 + c0 + 512),
                                            fts(sb, dmg, 1 + c0, 1 + c0 + 512),
                                            ps["A"][:, c0:c0 + 512], ALU.add)
                p.inc(lastv, "v")
                p.wait(T, "v")
        p._oproj_t_ck = p.cnt["t"]
        if L0FFN_MLP:
            emit_shiftcast(p, sb)
            emit_inner_ln(p, dr, sb, ps, l)
            emit_l0mlp(p, dr, sb, ps, l)
        if l == 0 and L0FFN_DEPTH == 2:
            emit_shiftcast(p, sb)
    p.bar()


def emit_inner_ln(p, dr, sb, ps, l):
    T, V, S, G, Y = p.T, p.V, p.S, p.G, p.Y
    p.wait(S, "v")
    for tc in range(4):
        c0 = 512 * tc
        if tc > 0:
            p.wait(T, "v")
        for c in range(NCHUNK):
            par = c % 2
            sq = bau(sb, 12, 512 * par, 512 * par + 512)
            if tc > 0 or c > 1:
                p.wait(S, "t", lag=1)
            p.inc(S.activation(out=sq, in_=hbs(sb, c, 1 + c0, 1 + c0 + 512), func=AF.Square), "s")
            p.wait(T, "s")
            T.matmul(ps["B"][:, 0:512], sb["ones_bf"][:, :], hbs(sb, c, 1 + c0, 1 + c0 + 512),
                     start=(c == 0), stop=(c == NCHUNK - 1))
            p.inc(T.matmul(ps["B"][:, 512:1024], sb["ones_bf"][:, :], sq,
                           start=(c == 0), stop=(c == NCHUNK - 1)), "t")
        p.wait(V, "t")
        mean = sb["stat"][:, c0:c0 + 512]
        rstd = sb["stat"][:, N + c0:N + c0 + 512]
        p.seq(V, V.tensor_scalar(mean, ps["B"][:, 0:512], 1.0 / 1024.0, None,
                                 ALU.mult, ALU.bypass), "v")
        p.seq(V, V.tensor_scalar(rstd, ps["B"][:, 512:1024], 1.0 / 1024.0, None,
                                 ALU.mult, ALU.bypass), "v")
        p.seq(V, V.tensor_tensor(sb["sc2"][:, 0:512], mean, mean, ALU.mult), "v")
        p.seq(V, V.tensor_tensor(rstd, rstd, sb["sc2"][:, 0:512], ALU.subtract), "v")
        p.inc(V.tensor_scalar(rstd, rstd, 1e-5, None, ALU.add, ALU.bypass), "v")
        p.wait(S, "v")
        p.seq(S, S.activation(out=rstd, in_=rstd, func=AF.Ln), "s")
        p.inc(S.activation(out=rstd, in_=rstd, func=AF.Exp, scale=-0.5), "s")
    p.wait(V, "s")
    for c in range(NCHUNK):
        for tc in range(4):
            c0 = 512 * tc
            mean = sb["stat"][:, c0:c0 + 512]
            rstd = sb["stat"][:, N + c0:N + c0 + 512]
            p.seq(V, V.tensor_tensor(sb["sc2"][:, 512:1024], hbs(sb, c, 1 + c0, 1 + c0 + 512),
                                     mean, ALU.subtract), "v")
            p.seq(V, V.tensor_tensor(sb["sc2"][:, 512:1024], sb["sc2"][:, 512:1024],
                                     rstd, ALU.mult), "v")
            p.seq(V, V.tensor_scalar(hbs(sb, c, 1 + c0, 1 + c0 + 512), sb["sc2"][:, 512:1024],
                                     sb["vt"][:, 12 * c + (5 if l == 0 else 8):
                                              12 * c + (5 if l == 0 else 8) + 1],
                                     sb["vt"][:, 12 * c + (6 if l == 0 else 9):
                                              12 * c + (6 if l == 0 else 9) + 1],
                                     ALU.mult, ALU.add), "v")


def emit_l0mlp(p, dr, sb, ps, l):
    T, V, S, G, Y = p.T, p.V, p.S, p.G, p.Y
    groups = [(0, 6), (6, 12), (12, 18), (18, 22)]

    def w2s(bb, c0, c1):
        return sb["ba"][:, 8 * CPAD + 1024 * bb + c0: 8 * CPAD + 1024 * bb + c1]

    w1_ck = [p.cnt["t"]] * 2
    ag_s_ck = [p.cnt["s"]] * 2
    ag_v_ck = [p.cnt["v"]] * 2
    sc2_v_ck = [p.cnt["v"]] * 2
    w2_t_ck = p.cnt["t"]
    for gi, (g0, g1) in enumerate(groups):
        for b in range(g0, g1):
            bu = 6 + b % 2
            if gi == 0 and b == g0:
                p.wait(G, "t", upto=getattr(p, "_oproj_t_ck", p.cnt["t"]))
            p.wait(G, "t", upto=w1_ck[b % 2])
            if b == g0:
                p.wait(G, "t", upto=w2_t_ck)
            p.dma(G, "dw", lambda: G.dma_start(
                out=bau(sb, bu, 0, 2048),
                in_=bass.AP(dr["w1pb"], 262144 * (NBLK * l + b),
                            [[256, 128], [32768, 8], [1, 256]])))
            p.dma(G, "dw", lambda: G.dma_start(
                out=w2s(b - g0, 0, 1024),
                in_=bass.AP(dr["w2pb"], 131072 * (NBLK * l + b), [[1024, 128], [1, 1024]])))
            p.wait(T, "dw")
            for tc in range(4):
                c0 = 512 * tc
                k = 4 * (b - g0) + tc
                par = k % 2
                pa = ps["B"] if par == 0 else ps["A"]
                p.wait(T, "s", upto=ag_s_ck[par])
                p.wait(T, "v", upto=ag_v_ck[par])
                mm = None
                for half in range(2):
                    for ic in range(NCHUNK):
                        mm = T.matmul(pa[:, 512 * half:512 * half + 512],
                                      bau(sb, bu, 256 * ic + 128 * half,
                                          256 * ic + 128 * half + 128),
                                      hbs(sb, ic, 1 + c0, 1 + c0 + 512),
                                      start=(ic == 0), stop=(ic == NCHUNK - 1))
                p.inc(mm, "t")
                t_after_mm = p.cnt["t"]
                p.wait(S, "t")
                p.wait(S, "v", upto=sc2_v_ck[par])
                p.inc(S.activation(out=sb["sc2"][:, 512 * par:512 * par + 512],
                                   in_=pa[:, 512:1024], func=AF.Gelu,
                                   bias=sb["b1t"][:, 44 * l + 2 * b + 1:44 * l + 2 * b + 2]), "s")
                ag_s_ck[par] = p.cnt["s"]
                p.wait(V, "s")
                if tc == 0:
                    p.wait(V, "t", upto=w2_t_ck)
                p.inc(V.scalar_tensor_tensor(bau(sb, b - g0, c0, c0 + 512), pa[:, 0:512],
                                             sb["b1t"][:, 44 * l + 2 * b:44 * l + 2 * b + 1],
                                             sb["sc2"][:, 512 * par:512 * par + 512],
                                             ALU.add, ALU.mult), "v")
                ag_v_ck[par] = p.cnt["v"]
                sc2_v_ck[par] = p.cnt["v"]
            w1_ck[b % 2] = t_after_mm
        p.wait(T, "v")
        rot_v_ck = [p.cnt["v"]] * 3
        banks = [(ps["A"], 0), (ps["A"], 1024), (ps["B"], 0)]
        for tc in range(4):
            c0 = 512 * tc
            for dq in range(4):
                rot = (4 * tc + dq) % 3
                pa, po = banks[rot]
                p.wait(T, "v", upto=rot_v_ck[rot])
                mm = None
                for b in range(g0, g1):
                    for dm in range(2):
                        dmg = 2 * dq + dm
                        mm = T.matmul(pa[:, po + 512 * dm:po + 512 * dm + 512],
                                      w2s(b - g0, 128 * dmg, 128 * dmg + 128),
                                      bau(sb, b - g0, c0, c0 + 512),
                                      start=(b == g0), stop=(b == g1 - 1))
                p.inc(mm, "t")
                p.wait(V, "t")
                for dm in range(2):
                    dmg = 2 * dq + dm
                    if g0 == 0:
                        lastv = V.scalar_tensor_tensor(
                            fts(sb, dmg, 1 + c0, 1 + c0 + 512), pa[:, po + 512 * dm:po + 512 * dm + 512],
                            sb["vt"][:, 12 * dmg + (7 if l == 0 else 10):
                                     12 * dmg + (7 if l == 0 else 10) + 1],
                            fts(sb, dmg, 1 + c0, 1 + c0 + 512), ALU.add, ALU.add)
                    else:
                        lastv = V.tensor_tensor(fts(sb, dmg, 1 + c0, 1 + c0 + 512),
                                                fts(sb, dmg, 1 + c0, 1 + c0 + 512),
                                                pa[:, po + 512 * dm:po + 512 * dm + 512], ALU.add)
                p.inc(lastv, "v")
                rot_v_ck[rot] = p.cnt["v"]
        w2_t_ck = p.cnt["t"]
        ag_v_ck = [p.cnt["v"], p.cnt["v"]]


def emit_l0att(p, dr, sb, ps):
    T, V, S, G, Y = p.T, p.V, p.S, p.G, p.Y
    p.wait(V, "dx")

    def wkv_slice(col0, ncol):
        return bass.AP(dr["l0a_wkv"], col0, [[2048, 128], [262144, 8], [1, ncol]])

    def wq_slice(col0, ncol):
        return bass.AP(dr["l0a_wq"], col0, [[1024, 128], [131072, 8], [1, ncol]])

    def wo_slice(row0):
        return bass.AP(dr["l0a_wo"], row0 * 1024, [[1024, 128], [131072, 2], [1, 1024]])

    def v4s(kc, hh):
        return bau(sb, 0 + kc // 8, 256 * (kc % 8) + 64 * hh, 256 * (kc % 8) + 64 * hh + 64)

    def pts_pv(kc):
        return bass.AP(sb["ba"], 8 * CPAD + 128 * kc, [[BAU * CPAD, 128], [CPAD, 2], [1, 128]])

    for g in range(4):
        p.wait(G, "t")
        p.dma(G, "dw", lambda: G.dma_start(out=bau(sb, 12, 0, 2048),
                                           in_=wkv_slice(1024 + 256 * g, 256)))
        p.wait(T, "dw")
        v4_ck = [p.cnt["v"]] * 4
        for tt in range(16):
            mm = None
            p.wait(T, "v", upto=v4_ck[tt % 4])
            for ic in range(NCHUNK):
                mm = T.matmul(ps["A"][:, 512 * (tt % 4):512 * (tt % 4) + 256],
                              hbs(sb, ic, 1 + 128 * tt, 1 + 128 * tt + 128),
                              bau(sb, 12, 256 * ic, 256 * ic + 256),
                              start=(ic == 0), stop=(ic == NCHUNK - 1))
            p.inc(mm, "t")
            p.wait(V, "t")
            p.inc(V.tensor_copy(bau(sb, tt // 8, 256 * (tt % 8), 256 * (tt % 8) + 256),
                                ps["A"][:, 512 * (tt % 4):512 * (tt % 4) + 256]), "v")
            v4_ck[tt % 4] = p.cnt["v"]
        for hp in range(2):
            h0 = 4 * g + 2 * hp
            p.wait(G, "t")
            p.dma(G, "dw", lambda: G.dma_start(out=bau(sb, 10, 0, 1024),
                                               in_=wq_slice(64 * h0, 128)))
            p.dma(G, "dw", lambda: G.dma_start(out=bau(sb, 10, 1024, 2048),
                                               in_=wkv_slice(64 * h0, 128)))
            p.wait(T, "dw"); p.wait(T, "s"); p.wait(T, "v")
            for which, unit, scl in ((0, 4, 0.125), (1, 5, 1.0)):
                for tc in range(4):
                    c0 = 512 * tc
                    mm = None
                    for ic in range(NCHUNK):
                        mm = T.matmul(ps["B"][:, 512 * (tc % 2):512 * (tc % 2) + 512],
                                      bau(sb, 10, 1024 * which + 128 * ic,
                                          1024 * which + 128 * ic + 128),
                                      hbs(sb, ic, 1 + c0, 1 + c0 + 512),
                                      start=(ic == 0), stop=(ic == NCHUNK - 1))
                    p.inc(mm, "t")
                    p.wait(S, "t")
                    p.inc(S.activation(out=bau(sb, unit, c0, c0 + 512),
                                       in_=ps["B"][:, 512 * (tc % 2):512 * (tc % 2) + 512],
                                       func=AF.Identity, scale=scl), "s")
                    p.wait(T, "s", lag=1)
            p.wait(T, "s")
            for hh2 in range(2):
                h = h0 + hh2
                hh = 2 * hp + hh2
                b0 = 64 * (h % 2)
                hkoff = 448 * (h % 2)
                p.wait(G, "t")
                p.dma(G, "dw", lambda: G.dma_start(
                    out=bau(sb, 12, hkoff, hkoff + 308),
                    in_=bass.AP(dr["gbuf"], 576 * h, [[1, 128], [1, 308]])))
                p.wait(T, "dw")
                ck = p.__dict__.setdefault("_l0att_ck", {
                    "ttr_v": [0, 0], "exp_s": [0, 0, 0, 0], "tr_t": [0, 0],
                    "oe_v": 0, "nm_s": [0, 0], "exp_par_s": [0, 0], "pv_t": [0, 0],
                    "pte": [0, 0]})

                def scbuf(k):
                    j = k % 4
                    t_ = sb["stat"] if j < 2 else sb["xcb"]
                    return t_[:, 2048 * (j % 2):2048 * (j % 2) + 2048]

                def mxcol(k, which):
                    return sb["sc1"][:, 32 + 8 * (k % 2) + which:33 + 8 * (k % 2) + which]

                def nmc(k, j):
                    return sb["sc1"][:, 16 + 8 * (k % 2) + j:17 + 8 * (k % 2) + j]

                t_sc = {}

                def st_t1(qt):
                    i0 = 128 * qt
                    sb0 = i0 - 90
                    slo = max(0, sb0)
                    shi = min(N, sb0 + 308)
                    for half in range(2):
                        hlo = 1024 * half
                        p.wait(T, "v", upto=ck["ttr_v"][half])
                        mm = None
                        for nb in range(2):
                            b_lo = hlo + 512 * nb
                            bank_strip = max(slo, b_lo) < min(shi, b_lo + 512)
                            mm = T.matmul(ps["A"][:, b_lo:b_lo + 512],
                                          bau(sb, 4, i0, i0 + 128)[b0:b0 + 64, :],
                                          bau(sb, 5, b_lo, b_lo + 512)[b0:b0 + 64, :],
                                          start=True, stop=not bank_strip)
                            if bank_strip:
                                a, b = max(slo, b_lo), min(shi, b_lo + 512)
                                mm = T.matmul(ps["A"][:, a:b], sb["j_bf"][:, :],
                                              bau(sb, 12, hkoff + a - sb0, hkoff + b - sb0),
                                              start=False, stop=True)
                        p.inc(mm, "t")
                        t_sc[(qt, half)] = p.cnt["t"]

                def st_v1(qt):
                    for half in range(2):
                        hlo = 1024 * half
                        p.wait(V, "t", upto=t_sc.pop((qt, half)))
                        p.wait(V, "s", upto=ck["exp_s"][qt % 4])
                        p.wait(V, "s", upto=ck["nm_s"][qt % 2])
                        p.inc(V.tensor_scalar(scbuf(qt)[:, hlo:hlo + 1024],
                                              ps["A"][:, hlo:hlo + 1024], 0.0, None,
                                              ALU.add, ALU.max, accum_out=mxcol(qt, half)), "v")
                        ck["ttr_v"][half] = p.cnt["v"]
                    nm, bl = nmc(qt, 0), nmc(qt, 2)
                    p.wait(V, "v")
                    p.seq(V, V.tensor_tensor(nm, mxcol(qt, 0), mxcol(qt, 1), ALU.max), "v")
                    p.seq(V, V.tensor_scalar(nm, nm, -1.0, None, ALU.mult, ALU.bypass), "v")
                    p.inc(V.tensor_scalar(bl, nm, sb["dct"][:, h:h + 1], None,
                                          ALU.add, ALU.bypass), "v")
                    ck.setdefault("bias_v", {})[qt] = p.cnt["v"]

                def st_s(qt):
                    par = qt % 2
                    lw = max(0, 128 * qt - 90)
                    pu = 6 + par
                    sums = [sb["sc1"][:, 19 + 8 * par + j:20 + 8 * par + j] for j in range(2)]
                    cuts = ([(0, lw, nmc(qt, 2))] if lw > 0 else []) + [(lw, N, nmc(qt, 0))]
                    p.wait(S, "v", upto=ck["bias_v"].pop(qt))
                    p.wait(S, "t", upto=ck["tr_t"][par])
                    p.wait(S, "v", upto=ck.setdefault("sums_v", [0, 0])[par])
                    lasts = None
                    for j, (a, b, bias) in enumerate(cuts):
                        lasts = S.activation(out=bau(sb, pu, a, b), in_=scbuf(qt)[:, a:b],
                                             func=AF.Exp, bias=bias, accum_out=sums[j])
                    p.inc(lasts, "s")
                    ck["exp_s"][qt % 4] = p.cnt["s"]
                    ck["nm_s"][par] = p.cnt["s"]
                    ck["exp_par_s"][par] = p.cnt["s"]
                    ck.setdefault("ncuts", {})[qt] = len(cuts)

                def st_v2(qt):
                    par = qt % 2
                    pu = 6 + par
                    sums = [sb["sc1"][:, 19 + 8 * par + j:20 + 8 * par + j] for j in range(2)]
                    rc = sb["sc1"][:, 22 + 8 * par:23 + 8 * par]
                    p.wait(V, "s", upto=ck["exp_par_s"][par])
                    if ck["ncuts"].pop(qt) > 1:
                        p.seq(V, V.tensor_tensor(sums[0], sums[0], sums[1], ALU.add), "v")
                    p.seq(V, V.reciprocal(rc, sums[0]), "v")
                    ck.setdefault("sums_v", [0, 0])[par] = p.cnt["v"]
                    p.inc(V.tensor_scalar(bau(sb, pu, 0, N), bau(sb, pu, 0, N), rc,
                                          None, ALU.mult, ALU.bypass), "v")
                    ck.setdefault("norm_v", {})[qt] = p.cnt["v"]

                def st_t2(qt):
                    par = qt % 2
                    qp = qt // 2
                    pu = 6 + par
                    p.wait(T, "v", upto=ck["norm_v"].pop(qt))
                    for bt in range(2):
                        mm = None
                        for k8 in range(8):
                            kc = 8 * bt + k8
                            mm = T.transpose(ps["T"][:, 1024 * bt + 128 * k8:
                                                     1024 * bt + 128 * k8 + 128],
                                             bau(sb, pu, 128 * kc, 128 * kc + 128),
                                             sb["id_bf"][:, :])
                        p.inc(mm, "t")
                        t_tr = p.cnt["t"]
                        if bt == 0:
                            p.wait(V, "t", upto=t_tr)
                            p.wait(V, "t", upto=ck["pv_t"][par])
                            p.inc(V.tensor_copy(bau(sb, 8 + par, 0, 1024),
                                                ps["T"][:, 0:1024]), "v")
                            p.wait(T, "v", upto=p.cnt["v"])
                        else:
                            p.wait(S, "t", upto=t_tr)
                            p.wait(S, "t", upto=ck["pv_t"][par])
                            p.inc(S.activation(out=bau(sb, 8 + par, 1024, 2048),
                                               in_=ps["T"][:, 1024:2048], func=AF.Copy), "s")
                            p.wait(T, "s", upto=p.cnt["s"])
                    ck["tr_t"][par] = p.cnt["t"]
                    if par == 1:
                        p.wait(T, "v", upto=ck["oe_v"])
                        mm = None
                        for kc in range(16):
                            mm = T.matmul(ps["B"][b0:b0 + 64, 0:256], v4s(kc, hh),
                                          pts_pv(kc), start=(kc == 0), stop=(kc == 15))
                        p.inc(mm, "t")
                        ck["pv_t"][0] = p.cnt["t"]
                        ck["pv_t"][1] = p.cnt["t"]
                        p.wait(V, "t", upto=p.cnt["t"])
                        p.inc(V.tensor_copy(bau(sb, 2 + hh // 2, 256 * qp, 256 * qp + 256)
                                            [b0:b0 + 64, :], ps["B"][b0:b0 + 64, 0:256]), "v")
                        ck["oe_v"] = p.cnt["v"]

                st_t1(0); st_v1(0); st_s(0); st_t1(1); st_v1(1)
                for qt in range(16):
                    if qt + 2 < 16:
                        st_t1(qt + 2)
                    st_v2(qt)
                    st_t2(qt)
                    if qt + 2 < 16:
                        st_v1(qt + 2)
                    if qt + 1 < 16:
                        st_s(qt + 1)

        p.wait(G, "t")
        p.dma(G, "dw", lambda: G.dma_start(out=bau(sb, 11, 0, 2048), in_=wo_slice(256 * g)))
        p.wait(T, "dw"); p.wait(T, "v"); p.wait(T, "s")
        wo_ck = [p.cnt["v"]] * 4
        for dm in range(8):
            for t4 in range(4):
                c0 = 512 * t4
                p.wait(T, "v", upto=wo_ck[t4])
                mm = None
                for j in range(2):
                    mm = T.matmul(ps["A"][:, c0:c0 + 512],
                                  bau(sb, 11, 1024 * j + 128 * dm, 1024 * j + 128 * dm + 128),
                                  bau(sb, 2 + j, c0, c0 + 512),
                                  start=(j == 0), stop=(j == 1))
                p.inc(mm, "t")
                p.wait(V, "t")
                if g == 0:
                    lastv = V.tensor_scalar(fts(sb, dm, 1 + c0, 1 + c0 + 512),
                                            ps["A"][:, c0:c0 + 512],
                                            sb["vt"][:, 12 * dm + 0:12 * dm + 1],
                                            None, ALU.add, ALU.bypass)
                else:
                    lastv = V.tensor_tensor(fts(sb, dm, 1 + c0, 1 + c0 + 512),
                                            fts(sb, dm, 1 + c0, 1 + c0 + 512),
                                            ps["A"][:, c0:c0 + 512], ALU.add)
                p.inc(lastv, "v")
                wo_ck[t4] = p.cnt["v"]
    p.bar()



_NC_CACHE = {}


def kernel(**inputs):
    sh, percore = host_prep(inputs)
    key = PHASES
    if key not in _NC_CACHE:
        _NC_CACHE[key] = build_nc()
    nc = _NC_CACHE[key]
    in_maps = []
    for c in range(8):
        m = dict(sh)
        m.update(percore[c])
        in_maps.append(m)
    res = run_bass_kernel_spmd(nc, in_maps, core_ids=list(range(8)))
    out = np.stack([np.ascontiguousarray(np.asarray(res.results[c]["yt"]).T) for c in range(8)])
    return out.astype(np.float32)
